# version 03
# baseline (speedup 1.0000x reference)
import contextlib
import numpy as np
import ml_dtypes
import concourse.bass as bass
import concourse.mybir as mybir

F32 = mybir.dt.float32
BF16 = mybir.dt.bfloat16
AF = mybir.ActivationFunctionType
ALU = mybir.AluOpType
AXX = mybir.AxisListType.X

D = 1024
NH = 16
HD = 64
ASCALE = HD ** (-0.5)
DI = 2048
NE0 = 16
F0 = 512
NE1 = 8
F1 = 4096
DECAY = 0.9
NEG = -30000.0

DMA_CHANS = {"wa": "g", "wb": "g", "la": "y", "lb": "y", "sa": "y", "sb": "y",
             "cc": "g"}
ENG_SEM = {"t": "T", "v": "V", "s": "S", "g": "G", "y": "Y"}


class Prog:
    def __init__(self):
        self.ops = []
        self.cnt = {}
        self.writers = {}
        self.readers = {}
        self.waited = {}

    def _waits(self, eng, toks):
        best = {}
        for (sem, val) in toks:
            if eng == "t" and sem == "T":
                continue
            if val > best.get(sem, 0):
                best[sem] = val
        waits = []
        for sem, val in best.items():
            if self.waited.get((eng, sem), 0) < val:
                waits.append((sem, val))
                self.waited[(eng, sem)] = val
        return waits

    def op(self, eng, emit, reads=(), writes=(), sem=None, amt=1):
        sem = sem or ENG_SEM[eng]
        if eng in ("v", "s"):
            pbanks = {buf[1] for buf in tuple(reads) + tuple(writes)
                      if isinstance(buf, tuple) and buf and buf[0] == "ps"}
            if pbanks:
                writes = tuple(writes) + tuple(("plock", b) for b in sorted(pbanks))
        toks = []
        for buf in reads:
            toks += self.writers.get(buf, [])
        for buf in writes:
            toks += self.writers.get(buf, [])
            toks += self.readers.get(buf, [])
        waits = self._waits(eng, toks)
        self.cnt[sem] = self.cnt.get(sem, 0) + amt
        tok = (sem, self.cnt[sem])
        for buf in writes:
            self.writers[buf] = [tok]
            self.readers[buf] = []
        for buf in reads:
            self.readers.setdefault(buf, []).append(tok)
        self.ops.append((eng, waits, emit, sem, amt))
        return tok

    def dma(self, chan, out_ap, in_ap, reads=(), writes=()):
        return self.op(DMA_CHANS[chan],
                       lambda e: e.dma_start(out=out_ap, in_=in_ap),
                       reads=tuple(reads) + (("chan", chan),),
                       writes=tuple(writes) + (("chan", chan),),
                       sem=chan, amt=16)

    def rekey(self, old_keys, new_keys):
        toks = []
        for k in old_keys:
            toks += self.writers.pop(k, [])
            toks += self.readers.pop(k, [])
        for k in new_keys:
            self.writers[k] = self.writers.get(k, []) + toks

    def barrier(self, eng, bufs):
        toks = []
        for k in bufs:
            toks += self.writers.get(k, [])
            toks += self.readers.get(k, [])
        waits = self._waits(eng, toks)
        if waits:
            self.ops.append((eng, waits, lambda e: None, ENG_SEM[eng], 0))

    def emit_all(self, blk, sems):
        streams = {}
        for rec in self.ops:
            streams.setdefault(rec[0], []).append(rec)

        def run(e, engname):
            for (_eng, waits, emit, sem, amt) in streams[engname]:
                for (s_, v_) in waits:
                    e.wait_ge(sems[s_], v_)
                ins = emit(e)
                if ins is not None and amt > 0:
                    ins.then_inc(sems[sem], amt)

        for name, meth in (("t", blk.tensor), ("v", blk.vector), ("s", blk.scalar),
                           ("g", blk.gpsimd), ("y", blk.sync)):
            if name in streams:
                meth(lambda e, name=name: run(e, name))


class _Done(Exception):
    pass


def build(T=2048, taps=(), world=8, upto="E"):
    assert T % 512 == 0
    NT = T // 128
    NST = T // 512
    TH = T // 2
    NSL = TH // 128
    TW = T + 16
    W5 = min(512, TH)
    NW5 = max(1, TH // 512)

    nc = bass.Bass("TRN2", target_bir_lowering=False)
    dt = nc.dram_tensor

    x_in = dt("x", [TW, D], F32, kind="ExternalInput").ap()
    Wqkv = dt("L0_attn_Wqkv", [D, 3 * D], F32, kind="ExternalInput").ap()
    Wproj = dt("L0_attn_Wproj", [D, D], F32, kind="ExternalInput").ap()
    bproj = dt("L0_attn_bproj", [D], F32, kind="ExternalInput").ap()
    Wr = dt("L0_ffn_Wr", [D, NE0], F32, kind="ExternalInput").ap()
    Wg0 = dt("L0_ffn_Wg", [NE0, D, F0], F32, kind="ExternalInput").ap()
    Wu0 = dt("L0_ffn_Wu", [NE0, D, F0], F32, kind="ExternalInput").ap()
    Wd0 = dt("L0_ffn_Wd", [NE0, F0, D], F32, kind="ExternalInput").ap()
    Sg = dt("L0_ffn_Sg", [D, D], F32, kind="ExternalInput").ap()
    Su = dt("L0_ffn_Su", [D, D], F32, kind="ExternalInput").ap()
    Sd = dt("L0_ffn_Sd", [D, D], F32, kind="ExternalInput").ap()
    Win = dt("L1_attn_Win", [D, DI], F32, kind="ExternalInput").ap()
    bin_ = dt("L1_attn_bin_", [DI], F32, kind="ExternalInput").ap()
    Bp = dt("L1_attn_Bp", [DI, 16], F32, kind="ExternalInput").ap()
    Cp = dt("L1_attn_Cp", [16, DI], F32, kind="ExternalInput").ap()
    Dp = dt("L1_attn_Dp", [DI], F32, kind="ExternalInput").ap()
    Wout = dt("L1_attn_Wout", [DI, D], F32, kind="ExternalInput").ap()
    bout = dt("L1_attn_bout", [D], F32, kind="ExternalInput").ap()
    Wg1 = dt("L1_ffn_Wg", [D, NE1], F32, kind="ExternalInput").ap()
    bg1 = dt("L1_ffn_bg", [NE1], F32, kind="ExternalInput").ap()
    W1 = dt("L1_ffn_W1", [NE1, D, F1], F32, kind="ExternalInput").ap()
    b1 = dt("L1_ffn_b1", [NE1, F1], F32, kind="ExternalInput").ap()
    W2 = dt("L1_ffn_W2", [NE1, F1, D], F32, kind="ExternalInput").ap()
    b2 = dt("L1_ffn_b2", [NE1, D], F32, kind="ExternalInput").ap()
    identf = dt("identf", [128, 128], F32, kind="ExternalInput").ap()
    identb = dt("identb", [128, 128], BF16, kind="ExternalInput").ap()
    maskp = dt("maskp", [3, 128, 1152], BF16, kind="ExternalInput").ap()
    dpow_in = dt("dpow", [128, T], F32, kind="ExternalInput").ap()
    fsel_in = dt("fsel", [world, 1], F32, kind="ExternalInput").ap()

    out = dt("out", [T, D], F32, kind="ExternalOutput").ap()
    tap_shapes = {"xnT": ([128, 8, TW], BF16), "logits0": ([128, NT * NE0], F32),
                  "logits1": ([128, NT * NE1], F32),
                  "s0": ([T, D], F32), "x1": ([T, D], F32),
                  "acc0": ([128, NSL * D], F32), "xi": ([16, 128, T], F32),
                  "ypart": ([16, 128, T], F32), "s1": ([T, D], F32),
                  "cw0": ([128, NT * NE0], F32), "cw1": ([128, NT * NE1], F32),
                  "fprev": ([128, 16], F32), "rvec": ([128, 16], F32)}
    tap_aps = {nm: dt("tap_" + nm, shp, dty, kind="ExternalOutput").ap()
               for nm, (shp, dty) in tap_shapes.items() if nm in taps}

    s_scr = dt("s_scr", [T, D], F32)
    x1_scr = dt("x1_scr", [T, D], F32)
    yp_scr = dt("yp_scr", [16, 128, T], F32)
    cc_in = dt("cc_in", [1, DI], F32)
    cc_out = dt("cc_out", [world, DI], F32)
    fr_scr = dt("fr_scr", [1, DI], F32)

    with contextlib.ExitStack() as es:
        E = es.enter_context
        xnT = E(nc.sbuf_tensor("xnT", [128, 8 * TW], BF16))
        arw = E(nc.sbuf_tensor("arw", [128, 32768], BF16))
        big = E(nc.sbuf_tensor("big", [128, 8192], F32))
        aux = E(nc.sbuf_tensor("aux", [128, 16896], BF16))
        xtile = E(nc.sbuf_tensor("xtile", [128, D], F32))
        stile = E(nc.sbuf_tensor("stile", [128, D], F32))
        ntile = E(nc.sbuf_tensor("ntile", [128, D], F32))
        masks = E(nc.sbuf_tensor("masks", [128, 3 * 1152], BF16))
        stats = E(nc.sbuf_tensor("stats", [128, 64], F32))
        logit0 = E(nc.sbuf_tensor("logit0", [128, NT * NE0], F32))
        logit1 = E(nc.sbuf_tensor("logit1", [128, NT * NE1], F32))
        cwtm0 = E(nc.sbuf_tensor("cwtm0", [128, NT * NE0], F32))
        cwtm1 = E(nc.sbuf_tensor("cwtm1", [128, NT * NE1], F32))
        cwrows = E(nc.sbuf_tensor("cwrows", [128, 2048], F32))
        rows2 = E(nc.sbuf_tensor("rows2", [128, 1536], F32))
        bb = E(nc.sbuf_tensor("bb", [128, 1024], F32))
        smallv = E(nc.sbuf_tensor("smallv", [128, 176], F32))
        wrg = E(nc.sbuf_tensor("wrg", [128, 192], F32))
        tgt = E(nc.sbuf_tensor("tgt", [128, 512], BF16))
        identf_sb = E(nc.sbuf_tensor("identf_sb", [128, 128], F32))
        identb_sb = E(nc.sbuf_tensor("identb_sb", [128, 128], BF16))
        fsel_sb = E(nc.sbuf_tensor("fsel_sb", [world, 1], F32))
        ps = E(nc.psum_tensor("ps", [128, 4096], F32))

        psb = ps[:, :].bitcast(BF16)
        auxf = aux[:, :].bitcast(F32)
        blk = E(nc.Block())
        P = Prog()

        SV_BIN, SV_DP, SV_R, SV_FP, SV_FR, SV_B1, SV_CW = 0, 16, 32, 48, 64, 96, 128
        ST_NM, ST_SQ, ST_IV, ST_M1, ST_M2, ST_DEN = 0, 1, 2, 3, 4, 5
        R_CP, R_FT, R_FALL = 32, 96, 0
        bgrow_sb = stats[:, 48:56]

        def _finish():
            sem_names = set(list(P.cnt.keys()) + ["T", "V", "S", "G", "Y"])
            P.barrier("y", [k for k in P.writers if k[0] == "outw"]
                      + [("chan", c) for c in ("sa", "sb", "wa", "wb", "la", "lb", "cc")
                         if P.cnt.get(c)])
            sems = {k: E(nc.semaphore("sem_" + k)) for k in sorted(sem_names)}
            P.emit_all(blk, sems)
            return nc, dict(tap_aps=tap_aps, n_ops=len(P.ops))

        P.dma("la", identf_sb[:, :], identf, writes=[("idf",)])
        P.dma("lb", identb_sb[:, :], identb, writes=[("idb",)])
        P.dma("la", masks[:, :].rearrange("p (v n) -> p v n", v=3),
              maskp.rearrange("v p n -> p v n"), writes=[("masks",)])
        P.dma("lb", fsel_sb[:, :], fsel_in, writes=[("fsel",)])
        P.op("v", lambda e: e.memset(rows2[0:1, 0:128], 1.0), writes=[("r2", "ones")])
        for k in range(8):
            P.dma("la", wrg[:, 16 * k:16 * (k + 1)], Wr[128 * k:128 * (k + 1), :],
                  writes=[("wrg", 0)])
            P.dma("lb", wrg[:, 128 + 8 * k:128 + 8 * (k + 1)],
                  Wg1[128 * k:128 * (k + 1), :], writes=[("wrg", 1)])

        def bias_bcast(src_dram, chan="la"):
            P.dma(chan, rows2[0:1, 512:1536], src_dram.unsqueeze(0) if src_dram.ndim == 1 else src_dram,
                  reads=[("r2", "bstage")], writes=[("r2", "bstage")])

            def mm(e):
                for n2 in range(2):
                    ins = e.matmul(ps[:, 2048 + 512 * n2:2048 + 512 * (n2 + 1)],
                                   rows2[0:1, 0:128],
                                   rows2[0:1, 512 + 512 * n2:512 + 512 * (n2 + 1)],
                                   start=True, stop=True)
                return ins
            P.op("t", mm, reads=[("r2", "ones"), ("r2", "bstage")],
                 writes=[("ps", 4), ("ps", 5)])
            P.op("v", lambda e: e.tensor_copy(bb[:, :], ps[:, 2048:3072]),
                 reads=[("ps", 4), ("ps", 5)], writes=[("bb",)])

        def emit_ln(rows, xsrc, dst_off, lg_kind, lg_col, chan):
            P.dma(chan, xtile[0:rows, :], xsrc, writes=[("xt",)])
            P.op("v", lambda e: e.reduce_sum(stats[0:rows, ST_NM:ST_NM + 1],
                                             xtile[0:rows, :], AXX),
                 reads=[("xt",)], writes=[("stats", "nm")])
            P.op("v", lambda e: e.tensor_scalar(stats[0:rows, ST_NM:ST_NM + 1],
                                                stats[0:rows, ST_NM:ST_NM + 1],
                                                -1.0 / D, None, ALU.mult),
                 reads=[("stats", "nm")], writes=[("stats", "nm")])
            P.op("s", lambda e: e.activation(stile[0:rows, :], xtile[0:rows, :],
                                             AF.Square,
                                             bias=stats[0:rows, ST_NM:ST_NM + 1],
                                             scale=1.0,
                                             accum_out=stats[0:rows, ST_SQ:ST_SQ + 1]),
                 reads=[("xt",), ("stats", "nm")], writes=[("st",), ("stats", "sq")])
            P.op("v", lambda e: e.tensor_scalar(stats[0:rows, ST_IV:ST_IV + 1],
                                                stats[0:rows, ST_SQ:ST_SQ + 1],
                                                1.0 / D, 1e-5, ALU.mult, ALU.add),
                 reads=[("stats", "sq")], writes=[("stats", "iv")])
            P.op("s", lambda e: e.activation(stats[0:rows, ST_IV:ST_IV + 1],
                                             stats[0:rows, ST_IV:ST_IV + 1], AF.Sqrt),
                 reads=[("stats", "iv")], writes=[("stats", "iv")])
            P.op("v", lambda e: e.reciprocal(stats[0:rows, ST_IV:ST_IV + 1],
                                             stats[0:rows, ST_IV:ST_IV + 1]),
                 reads=[("stats", "iv")], writes=[("stats", "iv")])
            P.op("v", lambda e: e.tensor_scalar(ntile[0:rows, :], xtile[0:rows, :],
                                                stats[0:rows, ST_NM:ST_NM + 1],
                                                stats[0:rows, ST_IV:ST_IV + 1],
                                                ALU.add, ALU.mult),
                 reads=[("xt",), ("stats", "nm"), ("stats", "iv")], writes=[("nt",)])

            def tr(e):
                for k in range(8):
                    col = 512 * (k % 2) + 128 * (k // 2)
                    ins = e.transpose(ps[:, col:col + rows],
                                      ntile[0:rows, 128 * k:128 * (k + 1)],
                                      identf_sb[0:rows, 0:rows])
                return ins
            P.op("t", tr, reads=[("nt",), ("idf",)], writes=[("ps", 0), ("ps", 1)])
            xv = xnT[:, :].rearrange("p (k n) -> p k n", k=8)

            def ev(e):
                for k in range(8):
                    col = 512 * (k % 2) + 128 * (k // 2)
                    ins = e.tensor_copy(xv[:, k, dst_off:dst_off + rows],
                                        ps[:, col:col + rows])
                return ins
            P.op("v", ev, reads=[("ps", 0), ("ps", 1)], writes=[("xnT",)])

            if lg_kind is not None:
                which, ne = lg_kind
                skip_mm = which.startswith("skip")
                which = which.replace("skip", "")
                wcol = 0 if which == "l0" else 128
                lg_sb = logit0 if which == "l0" else logit1

                def evf(e):
                    for k in range(8):
                        col = 512 * (k % 2) + 128 * (k // 2)
                        ins = e.copy(auxf[:, 128 * k:128 * (k + 1)],
                                     ps[:, col:col + rows])
                    return ins
                P.op("s", evf, reads=[("ps", 0), ("ps", 1)], writes=[("aux", "xf")])
                if skip_mm:
                    return

                def rmm(e):
                    for k in range(8):
                        ins = e.matmul(ps[:, 1024:1024 + ne],
                                       auxf[:, 128 * k:128 * (k + 1)],
                                       wrg[:, wcol:wcol + 8 * ne]
                                       .rearrange("p (k j) -> p k j", k=8)[:, k, :],
                                       start=(k == 0), stop=(k == 7))
                    return ins
                P.op("t", rmm, reads=[("aux", "xf"), ("wrg", 0 if which == "l0" else 1)],
                     writes=[("ps", 2)])
                P.op("v", lambda e: e.tensor_copy(lg_sb[:, lg_col:lg_col + ne],
                                                  ps[:, 1024:1024 + ne]),
                     reads=[("ps", 2)], writes=[(which,)])

        emit_ln(8, x_in[0:8, :], 0, None, 0, "la")
        for t_ in range(NT):
            lgk = (None if upto == "A0" else
                   (("skipl0", NE0) if upto == "A1" else ("l0", NE0)))
            emit_ln(128, x_in[8 + 128 * t_:8 + 128 * (t_ + 1), :], 8 + 128 * t_,
                    lgk, t_ * NE0, "la" if t_ % 2 == 0 else "lb")
        emit_ln(8, x_in[8 + T:16 + T, :], 8 + T, None, 0, "lb")

        if "xnT" in taps:
            P.dma("sa", tap_aps["xnT"], xnT[:, :].rearrange("p (k n) -> p k n", k=8),
                  reads=[("xnT",)])
        if "logits0" in taps:
            P.dma("sb", tap_aps["logits0"], logit0[:, 0:NT * NE0], reads=[("l0",)])

        if upto in ("A", "A0", "A1"):
            return _finish()
        WQKV, WPROJ = 0, 24576
        wq_v = arw[:, WQKV:WQKV + 24576].rearrange("p (k n) -> p k n", k=8)
        wp_v = arw[:, WPROJ:WPROJ + 8192].rearrange("p (k n) -> p k n", k=8)
        for j, ch in ((0, "wa"), (1, "wb"), (2, "wa")):
            P.dma(ch, wq_v[:, :, 1024 * j:1024 * (j + 1)],
                  Wqkv.rearrange("(k p) n -> p k n", p=128)[:, :, 1024 * j:1024 * (j + 1)],
                  writes=[("arw", "qkv", j)])
        P.dma("wb", wp_v, Wproj.rearrange("(k p) n -> p k n", p=128),
              writes=[("arw", "proj")])
        bias_bcast(bproj, "la")

        AQ, AK, AV_, AT2, AA, AO = 0, 4096, 8320, 13440, 14464, 15616
        qv = aux[:, AQ:AQ + 4096].rearrange("p (k n) -> p k n", k=8)
        kv = aux[:, AK:AK + 4224].rearrange("p (k n) -> p k n", k=8)
        vv = aux[:, AV_:AV_ + 5120].rearrange("p (j n) -> p j n", j=5)
        at2sb = aux[0:16, AT2:AT2 + 1024]
        asb = aux[:, AA:AA + 1152]
        at1sb = aux[:, AA:AA + 1024]
        osb = aux[:, AO:AO + 1024]
        xkv = xnT[:, :].rearrange("p (k n) -> p k n", k=8)
        mk3 = masks[:, :].rearrange("p (v s x) -> p v s x", v=3, s=8)
        AUXB = ([("auxq", m) for m in range(8)] + [("auxk", m) for m in range(8)]
                + [("auxv", j) for j in range(5)]
                + [("auxa", h) for h in range(8)] + [("auxat1",), ("auxat2",),
                   ("auxo", 0), ("auxo", 1)])
        P.rekey([("aux", "xf")], AUXB)

        BLV = {"Bs0": -1, "Bs": 0, "Bsm": 1, "Bat": 2, "Bav": 3}.get(upto, 4)
        for st in range(NST):
            s0c = 512 * st
            for m in range(8):
                bk = m % 2

                def qmm(e, m=m, bk=bk, s0c=s0c):
                    for k in range(8):
                        ins = e.matmul(ps[:, 512 * bk:512 * bk + 512],
                                       wq_v[:, k, 128 * m:128 * (m + 1)],
                                       xkv[:, k, 8 + s0c:8 + s0c + 512],
                                       start=(k == 0), stop=(k == 7))
                    return ins
                P.op("t", qmm, reads=[("arw", "qkv", 0), ("xnT",)], writes=[("ps", bk)])
                P.op("s", lambda e, m=m, bk=bk: e.activation(
                        qv[:, m, :], ps[:, 512 * bk:512 * bk + 512], AF.Copy,
                        scale=ASCALE),
                     reads=[("ps", bk)], writes=[("auxq", m)])
            for m in range(8):
                bk = m % 2

                def kmm(e, m=m, bk=bk, s0c=s0c):
                    for k in range(8):
                        e.matmul(ps[:, 512 * bk:512 * bk + 512],
                                 wq_v[:, k, 1024 + 128 * m:1024 + 128 * (m + 1)],
                                 xkv[:, k, s0c:s0c + 512],
                                 start=(k == 0), stop=(k == 7))
                    for k in range(8):
                        ins = e.matmul(ps[:, 2048 + 16 * bk:2048 + 16 * (bk + 1)],
                                       wq_v[:, k, 1024 + 128 * m:1024 + 128 * (m + 1)],
                                       xkv[:, k, s0c + 512:s0c + 528],
                                       start=(k == 0), stop=(k == 7))
                    return ins
                P.op("t", kmm, reads=[("arw", "qkv", 1), ("xnT",)],
                     writes=[("ps", bk), ("ps", 4)])

                def kev(e, m=m, bk=bk):
                    e.copy(kv[:, m, 0:512], ps[:, 512 * bk:512 * bk + 512])
                    return e.copy(kv[:, m, 512:528],
                                  ps[:, 2048 + 16 * bk:2048 + 16 * (bk + 1)])
                P.op("s", kev, reads=[("ps", bk), ("ps", 4)], writes=[("auxk", m)])
            for j in range(5):
                rj = 128 if (j < 4 or st < NST - 1) else 16

                def vmm(e, j=j, s0c=s0c, rj=rj):
                    for n2 in range(2):
                        for k in range(8):
                            ins = e.matmul(ps[0:rj, 512 * n2:512 * (n2 + 1)],
                                           xkv[:, k, s0c + 128 * j:s0c + 128 * j + rj],
                                           wq_v[:, k, 2048 + 512 * n2:2048 + 512 * (n2 + 1)],
                                           start=(k == 0), stop=(k == 7))
                    return ins
                P.op("t", vmm, reads=[("arw", "qkv", 2), ("xnT",)],
                     writes=[("ps", 0), ("ps", 1)])
                P.op("v", lambda e, j=j, rj=rj: e.tensor_copy(vv[0:rj, j, :],
                                                              ps[0:rj, 0:1024]),
                     reads=[("ps", 0), ("ps", 1)], writes=[("auxv", j)])

            if upto == "Bq":
                continue
            for tl in range(4):
                t_ = 4 * st + tl
                mvar = 1 if t_ == 0 else (2 if t_ == NT - 1 else 0)
                for hg in range(2):
                    def smm(e, tl=tl, hg=hg):
                        for hh in range(8):
                            h = 8 * hg + hh
                            po, chn = 64 * (h % 2), h // 2
                            s_ = (hh % 2) * 4 + hh // 2
                            ins = e.matmul(
                                ps[:, 256 * s_:256 * s_ + 144],
                                qv[po:po + 64, chn, 128 * tl:128 * (tl + 1)],
                                kv[po:po + 64, chn, 128 * tl:128 * tl + 144],
                                start=True, stop=True)
                        return ins
                    P.op("t", smm,
                         reads=[("auxq", m) for m in range(8)]
                               + [("auxk", m) for m in range(8)],
                         writes=[("ps", 0), ("ps", 1), ("ps", 2), ("ps", 3)])
                    if BLV < 0:
                        continue
                    if upto == "Bnm":
                        def mka(e, mvar=mvar):
                            for hh in range(8):
                                ins = e.tensor_tensor(
                                    ps[:, 256 * hh:256 * hh + 144],
                                    ps[:, 256 * hh:256 * hh + 144],
                                    mk3[:, mvar, hh, :], ALU.add)
                            return ins
                        P.op("v", mka,
                             reads=[("ps", 0), ("ps", 1), ("ps", 2), ("ps", 3),
                                    ("masks",)],
                             writes=[("ps", 0), ("ps", 1), ("ps", 2), ("ps", 3)])
                    else:
                        psv = ps[:, 0:2048].rearrange("p (sl x) -> p sl x", sl=8)
                        P.op("v", lambda e, psv=psv, mvar=mvar: e.tensor_tensor(
                                psv[:, :, 0:144], psv[:, :, 0:144],
                                mk3[:, mvar, :, :], ALU.add),
                             reads=[("ps", 0), ("ps", 1), ("ps", 2), ("ps", 3),
                                    ("masks",)],
                             writes=[("ps", 0), ("ps", 1), ("ps", 2), ("ps", 3)])
                    av8 = asb[:, :].rearrange("p (sl x) -> p sl x", sl=8)
                    if BLV < 1:
                        continue
                    for hh in range(8):
                        s_ = (hh % 2) * 4 + hh // 2
                        P.op("v", lambda e, s_=s_, hh=hh: e.reduce_max(
                                stats[:, 32 + hh:33 + hh],
                                ps[:, 256 * s_:256 * s_ + 144], AXX, negate=True),
                             reads=[("ps", s_ // 2)], writes=[("stats", "nx", hh)])
                    for hh in range(8):
                        s_ = (hh % 2) * 4 + hh // 2
                        P.op("s", lambda e, s_=s_, hh=hh, av8=av8: e.activation(
                                av8[:, hh, :], ps[:, 256 * s_:256 * s_ + 144], AF.Exp,
                                bias=stats[:, 32 + hh:33 + hh], scale=1.0,
                                accum_out=stats[:, 40 + hh:41 + hh]),
                             reads=[("ps", s_ // 2), ("stats", "nx", hh)],
                             writes=[("auxa", hh), ("stats", "l", hh)])
                    P.op("v", lambda e: e.reciprocal(stats[:, 40:48], stats[:, 40:48]),
                         reads=[("stats", "l", h) for h in range(8)],
                         writes=[("stats", "linv")])
                    for hh in range(8):
                        P.op("v", lambda e, hh=hh, av8=av8: e.tensor_scalar(
                                av8[:, hh, :], av8[:, hh, :],
                                stats[:, 40 + hh:41 + hh], None, ALU.mult),
                             reads=[("auxa", hh), ("stats", "linv")],
                             writes=[("auxa", hh)])

                    if BLV < 2:
                        continue

                    def atr(e, av8=av8):
                        for hh in range(8):
                            e.transpose(psb[:, 4096 + 128 * hh:4096 + 128 * (hh + 1)],
                                        av8[:, hh, 0:128], identb_sb[:, :])
                            ins = e.transpose(
                                psb[0:16, 5120 + 128 * hh:5120 + 128 * (hh + 1)],
                                av8[:, hh, 128:144], identb_sb[:, :])
                        return ins
                    P.op("t", atr,
                         reads=[("auxa", h) for h in range(8)] + [("idb",)],
                         writes=[("ps", 4), ("ps", 5)])
                    P.op("v", lambda e: e.tensor_copy(at1sb[:, :], psb[:, 4096:5120]),
                         reads=[("ps", 4)] + [("auxa", h) for h in range(8)],
                         writes=[("auxat1",)])
                    P.op("s", lambda e: e.copy(at2sb[:, :], psb[0:16, 5120:6144]),
                         reads=[("ps", 5)], writes=[("auxat2",)])
                    if BLV < 3:
                        continue

                    def avmm(e, tl=tl, hg=hg):
                        for hh in range(8):
                            h = 8 * hg + hh
                            po = 64 * (hh % 2)
                            c2 = hh // 2
                            e.matmul(ps[po:po + 64,
                                        3072 + 128 * c2:3072 + 128 * (c2 + 1)],
                                     vv[:, tl, 64 * h:64 * (h + 1)],
                                     at1sb[:, 128 * hh:128 * (hh + 1)],
                                     start=(hh <= 1), stop=False,
                                     skip_group_check=(hh > 1))
                            ins = e.matmul(
                                ps[po:po + 64, 3072 + 128 * c2:3072 + 128 * (c2 + 1)],
                                vv[0:16, tl + 1, 64 * h:64 * (h + 1)],
                                at2sb[:, 128 * hh:128 * (hh + 1)],
                                start=False, stop=(hh >= 6),
                                skip_group_check=(hh < 6))
                        return ins
                    P.op("t", avmm,
                         reads=[("auxat1",), ("auxat2",), ("auxv", tl),
                                ("auxv", tl + 1)],
                         writes=[("ps", 6)])
                    P.op("v", lambda e, hg=hg: e.tensor_copy(
                            osb[:, 512 * hg:512 * (hg + 1)], ps[:, 3072:3584]),
                         reads=[("ps", 6)],
                         writes=[("auxo", hg)])
                if BLV < 4:
                    continue
                ov = osb.rearrange("p (c n) -> p c n", c=8)
                for n2 in range(2):
                    def pmm(e, n2=n2):
                        for k in range(8):
                            ins = e.matmul(ps[:, 512 * n2:512 * (n2 + 1)],
                                           ov[:, k, :],
                                           wp_v[:, k, 512 * n2:512 * (n2 + 1)],
                                           start=(k == 0), stop=(k == 7))
                        return ins
                    P.op("t", pmm, reads=[("auxo", 0), ("auxo", 1), ("arw", "proj")],
                         writes=[("ps", n2)])
                P.op("v", lambda e: e.tensor_tensor(stile[:, :], ps[:, 0:1024],
                                                    bb[:, :], ALU.add),
                     reads=[("ps", 0), ("ps", 1), ("bb",)], writes=[("st",)])
                P.dma("sa" if t_ % 2 == 0 else "sb",
                      s_scr.ap()[128 * t_:128 * (t_ + 1), :], stile[:, :],
                      reads=[("st",)], writes=[("scr_s", t_)])

        if "s0" in taps:
            P.dma("sa", tap_aps["s0"], s_scr.ap(),
                  reads=[("scr_s", t_) for t_ in range(NT)])

        if upto in ("B", "Bq", "Bnm", "Bs", "Bs0", "Bsm", "Bat", "Bav"):
            return _finish()
        def emit_cw(which, ne, tile, normalize):
            lg_sb = logit0 if which == "l0" else logit1
            cwt = cwtm0 if which == "l0" else cwtm1
            lg = lg_sb[:, ne * tile:ne * (tile + 1)]
            cwtmp = smallv[:, SV_CW:SV_CW + ne]
            lkey = (which,)
            if which == "l1":
                P.op("v", lambda e, lg=lg: e.tensor_tensor(
                        smallv[:, SV_CW + 32:SV_CW + 32 + ne], lg,
                        bgrow_sb[:, 0:ne], ALU.add),
                     reads=[lkey, ("bgB",)], writes=[("svl1",)])
                lg = smallv[:, SV_CW + 32:SV_CW + 32 + ne]
                lkey = ("svl1",)
            P.op("v", lambda e, lg=lg: e.reduce_max(stats[:, ST_NM:ST_NM + 1], lg,
                                                    AXX, negate=True),
                 reads=[lkey], writes=[("stats", "nm")])
            P.op("s", lambda e, lg=lg: e.activation(cwtmp, lg, AF.Exp,
                                                    bias=stats[:, ST_NM:ST_NM + 1],
                                                    scale=1.0,
                                                    accum_out=stats[:, ST_SQ:ST_SQ + 1]),
                 reads=[lkey, ("stats", "nm")], writes=[("svcw",), ("stats", "sq")])
            P.op("v", lambda e: e.reciprocal(stats[:, ST_IV:ST_IV + 1],
                                             stats[:, ST_SQ:ST_SQ + 1]),
                 reads=[("stats", "sq")], writes=[("stats", "iv")])
            P.op("v", lambda e: e.tensor_scalar(cwtmp, cwtmp,
                                                stats[:, ST_IV:ST_IV + 1], None,
                                                ALU.mult),
                 reads=[("svcw",), ("stats", "iv")], writes=[("svcw",)])
            P.op("v", lambda e: e.reduce_max(stats[:, ST_M1:ST_M1 + 1], cwtmp, AXX),
                 reads=[("svcw",)], writes=[("stats", "m1")])
            P.op("v", lambda e: e.tensor_scalar(ntile[:, 0:ne], cwtmp,
                                                stats[:, ST_M1:ST_M1 + 1], None,
                                                ALU.is_ge),
                 reads=[("svcw",), ("stats", "m1")], writes=[("nt",)])
            P.op("v", lambda e: e.scalar_tensor_tensor(ntile[:, 0:ne], ntile[:, 0:ne],
                                                       -4.0, cwtmp, ALU.mult, ALU.add),
                 reads=[("nt",), ("svcw",)], writes=[("nt",)])
            P.op("v", lambda e: e.reduce_max(stats[:, ST_M2:ST_M2 + 1], ntile[:, 0:ne],
                                             AXX),
                 reads=[("nt",)], writes=[("stats", "m2")])
            P.op("v", lambda e: e.tensor_scalar(ntile[:, 0:ne], cwtmp,
                                                stats[:, ST_M2:ST_M2 + 1], None,
                                                ALU.is_ge),
                 reads=[("svcw",), ("stats", "m2")], writes=[("nt",)])
            P.op("v", lambda e: e.tensor_tensor(cwtmp, cwtmp, ntile[:, 0:ne], ALU.mult),
                 reads=[("svcw",), ("nt",)], writes=[("svcw",)])
            if normalize:
                P.op("v", lambda e: e.tensor_tensor(stats[:, ST_DEN:ST_DEN + 1],
                                                    stats[:, ST_M1:ST_M1 + 1],
                                                    stats[:, ST_M2:ST_M2 + 1], ALU.add),
                     reads=[("stats", "m1"), ("stats", "m2")],
                     writes=[("stats", "den")])
                P.op("v", lambda e: e.tensor_scalar(stats[:, ST_DEN:ST_DEN + 1],
                                                    stats[:, ST_DEN:ST_DEN + 1],
                                                    1e-8, None, ALU.add),
                     reads=[("stats", "den")], writes=[("stats", "den")])
                P.op("v", lambda e: e.reciprocal(stats[:, ST_DEN:ST_DEN + 1],
                                                 stats[:, ST_DEN:ST_DEN + 1]),
                     reads=[("stats", "den")], writes=[("stats", "den")])
                P.op("v", lambda e: e.tensor_scalar(cwtmp, cwtmp,
                                                    stats[:, ST_DEN:ST_DEN + 1], None,
                                                    ALU.mult),
                     reads=[("svcw",), ("stats", "den")], writes=[("svcw",)])
            P.op("v", lambda e, tile=tile: e.tensor_copy(
                    cwt[:, ne * tile:ne * (tile + 1)], cwtmp),
                 reads=[("svcw",)], writes=[("cw" + which, tile)])

        for t_ in range(NT):
            emit_cw("l0", NE0, t_, False)
        if "cw0" in taps:
            P.dma("sb", tap_aps["cw0"], cwtm0[:, :],
                  reads=[("cwl0", t_) for t_ in range(NT)])

        if upto == "cw0":
            return _finish()
        HA, HB = 0, 4096
        SLOT = [0, 12288]
        P.rekey(AUXB, [("aux", "hA"), ("aux", "hB")])
        P.rekey([("arw", "qkv", 0), ("arw", "qkv", 1), ("arw", "qkv", 2),
                 ("arw", "proj")],
                [("arw", "ug", 0), ("arw", "uu", 0), ("arw", "ud", 0),
                 ("arw", "ug", 1), ("arw", "uu", 1), ("arw", "ud", 1)])

        def c_unit_dma(u):
            slot = SLOT[u % 2]
            ch = "wa" if u % 2 == 0 else "wb"
            gv = arw[:, slot:slot + 4096].rearrange("p (k n) -> p k n", k=8)
            uv = arw[:, slot + 4096:slot + 8192].rearrange("p (k n) -> p k n", k=8)
            dv = arw[:, slot + 8192:slot + 12288].rearrange("p (k n) -> p k n", k=4)
            if u < NE0:
                P.dma(ch, gv, Wg0[u].rearrange("(k p) n -> p k n", p=128),
                      writes=[("arw", "ug", u % 2)])
                P.dma(ch, uv, Wu0[u].rearrange("(k p) n -> p k n", p=128),
                      writes=[("arw", "uu", u % 2)])
                P.dma(ch, dv, Wd0[u].rearrange("(k p) n -> p k n", p=128),
                      writes=[("arw", "ud", u % 2)])
            else:
                f = u - NE0
                P.dma(ch, gv,
                      Sg.rearrange("(k p) n -> p k n", p=128)[:, :, 512 * f:512 * (f + 1)],
                      writes=[("arw", "ug", u % 2)])
                P.dma(ch, uv,
                      Su.rearrange("(k p) n -> p k n", p=128)[:, :, 512 * f:512 * (f + 1)],
                      writes=[("arw", "uu", u % 2)])
                P.dma(ch, dv,
                      Sd.rearrange("(k p) n -> p k n", p=128)[:, 4 * f:4 * (f + 1), :],
                      writes=[("arw", "ud", u % 2)])

        def c_unit_compute(u, th):
            slot = SLOT[u % 2]
            hoff = HA if u % 2 == 0 else HB
            hkey = ("aux", "hA" if u % 2 == 0 else "hB")
            gv = arw[:, slot:slot + 4096].rearrange("p (k n) -> p k n", k=8)
            uv = arw[:, slot + 4096:slot + 8192].rearrange("p (k n) -> p k n", k=8)
            dv = arw[:, slot + 8192:slot + 12288].rearrange("p (k n) -> p k n", k=4)
            hv = aux[:, hoff:hoff + 4 * TH].rearrange("p (m n) -> p m n", m=4)
            for m in range(4):
                for n2 in range(NW5):
                    def gmm(e, m=m, n2=n2, th=th, gv=gv):
                        for k in range(8):
                            ins = e.matmul(ps[:, 0:W5],
                                           gv[:, k, 128 * m:128 * (m + 1)],
                                           xkv[:, k, 8 + TH * th + W5 * n2:
                                               8 + TH * th + W5 * (n2 + 1)],
                                           start=(k == 0), stop=(k == 7))
                        return ins
                    P.op("t", gmm, reads=[("arw", "ug", u % 2), ("xnT",)],
                         writes=[("ps", 0)])
                    P.op("s", lambda e: e.activation(tgt[:, 0:W5], ps[:, 0:W5],
                                                     AF.Silu),
                         reads=[("ps", 0)], writes=[("tg",)])

                    def umm(e, m=m, n2=n2, th=th, uv=uv):
                        for k in range(8):
                            ins = e.matmul(ps[:, 512:512 + W5],
                                           uv[:, k, 128 * m:128 * (m + 1)],
                                           xkv[:, k, 8 + TH * th + W5 * n2:
                                               8 + TH * th + W5 * (n2 + 1)],
                                           start=(k == 0), stop=(k == 7))
                        return ins
                    P.op("t", umm, reads=[("arw", "uu", u % 2), ("xnT",)],
                         writes=[("ps", 1)])
                    hslice = hv[:, m, W5 * n2:W5 * (n2 + 1)]
                    P.op("v", lambda e, hslice=hslice: e.tensor_tensor(
                            hslice, tgt[:, 0:W5], ps[:, 512:512 + W5], ALU.mult),
                         reads=[("tg",), ("ps", 1)], writes=[hkey])
            for sl in range(NSL):
                for n in range(2):
                    bk = 2 + (sl * 2 + n) % 2

                    def dmm(e, sl=sl, n=n, bk=bk, hv=hv, dv=dv):
                        for k in range(4):
                            ins = e.matmul(ps[:, 512 * bk:512 * (bk + 1)],
                                           hv[:, k, 128 * sl:128 * (sl + 1)],
                                           dv[:, k, 512 * n:512 * (n + 1)],
                                           start=(k == 0), stop=(k == 3))
                        return ins
                    P.op("t", dmm, reads=[hkey, ("arw", "ud", u % 2)],
                         writes=[("ps", bk)])
                    acs = big[:, 1024 * sl + 512 * n:1024 * sl + 512 * (n + 1)]
                    t_ = NSL * th + sl
                    if u < NE0:
                        cwc = cwtm0[:, NE0 * t_ + u:NE0 * t_ + u + 1]
                        P.op("v", lambda e, acs=acs, bk=bk, cwc=cwc:
                                e.scalar_tensor_tensor(
                                    acs, ps[:, 512 * bk:512 * (bk + 1)], cwc, acs,
                                    ALU.mult, ALU.add),
                             reads=[("ps", bk), ("acc", sl, n), ("cwl0", t_)],
                             writes=[("acc", sl, n)])
                    else:
                        P.op("v", lambda e, acs=acs, bk=bk: e.tensor_tensor(
                                acs, acs, ps[:, 512 * bk:512 * (bk + 1)], ALU.add),
                             reads=[("ps", bk), ("acc", sl, n)],
                             writes=[("acc", sl, n)])

        for th in range(2):
            for sl in range(NSL):
                for n in range(2):
                    P.op("v", lambda e, sl=sl, n=n: e.memset(
                            big[:, 1024 * sl + 512 * n:1024 * sl + 512 * (n + 1)], 0.0),
                         writes=[("acc", sl, n)])
            c_unit_dma(0)
            c_unit_dma(1)
            for u in range(18):
                c_unit_compute(u, th)
                if u + 2 < 18:
                    c_unit_dma(u + 2)
            for tt in range(NSL):
                t_ = NSL * th + tt
                P.op("s", lambda e, tt=tt: e.activation(
                        ntile[:, :], big[:, 1024 * tt:1024 * (tt + 1)], AF.Silu),
                     reads=[("acc", tt, 0), ("acc", tt, 1)], writes=[("nt",)])
                P.dma("la", xtile[:, :], x_in[8 + 128 * t_:8 + 128 * (t_ + 1), :],
                      writes=[("xt",)])
                P.dma("lb", stile[:, :], s_scr.ap()[128 * t_:128 * (t_ + 1), :],
                      reads=[("scr_s", t_)], writes=[("st",)])
                P.op("v", lambda e: e.tensor_tensor(ntile[:, :], stile[:, :],
                                                    ntile[:, :], ALU.mult),
                     reads=[("st",), ("nt",)], writes=[("nt",)])
                P.op("v", lambda e: e.tensor_tensor(xtile[:, :], xtile[:, :],
                                                    ntile[:, :], ALU.add),
                     reads=[("nt",), ("xt",)], writes=[("xt",)])
                P.dma("sa" if t_ % 2 == 0 else "sb",
                      x1_scr.ap()[128 * t_:128 * (t_ + 1), :], xtile[:, :],
                      reads=[("xt",)], writes=[("scr_x1", t_)])
        if "acc0" in taps:
            P.dma("sb", tap_aps["acc0"], big[:, 0:NSL * 1024],
                  reads=[("acc", sl, n) for sl in range(NSL) for n in range(2)])
        if "x1" in taps:
            P.dma("sa", tap_aps["x1"], x1_scr.ap(),
                  reads=[("scr_x1", t_) for t_ in range(NT)])

        if upto == "C":
            return _finish()
        P.rekey([("aux", "hA"), ("aux", "hB")], [("aux", "xf")])
        for t_ in range(NT):
            emit_ln(128, x1_scr.ap()[128 * t_:128 * (t_ + 1), :], 8 + 128 * t_,
                    ("l1", NE1), t_ * NE1, "la" if t_ % 2 == 0 else "lb")
        if "logits1" in taps:
            P.dma("sb", tap_aps["logits1"], logit1[:, 0:NT * NE1], reads=[("l1",)])

        P.dma("la", rows2[0:1, 128:128 + NE1], bg1.unsqueeze(0), writes=[("r2", "bg")])
        P.op("t", lambda e: e.matmul(ps[:, 1024:1024 + NE1],
                                     rows2[0:1, 0:128],
                                     rows2[0:1, 128:128 + NE1],
                                     start=True, stop=True),
             reads=[("r2", "ones"), ("r2", "bg")], writes=[("ps", 2)])
        P.op("v", lambda e: e.tensor_copy(bgrow_sb, ps[:, 1024:1024 + NE1]),
             reads=[("ps", 2)], writes=[("bgB",)])

        for t_ in range(NT):
            emit_cw("l1", NE1, t_, True)
        if "cw1" in taps:
            P.dma("sb", tap_aps["cw1"], cwtm1[:, :],
                  reads=[("cwl1", t_) for t_ in range(NT)])

        if upto == "ln1":
            return _finish()
        P.dma("la", cwrows[R_CP:R_CP + 16, 0:DI], Cp, writes=[("cp",)])
        P.dma("lb", rows2[32:48, 0:128], bin_.rearrange("(c p) -> c p", p=128),
              reads=[("r2", "stage")], writes=[("r2", "stage")])
        P.op("t", lambda e: e.transpose(ps[:, 3584:3584 + 16],
                                        rows2[32:48, 0:128],
                                        identf_sb[32:48, 32:48]),
             reads=[("r2", "stage"), ("idf",)], writes=[("ps", 7)])
        P.op("v", lambda e: e.tensor_copy(smallv[:, SV_BIN:SV_BIN + 16],
                                          ps[:, 3584:3584 + 16]),
             reads=[("ps", 7)], writes=[("sv", "bin")])
        P.dma("lb", rows2[32:48, 0:128], Dp.rearrange("(c p) -> c p", p=128),
              reads=[("r2", "stage")], writes=[("r2", "stage")])
        P.op("t", lambda e: e.transpose(ps[:, 3584:3584 + 16],
                                        rows2[32:48, 0:128],
                                        identf_sb[32:48, 32:48]),
             reads=[("r2", "stage"), ("idf",)], writes=[("ps", 7)])
        P.op("v", lambda e: e.tensor_copy(smallv[:, SV_DP:SV_DP + 16],
                                          ps[:, 3584:3584 + 16]),
             reads=[("ps", 7)], writes=[("sv", "dp")])
        for c in range(16):
            P.dma("la" if c % 2 == 0 else "lb", xtile[:, 0:16],
                  Bp[128 * c:128 * (c + 1), :], reads=[("xt",)], writes=[("xt",)])
            P.op("t", lambda e, c=c: e.transpose(ps[:, 3584:3584 + 16],
                                                 cwrows[R_CP:R_CP + 16,
                                                        128 * c:128 * (c + 1)],
                                                 identf_sb[32:48, 32:48]),
                 reads=[("cp",), ("idf",)], writes=[("ps", 7)])
            P.op("v", lambda e: e.tensor_tensor(ntile[:, 0:16], xtile[:, 0:16],
                                                ps[:, 3584:3584 + 16], ALU.mult),
                 reads=[("xt",), ("ps", 7)], writes=[("nt",)])
            P.op("v", lambda e, c=c: e.reduce_sum(smallv[:, SV_R + c:SV_R + c + 1],
                                                  ntile[:, 0:16], AXX),
                 reads=[("nt",)], writes=[("sv", "r", c)])
        if "rvec" in taps:
            P.dma("sb", tap_aps["rvec"], smallv[:, SV_R:SV_R + 16],
                  reads=[("sv", "r", c) for c in range(16)])

        bias_bcast(bout, "la")

        P.rekey([("acc", sl, n) for sl in range(NSL) for n in range(2)],
                [("bd", "xi"), ("bd", "c"), ("bd", "y"), ("bd", "dec")])
        xi_sb = big[:, 0:T]
        csc_sb = big[:, 2048:2048 + T]
        y_sb = big[:, 4096:4096 + T]
        dec_sb = big[:, 6144:6144 + T]
        P.op("v", lambda e: e.memset(dec_sb, DECAY), writes=[("bd", "dec")])

        P.rekey([("arw", "ug", 0), ("arw", "ug", 1),
                 ("arw", "uu", 0), ("arw", "uu", 1), ("arw", "ud", 0),
                 ("arw", "ud", 1)],
                [("arw", "win"), ("arw", "wout")])
        win_v = arw[:, 0:16384].rearrange("p (k n) -> p k n", k=8)
        wout_v = arw[:, 16384:32768].rearrange("p (k n) -> p k n", k=16)
        P.dma("wa", win_v[:, :, 0:1024],
              Win.rearrange("(k p) n -> p k n", p=128)[:, :, 0:1024],
              writes=[("arw", "win")])
        P.dma("wb", win_v[:, :, 1024:2048],
              Win.rearrange("(k p) n -> p k n", p=128)[:, :, 1024:2048],
              writes=[("arw", "win")])
        for hw_ in range(2):
            P.dma("wa", wout_v[:, 8 * hw_:8 * (hw_ + 1), :],
                  Wout.rearrange("(k p) n -> p k n", p=128)[:, 8 * hw_:8 * (hw_ + 1), :],
                  writes=[("arw", "wout")])

        NXI = T // 512
        for c in range(16):
            def xmm(e, c=c):
                for n2 in range(NXI):
                    for k in range(8):
                        ins = e.matmul(ps[:, 512 * n2:512 * (n2 + 1)],
                                       win_v[:, k, 128 * c:128 * (c + 1)],
                                       xkv[:, k, 8 + 512 * n2:8 + 512 * (n2 + 1)],
                                       start=(k == 0), stop=(k == 7))
                return ins
            P.op("t", xmm, reads=[("arw", "win"), ("xnT",)],
                 writes=[("ps", bnk) for bnk in range(NXI)])
            P.op("s", lambda e, c=c: e.activation(xi_sb, ps[:, 0:T], AF.Silu,
                                                  bias=smallv[:, SV_BIN + c:
                                                              SV_BIN + c + 1],
                                                  scale=1.0),
                 reads=[("ps", bnk) for bnk in range(NXI)] + [("sv", "bin")],
                 writes=[("bd", "xi")])
            if "xi" in taps:
                P.dma("sb", tap_aps["xi"][c], xi_sb, reads=[("bd", "xi")])
            P.op("v", lambda e: e.tensor_tensor_scan(csc_sb, dec_sb, xi_sb, 0.0,
                                                     ALU.mult, ALU.add),
                 reads=[("bd", "dec"), ("bd", "xi")], writes=[("bd", "c")])
            P.op("v", lambda e, c=c: e.tensor_copy(smallv[:, SV_FR + c:SV_FR + c + 1],
                                                   csc_sb[:, T - 1:T]),
                 reads=[("bd", "c")], writes=[("sv", "f", c)])
            P.op("v", lambda e, c=c: e.tensor_scalar(y_sb, xi_sb,
                                                     smallv[:, SV_DP + c:
                                                            SV_DP + c + 1],
                                                     None, ALU.mult),
                 reads=[("bd", "xi"), ("sv", "dp")], writes=[("bd", "y")])
            P.op("v", lambda e, c=c: e.scalar_tensor_tensor(
                    y_sb, csc_sb, smallv[:, SV_R + c:SV_R + c + 1], y_sb,
                    ALU.mult, ALU.add),
                 reads=[("bd", "c"), ("sv", "r", c), ("bd", "y")],
                 writes=[("bd", "y")])
            P.dma("sa" if c % 2 == 0 else "sb", yp_scr.ap()[c], y_sb,
                  reads=[("bd", "y")], writes=[("scr_yp", c)])
        if "ypart" in taps:
            P.dma("sa", tap_aps["ypart"], yp_scr.ap(),
                  reads=[("scr_yp", c) for c in range(16)])

        P.op("t", lambda e: e.transpose(ps[0:16, 1024:1024 + 128],
                                        smallv[:, SV_FR:SV_FR + 16], identf_sb[:, :]),
             reads=[("sv", "f", c) for c in range(16)] + [("idf",)],
             writes=[("ps", 2)])
        P.op("v", lambda e: e.tensor_copy(cwrows[R_FT:R_FT + 16, 0:128],
                                          ps[0:16, 1024:1024 + 128]),
             reads=[("ps", 2)], writes=[("fT",)])
        P.dma("cc", cc_in.ap().rearrange("a (c p) -> (a c) p", p=128),
              cwrows[R_FT:R_FT + 16, 0:128], reads=[("fT",)], writes=[("ccin",)])
        P.op("g", lambda e: e.collective_compute(
                "AllGather", ALU.bypass, replica_groups=[list(range(world))],
                ins=[cc_in.ap()], outs=[cc_out.ap()]),
             reads=[("ccin",)], writes=[("ccout",)], sem="ccc")
        P.dma("cc", cwrows[R_FALL:R_FALL + world, 0:DI], cc_out.ap(),
              reads=[("ccout",)], writes=[("Fall",)])

        def fsel_mm(e):
            for n2 in range(DI // 512):
                ins = e.matmul(ps[0:1, 512 * n2:512 * (n2 + 1)], fsel_sb[0:world, :],
                               cwrows[R_FALL:R_FALL + world, 512 * n2:512 * (n2 + 1)],
                               start=True, stop=True)
            return ins
        P.op("t", fsel_mm, reads=[("Fall",), ("fsel",)],
             writes=[("ps", 0), ("ps", 1), ("ps", 2), ("ps", 3)])
        frow = auxf[0:1, 3584:3584 + DI]
        P.rekey([("aux", "xf")], [("aux", "dpow"), ("aux", "ybf"), ("aux", "ybL"),
                                  ("aux", "frow")])
        P.op("v", lambda e: e.tensor_copy(frow, ps[0:1, 0:DI]),
             reads=[("ps", 0), ("ps", 1), ("ps", 2), ("ps", 3)],
             writes=[("aux", "frow")])

        P.dma("sb", fr_scr.ap(), frow, reads=[("aux", "frow")], writes=[("frs",)])
        P.dma("lb", rows2[32:48, 0:128],
              fr_scr.ap().rearrange("a (c p) -> (a c) p", p=128),
              reads=[("frs",), ("r2", "stage")], writes=[("r2", "stage")])
        P.op("t", lambda e: e.transpose(ps[:, 3584:3584 + 16],
                                        rows2[32:48, 0:128],
                                        identf_sb[32:48, 32:48]),
             reads=[("r2", "stage"), ("idf",)], writes=[("ps", 7)])
        P.op("v", lambda e: e.tensor_copy(smallv[:, SV_FP:SV_FP + 16],
                                          ps[:, 3584:3584 + 16]),
             reads=[("ps", 7)], writes=[("sv", "fp")])
        if "fprev" in taps:
            P.dma("sb", tap_aps["fprev"], smallv[:, SV_FP:SV_FP + 16],
                  reads=[("sv", "fp")])
        P.op("v", lambda e: e.tensor_tensor(smallv[:, SV_FP:SV_FP + 16],
                                            smallv[:, SV_FP:SV_FP + 16],
                                            smallv[:, SV_R:SV_R + 16], ALU.mult),
             reads=[("sv", "fp")] + [("sv", "r", c) for c in range(16)],
             writes=[("sv", "fp")])

        dpow_sb = auxf[:, 512:512 + T]
        ybf = aux[:, 0:1024]
        ybL = auxf[:, 512 + T:512 + T + TH]
        P.dma("la", dpow_sb, dpow_in, writes=[("aux", "dpow")])

        P.rekey([("bd", "xi"), ("bd", "c"), ("bd", "y"), ("bd", "dec")],
                [("s1acc", g) for g in range(NSL)])
        for th in range(2):
            for sl in range(NSL):
                gsl = th * NSL + sl
                P.op("v", lambda e, sl=sl: e.tensor_copy(
                        big[:, 1024 * sl:1024 * (sl + 1)], bb[:, :]),
                     reads=[("bb",)], writes=[("s1acc", sl)])
            for c in range(16):
                P.dma("la" if c % 2 == 0 else "lb", ybL,
                      yp_scr.ap()[c][:, TH * th:TH * (th + 1)],
                      reads=[("scr_yp", c), ("aux", "ybL")], writes=[("aux", "ybL")])
                P.op("v", lambda e, c=c, th=th: e.scalar_tensor_tensor(
                        ybf[:, 0:TH], dpow_sb[:, TH * th:TH * (th + 1)],
                        smallv[:, SV_FP + c:SV_FP + c + 1], ybL,
                        ALU.mult, ALU.add),
                     reads=[("aux", "dpow"), ("sv", "fp"), ("aux", "ybL")],
                     writes=[("aux", "ybf")])
                for sl in range(NSL):
                    for n in range(2):
                        bk = 4 + (sl * 2 + n) % 2
                        gsl = th * NSL + sl

                        def wmm(e, c=c, sl=sl, n=n, bk=bk):
                            return e.matmul(ps[:, 512 * bk:512 * (bk + 1)],
                                            ybf[:, 128 * sl:128 * (sl + 1)],
                                            wout_v[:, c, 512 * n:512 * (n + 1)],
                                            start=True, stop=True)
                        P.op("t", wmm, reads=[("aux", "ybf"), ("arw", "wout")],
                             writes=[("ps", bk)])
                        acs = big[:, 1024 * sl + 512 * n:1024 * sl + 512 * (n + 1)]
                        P.op("v", lambda e, acs=acs, bk=bk: e.tensor_tensor(
                                acs, acs, ps[:, 512 * bk:512 * (bk + 1)], ALU.add),
                             reads=[("ps", bk), ("s1acc", sl)],
                             writes=[("s1acc", sl)])
            for sl in range(NSL):
                t_ = th * NSL + sl
                P.dma("sa" if t_ % 2 == 0 else "sb",
                      s_scr.ap()[128 * t_:128 * (t_ + 1), :],
                      big[:, 1024 * sl:1024 * (sl + 1)],
                      reads=[("s1acc", sl)], writes=[("scr_s", t_)])
        if "s1" in taps:
            P.dma("sa", tap_aps["s1"], s_scr.ap(),
                  reads=[("scr_s", t_) for t_ in range(NT)])

        if upto == "D":
            return _finish()
        P.rekey([("arw", "win"), ("arw", "wout")],
                [("arw", "w1", 0), ("arw", "w1", 1), ("arw", "w2")])
        P.rekey([("aux", "dpow"), ("aux", "ybf"), ("aux", "ybL"), ("aux", "frow")],
                [("hE", k) for k in range(16)])
        P.rekey([("s1acc", g) for g in range(NSL)],
                [("eacc", sl, n) for sl in range(NSL) for n in range(2)])
        w1s = [arw[:, 0:4096].rearrange("p (k n) -> p k n", k=8),
               arw[:, 4096:8192].rearrange("p (k n) -> p k n", k=8)]
        w2v = arw[:, 8192:24576].rearrange("p (k n) -> p k n", k=16)
        hEv = aux[:, 0:16 * TH].rearrange("p (k n) -> p k n", k=16)

        for th in range(2):
            for sl in range(NSL):
                for n in range(2):
                    P.op("v", lambda e, sl=sl, n=n: e.memset(
                            big[:, 1024 * sl + 512 * n:1024 * sl + 512 * (n + 1)], 0.0),
                         writes=[("eacc", sl, n)])
            for eix in range(NE1):
                P.dma("la", rows2[32:64, 0:128],
                      b1[eix].rearrange("(c p) -> c p", p=128),
                      reads=[("r2", "stage")], writes=[("r2", "stage")])
                P.op("t", lambda e: e.transpose(ps[:, 3584:3584 + 32],
                                                rows2[32:64, 0:128],
                                                identf_sb[32:64, 32:64]),
                     reads=[("r2", "stage"), ("idf",)], writes=[("ps", 7)])
                P.op("v", lambda e: e.tensor_copy(smallv[:, SV_B1:SV_B1 + 32],
                                                  ps[:, 3584:3584 + 32]),
                     reads=[("ps", 7)], writes=[("sv", "b1")])
                bias_bcast(b2[eix], "lb")
                for hk in range(2):
                    for hw_ in range(2):
                        P.dma("wb", w2v[:, 8 * hw_:8 * (hw_ + 1), :],
                              W2[eix].rearrange("(k p) n -> p k n", p=128)
                              [:, 16 * hk + 8 * hw_:16 * hk + 8 * (hw_ + 1), :],
                              writes=[("arw", "w2")])
                    for cc4 in range(4):
                        cslot = cc4 % 2
                        P.dma("wa", w1s[cslot],
                              W1[eix].rearrange("(k p) n -> p k n", p=128)
                              [:, :, 2048 * hk + 512 * cc4:2048 * hk + 512 * (cc4 + 1)],
                              writes=[("arw", "w1", cslot)])
                        for m in range(4):
                            mg = 4 * cc4 + m
                            for n2 in range(NW5):
                                bk = (m + n2) % 2

                                def hmm(e, m=m, n2=n2, bk=bk, cslot=cslot, th=th):
                                    for k in range(8):
                                        ins = e.matmul(
                                            ps[:, 512 * bk:512 * bk + W5],
                                            w1s[cslot][:, k, 128 * m:128 * (m + 1)],
                                            xkv[:, k, 8 + TH * th + W5 * n2:
                                                8 + TH * th + W5 * (n2 + 1)],
                                            start=(k == 0), stop=(k == 7))
                                    return ins
                                P.op("t", hmm, reads=[("arw", "w1", cslot), ("xnT",)],
                                     writes=[("ps", bk)])
                                hsl = hEv[:, mg, W5 * n2:W5 * (n2 + 1)]
                                bcol = SV_B1 + 16 * hk + mg
                                P.op("s", lambda e, hsl=hsl, bk=bk, bcol=bcol:
                                        e.activation(
                                        hsl, ps[:, 512 * bk:512 * bk + W5],
                                        AF.Gelu_apprx_tanh,
                                        bias=smallv[:, bcol:bcol + 1], scale=1.0),
                                     reads=[("ps", bk), ("sv", "b1")],
                                     writes=[("hE", mg)])
                    for sl in range(NSL):
                        for n in range(2):
                            bk = 2 + (sl * 2 + n) % 2

                            def emm(e, sl=sl, n=n, bk=bk):
                                for k in range(16):
                                    ins = e.matmul(ps[:, 512 * bk:512 * (bk + 1)],
                                                   hEv[:, k, 128 * sl:128 * (sl + 1)],
                                                   w2v[:, k, 512 * n:512 * (n + 1)],
                                                   start=(k == 0), stop=(k == 15))
                                return ins
                            P.op("t", emm,
                                 reads=[("hE", k) for k in range(16)] + [("arw", "w2")],
                                 writes=[("ps", bk)])
                            t_ = NSL * th + sl
                            acs = big[:, 1024 * sl + 512 * n:1024 * sl + 512 * (n + 1)]
                            cwc = cwtm1[:, NE1 * t_ + eix:NE1 * t_ + eix + 1]
                            P.op("v", lambda e, acs=acs, bk=bk, cwc=cwc:
                                    e.scalar_tensor_tensor(
                                        acs, ps[:, 512 * bk:512 * (bk + 1)], cwc, acs,
                                        ALU.mult, ALU.add),
                                 reads=[("ps", bk), ("eacc", sl, n), ("cwl1", t_)],
                                 writes=[("eacc", sl, n)])
                for sl in range(NSL):
                    t_ = NSL * th + sl
                    cwc = cwtm1[:, NE1 * t_ + eix:NE1 * t_ + eix + 1]
                    for n in range(2):
                        acs = big[:, 1024 * sl + 512 * n:1024 * sl + 512 * (n + 1)]
                        P.op("v", lambda e, acs=acs, n=n, cwc=cwc:
                                e.scalar_tensor_tensor(
                                    acs, bb[:, 512 * n:512 * (n + 1)], cwc, acs,
                                    ALU.mult, ALU.add),
                             reads=[("bb",), ("eacc", sl, n), ("cwl1", t_)],
                             writes=[("eacc", sl, n)])
            for tt in range(NSL):
                t_ = NSL * th + tt
                P.op("s", lambda e, tt=tt: e.activation(
                        ntile[:, :], big[:, 1024 * tt:1024 * (tt + 1)], AF.Silu),
                     reads=[("eacc", tt, 0), ("eacc", tt, 1)], writes=[("nt",)])
                P.dma("la", xtile[:, :], x1_scr.ap()[128 * t_:128 * (t_ + 1), :],
                      reads=[("scr_x1", t_)], writes=[("xt",)])
                P.dma("lb", stile[:, :], s_scr.ap()[128 * t_:128 * (t_ + 1), :],
                      reads=[("scr_s", t_)], writes=[("st",)])
                P.op("v", lambda e: e.tensor_tensor(ntile[:, :], stile[:, :],
                                                    ntile[:, :], ALU.mult),
                     reads=[("st",), ("nt",)], writes=[("nt",)])
                P.op("v", lambda e: e.tensor_tensor(xtile[:, :], xtile[:, :],
                                                    ntile[:, :], ALU.add),
                     reads=[("nt",), ("xt",)], writes=[("xt",)])
                P.dma("sa" if t_ % 2 == 0 else "sb",
                      out[128 * t_:128 * (t_ + 1), :], xtile[:, :],
                      reads=[("xt",)], writes=[("outw", t_)])

        return _finish()



def make_masks(q):
    i = np.arange(128)[:, None]
    j = np.arange(144)[None, :]
    interior = (j >= i) & (j <= i + 16)
    first = ((i <= 8) & (j >= 8) & (j <= i + 16)) | ((i > 8) & interior)
    last = ((128 - 1 - i <= 7) & (j >= i) & (j <= 135)) | ((i < 120) & interior)
    v1 = first if q == 0 else interior
    v2 = last if q == 3 else interior
    m = np.full((3, 128, 144), NEG, np.float32)
    for v, msk in enumerate((interior, v1, v2)):
        m[v][msk] = 0.0
    full = np.tile(m, (1, 1, 8))
    return full.astype(ml_dtypes.bfloat16)


def make_core_inputs(inputs, T=2048):
    x = np.asarray(inputs["L0_attn_x"])
    B, L, _ = x.shape
    percore = L // 4
    identf = np.eye(128, dtype=np.float32)
    identb = np.eye(128, dtype=ml_dtypes.bfloat16)
    dpow = np.tile((DECAY ** (np.arange(T, dtype=np.float64) + 1)
                    ).astype(np.float32)[None, :], (128, 1))
    wkeys = [k for k in inputs if k != "L0_attn_x"]
    in_maps = []
    for c in range(8):
        bsel, q = c // 4, c % 4
        lo, hi = q * percore, (q + 1) * percore
        xs = np.zeros((T + 16, D), np.float32)
        src_lo, src_hi = max(0, lo - 8), min(L, hi + 8)
        xs[8 - (lo - src_lo):8 + (src_hi - lo)] = x[bsel, src_lo:src_hi]
        fsel = np.zeros((8, 1), np.float32)
        if q > 0:
            fsel[c - 1, 0] = 1.0
        im = {k: np.ascontiguousarray(np.asarray(inputs[k])) for k in wkeys}
        im["x"] = xs
        im["identf"] = identf
        im["identb"] = identb
        im["maskp"] = np.ascontiguousarray(make_masks(q))
        im["dpow"] = dpow
        im["fsel"] = fsel
        in_maps.append(im)
    return in_maps


def assemble(results, T=2048):
    outs = [np.asarray(r["out"]) for r in results]
    return np.stack([np.concatenate(outs[0:4], 0),
                     np.concatenate(outs[4:8], 0)], 0)



def kernel(**inputs):
    from concourse.bass_utils import run_bass_kernel_spmd
    nc, _meta = build(T=2048)
    in_maps = make_core_inputs(inputs, T=2048)
    res = run_bass_kernel_spmd(nc, in_maps, core_ids=list(range(8)))
    return assemble(res.results, T=2048)


# version 04
# speedup vs baseline: 1.1558x; 1.1558x over previous
import contextlib
import numpy as np
import ml_dtypes
import concourse.bass as bass
import concourse.mybir as mybir

F32 = mybir.dt.float32
BF16 = mybir.dt.bfloat16
AF = mybir.ActivationFunctionType
ALU = mybir.AluOpType
AXX = mybir.AxisListType.X

D = 1024
NH = 16
HD = 64
ASCALE = HD ** (-0.5)
DI = 2048
NE0 = 16
F0 = 512
NE1 = 8
F1 = 4096
DECAY = 0.9
NEG = -30000.0

DMA_CHANS = {"wa": "g", "wb": "g", "wc": "s", "la": "y", "lb": "y", "sa": "y",
             "sb": "y", "cc": "g"}
ENG_SEM = {"t": "T", "v": "V", "s": "S", "g": "G", "y": "Y"}


class Prog:
    def __init__(self):
        self.ops = []
        self.cnt = {}
        self.writers = {}
        self.readers = {}
        self.waited = {}

    def _waits(self, eng, toks):
        best = {}
        for (sem, val) in toks:
            if eng == "t" and sem == "T":
                continue
            if val > best.get(sem, 0):
                best[sem] = val
        waits = []
        for sem, val in best.items():
            if self.waited.get((eng, sem), 0) < val:
                waits.append((sem, val))
                self.waited[(eng, sem)] = val
        return waits

    def op(self, eng, emit, reads=(), writes=(), sem=None, amt=1):
        sem = sem or ENG_SEM[eng]
        if eng in ("v", "s"):
            pbanks = {buf[1] for buf in tuple(reads) + tuple(writes)
                      if isinstance(buf, tuple) and buf and buf[0] == "ps"}
            if pbanks:
                writes = tuple(writes) + tuple(("plock", b) for b in sorted(pbanks))
        toks = []
        for buf in reads:
            toks += self.writers.get(buf, [])
        for buf in writes:
            toks += self.writers.get(buf, [])
            toks += self.readers.get(buf, [])
        waits = self._waits(eng, toks)
        self.cnt[sem] = self.cnt.get(sem, 0) + amt
        tok = (sem, self.cnt[sem])
        for buf in writes:
            self.writers[buf] = [tok]
            self.readers[buf] = []
        for buf in reads:
            self.readers.setdefault(buf, []).append(tok)
        self.ops.append((eng, waits, emit, sem, amt))
        return tok

    def dma(self, chan, out_ap, in_ap, reads=(), writes=()):
        return self.op(DMA_CHANS[chan],
                       lambda e: e.dma_start(out=out_ap, in_=in_ap),
                       reads=tuple(reads) + (("chan", chan),),
                       writes=tuple(writes) + (("chan", chan),),
                       sem=chan, amt=16)

    def rekey(self, old_keys, new_keys):
        toks = []
        for k in old_keys:
            toks += self.writers.pop(k, [])
            toks += self.readers.pop(k, [])
        for k in new_keys:
            self.writers[k] = self.writers.get(k, []) + toks

    def barrier(self, eng, bufs):
        toks = []
        for k in bufs:
            toks += self.writers.get(k, [])
            toks += self.readers.get(k, [])
        waits = self._waits(eng, toks)
        if waits:
            self.ops.append((eng, waits, lambda e: None, ENG_SEM[eng], 0))

    def emit_all(self, blk, sems):
        streams = {}
        for rec in self.ops:
            streams.setdefault(rec[0], []).append(rec)

        def run(e, engname):
            for (_eng, waits, emit, sem, amt) in streams[engname]:
                for (s_, v_) in waits:
                    e.wait_ge(sems[s_], v_)
                ins = emit(e)
                if ins is not None and amt > 0:
                    ins.then_inc(sems[sem], amt)

        for name, meth in (("t", blk.tensor), ("v", blk.vector), ("s", blk.scalar),
                           ("g", blk.gpsimd), ("y", blk.sync)):
            if name in streams:
                meth(lambda e, name=name: run(e, name))


class _Done(Exception):
    pass


def build(T=2048, taps=(), world=8, upto="E"):
    assert T % 512 == 0
    NT = T // 128
    NST = T // 512
    TH = T // 2
    NSL = TH // 128
    TW = T + 16
    W5 = min(512, TH)
    NW5 = max(1, TH // 512)

    nc = bass.Bass("TRN2", target_bir_lowering=False)
    dt = nc.dram_tensor

    x_in = dt("x", [TW, D], F32, kind="ExternalInput").ap()
    Wqkv = dt("L0_attn_Wqkv", [D, 3 * D], F32, kind="ExternalInput").ap()
    Wproj = dt("L0_attn_Wproj", [D, D], F32, kind="ExternalInput").ap()
    bproj = dt("L0_attn_bproj", [D], F32, kind="ExternalInput").ap()
    Wr = dt("L0_ffn_Wr", [D, NE0], F32, kind="ExternalInput").ap()
    Wg0 = dt("L0_ffn_Wg", [NE0, D, F0], F32, kind="ExternalInput").ap()
    Wu0 = dt("L0_ffn_Wu", [NE0, D, F0], F32, kind="ExternalInput").ap()
    Wd0 = dt("L0_ffn_Wd", [NE0, F0, D], F32, kind="ExternalInput").ap()
    Sg = dt("L0_ffn_Sg", [D, D], F32, kind="ExternalInput").ap()
    Su = dt("L0_ffn_Su", [D, D], F32, kind="ExternalInput").ap()
    Sd = dt("L0_ffn_Sd", [D, D], F32, kind="ExternalInput").ap()
    Win = dt("L1_attn_Win", [D, DI], F32, kind="ExternalInput").ap()
    bin_ = dt("L1_attn_bin_", [DI], F32, kind="ExternalInput").ap()
    Bp = dt("L1_attn_Bp", [DI, 16], F32, kind="ExternalInput").ap()
    Cp = dt("L1_attn_Cp", [16, DI], F32, kind="ExternalInput").ap()
    Dp = dt("L1_attn_Dp", [DI], F32, kind="ExternalInput").ap()
    Wout = dt("L1_attn_Wout", [DI, D], F32, kind="ExternalInput").ap()
    bout = dt("L1_attn_bout", [D], F32, kind="ExternalInput").ap()
    Wg1 = dt("L1_ffn_Wg", [D, NE1], F32, kind="ExternalInput").ap()
    bg1 = dt("L1_ffn_bg", [NE1], F32, kind="ExternalInput").ap()
    W1 = dt("L1_ffn_W1", [NE1, D, F1], F32, kind="ExternalInput").ap()
    b1 = dt("L1_ffn_b1", [NE1, F1], F32, kind="ExternalInput").ap()
    W2 = dt("L1_ffn_W2", [NE1, F1, D], F32, kind="ExternalInput").ap()
    b2 = dt("L1_ffn_b2", [NE1, D], F32, kind="ExternalInput").ap()
    identf = dt("identf", [128, 128], F32, kind="ExternalInput").ap()
    identb = dt("identb", [128, 128], BF16, kind="ExternalInput").ap()
    maskp = dt("maskp", [3, 128, 1152], BF16, kind="ExternalInput").ap()
    dpow_in = dt("dpow", [128, T], F32, kind="ExternalInput").ap()
    fsel_in = dt("fsel", [world, 1], F32, kind="ExternalInput").ap()

    out = dt("out", [T, D], F32, kind="ExternalOutput").ap()
    tap_shapes = {"xnT": ([128, 8, TW], BF16), "logits0": ([128, NT * NE0], F32),
                  "logits1": ([128, NT * NE1], F32),
                  "s0": ([T, D], F32), "x1": ([T, D], F32),
                  "acc0": ([128, NSL * D], F32), "xi": ([16, 128, T], F32),
                  "ypart": ([16, 128, T], F32), "s1": ([T, D], F32),
                  "cw0": ([128, NT * NE0], F32), "cw1": ([128, NT * NE1], F32),
                  "fprev": ([128, 16], F32), "rvec": ([128, 16], F32)}
    tap_aps = {nm: dt("tap_" + nm, shp, dty, kind="ExternalOutput").ap()
               for nm, (shp, dty) in tap_shapes.items() if nm in taps}

    s_scr = dt("s_scr", [T, D], F32)
    x1_scr = dt("x1_scr", [T, D], F32)
    yp_scr = dt("yp_scr", [16, 128, T], F32)
    cc_in = dt("cc_in", [1, DI], F32)
    cc_out = dt("cc_out", [world, DI], F32)
    fr_scr = dt("fr_scr", [1, DI], F32)

    with contextlib.ExitStack() as es:
        E = es.enter_context
        xnT = E(nc.sbuf_tensor("xnT", [128, 8 * TW], BF16))
        arw = E(nc.sbuf_tensor("arw", [128, 32768], BF16))
        big = E(nc.sbuf_tensor("big", [128, 8192], F32))
        aux = E(nc.sbuf_tensor("aux", [128, 16896], BF16))
        xtile = E(nc.sbuf_tensor("xtile", [128, D], F32))
        stile = E(nc.sbuf_tensor("stile", [128, D], F32))
        ntile = E(nc.sbuf_tensor("ntile", [128, D], F32))
        masks = E(nc.sbuf_tensor("masks", [128, 3 * 1152], BF16))
        stats = E(nc.sbuf_tensor("stats", [128, 64], F32))
        logit0 = E(nc.sbuf_tensor("logit0", [128, NT * NE0], F32))
        logit1 = E(nc.sbuf_tensor("logit1", [128, NT * NE1], F32))
        cwtm0 = E(nc.sbuf_tensor("cwtm0", [128, NT * NE0], F32))
        cwtm1 = E(nc.sbuf_tensor("cwtm1", [128, NT * NE1], F32))
        cwrows = E(nc.sbuf_tensor("cwrows", [128, 2048], F32))
        rows2 = E(nc.sbuf_tensor("rows2", [128, 1536], F32))
        bb = E(nc.sbuf_tensor("bb", [128, 1024], F32))
        smallv = E(nc.sbuf_tensor("smallv", [128, 176], F32))
        wrg = E(nc.sbuf_tensor("wrg", [128, 192], F32))
        tgt = E(nc.sbuf_tensor("tgt", [128, 512], BF16))
        identf_sb = E(nc.sbuf_tensor("identf_sb", [128, 128], F32))
        identb_sb = E(nc.sbuf_tensor("identb_sb", [128, 128], BF16))
        fsel_sb = E(nc.sbuf_tensor("fsel_sb", [world, 1], F32))
        ps = E(nc.psum_tensor("ps", [128, 4096], F32))

        psb = ps[:, :].bitcast(BF16)
        auxf = aux[:, :].bitcast(F32)
        blk = E(nc.Block())
        P = Prog()

        SV_BIN, SV_DP, SV_R, SV_FP, SV_FR, SV_B1, SV_CW = 0, 16, 32, 48, 64, 96, 128
        ST_NM, ST_SQ, ST_IV, ST_M1, ST_M2, ST_DEN = 0, 1, 2, 3, 4, 5
        R_CP, R_FT, R_FALL = 32, 96, 0
        bgrow_sb = stats[:, 48:56]

        def _finish():
            sem_names = set(list(P.cnt.keys()) + ["T", "V", "S", "G", "Y"])
            P.barrier("y", [k for k in P.writers if k[0] == "outw"]
                      + [("chan", c) for c in ("sa", "sb", "wa", "wb", "wc",
                                               "la", "lb", "cc")
                         if P.cnt.get(c)])
            sems = {k: E(nc.semaphore("sem_" + k)) for k in sorted(sem_names)}
            P.emit_all(blk, sems)
            return nc, dict(tap_aps=tap_aps, n_ops=len(P.ops))

        P.dma("la", identf_sb[:, :], identf, writes=[("idf",)])
        P.dma("lb", identb_sb[:, :], identb, writes=[("idb",)])
        P.dma("la", masks[:, :].rearrange("p (v n) -> p v n", v=3),
              maskp.rearrange("v p n -> p v n"), writes=[("masks",)])
        P.dma("lb", fsel_sb[:, :], fsel_in, writes=[("fsel",)])
        P.op("v", lambda e: e.memset(rows2[0:1, 0:128], 1.0), writes=[("r2", "ones")])
        for k in range(8):
            P.dma("la", wrg[:, 16 * k:16 * (k + 1)], Wr[128 * k:128 * (k + 1), :],
                  writes=[("wrg", 0)])
            P.dma("lb", wrg[:, 128 + 8 * k:128 + 8 * (k + 1)],
                  Wg1[128 * k:128 * (k + 1), :], writes=[("wrg", 1)])

        def bias_bcast(src_dram, chan="la"):
            P.dma(chan, rows2[0:1, 512:1536], src_dram.unsqueeze(0) if src_dram.ndim == 1 else src_dram,
                  reads=[("r2", "bstage")], writes=[("r2", "bstage")])

            def mm(e):
                for n2 in range(2):
                    ins = e.matmul(ps[:, 2048 + 512 * n2:2048 + 512 * (n2 + 1)],
                                   rows2[0:1, 0:128],
                                   rows2[0:1, 512 + 512 * n2:512 + 512 * (n2 + 1)],
                                   start=True, stop=True)
                return ins
            P.op("t", mm, reads=[("r2", "ones"), ("r2", "bstage")],
                 writes=[("ps", 4), ("ps", 5)])
            P.op("v", lambda e: e.tensor_copy(bb[:, :], ps[:, 2048:3072]),
                 reads=[("ps", 4), ("ps", 5)], writes=[("bb",)])

        def emit_ln(rows, xsrc, dst_off, lg_kind, lg_col, chan):
            P.dma(chan, xtile[0:rows, :], xsrc, writes=[("xt",)])
            P.op("v", lambda e: e.reduce_sum(stats[0:rows, ST_NM:ST_NM + 1],
                                             xtile[0:rows, :], AXX),
                 reads=[("xt",)], writes=[("stats", "nm")])
            P.op("v", lambda e: e.tensor_scalar(stats[0:rows, ST_NM:ST_NM + 1],
                                                stats[0:rows, ST_NM:ST_NM + 1],
                                                -1.0 / D, None, ALU.mult),
                 reads=[("stats", "nm")], writes=[("stats", "nm")])
            P.op("s", lambda e: e.activation(stile[0:rows, :], xtile[0:rows, :],
                                             AF.Square,
                                             bias=stats[0:rows, ST_NM:ST_NM + 1],
                                             scale=1.0,
                                             accum_out=stats[0:rows, ST_SQ:ST_SQ + 1]),
                 reads=[("xt",), ("stats", "nm")], writes=[("st",), ("stats", "sq")])
            P.op("v", lambda e: e.tensor_scalar(stats[0:rows, ST_IV:ST_IV + 1],
                                                stats[0:rows, ST_SQ:ST_SQ + 1],
                                                1.0 / D, 1e-5, ALU.mult, ALU.add),
                 reads=[("stats", "sq")], writes=[("stats", "iv")])
            P.op("s", lambda e: e.activation(stats[0:rows, ST_IV:ST_IV + 1],
                                             stats[0:rows, ST_IV:ST_IV + 1], AF.Sqrt),
                 reads=[("stats", "iv")], writes=[("stats", "iv")])
            P.op("v", lambda e: e.reciprocal(stats[0:rows, ST_IV:ST_IV + 1],
                                             stats[0:rows, ST_IV:ST_IV + 1]),
                 reads=[("stats", "iv")], writes=[("stats", "iv")])
            P.op("v", lambda e: e.tensor_scalar(ntile[0:rows, :], xtile[0:rows, :],
                                                stats[0:rows, ST_NM:ST_NM + 1],
                                                stats[0:rows, ST_IV:ST_IV + 1],
                                                ALU.add, ALU.mult),
                 reads=[("xt",), ("stats", "nm"), ("stats", "iv")], writes=[("nt",)])

            def tr(e):
                for k in range(8):
                    col = 512 * (k % 2) + 128 * (k // 2)
                    ins = e.transpose(ps[:, col:col + rows],
                                      ntile[0:rows, 128 * k:128 * (k + 1)],
                                      identf_sb[0:rows, 0:rows])
                return ins
            P.op("t", tr, reads=[("nt",), ("idf",)], writes=[("ps", 0), ("ps", 1)])
            xv = xnT[:, :].rearrange("p (k n) -> p k n", k=8)

            def ev(e):
                for k in range(8):
                    col = 512 * (k % 2) + 128 * (k // 2)
                    ins = e.tensor_copy(xv[:, k, dst_off:dst_off + rows],
                                        ps[:, col:col + rows])
                return ins
            P.op("v", ev, reads=[("ps", 0), ("ps", 1)], writes=[("xnT",)])

            if lg_kind is not None:
                which, ne = lg_kind
                skip_mm = which.startswith("skip")
                which = which.replace("skip", "")
                wcol = 0 if which == "l0" else 128
                lg_sb = logit0 if which == "l0" else logit1

                def evf(e):
                    for k in range(8):
                        col = 512 * (k % 2) + 128 * (k // 2)
                        ins = e.copy(auxf[:, 128 * k:128 * (k + 1)],
                                     ps[:, col:col + rows])
                    return ins
                P.op("s", evf, reads=[("ps", 0), ("ps", 1)], writes=[("aux", "xf")])
                if skip_mm:
                    return

                def rmm(e):
                    for k in range(8):
                        ins = e.matmul(ps[:, 1024:1024 + ne],
                                       auxf[:, 128 * k:128 * (k + 1)],
                                       wrg[:, wcol:wcol + 8 * ne]
                                       .rearrange("p (k j) -> p k j", k=8)[:, k, :],
                                       start=(k == 0), stop=(k == 7))
                    return ins
                P.op("t", rmm, reads=[("aux", "xf"), ("wrg", 0 if which == "l0" else 1)],
                     writes=[("ps", 2)])
                P.op("v", lambda e: e.tensor_copy(lg_sb[:, lg_col:lg_col + ne],
                                                  ps[:, 1024:1024 + ne]),
                     reads=[("ps", 2)], writes=[(which,)])

        emit_ln(8, x_in[0:8, :], 0, None, 0, "la")
        for t_ in range(NT):
            lgk = (None if upto == "A0" else
                   (("skipl0", NE0) if upto == "A1" else ("l0", NE0)))
            emit_ln(128, x_in[8 + 128 * t_:8 + 128 * (t_ + 1), :], 8 + 128 * t_,
                    lgk, t_ * NE0, "la" if t_ % 2 == 0 else "lb")
        emit_ln(8, x_in[8 + T:16 + T, :], 8 + T, None, 0, "lb")

        if "xnT" in taps:
            P.dma("sa", tap_aps["xnT"], xnT[:, :].rearrange("p (k n) -> p k n", k=8),
                  reads=[("xnT",)])
        if "logits0" in taps:
            P.dma("sb", tap_aps["logits0"], logit0[:, 0:NT * NE0], reads=[("l0",)])

        if upto in ("A", "A0", "A1"):
            return _finish()
        WQKV, WPROJ = 0, 24576
        wq_v = arw[:, WQKV:WQKV + 24576].rearrange("p (k n) -> p k n", k=8)
        wp_v = arw[:, WPROJ:WPROJ + 8192].rearrange("p (k n) -> p k n", k=8)
        for j, ch in ((0, "wa"), (1, "wb"), (2, "wa")):
            P.dma(ch, wq_v[:, :, 1024 * j:1024 * (j + 1)],
                  Wqkv.rearrange("(k p) n -> p k n", p=128)[:, :, 1024 * j:1024 * (j + 1)],
                  writes=[("arw", "qkv", j)])
        P.dma("wb", wp_v, Wproj.rearrange("(k p) n -> p k n", p=128),
              writes=[("arw", "proj")])
        bias_bcast(bproj, "la")

        AQ, AK, AV_, AT2, AA, AO = 0, 4096, 8320, 13440, 14464, 15616
        qv = aux[:, AQ:AQ + 4096].rearrange("p (k n) -> p k n", k=8)
        kv = aux[:, AK:AK + 4224].rearrange("p (k n) -> p k n", k=8)
        vv = aux[:, AV_:AV_ + 5120].rearrange("p (j n) -> p j n", j=5)
        at2sb = aux[0:16, AT2:AT2 + 1024]
        asb = aux[:, AA:AA + 1152]
        at1sb = aux[:, AA:AA + 1024]
        osb = aux[:, AO:AO + 1024]
        xkv = xnT[:, :].rearrange("p (k n) -> p k n", k=8)
        mk3 = masks[:, :].rearrange("p (v s x) -> p v s x", v=3, s=8)
        AUXB = ([("auxq", m) for m in range(8)] + [("auxk", m) for m in range(8)]
                + [("auxv", j) for j in range(5)]
                + [("auxa", h) for h in range(8)] + [("auxat1",), ("auxat2",),
                   ("auxo", 0), ("auxo", 1)])
        P.rekey([("aux", "xf")], AUXB)

        BLV = {"Bs0": -1, "Bs": 0, "Bsm": 1, "Bat": 2, "Bav": 3}.get(upto, 4)
        for st in range(NST):
            s0c = 512 * st
            for m in range(8):
                bk = m % 2

                def qmm(e, m=m, bk=bk, s0c=s0c):
                    for k in range(8):
                        ins = e.matmul(ps[:, 512 * bk:512 * bk + 512],
                                       wq_v[:, k, 128 * m:128 * (m + 1)],
                                       xkv[:, k, 8 + s0c:8 + s0c + 512],
                                       start=(k == 0), stop=(k == 7))
                    return ins
                P.op("t", qmm, reads=[("arw", "qkv", 0), ("xnT",)], writes=[("ps", bk)])
                P.op("s", lambda e, m=m, bk=bk: e.activation(
                        qv[:, m, :], ps[:, 512 * bk:512 * bk + 512], AF.Copy,
                        scale=ASCALE),
                     reads=[("ps", bk)], writes=[("auxq", m)])
            for m in range(8):
                bk = m % 2

                def kmm(e, m=m, bk=bk, s0c=s0c):
                    for k in range(8):
                        e.matmul(ps[:, 512 * bk:512 * bk + 512],
                                 wq_v[:, k, 1024 + 128 * m:1024 + 128 * (m + 1)],
                                 xkv[:, k, s0c:s0c + 512],
                                 start=(k == 0), stop=(k == 7))
                    for k in range(8):
                        ins = e.matmul(ps[:, 2048 + 16 * bk:2048 + 16 * (bk + 1)],
                                       wq_v[:, k, 1024 + 128 * m:1024 + 128 * (m + 1)],
                                       xkv[:, k, s0c + 512:s0c + 528],
                                       start=(k == 0), stop=(k == 7))
                    return ins
                P.op("t", kmm, reads=[("arw", "qkv", 1), ("xnT",)],
                     writes=[("ps", bk), ("ps", 4)])

                def kev(e, m=m, bk=bk):
                    e.copy(kv[:, m, 0:512], ps[:, 512 * bk:512 * bk + 512])
                    return e.copy(kv[:, m, 512:528],
                                  ps[:, 2048 + 16 * bk:2048 + 16 * (bk + 1)])
                P.op("s", kev, reads=[("ps", bk), ("ps", 4)], writes=[("auxk", m)])
            for j in range(5):
                rj = 128 if (j < 4 or st < NST - 1) else 16

                def vmm(e, j=j, s0c=s0c, rj=rj):
                    for n2 in range(2):
                        for k in range(8):
                            ins = e.matmul(ps[0:rj, 512 * n2:512 * (n2 + 1)],
                                           xkv[:, k, s0c + 128 * j:s0c + 128 * j + rj],
                                           wq_v[:, k, 2048 + 512 * n2:2048 + 512 * (n2 + 1)],
                                           start=(k == 0), stop=(k == 7))
                    return ins
                P.op("t", vmm, reads=[("arw", "qkv", 2), ("xnT",)],
                     writes=[("ps", 0), ("ps", 1)])
                P.op("v", lambda e, j=j, rj=rj: e.tensor_copy(vv[0:rj, j, :],
                                                              ps[0:rj, 0:1024]),
                     reads=[("ps", 0), ("ps", 1)], writes=[("auxv", j)])

            if upto == "Bq":
                continue
            for tl in range(4):
                t_ = 4 * st + tl
                mvar = 1 if t_ == 0 else (2 if t_ == NT - 1 else 0)
                for hg in range(2):
                    def smm(e, tl=tl, hg=hg):
                        for hh in range(8):
                            h = 8 * hg + hh
                            po, chn = 64 * (h % 2), h // 2
                            s_ = (hh % 2) * 4 + hh // 2
                            ins = e.matmul(
                                ps[:, 256 * s_:256 * s_ + 144],
                                qv[po:po + 64, chn, 128 * tl:128 * (tl + 1)],
                                kv[po:po + 64, chn, 128 * tl:128 * tl + 144],
                                start=True, stop=True)
                        return ins
                    P.op("t", smm,
                         reads=[("auxq", m) for m in range(8)]
                               + [("auxk", m) for m in range(8)],
                         writes=[("ps", 0), ("ps", 1), ("ps", 2), ("ps", 3)])
                    if BLV < 0:
                        continue
                    if upto == "Bnm":
                        def mka(e, mvar=mvar):
                            for hh in range(8):
                                ins = e.tensor_tensor(
                                    ps[:, 256 * hh:256 * hh + 144],
                                    ps[:, 256 * hh:256 * hh + 144],
                                    mk3[:, mvar, hh, :], ALU.add)
                            return ins
                        P.op("v", mka,
                             reads=[("ps", 0), ("ps", 1), ("ps", 2), ("ps", 3),
                                    ("masks",)],
                             writes=[("ps", 0), ("ps", 1), ("ps", 2), ("ps", 3)])
                    else:
                        psv = ps[:, 0:2048].rearrange("p (sl x) -> p sl x", sl=8)
                        P.op("v", lambda e, psv=psv, mvar=mvar: e.tensor_tensor(
                                psv[:, :, 0:144], psv[:, :, 0:144],
                                mk3[:, mvar, :, :], ALU.add),
                             reads=[("ps", 0), ("ps", 1), ("ps", 2), ("ps", 3),
                                    ("masks",)],
                             writes=[("ps", 0), ("ps", 1), ("ps", 2), ("ps", 3)])
                    av8 = asb[:, :].rearrange("p (sl x) -> p sl x", sl=8)
                    if BLV < 1:
                        continue
                    for hh in range(8):
                        s_ = (hh % 2) * 4 + hh // 2
                        P.op("v", lambda e, s_=s_, hh=hh: e.reduce_max(
                                stats[:, 32 + hh:33 + hh],
                                ps[:, 256 * s_:256 * s_ + 144], AXX, negate=True),
                             reads=[("ps", s_ // 2)], writes=[("stats", "nx", hh)])
                    for hh in range(8):
                        s_ = (hh % 2) * 4 + hh // 2
                        P.op("s", lambda e, s_=s_, hh=hh, av8=av8: e.activation(
                                av8[:, hh, :], ps[:, 256 * s_:256 * s_ + 144], AF.Exp,
                                bias=stats[:, 32 + hh:33 + hh], scale=1.0,
                                accum_out=stats[:, 40 + hh:41 + hh]),
                             reads=[("ps", s_ // 2), ("stats", "nx", hh)],
                             writes=[("auxa", hh), ("stats", "l", hh)])
                    P.op("v", lambda e: e.reciprocal(stats[:, 40:48], stats[:, 40:48]),
                         reads=[("stats", "l", h) for h in range(8)],
                         writes=[("stats", "linv")])
                    for hh in range(8):
                        P.op("v", lambda e, hh=hh, av8=av8: e.tensor_scalar(
                                av8[:, hh, :], av8[:, hh, :],
                                stats[:, 40 + hh:41 + hh], None, ALU.mult),
                             reads=[("auxa", hh), ("stats", "linv")],
                             writes=[("auxa", hh)])

                    if BLV < 2:
                        continue

                    def atr(e, av8=av8):
                        for hh in range(8):
                            e.transpose(psb[:, 4096 + 128 * hh:4096 + 128 * (hh + 1)],
                                        av8[:, hh, 0:128], identb_sb[:, :])
                            ins = e.transpose(
                                psb[0:16, 5120 + 128 * hh:5120 + 128 * (hh + 1)],
                                av8[:, hh, 128:144], identb_sb[:, :])
                        return ins
                    P.op("t", atr,
                         reads=[("auxa", h) for h in range(8)] + [("idb",)],
                         writes=[("ps", 4), ("ps", 5)])
                    P.op("v", lambda e: e.tensor_copy(at1sb[:, :], psb[:, 4096:5120]),
                         reads=[("ps", 4)] + [("auxa", h) for h in range(8)],
                         writes=[("auxat1",)])
                    P.op("s", lambda e: e.copy(at2sb[:, :], psb[0:16, 5120:6144]),
                         reads=[("ps", 5)], writes=[("auxat2",)])
                    if BLV < 3:
                        continue

                    def avmm(e, tl=tl, hg=hg):
                        for hh in range(8):
                            h = 8 * hg + hh
                            po = 64 * (hh % 2)
                            c2 = hh // 2
                            e.matmul(ps[po:po + 64,
                                        3072 + 128 * c2:3072 + 128 * (c2 + 1)],
                                     vv[:, tl, 64 * h:64 * (h + 1)],
                                     at1sb[:, 128 * hh:128 * (hh + 1)],
                                     start=(hh <= 1), stop=False,
                                     skip_group_check=(hh > 1))
                            ins = e.matmul(
                                ps[po:po + 64, 3072 + 128 * c2:3072 + 128 * (c2 + 1)],
                                vv[0:16, tl + 1, 64 * h:64 * (h + 1)],
                                at2sb[:, 128 * hh:128 * (hh + 1)],
                                start=False, stop=(hh >= 6),
                                skip_group_check=(hh < 6))
                        return ins
                    P.op("t", avmm,
                         reads=[("auxat1",), ("auxat2",), ("auxv", tl),
                                ("auxv", tl + 1)],
                         writes=[("ps", 6)])
                    P.op("v", lambda e, hg=hg: e.tensor_copy(
                            osb[:, 512 * hg:512 * (hg + 1)], ps[:, 3072:3584]),
                         reads=[("ps", 6)],
                         writes=[("auxo", hg)])
                if BLV < 4:
                    continue
                ov = osb.rearrange("p (c n) -> p c n", c=8)
                for n2 in range(2):
                    def pmm(e, n2=n2):
                        for k in range(8):
                            ins = e.matmul(ps[:, 512 * n2:512 * (n2 + 1)],
                                           ov[:, k, :],
                                           wp_v[:, k, 512 * n2:512 * (n2 + 1)],
                                           start=(k == 0), stop=(k == 7))
                        return ins
                    P.op("t", pmm, reads=[("auxo", 0), ("auxo", 1), ("arw", "proj")],
                         writes=[("ps", n2)])
                P.op("v", lambda e: e.tensor_tensor(stile[:, :], ps[:, 0:1024],
                                                    bb[:, :], ALU.add),
                     reads=[("ps", 0), ("ps", 1), ("bb",)], writes=[("st",)])
                P.dma("sa" if t_ % 2 == 0 else "sb",
                      s_scr.ap()[128 * t_:128 * (t_ + 1), :], stile[:, :],
                      reads=[("st",)], writes=[("scr_s", t_)])

        if "s0" in taps:
            P.dma("sa", tap_aps["s0"], s_scr.ap(),
                  reads=[("scr_s", t_) for t_ in range(NT)])

        if upto in ("B", "Bq", "Bnm", "Bs", "Bs0", "Bsm", "Bat", "Bav"):
            return _finish()
        def emit_cw(which, ne, tile, normalize):
            lg_sb = logit0 if which == "l0" else logit1
            cwt = cwtm0 if which == "l0" else cwtm1
            lg = lg_sb[:, ne * tile:ne * (tile + 1)]
            cwtmp = smallv[:, SV_CW:SV_CW + ne]
            lkey = (which,)
            if which == "l1":
                P.op("v", lambda e, lg=lg: e.tensor_tensor(
                        smallv[:, SV_CW + 32:SV_CW + 32 + ne], lg,
                        bgrow_sb[:, 0:ne], ALU.add),
                     reads=[lkey, ("bgB",)], writes=[("svl1",)])
                lg = smallv[:, SV_CW + 32:SV_CW + 32 + ne]
                lkey = ("svl1",)
            P.op("v", lambda e, lg=lg: e.reduce_max(stats[:, ST_NM:ST_NM + 1], lg,
                                                    AXX, negate=True),
                 reads=[lkey], writes=[("stats", "nm")])
            P.op("s", lambda e, lg=lg: e.activation(cwtmp, lg, AF.Exp,
                                                    bias=stats[:, ST_NM:ST_NM + 1],
                                                    scale=1.0,
                                                    accum_out=stats[:, ST_SQ:ST_SQ + 1]),
                 reads=[lkey, ("stats", "nm")], writes=[("svcw",), ("stats", "sq")])
            P.op("v", lambda e: e.reciprocal(stats[:, ST_IV:ST_IV + 1],
                                             stats[:, ST_SQ:ST_SQ + 1]),
                 reads=[("stats", "sq")], writes=[("stats", "iv")])
            P.op("v", lambda e: e.tensor_scalar(cwtmp, cwtmp,
                                                stats[:, ST_IV:ST_IV + 1], None,
                                                ALU.mult),
                 reads=[("svcw",), ("stats", "iv")], writes=[("svcw",)])
            P.op("v", lambda e: e.reduce_max(stats[:, ST_M1:ST_M1 + 1], cwtmp, AXX),
                 reads=[("svcw",)], writes=[("stats", "m1")])
            P.op("v", lambda e: e.tensor_scalar(ntile[:, 0:ne], cwtmp,
                                                stats[:, ST_M1:ST_M1 + 1], None,
                                                ALU.is_ge),
                 reads=[("svcw",), ("stats", "m1")], writes=[("nt",)])
            P.op("v", lambda e: e.scalar_tensor_tensor(ntile[:, 0:ne], ntile[:, 0:ne],
                                                       -4.0, cwtmp, ALU.mult, ALU.add),
                 reads=[("nt",), ("svcw",)], writes=[("nt",)])
            P.op("v", lambda e: e.reduce_max(stats[:, ST_M2:ST_M2 + 1], ntile[:, 0:ne],
                                             AXX),
                 reads=[("nt",)], writes=[("stats", "m2")])
            P.op("v", lambda e: e.tensor_scalar(ntile[:, 0:ne], cwtmp,
                                                stats[:, ST_M2:ST_M2 + 1], None,
                                                ALU.is_ge),
                 reads=[("svcw",), ("stats", "m2")], writes=[("nt",)])
            P.op("v", lambda e: e.tensor_tensor(cwtmp, cwtmp, ntile[:, 0:ne], ALU.mult),
                 reads=[("svcw",), ("nt",)], writes=[("svcw",)])
            if normalize:
                P.op("v", lambda e: e.tensor_tensor(stats[:, ST_DEN:ST_DEN + 1],
                                                    stats[:, ST_M1:ST_M1 + 1],
                                                    stats[:, ST_M2:ST_M2 + 1], ALU.add),
                     reads=[("stats", "m1"), ("stats", "m2")],
                     writes=[("stats", "den")])
                P.op("v", lambda e: e.tensor_scalar(stats[:, ST_DEN:ST_DEN + 1],
                                                    stats[:, ST_DEN:ST_DEN + 1],
                                                    1e-8, None, ALU.add),
                     reads=[("stats", "den")], writes=[("stats", "den")])
                P.op("v", lambda e: e.reciprocal(stats[:, ST_DEN:ST_DEN + 1],
                                                 stats[:, ST_DEN:ST_DEN + 1]),
                     reads=[("stats", "den")], writes=[("stats", "den")])
                P.op("v", lambda e: e.tensor_scalar(cwtmp, cwtmp,
                                                    stats[:, ST_DEN:ST_DEN + 1], None,
                                                    ALU.mult),
                     reads=[("svcw",), ("stats", "den")], writes=[("svcw",)])
            P.op("v", lambda e, tile=tile: e.tensor_copy(
                    cwt[:, ne * tile:ne * (tile + 1)], cwtmp),
                 reads=[("svcw",)], writes=[("cw" + which, tile)])

        for t_ in range(NT):
            emit_cw("l0", NE0, t_, False)
        if "cw0" in taps:
            P.dma("sb", tap_aps["cw0"], cwtm0[:, :],
                  reads=[("cwl0", t_) for t_ in range(NT)])

        if upto == "cw0":
            return _finish()
        HA, HB = 0, 4096
        SLOT = [0, 12288]
        P.rekey(AUXB, [("aux", "hA"), ("aux", "hB")])
        P.rekey([("arw", "qkv", 0), ("arw", "qkv", 1), ("arw", "qkv", 2),
                 ("arw", "proj")],
                [("arw", "ug", 0), ("arw", "uu", 0), ("arw", "ud", 0),
                 ("arw", "ug", 1), ("arw", "uu", 1), ("arw", "ud", 1)])

        def c_unit_dma(u):
            slot = SLOT[u % 2]
            ch = "wa" if u % 2 == 0 else "wb"
            gv = arw[:, slot:slot + 4096].rearrange("p (k n) -> p k n", k=8)
            uv = arw[:, slot + 4096:slot + 8192].rearrange("p (k n) -> p k n", k=8)
            dv = arw[:, slot + 8192:slot + 12288].rearrange("p (k n) -> p k n", k=4)
            if u < NE0:
                P.dma(ch, gv, Wg0[u].rearrange("(k p) n -> p k n", p=128),
                      writes=[("arw", "ug", u % 2)])
                P.dma(ch, uv, Wu0[u].rearrange("(k p) n -> p k n", p=128),
                      writes=[("arw", "uu", u % 2)])
                P.dma(ch, dv, Wd0[u].rearrange("(k p) n -> p k n", p=128),
                      writes=[("arw", "ud", u % 2)])
            else:
                f = u - NE0
                P.dma(ch, gv,
                      Sg.rearrange("(k p) n -> p k n", p=128)[:, :, 512 * f:512 * (f + 1)],
                      writes=[("arw", "ug", u % 2)])
                P.dma(ch, uv,
                      Su.rearrange("(k p) n -> p k n", p=128)[:, :, 512 * f:512 * (f + 1)],
                      writes=[("arw", "uu", u % 2)])
                P.dma(ch, dv,
                      Sd.rearrange("(k p) n -> p k n", p=128)[:, 4 * f:4 * (f + 1), :],
                      writes=[("arw", "ud", u % 2)])

        def c_unit_compute(u, th):
            slot = SLOT[u % 2]
            hoff = HA if u % 2 == 0 else HB
            hkey = ("aux", "hA" if u % 2 == 0 else "hB")
            gv = arw[:, slot:slot + 4096].rearrange("p (k n) -> p k n", k=8)
            uv = arw[:, slot + 4096:slot + 8192].rearrange("p (k n) -> p k n", k=8)
            dv = arw[:, slot + 8192:slot + 12288].rearrange("p (k n) -> p k n", k=4)
            hv = aux[:, hoff:hoff + 4 * TH].rearrange("p (m n) -> p m n", m=4)
            for m in range(4):
                for n2 in range(NW5):
                    def gmm(e, m=m, n2=n2, th=th, gv=gv):
                        for k in range(8):
                            ins = e.matmul(ps[:, 0:W5],
                                           gv[:, k, 128 * m:128 * (m + 1)],
                                           xkv[:, k, 8 + TH * th + W5 * n2:
                                               8 + TH * th + W5 * (n2 + 1)],
                                           start=(k == 0), stop=(k == 7))
                        return ins
                    P.op("t", gmm, reads=[("arw", "ug", u % 2), ("xnT",)],
                         writes=[("ps", 0)])
                    P.op("s", lambda e: e.activation(tgt[:, 0:W5], ps[:, 0:W5],
                                                     AF.Silu),
                         reads=[("ps", 0)], writes=[("tg",)])

                    def umm(e, m=m, n2=n2, th=th, uv=uv):
                        for k in range(8):
                            ins = e.matmul(ps[:, 512:512 + W5],
                                           uv[:, k, 128 * m:128 * (m + 1)],
                                           xkv[:, k, 8 + TH * th + W5 * n2:
                                               8 + TH * th + W5 * (n2 + 1)],
                                           start=(k == 0), stop=(k == 7))
                        return ins
                    P.op("t", umm, reads=[("arw", "uu", u % 2), ("xnT",)],
                         writes=[("ps", 1)])
                    hslice = hv[:, m, W5 * n2:W5 * (n2 + 1)]
                    P.op("v", lambda e, hslice=hslice: e.tensor_tensor(
                            hslice, tgt[:, 0:W5], ps[:, 512:512 + W5], ALU.mult),
                         reads=[("tg",), ("ps", 1)], writes=[hkey])
            for sl in range(NSL):
                for n in range(2):
                    bk = 2 + (sl * 2 + n) % 2

                    def dmm(e, sl=sl, n=n, bk=bk, hv=hv, dv=dv):
                        for k in range(4):
                            ins = e.matmul(ps[:, 512 * bk:512 * (bk + 1)],
                                           hv[:, k, 128 * sl:128 * (sl + 1)],
                                           dv[:, k, 512 * n:512 * (n + 1)],
                                           start=(k == 0), stop=(k == 3))
                        return ins
                    P.op("t", dmm, reads=[hkey, ("arw", "ud", u % 2)],
                         writes=[("ps", bk)])
                    acs = big[:, 1024 * sl + 512 * n:1024 * sl + 512 * (n + 1)]
                    t_ = NSL * th + sl
                    if u < NE0:
                        cwc = cwtm0[:, NE0 * t_ + u:NE0 * t_ + u + 1]
                        P.op("v", lambda e, acs=acs, bk=bk, cwc=cwc:
                                e.scalar_tensor_tensor(
                                    acs, ps[:, 512 * bk:512 * (bk + 1)], cwc, acs,
                                    ALU.mult, ALU.add),
                             reads=[("ps", bk), ("acc", sl, n), ("cwl0", t_)],
                             writes=[("acc", sl, n)])
                    else:
                        P.op("v", lambda e, acs=acs, bk=bk: e.tensor_tensor(
                                acs, acs, ps[:, 512 * bk:512 * (bk + 1)], ALU.add),
                             reads=[("ps", bk), ("acc", sl, n)],
                             writes=[("acc", sl, n)])

        for th in range(2):
            for sl in range(NSL):
                for n in range(2):
                    P.op("v", lambda e, sl=sl, n=n: e.memset(
                            big[:, 1024 * sl + 512 * n:1024 * sl + 512 * (n + 1)], 0.0),
                         writes=[("acc", sl, n)])
            c_unit_dma(0)
            c_unit_dma(1)
            for u in range(18):
                c_unit_compute(u, th)
                if u + 2 < 18:
                    c_unit_dma(u + 2)
            for tt in range(NSL):
                t_ = NSL * th + tt
                P.op("s", lambda e, tt=tt: e.activation(
                        ntile[:, :], big[:, 1024 * tt:1024 * (tt + 1)], AF.Silu),
                     reads=[("acc", tt, 0), ("acc", tt, 1)], writes=[("nt",)])
                P.dma("la", xtile[:, :], x_in[8 + 128 * t_:8 + 128 * (t_ + 1), :],
                      writes=[("xt",)])
                P.dma("lb", stile[:, :], s_scr.ap()[128 * t_:128 * (t_ + 1), :],
                      reads=[("scr_s", t_)], writes=[("st",)])
                P.op("v", lambda e: e.tensor_tensor(ntile[:, :], stile[:, :],
                                                    ntile[:, :], ALU.mult),
                     reads=[("st",), ("nt",)], writes=[("nt",)])
                P.op("v", lambda e: e.tensor_tensor(xtile[:, :], xtile[:, :],
                                                    ntile[:, :], ALU.add),
                     reads=[("nt",), ("xt",)], writes=[("xt",)])
                P.dma("sa" if t_ % 2 == 0 else "sb",
                      x1_scr.ap()[128 * t_:128 * (t_ + 1), :], xtile[:, :],
                      reads=[("xt",)], writes=[("scr_x1", t_)])
        if "acc0" in taps:
            P.dma("sb", tap_aps["acc0"], big[:, 0:NSL * 1024],
                  reads=[("acc", sl, n) for sl in range(NSL) for n in range(2)])
        if "x1" in taps:
            P.dma("sa", tap_aps["x1"], x1_scr.ap(),
                  reads=[("scr_x1", t_) for t_ in range(NT)])

        if upto == "C":
            return _finish()
        P.rekey([("aux", "hA"), ("aux", "hB")], [("aux", "xf")])
        for t_ in range(NT):
            emit_ln(128, x1_scr.ap()[128 * t_:128 * (t_ + 1), :], 8 + 128 * t_,
                    ("l1", NE1), t_ * NE1, "la" if t_ % 2 == 0 else "lb")
        if "logits1" in taps:
            P.dma("sb", tap_aps["logits1"], logit1[:, 0:NT * NE1], reads=[("l1",)])

        P.dma("la", rows2[0:1, 128:128 + NE1], bg1.unsqueeze(0), writes=[("r2", "bg")])
        P.op("t", lambda e: e.matmul(ps[:, 1024:1024 + NE1],
                                     rows2[0:1, 0:128],
                                     rows2[0:1, 128:128 + NE1],
                                     start=True, stop=True),
             reads=[("r2", "ones"), ("r2", "bg")], writes=[("ps", 2)])
        P.op("v", lambda e: e.tensor_copy(bgrow_sb, ps[:, 1024:1024 + NE1]),
             reads=[("ps", 2)], writes=[("bgB",)])

        for t_ in range(NT):
            emit_cw("l1", NE1, t_, True)
        if "cw1" in taps:
            P.dma("sb", tap_aps["cw1"], cwtm1[:, :],
                  reads=[("cwl1", t_) for t_ in range(NT)])

        if upto == "ln1":
            return _finish()
        P.dma("la", cwrows[R_CP:R_CP + 16, 0:DI], Cp, writes=[("cp",)])
        P.dma("lb", rows2[32:48, 0:128], bin_.rearrange("(c p) -> c p", p=128),
              reads=[("r2", "stage")], writes=[("r2", "stage")])
        P.op("t", lambda e: e.transpose(ps[:, 3584:3584 + 16],
                                        rows2[32:48, 0:128],
                                        identf_sb[32:48, 32:48]),
             reads=[("r2", "stage"), ("idf",)], writes=[("ps", 7)])
        P.op("v", lambda e: e.tensor_copy(smallv[:, SV_BIN:SV_BIN + 16],
                                          ps[:, 3584:3584 + 16]),
             reads=[("ps", 7)], writes=[("sv", "bin")])
        P.dma("lb", rows2[32:48, 0:128], Dp.rearrange("(c p) -> c p", p=128),
              reads=[("r2", "stage")], writes=[("r2", "stage")])
        P.op("t", lambda e: e.transpose(ps[:, 3584:3584 + 16],
                                        rows2[32:48, 0:128],
                                        identf_sb[32:48, 32:48]),
             reads=[("r2", "stage"), ("idf",)], writes=[("ps", 7)])
        P.op("v", lambda e: e.tensor_copy(smallv[:, SV_DP:SV_DP + 16],
                                          ps[:, 3584:3584 + 16]),
             reads=[("ps", 7)], writes=[("sv", "dp")])
        for c in range(16):
            P.dma("la" if c % 2 == 0 else "lb", xtile[:, 0:16],
                  Bp[128 * c:128 * (c + 1), :], reads=[("xt",)], writes=[("xt",)])
            P.op("t", lambda e, c=c: e.transpose(ps[:, 3584:3584 + 16],
                                                 cwrows[R_CP:R_CP + 16,
                                                        128 * c:128 * (c + 1)],
                                                 identf_sb[32:48, 32:48]),
                 reads=[("cp",), ("idf",)], writes=[("ps", 7)])
            P.op("v", lambda e: e.tensor_tensor(ntile[:, 0:16], xtile[:, 0:16],
                                                ps[:, 3584:3584 + 16], ALU.mult),
                 reads=[("xt",), ("ps", 7)], writes=[("nt",)])
            P.op("v", lambda e, c=c: e.reduce_sum(smallv[:, SV_R + c:SV_R + c + 1],
                                                  ntile[:, 0:16], AXX),
                 reads=[("nt",)], writes=[("sv", "r", c)])
        if "rvec" in taps:
            P.dma("sb", tap_aps["rvec"], smallv[:, SV_R:SV_R + 16],
                  reads=[("sv", "r", c) for c in range(16)])

        bias_bcast(bout, "la")

        P.rekey([("acc", sl, n) for sl in range(NSL) for n in range(2)],
                [("bd", "xi"), ("bd", "c"), ("bd", "y"), ("bd", "dec")])
        xi_sb = big[:, 0:T]
        csc_sb = big[:, 2048:2048 + T]
        y_sb = big[:, 4096:4096 + T]
        dec_sb = big[:, 6144:6144 + T]
        P.op("v", lambda e: e.memset(dec_sb, DECAY), writes=[("bd", "dec")])

        P.rekey([("arw", "ug", 0), ("arw", "ug", 1),
                 ("arw", "uu", 0), ("arw", "uu", 1), ("arw", "ud", 0),
                 ("arw", "ud", 1)],
                [("arw", "win"), ("arw", "wout")])
        win_v = arw[:, 0:16384].rearrange("p (k n) -> p k n", k=8)
        wout_v = arw[:, 16384:32768].rearrange("p (k n) -> p k n", k=16)
        P.dma("wa", win_v[:, :, 0:1024],
              Win.rearrange("(k p) n -> p k n", p=128)[:, :, 0:1024],
              writes=[("arw", "win")])
        P.dma("wb", win_v[:, :, 1024:2048],
              Win.rearrange("(k p) n -> p k n", p=128)[:, :, 1024:2048],
              writes=[("arw", "win")])
        for hw_ in range(2):
            P.dma("wa", wout_v[:, 8 * hw_:8 * (hw_ + 1), :],
                  Wout.rearrange("(k p) n -> p k n", p=128)[:, 8 * hw_:8 * (hw_ + 1), :],
                  writes=[("arw", "wout")])

        NXI = T // 512
        for c in range(16):
            po_ = 2048 * (c % 2)
            bnks = [4 * (c % 2) + j for j in range(NXI)]

            def xmm(e, c=c, po_=po_):
                for n2 in range(NXI):
                    for k in range(8):
                        ins = e.matmul(ps[:, po_ + 512 * n2:po_ + 512 * (n2 + 1)],
                                       win_v[:, k, 128 * c:128 * (c + 1)],
                                       xkv[:, k, 8 + 512 * n2:8 + 512 * (n2 + 1)],
                                       start=(k == 0), stop=(k == 7))
                return ins
            P.op("t", xmm, reads=[("arw", "win"), ("xnT",)],
                 writes=[("ps", bnk) for bnk in bnks])
            P.op("s", lambda e, c=c, po_=po_: e.activation(
                    xi_sb, ps[:, po_:po_ + T], AF.Silu,
                    bias=smallv[:, SV_BIN + c:SV_BIN + c + 1], scale=1.0),
                 reads=[("ps", bnk) for bnk in bnks] + [("sv", "bin")],
                 writes=[("bd", "xi")])
            if "xi" in taps:
                P.dma("sb", tap_aps["xi"][c], xi_sb, reads=[("bd", "xi")])
            P.op("v", lambda e: e.tensor_tensor_scan(csc_sb, dec_sb, xi_sb, 0.0,
                                                     ALU.mult, ALU.add),
                 reads=[("bd", "dec"), ("bd", "xi")], writes=[("bd", "c")])
            P.op("v", lambda e, c=c: e.tensor_copy(smallv[:, SV_FR + c:SV_FR + c + 1],
                                                   csc_sb[:, T - 1:T]),
                 reads=[("bd", "c")], writes=[("sv", "f", c)])
            P.op("v", lambda e, c=c: e.tensor_scalar(y_sb, xi_sb,
                                                     smallv[:, SV_DP + c:
                                                            SV_DP + c + 1],
                                                     None, ALU.mult),
                 reads=[("bd", "xi"), ("sv", "dp")], writes=[("bd", "y")])
            P.op("v", lambda e, c=c: e.scalar_tensor_tensor(
                    y_sb, csc_sb, smallv[:, SV_R + c:SV_R + c + 1], y_sb,
                    ALU.mult, ALU.add),
                 reads=[("bd", "c"), ("sv", "r", c), ("bd", "y")],
                 writes=[("bd", "y")])
            P.dma("sa" if c % 2 == 0 else "sb", yp_scr.ap()[c], y_sb,
                  reads=[("bd", "y")], writes=[("scr_yp", c)])
        if "ypart" in taps:
            P.dma("sa", tap_aps["ypart"], yp_scr.ap(),
                  reads=[("scr_yp", c) for c in range(16)])

        P.op("t", lambda e: e.transpose(ps[0:16, 1024:1024 + 128],
                                        smallv[:, SV_FR:SV_FR + 16], identf_sb[:, :]),
             reads=[("sv", "f", c) for c in range(16)] + [("idf",)],
             writes=[("ps", 2)])
        P.op("v", lambda e: e.tensor_copy(cwrows[R_FT:R_FT + 16, 0:128],
                                          ps[0:16, 1024:1024 + 128]),
             reads=[("ps", 2)], writes=[("fT",)])
        P.dma("cc", cc_in.ap().rearrange("a (c p) -> (a c) p", p=128),
              cwrows[R_FT:R_FT + 16, 0:128], reads=[("fT",)], writes=[("ccin",)])
        P.op("g", lambda e: e.collective_compute(
                "AllGather", ALU.bypass, replica_groups=[list(range(world))],
                ins=[cc_in.ap()], outs=[cc_out.ap()]),
             reads=[("ccin",)], writes=[("ccout",)], sem="ccc")
        P.dma("cc", cwrows[R_FALL:R_FALL + world, 0:DI], cc_out.ap(),
              reads=[("ccout",)], writes=[("Fall",)])

        def fsel_mm(e):
            for n2 in range(DI // 512):
                ins = e.matmul(ps[0:1, 512 * n2:512 * (n2 + 1)], fsel_sb[0:world, :],
                               cwrows[R_FALL:R_FALL + world, 512 * n2:512 * (n2 + 1)],
                               start=True, stop=True)
            return ins
        P.op("t", fsel_mm, reads=[("Fall",), ("fsel",)],
             writes=[("ps", 0), ("ps", 1), ("ps", 2), ("ps", 3)])
        frow = auxf[0:1, 3584:3584 + DI]
        P.rekey([("aux", "xf")], [("aux", "dpow"), ("aux", "ybf", 0),
                                  ("aux", "ybf", 1), ("aux", "ybL", 0),
                                  ("aux", "ybL", 1), ("aux", "frow")])
        P.op("v", lambda e: e.tensor_copy(frow, ps[0:1, 0:DI]),
             reads=[("ps", 0), ("ps", 1), ("ps", 2), ("ps", 3)],
             writes=[("aux", "frow")])

        P.dma("sb", fr_scr.ap(), frow, reads=[("aux", "frow")], writes=[("frs",)])
        P.dma("lb", rows2[32:48, 0:128],
              fr_scr.ap().rearrange("a (c p) -> (a c) p", p=128),
              reads=[("frs",), ("r2", "stage")], writes=[("r2", "stage")])
        P.op("t", lambda e: e.transpose(ps[:, 3584:3584 + 16],
                                        rows2[32:48, 0:128],
                                        identf_sb[32:48, 32:48]),
             reads=[("r2", "stage"), ("idf",)], writes=[("ps", 7)])
        P.op("v", lambda e: e.tensor_copy(smallv[:, SV_FP:SV_FP + 16],
                                          ps[:, 3584:3584 + 16]),
             reads=[("ps", 7)], writes=[("sv", "fp")])
        if "fprev" in taps:
            P.dma("sb", tap_aps["fprev"], smallv[:, SV_FP:SV_FP + 16],
                  reads=[("sv", "fp")])
        P.op("v", lambda e: e.tensor_tensor(smallv[:, SV_FP:SV_FP + 16],
                                            smallv[:, SV_FP:SV_FP + 16],
                                            smallv[:, SV_R:SV_R + 16], ALU.mult),
             reads=[("sv", "fp")] + [("sv", "r", c) for c in range(16)],
             writes=[("sv", "fp")])

        dpow_sb = auxf[:, 512:512 + T]
        ybfs = [aux[:, 0:1024], aux[:, 12288:13312]]
        ybLs = [auxf[:, 512 + T:512 + T + TH], auxf[:, 6656:6656 + TH]]
        P.dma("la", dpow_sb, dpow_in, writes=[("aux", "dpow")])

        def yb_load(th, c):
            p = c % 2
            P.dma("la" if c % 2 == 0 else "lb", ybLs[p],
                  yp_scr.ap()[c][:, TH * th:TH * (th + 1)],
                  reads=[("scr_yp", c), ("aux", "ybL", p)],
                  writes=[("aux", "ybL", p)])
            P.op("v", lambda e, c=c, th=th, p=p: e.scalar_tensor_tensor(
                    ybfs[p][:, 0:TH], dpow_sb[:, TH * th:TH * (th + 1)],
                    smallv[:, SV_FP + c:SV_FP + c + 1], ybLs[p],
                    ALU.mult, ALU.add),
                 reads=[("aux", "dpow"), ("sv", "fp"), ("aux", "ybL", p),
                        ("aux", "ybf", p)],
                 writes=[("aux", "ybf", p)])

        P.rekey([("bd", "xi"), ("bd", "c"), ("bd", "y"), ("bd", "dec")],
                [("s1acc", g) for g in range(NSL)])
        for th in range(2):
            for sl in range(NSL):
                gsl = th * NSL + sl
                P.op("v", lambda e, sl=sl: e.tensor_copy(
                        big[:, 1024 * sl:1024 * (sl + 1)], bb[:, :]),
                     reads=[("bb",)], writes=[("s1acc", sl)])
            yb_load(th, 0)
            for c in range(16):
                p = c % 2
                for sl in range(NSL):
                    for n in range(2):
                        bk = 4 + (sl * 2 + n) % 2
                        gsl = th * NSL + sl

                        def wmm(e, c=c, sl=sl, n=n, bk=bk, p=p):
                            return e.matmul(ps[:, 512 * bk:512 * (bk + 1)],
                                            ybfs[p][:, 128 * sl:128 * (sl + 1)],
                                            wout_v[:, c, 512 * n:512 * (n + 1)],
                                            start=True, stop=True)
                        P.op("t", wmm, reads=[("aux", "ybf", p), ("arw", "wout")],
                             writes=[("ps", bk)])
                        acs = big[:, 1024 * sl + 512 * n:1024 * sl + 512 * (n + 1)]
                        P.op("v", lambda e, acs=acs, bk=bk: e.tensor_tensor(
                                acs, acs, ps[:, 512 * bk:512 * (bk + 1)], ALU.add),
                             reads=[("ps", bk), ("s1acc", sl)],
                             writes=[("s1acc", sl)])
                        if sl == 0 and n == 0 and c + 1 < 16:
                            yb_load(th, c + 1)
            for sl in range(NSL):
                t_ = th * NSL + sl
                P.dma("sa" if t_ % 2 == 0 else "sb",
                      s_scr.ap()[128 * t_:128 * (t_ + 1), :],
                      big[:, 1024 * sl:1024 * (sl + 1)],
                      reads=[("s1acc", sl)], writes=[("scr_s", t_)])
        if "s1" in taps:
            P.dma("sa", tap_aps["s1"], s_scr.ap(),
                  reads=[("scr_s", t_) for t_ in range(NT)])

        if upto == "D":
            return _finish()
        P.rekey([("arw", "win"), ("arw", "wout")],
                [("arw", "w1", 0), ("arw", "w1", 1), ("arw", "w2")])
        P.rekey([("aux", "dpow"), ("aux", "ybf", 0), ("aux", "ybf", 1),
                 ("aux", "ybL", 0), ("aux", "ybL", 1), ("aux", "frow")],
                [("hE", k) for k in range(16)])
        P.rekey([("s1acc", g) for g in range(NSL)],
                [("eacc", sl, n) for sl in range(NSL) for n in range(2)])
        w1s = [arw[:, 0:4096].rearrange("p (k n) -> p k n", k=8),
               arw[:, 4096:8192].rearrange("p (k n) -> p k n", k=8)]
        w2v = arw[:, 8192:24576].rearrange("p (k n) -> p k n", k=16)
        hEv = aux[:, 0:16 * TH].rearrange("p (k n) -> p k n", k=16)

        for th in range(2):
            for sl in range(NSL):
                for n in range(2):
                    P.op("v", lambda e, sl=sl, n=n: e.memset(
                            big[:, 1024 * sl + 512 * n:1024 * sl + 512 * (n + 1)], 0.0),
                         writes=[("eacc", sl, n)])
            for eix in range(NE1):
                P.dma("la", rows2[32:64, 0:128],
                      b1[eix].rearrange("(c p) -> c p", p=128),
                      reads=[("r2", "stage")], writes=[("r2", "stage")])
                P.op("t", lambda e: e.transpose(ps[:, 3584:3584 + 32],
                                                rows2[32:64, 0:128],
                                                identf_sb[32:64, 32:64]),
                     reads=[("r2", "stage"), ("idf",)], writes=[("ps", 7)])
                P.op("v", lambda e: e.tensor_copy(smallv[:, SV_B1:SV_B1 + 32],
                                                  ps[:, 3584:3584 + 32]),
                     reads=[("ps", 7)], writes=[("sv", "b1")])
                bias_bcast(b2[eix], "lb")
                for hk in range(2):
                    def w1_dma(cc4, eix=eix, hk=hk):
                        cslot = cc4 % 2
                        P.dma("wa", w1s[cslot],
                              W1[eix].rearrange("(k p) n -> p k n", p=128)
                              [:, :, 2048 * hk + 512 * cc4:2048 * hk + 512 * (cc4 + 1)],
                              writes=[("arw", "w1", cslot)])
                    w1_dma(0)
                    w1_dma(1)
                    for hw_ in range(2):
                        P.dma("wa", w2v[:, 8 * hw_:8 * (hw_ + 1), :],
                              W2[eix].rearrange("(k p) n -> p k n", p=128)
                              [:, 16 * hk + 8 * hw_:16 * hk + 8 * (hw_ + 1), :],
                              writes=[("arw", "w2")])
                    for cc4 in range(4):
                        cslot = cc4 % 2
                        for m in range(4):
                            mg = 4 * cc4 + m
                            for n2 in range(NW5):
                                bk = (0, 1, 6)[(2 * m + n2) % 3]

                                def hmm(e, m=m, n2=n2, bk=bk, cslot=cslot, th=th):
                                    for k in range(8):
                                        ins = e.matmul(
                                            ps[:, 512 * bk:512 * bk + W5],
                                            w1s[cslot][:, k, 128 * m:128 * (m + 1)],
                                            xkv[:, k, 8 + TH * th + W5 * n2:
                                                8 + TH * th + W5 * (n2 + 1)],
                                            start=(k == 0), stop=(k == 7))
                                    return ins
                                P.op("t", hmm, reads=[("arw", "w1", cslot), ("xnT",)],
                                     writes=[("ps", bk)])
                                hsl = hEv[:, mg, W5 * n2:W5 * (n2 + 1)]
                                bcol = SV_B1 + 16 * hk + mg
                                P.op("s", lambda e, hsl=hsl, bk=bk, bcol=bcol:
                                        e.activation(
                                        hsl, ps[:, 512 * bk:512 * bk + W5],
                                        AF.Gelu_apprx_tanh,
                                        bias=smallv[:, bcol:bcol + 1], scale=1.0),
                                     reads=[("ps", bk), ("sv", "b1")],
                                     writes=[("hE", mg)])
                        if cc4 < 2:
                            w1_dma(cc4 + 2)
                    for sl in range(NSL):
                        for n in range(2):
                            bk = 2 + (sl * 2 + n) % 2

                            def emm(e, sl=sl, n=n, bk=bk):
                                for k in range(16):
                                    ins = e.matmul(ps[:, 512 * bk:512 * (bk + 1)],
                                                   hEv[:, k, 128 * sl:128 * (sl + 1)],
                                                   w2v[:, k, 512 * n:512 * (n + 1)],
                                                   start=(k == 0), stop=(k == 15))
                                return ins
                            P.op("t", emm,
                                 reads=[("hE", k) for k in range(16)] + [("arw", "w2")],
                                 writes=[("ps", bk)])
                            t_ = NSL * th + sl
                            acs = big[:, 1024 * sl + 512 * n:1024 * sl + 512 * (n + 1)]
                            cwc = cwtm1[:, NE1 * t_ + eix:NE1 * t_ + eix + 1]
                            P.op("v", lambda e, acs=acs, bk=bk, cwc=cwc:
                                    e.scalar_tensor_tensor(
                                        acs, ps[:, 512 * bk:512 * (bk + 1)], cwc, acs,
                                        ALU.mult, ALU.add),
                                 reads=[("ps", bk), ("eacc", sl, n), ("cwl1", t_)],
                                 writes=[("eacc", sl, n)])
                for sl in range(NSL):
                    t_ = NSL * th + sl
                    cwc = cwtm1[:, NE1 * t_ + eix:NE1 * t_ + eix + 1]
                    for n in range(2):
                        acs = big[:, 1024 * sl + 512 * n:1024 * sl + 512 * (n + 1)]
                        P.op("v", lambda e, acs=acs, n=n, cwc=cwc:
                                e.scalar_tensor_tensor(
                                    acs, bb[:, 512 * n:512 * (n + 1)], cwc, acs,
                                    ALU.mult, ALU.add),
                             reads=[("bb",), ("eacc", sl, n), ("cwl1", t_)],
                             writes=[("eacc", sl, n)])
            for tt in range(NSL):
                t_ = NSL * th + tt
                P.op("s", lambda e, tt=tt: e.activation(
                        ntile[:, :], big[:, 1024 * tt:1024 * (tt + 1)], AF.Silu),
                     reads=[("eacc", tt, 0), ("eacc", tt, 1)], writes=[("nt",)])
                P.dma("la", xtile[:, :], x1_scr.ap()[128 * t_:128 * (t_ + 1), :],
                      reads=[("scr_x1", t_)], writes=[("xt",)])
                P.dma("lb", stile[:, :], s_scr.ap()[128 * t_:128 * (t_ + 1), :],
                      reads=[("scr_s", t_)], writes=[("st",)])
                P.op("v", lambda e: e.tensor_tensor(ntile[:, :], stile[:, :],
                                                    ntile[:, :], ALU.mult),
                     reads=[("st",), ("nt",)], writes=[("nt",)])
                P.op("v", lambda e: e.tensor_tensor(xtile[:, :], xtile[:, :],
                                                    ntile[:, :], ALU.add),
                     reads=[("nt",), ("xt",)], writes=[("xt",)])
                P.dma("sa" if t_ % 2 == 0 else "sb",
                      out[128 * t_:128 * (t_ + 1), :], xtile[:, :],
                      reads=[("xt",)], writes=[("outw", t_)])

        return _finish()



def make_masks(q):
    i = np.arange(128)[:, None]
    j = np.arange(144)[None, :]
    interior = (j >= i) & (j <= i + 16)
    first = ((i <= 8) & (j >= 8) & (j <= i + 16)) | ((i > 8) & interior)
    last = ((128 - 1 - i <= 7) & (j >= i) & (j <= 135)) | ((i < 120) & interior)
    v1 = first if q == 0 else interior
    v2 = last if q == 3 else interior
    m = np.full((3, 128, 144), NEG, np.float32)
    for v, msk in enumerate((interior, v1, v2)):
        m[v][msk] = 0.0
    full = np.tile(m, (1, 1, 8))
    return full.astype(ml_dtypes.bfloat16)


def make_core_inputs(inputs, T=2048):
    x = np.asarray(inputs["L0_attn_x"])
    B, L, _ = x.shape
    percore = L // 4
    identf = np.eye(128, dtype=np.float32)
    identb = np.eye(128, dtype=ml_dtypes.bfloat16)
    dpow = np.tile((DECAY ** (np.arange(T, dtype=np.float64) + 1)
                    ).astype(np.float32)[None, :], (128, 1))
    wkeys = [k for k in inputs if k != "L0_attn_x"]
    in_maps = []
    for c in range(8):
        bsel, q = c // 4, c % 4
        lo, hi = q * percore, (q + 1) * percore
        xs = np.zeros((T + 16, D), np.float32)
        src_lo, src_hi = max(0, lo - 8), min(L, hi + 8)
        xs[8 - (lo - src_lo):8 + (src_hi - lo)] = x[bsel, src_lo:src_hi]
        fsel = np.zeros((8, 1), np.float32)
        if q > 0:
            fsel[c - 1, 0] = 1.0
        im = {k: np.ascontiguousarray(np.asarray(inputs[k])) for k in wkeys}
        im["x"] = xs
        im["identf"] = identf
        im["identb"] = identb
        im["maskp"] = np.ascontiguousarray(make_masks(q))
        im["dpow"] = dpow
        im["fsel"] = fsel
        in_maps.append(im)
    return in_maps


def assemble(results, T=2048):
    outs = [np.asarray(r["out"]) for r in results]
    return np.stack([np.concatenate(outs[0:4], 0),
                     np.concatenate(outs[4:8], 0)], 0)



def kernel(**inputs):
    from concourse.bass_utils import run_bass_kernel_spmd
    nc, _meta = build(T=2048)
    in_maps = make_core_inputs(inputs, T=2048)
    res = run_bass_kernel_spmd(nc, in_maps, core_ids=list(range(8)))
    return assemble(res.results, T=2048)


# version 05
# speedup vs baseline: 1.1765x; 1.0179x over previous
import contextlib
import numpy as np
import ml_dtypes
import concourse.bass as bass
import concourse.mybir as mybir

F32 = mybir.dt.float32
BF16 = mybir.dt.bfloat16
AF = mybir.ActivationFunctionType
ALU = mybir.AluOpType
AXX = mybir.AxisListType.X

D = 1024
NH = 16
HD = 64
ASCALE = HD ** (-0.5)
DI = 2048
NE0 = 16
F0 = 512
NE1 = 8
F1 = 4096
DECAY = 0.9
NEG = -30000.0

DMA_CHANS = {"wa": "g", "wb": "g", "wc": "s", "la": "y", "lb": "y", "sa": "y",
             "sb": "y", "cc": "g"}
ENG_SEM = {"t": "T", "v": "V", "s": "S", "g": "G", "y": "Y"}


class Prog:
    def __init__(self):
        self.ops = []
        self.cnt = {}
        self.writers = {}
        self.readers = {}
        self.waited = {}

    def _waits(self, eng, toks):
        best = {}
        for (sem, val) in toks:
            if eng == "t" and sem == "T":
                continue
            if val > best.get(sem, 0):
                best[sem] = val
        waits = []
        for sem, val in best.items():
            if self.waited.get((eng, sem), 0) < val:
                waits.append((sem, val))
                self.waited[(eng, sem)] = val
        return waits

    def op(self, eng, emit, reads=(), writes=(), sem=None, amt=1):
        sem = sem or ENG_SEM[eng]
        if eng in ("v", "s"):
            pbanks = {buf[1] for buf in tuple(reads) + tuple(writes)
                      if isinstance(buf, tuple) and buf and buf[0] == "ps"}
            if pbanks:
                writes = tuple(writes) + tuple(("plock", b) for b in sorted(pbanks))
        toks = []
        for buf in reads:
            toks += self.writers.get(buf, [])
        for buf in writes:
            toks += self.writers.get(buf, [])
            toks += self.readers.get(buf, [])
        waits = self._waits(eng, toks)
        self.cnt[sem] = self.cnt.get(sem, 0) + amt
        tok = (sem, self.cnt[sem])
        for buf in writes:
            self.writers[buf] = [tok]
            self.readers[buf] = []
        for buf in reads:
            self.readers.setdefault(buf, []).append(tok)
        self.ops.append((eng, waits, emit, sem, amt))
        return tok

    def dma(self, chan, out_ap, in_ap, reads=(), writes=()):
        return self.op(DMA_CHANS[chan],
                       lambda e: e.dma_start(out=out_ap, in_=in_ap),
                       reads=tuple(reads) + (("chan", chan),),
                       writes=tuple(writes) + (("chan", chan),),
                       sem=chan, amt=16)

    def rekey(self, old_keys, new_keys):
        toks = []
        for k in old_keys:
            toks += self.writers.pop(k, [])
            toks += self.readers.pop(k, [])
        for k in new_keys:
            self.writers[k] = self.writers.get(k, []) + toks

    def barrier(self, eng, bufs):
        toks = []
        for k in bufs:
            toks += self.writers.get(k, [])
            toks += self.readers.get(k, [])
        waits = self._waits(eng, toks)
        if waits:
            self.ops.append((eng, waits, lambda e: None, ENG_SEM[eng], 0))

    def emit_all(self, blk, sems):
        streams = {}
        for rec in self.ops:
            streams.setdefault(rec[0], []).append(rec)

        def run(e, engname):
            for (_eng, waits, emit, sem, amt) in streams[engname]:
                for (s_, v_) in waits:
                    e.wait_ge(sems[s_], v_)
                ins = emit(e)
                if ins is not None and amt > 0:
                    ins.then_inc(sems[sem], amt)

        for name, meth in (("t", blk.tensor), ("v", blk.vector), ("s", blk.scalar),
                           ("g", blk.gpsimd), ("y", blk.sync)):
            if name in streams:
                meth(lambda e, name=name: run(e, name))


class _Done(Exception):
    pass


def build(T=2048, taps=(), world=8, upto="E"):
    assert T % 512 == 0
    NT = T // 128
    NST = T // 512
    TH = T // 2
    NSL = TH // 128
    TW = T + 16
    W5 = min(512, TH)
    NW5 = max(1, TH // 512)

    nc = bass.Bass("TRN2", target_bir_lowering=False)
    dt = nc.dram_tensor

    x_in = dt("x", [TW, D], F32, kind="ExternalInput").ap()
    Wqkv = dt("L0_attn_Wqkv", [D, 3 * D], F32, kind="ExternalInput").ap()
    Wproj = dt("L0_attn_Wproj", [D, D], F32, kind="ExternalInput").ap()
    bproj = dt("L0_attn_bproj", [D], F32, kind="ExternalInput").ap()
    Wr = dt("L0_ffn_Wr", [D, NE0], F32, kind="ExternalInput").ap()
    Wg0 = dt("L0_ffn_Wg", [NE0, D, F0], F32, kind="ExternalInput").ap()
    Wu0 = dt("L0_ffn_Wu", [NE0, D, F0], F32, kind="ExternalInput").ap()
    Wd0 = dt("L0_ffn_Wd", [NE0, F0, D], F32, kind="ExternalInput").ap()
    Sg = dt("L0_ffn_Sg", [D, D], F32, kind="ExternalInput").ap()
    Su = dt("L0_ffn_Su", [D, D], F32, kind="ExternalInput").ap()
    Sd = dt("L0_ffn_Sd", [D, D], F32, kind="ExternalInput").ap()
    Win = dt("L1_attn_Win", [D, DI], F32, kind="ExternalInput").ap()
    bin_ = dt("L1_attn_bin_", [DI], F32, kind="ExternalInput").ap()
    Bp = dt("L1_attn_Bp", [DI, 16], F32, kind="ExternalInput").ap()
    Cp = dt("L1_attn_Cp", [16, DI], F32, kind="ExternalInput").ap()
    Dp = dt("L1_attn_Dp", [DI], F32, kind="ExternalInput").ap()
    Wout = dt("L1_attn_Wout", [DI, D], F32, kind="ExternalInput").ap()
    bout = dt("L1_attn_bout", [D], F32, kind="ExternalInput").ap()
    Wg1 = dt("L1_ffn_Wg", [D, NE1], F32, kind="ExternalInput").ap()
    bg1 = dt("L1_ffn_bg", [NE1], F32, kind="ExternalInput").ap()
    W1 = dt("L1_ffn_W1", [NE1, D, F1], F32, kind="ExternalInput").ap()
    b1 = dt("L1_ffn_b1", [NE1, F1], F32, kind="ExternalInput").ap()
    W2 = dt("L1_ffn_W2", [NE1, F1, D], F32, kind="ExternalInput").ap()
    b2 = dt("L1_ffn_b2", [NE1, D], F32, kind="ExternalInput").ap()
    identf = dt("identf", [128, 128], F32, kind="ExternalInput").ap()
    identb = dt("identb", [128, 128], BF16, kind="ExternalInput").ap()
    maskp = dt("maskp", [3, 128, 1152], BF16, kind="ExternalInput").ap()
    dpow_in = dt("dpow", [128, T], F32, kind="ExternalInput").ap()
    fsel_in = dt("fsel", [world, 1], F32, kind="ExternalInput").ap()

    out = dt("out", [T, D], F32, kind="ExternalOutput").ap()
    tap_shapes = {"xnT": ([128, 8, TW], BF16), "logits0": ([128, NT * NE0], F32),
                  "logits1": ([128, NT * NE1], F32),
                  "s0": ([T, D], F32), "x1": ([T, D], F32),
                  "acc0": ([128, NSL * D], F32), "xi": ([16, 128, T], F32),
                  "ypart": ([16, 128, T], F32), "s1": ([T, D], F32),
                  "cw0": ([128, NT * NE0], F32), "cw1": ([128, NT * NE1], F32),
                  "fprev": ([128, 16], F32), "rvec": ([128, 16], F32)}
    tap_aps = {nm: dt("tap_" + nm, shp, dty, kind="ExternalOutput").ap()
               for nm, (shp, dty) in tap_shapes.items() if nm in taps}

    s_scr = dt("s_scr", [T, D], F32)
    x1_scr = dt("x1_scr", [T, D], F32)
    yp_scr = dt("yp_scr", [16, 128, T], F32)
    cc_in = dt("cc_in", [1, DI], F32)
    cc_out = dt("cc_out", [world, DI], F32)
    fr_scr = dt("fr_scr", [1, DI], F32)

    with contextlib.ExitStack() as es:
        E = es.enter_context
        xnT = E(nc.sbuf_tensor("xnT", [128, 8 * TW], BF16))
        arw = E(nc.sbuf_tensor("arw", [128, 32768], BF16))
        big = E(nc.sbuf_tensor("big", [128, 8192], F32))
        aux = E(nc.sbuf_tensor("aux", [128, 16896], BF16))
        xtile = E(nc.sbuf_tensor("xtile", [128, D], F32))
        stile = E(nc.sbuf_tensor("stile", [128, D], F32))
        ntile = E(nc.sbuf_tensor("ntile", [128, D], F32))
        masks = E(nc.sbuf_tensor("masks", [128, 3 * 1152], BF16))
        stats = E(nc.sbuf_tensor("stats", [128, 64], F32))
        logit0 = E(nc.sbuf_tensor("logit0", [128, NT * NE0], F32))
        logit1 = E(nc.sbuf_tensor("logit1", [128, NT * NE1], F32))
        cwtm0 = E(nc.sbuf_tensor("cwtm0", [128, NT * NE0], F32))
        cwtm1 = E(nc.sbuf_tensor("cwtm1", [128, NT * NE1], F32))
        cwrows = E(nc.sbuf_tensor("cwrows", [128, 2048], F32))
        rows2 = E(nc.sbuf_tensor("rows2", [128, 1536], F32))
        bb = E(nc.sbuf_tensor("bb", [128, 1024], F32))
        smallv = E(nc.sbuf_tensor("smallv", [128, 176], F32))
        wrg = E(nc.sbuf_tensor("wrg", [128, 192], F32))
        tgt = E(nc.sbuf_tensor("tgt", [128, 512], BF16))
        identf_sb = E(nc.sbuf_tensor("identf_sb", [128, 128], F32))
        identb_sb = E(nc.sbuf_tensor("identb_sb", [128, 128], BF16))
        fsel_sb = E(nc.sbuf_tensor("fsel_sb", [world, 1], F32))
        ps = E(nc.psum_tensor("ps", [128, 4096], F32))

        psb = ps[:, :].bitcast(BF16)
        auxf = aux[:, :].bitcast(F32)
        blk = E(nc.Block())
        P = Prog()

        SV_BIN, SV_DP, SV_R, SV_FP, SV_FR, SV_B1, SV_CW = 0, 16, 32, 48, 64, 96, 128
        ST_NM, ST_SQ, ST_IV, ST_M1, ST_M2, ST_DEN = 0, 1, 2, 3, 4, 5
        R_CP, R_FT, R_FALL = 32, 96, 0
        bgrow_sb = stats[:, 48:56]

        def _finish():
            sem_names = set(list(P.cnt.keys()) + ["T", "V", "S", "G", "Y"])
            P.barrier("y", [k for k in P.writers if k[0] == "outw"]
                      + [("chan", c) for c in ("sa", "sb", "wa", "wb", "wc",
                                               "la", "lb", "cc")
                         if P.cnt.get(c)])
            sems = {k: E(nc.semaphore("sem_" + k)) for k in sorted(sem_names)}
            P.emit_all(blk, sems)
            return nc, dict(tap_aps=tap_aps, n_ops=len(P.ops))

        P.dma("la", identf_sb[:, :], identf, writes=[("idf",)])
        P.dma("lb", identb_sb[:, :], identb, writes=[("idb",)])
        P.dma("la", masks[:, :].rearrange("p (v n) -> p v n", v=3),
              maskp.rearrange("v p n -> p v n"), writes=[("masks",)])
        P.dma("lb", fsel_sb[:, :], fsel_in, writes=[("fsel",)])
        P.op("v", lambda e: e.memset(rows2[0:1, 0:128], 1.0), writes=[("r2", "ones")])
        for k in range(8):
            P.dma("la", wrg[:, 16 * k:16 * (k + 1)], Wr[128 * k:128 * (k + 1), :],
                  writes=[("wrg", 0)])
            P.dma("lb", wrg[:, 128 + 8 * k:128 + 8 * (k + 1)],
                  Wg1[128 * k:128 * (k + 1), :], writes=[("wrg", 1)])

        def bias_bcast(src_dram, chan="la"):
            P.dma(chan, rows2[0:1, 512:1536], src_dram.unsqueeze(0) if src_dram.ndim == 1 else src_dram,
                  reads=[("r2", "bstage")], writes=[("r2", "bstage")])

            def mm(e):
                for n2 in range(2):
                    ins = e.matmul(ps[:, 2048 + 512 * n2:2048 + 512 * (n2 + 1)],
                                   rows2[0:1, 0:128],
                                   rows2[0:1, 512 + 512 * n2:512 + 512 * (n2 + 1)],
                                   start=True, stop=True)
                return ins
            P.op("t", mm, reads=[("r2", "ones"), ("r2", "bstage")],
                 writes=[("ps", 4), ("ps", 5)])
            P.op("v", lambda e: e.tensor_copy(bb[:, :], ps[:, 2048:3072]),
                 reads=[("ps", 4), ("ps", 5)], writes=[("bb",)])

        def emit_ln(rows, xsrc, dst_off, lg_kind, lg_col, chan, p=0):
            xt = xtile if p == 0 else big[:, 0:1024]
            stl = stile if p == 0 else big[:, 1024:2048]
            nt = ntile if p == 0 else big[:, 2048:3072]
            so = 8 * p
            KX = ("xt",) if p == 0 else ("xt", 1)
            KN = ("nt",) if p == 0 else ("nt", 1)
            KS = ("st",) if p == 0 else ("stl", 1)
            pcol = 1024 * p
            rb = 4 + p
            axo = 1024 * p
            P.dma(chan, xt[0:rows, :], xsrc, writes=[KX])
            P.op("v", lambda e: e.reduce_sum(stats[0:rows, so + ST_NM:so + ST_NM + 1],
                                             xt[0:rows, :], AXX),
                 reads=[KX], writes=[("stats", "nm", p)])
            P.op("v", lambda e: e.tensor_scalar(stats[0:rows, so + ST_NM:so + ST_NM + 1],
                                                stats[0:rows, so + ST_NM:so + ST_NM + 1],
                                                -1.0 / D, None, ALU.mult),
                 reads=[("stats", "nm", p)], writes=[("stats", "nm", p)])
            P.op("s", lambda e: e.activation(stl[0:rows, :], xt[0:rows, :],
                                             AF.Square,
                                             bias=stats[0:rows, so + ST_NM:so + ST_NM + 1],
                                             scale=1.0,
                                             accum_out=stats[0:rows, so + ST_SQ:so + ST_SQ + 1]),
                 reads=[KX, ("stats", "nm", p)],
                 writes=[KS, ("stats", "sq", p)])
            P.op("v", lambda e: e.tensor_scalar(stats[0:rows, so + ST_IV:so + ST_IV + 1],
                                                stats[0:rows, so + ST_SQ:so + ST_SQ + 1],
                                                1.0 / D, 1e-5, ALU.mult, ALU.add),
                 reads=[("stats", "sq", p)], writes=[("stats", "iv", p)])
            P.op("s", lambda e: e.activation(stats[0:rows, so + ST_IV:so + ST_IV + 1],
                                             stats[0:rows, so + ST_IV:so + ST_IV + 1],
                                             AF.Sqrt),
                 reads=[("stats", "iv", p)], writes=[("stats", "iv", p)])
            P.op("v", lambda e: e.reciprocal(stats[0:rows, so + ST_IV:so + ST_IV + 1],
                                             stats[0:rows, so + ST_IV:so + ST_IV + 1]),
                 reads=[("stats", "iv", p)], writes=[("stats", "iv", p)])
            P.op("v", lambda e: e.tensor_scalar(nt[0:rows, :], xt[0:rows, :],
                                                stats[0:rows, so + ST_NM:so + ST_NM + 1],
                                                stats[0:rows, so + ST_IV:so + ST_IV + 1],
                                                ALU.add, ALU.mult),
                 reads=[KX, ("stats", "nm", p), ("stats", "iv", p)], writes=[KN])

            def tr(e):
                for k in range(8):
                    col = pcol + 512 * (k % 2) + 128 * (k // 2)
                    ins = e.transpose(ps[:, col:col + rows],
                                      nt[0:rows, 128 * k:128 * (k + 1)],
                                      identf_sb[0:rows, 0:rows])
                return ins
            P.op("t", tr, reads=[KN, ("idf",)],
                 writes=[("ps", 2 * p), ("ps", 2 * p + 1)])
            xv = xnT[:, :].rearrange("p (k n) -> p k n", k=8)

            def ev(e):
                for k in range(8):
                    col = pcol + 512 * (k % 2) + 128 * (k // 2)
                    ins = e.tensor_copy(xv[:, k, dst_off:dst_off + rows],
                                        ps[:, col:col + rows])
                return ins
            P.op("v", ev, reads=[("ps", 2 * p), ("ps", 2 * p + 1)], writes=[("xnT",)])

            if lg_kind is not None:
                which, ne = lg_kind
                skip_mm = which.startswith("skip")
                which = which.replace("skip", "")
                wcol = 0 if which == "l0" else 128
                lg_sb = logit0 if which == "l0" else logit1

                def evf(e):
                    for k in range(8):
                        col = pcol + 512 * (k % 2) + 128 * (k // 2)
                        ins = e.copy(auxf[:, axo + 128 * k:axo + 128 * (k + 1)],
                                     ps[:, col:col + rows])
                    return ins
                P.op("s", evf, reads=[("ps", 2 * p), ("ps", 2 * p + 1)],
                     writes=[("aux", "xf", p)])
                if skip_mm:
                    return

                def rmm(e):
                    for k in range(8):
                        ins = e.matmul(ps[:, 512 * rb:512 * rb + ne],
                                       auxf[:, axo + 128 * k:axo + 128 * (k + 1)],
                                       wrg[:, wcol:wcol + 8 * ne]
                                       .rearrange("p (k j) -> p k j", k=8)[:, k, :],
                                       start=(k == 0), stop=(k == 7))
                    return ins
                P.op("t", rmm, reads=[("aux", "xf", p),
                                      ("wrg", 0 if which == "l0" else 1)],
                     writes=[("ps", rb)])
                P.op("v", lambda e: e.tensor_copy(lg_sb[:, lg_col:lg_col + ne],
                                                  ps[:, 512 * rb:512 * rb + ne]),
                     reads=[("ps", rb)], writes=[(which,)])

        emit_ln(8, x_in[0:8, :], 0, None, 0, "la", p=0)
        for t_ in range(NT):
            lgk = (None if upto == "A0" else
                   (("skipl0", NE0) if upto == "A1" else ("l0", NE0)))
            emit_ln(128, x_in[8 + 128 * t_:8 + 128 * (t_ + 1), :], 8 + 128 * t_,
                    lgk, t_ * NE0, "la" if t_ % 2 == 0 else "lb", p=(t_ + 1) % 2)
        emit_ln(8, x_in[8 + T:16 + T, :], 8 + T, None, 0, "lb", p=1)

        if "xnT" in taps:
            P.dma("sa", tap_aps["xnT"], xnT[:, :].rearrange("p (k n) -> p k n", k=8),
                  reads=[("xnT",)])
        if "logits0" in taps:
            P.dma("sb", tap_aps["logits0"], logit0[:, 0:NT * NE0], reads=[("l0",)])

        if upto in ("A", "A0", "A1"):
            return _finish()
        WQKV, WPROJ = 0, 24576
        wq_v = arw[:, WQKV:WQKV + 24576].rearrange("p (k n) -> p k n", k=8)
        wp_v = arw[:, WPROJ:WPROJ + 8192].rearrange("p (k n) -> p k n", k=8)
        for j, ch in ((0, "wa"), (1, "wb"), (2, "wa")):
            P.dma(ch, wq_v[:, :, 1024 * j:1024 * (j + 1)],
                  Wqkv.rearrange("(k p) n -> p k n", p=128)[:, :, 1024 * j:1024 * (j + 1)],
                  writes=[("arw", "qkv", j)])
        P.dma("wb", wp_v, Wproj.rearrange("(k p) n -> p k n", p=128),
              writes=[("arw", "proj")])
        bias_bcast(bproj, "la")

        AQ, AK, AV_, AT2, AA, AO = 0, 4096, 8320, 13440, 14464, 15616
        qv = aux[:, AQ:AQ + 4096].rearrange("p (k n) -> p k n", k=8)
        kv = aux[:, AK:AK + 4224].rearrange("p (k n) -> p k n", k=8)
        vv = aux[:, AV_:AV_ + 5120].rearrange("p (j n) -> p j n", j=5)
        at2sb = aux[0:16, AT2:AT2 + 1024]
        asb = aux[:, AA:AA + 1152]
        at1sb = aux[:, AA:AA + 1024]
        osb = aux[:, AO:AO + 1024]
        xkv = xnT[:, :].rearrange("p (k n) -> p k n", k=8)
        mk3 = masks[:, :].rearrange("p (v s x) -> p v s x", v=3, s=8)
        AUXB = ([("auxq", m) for m in range(8)] + [("auxk", m) for m in range(8)]
                + [("auxv", j) for j in range(5)]
                + [("auxa", h) for h in range(8)] + [("auxat1",), ("auxat2",),
                   ("auxo", 0), ("auxo", 1)])
        P.rekey([("aux", "xf", 0), ("aux", "xf", 1)], AUXB)

        BLV = {"Bs0": -1, "Bs": 0, "Bsm": 1, "Bat": 2, "Bav": 3}.get(upto, 4)
        for st in range(NST):
            s0c = 512 * st
            for m in range(8):
                bk = m % 2

                def qmm(e, m=m, bk=bk, s0c=s0c):
                    for k in range(8):
                        ins = e.matmul(ps[:, 512 * bk:512 * bk + 512],
                                       wq_v[:, k, 128 * m:128 * (m + 1)],
                                       xkv[:, k, 8 + s0c:8 + s0c + 512],
                                       start=(k == 0), stop=(k == 7))
                    return ins
                P.op("t", qmm, reads=[("arw", "qkv", 0), ("xnT",)], writes=[("ps", bk)])
                P.op("s", lambda e, m=m, bk=bk: e.activation(
                        qv[:, m, :], ps[:, 512 * bk:512 * bk + 512], AF.Copy,
                        scale=ASCALE),
                     reads=[("ps", bk)], writes=[("auxq", m)])
            for m in range(8):
                bk = m % 2

                def kmm(e, m=m, bk=bk, s0c=s0c):
                    for k in range(8):
                        e.matmul(ps[:, 512 * bk:512 * bk + 512],
                                 wq_v[:, k, 1024 + 128 * m:1024 + 128 * (m + 1)],
                                 xkv[:, k, s0c:s0c + 512],
                                 start=(k == 0), stop=(k == 7))
                    for k in range(8):
                        ins = e.matmul(ps[:, 2048 + 16 * bk:2048 + 16 * (bk + 1)],
                                       wq_v[:, k, 1024 + 128 * m:1024 + 128 * (m + 1)],
                                       xkv[:, k, s0c + 512:s0c + 528],
                                       start=(k == 0), stop=(k == 7))
                    return ins
                P.op("t", kmm, reads=[("arw", "qkv", 1), ("xnT",)],
                     writes=[("ps", bk), ("ps", 4)])

                def kev(e, m=m, bk=bk):
                    e.copy(kv[:, m, 0:512], ps[:, 512 * bk:512 * bk + 512])
                    return e.copy(kv[:, m, 512:528],
                                  ps[:, 2048 + 16 * bk:2048 + 16 * (bk + 1)])
                P.op("s", kev, reads=[("ps", bk), ("ps", 4)], writes=[("auxk", m)])
            for j in range(5):
                rj = 128 if (j < 4 or st < NST - 1) else 16

                def vmm(e, j=j, s0c=s0c, rj=rj):
                    for n2 in range(2):
                        for k in range(8):
                            ins = e.matmul(ps[0:rj, 512 * n2:512 * (n2 + 1)],
                                           xkv[:, k, s0c + 128 * j:s0c + 128 * j + rj],
                                           wq_v[:, k, 2048 + 512 * n2:2048 + 512 * (n2 + 1)],
                                           start=(k == 0), stop=(k == 7))
                    return ins
                P.op("t", vmm, reads=[("arw", "qkv", 2), ("xnT",)],
                     writes=[("ps", 0), ("ps", 1)])
                P.op("v", lambda e, j=j, rj=rj: e.tensor_copy(vv[0:rj, j, :],
                                                              ps[0:rj, 0:1024]),
                     reads=[("ps", 0), ("ps", 1)], writes=[("auxv", j)])

            if upto == "Bq":
                continue
            for tl in range(4):
                t_ = 4 * st + tl
                mvar = 1 if t_ == 0 else (2 if t_ == NT - 1 else 0)
                for hg in range(2):
                    def smm(e, tl=tl, hg=hg):
                        for hh in range(8):
                            h = 8 * hg + hh
                            po, chn = 64 * (h % 2), h // 2
                            s_ = (hh % 2) * 4 + hh // 2
                            ins = e.matmul(
                                ps[:, 256 * s_:256 * s_ + 144],
                                qv[po:po + 64, chn, 128 * tl:128 * (tl + 1)],
                                kv[po:po + 64, chn, 128 * tl:128 * tl + 144],
                                start=True, stop=True)
                        return ins
                    P.op("t", smm,
                         reads=[("auxq", m) for m in range(8)]
                               + [("auxk", m) for m in range(8)],
                         writes=[("ps", 0), ("ps", 1), ("ps", 2), ("ps", 3)])
                    if BLV < 0:
                        continue
                    if upto == "Bnm":
                        def mka(e, mvar=mvar):
                            for hh in range(8):
                                ins = e.tensor_tensor(
                                    ps[:, 256 * hh:256 * hh + 144],
                                    ps[:, 256 * hh:256 * hh + 144],
                                    mk3[:, mvar, hh, :], ALU.add)
                            return ins
                        P.op("v", mka,
                             reads=[("ps", 0), ("ps", 1), ("ps", 2), ("ps", 3),
                                    ("masks",)],
                             writes=[("ps", 0), ("ps", 1), ("ps", 2), ("ps", 3)])
                    else:
                        psv = ps[:, 0:2048].rearrange("p (sl x) -> p sl x", sl=8)
                        P.op("v", lambda e, psv=psv, mvar=mvar: e.tensor_tensor(
                                psv[:, :, 0:144], psv[:, :, 0:144],
                                mk3[:, mvar, :, :], ALU.add),
                             reads=[("ps", 0), ("ps", 1), ("ps", 2), ("ps", 3),
                                    ("masks",)],
                             writes=[("ps", 0), ("ps", 1), ("ps", 2), ("ps", 3)])
                    av8 = asb[:, :].rearrange("p (sl x) -> p sl x", sl=8)
                    if BLV < 1:
                        continue
                    for hh in range(8):
                        s_ = (hh % 2) * 4 + hh // 2
                        P.op("v", lambda e, s_=s_, hh=hh: e.reduce_max(
                                stats[:, 32 + hh:33 + hh],
                                ps[:, 256 * s_:256 * s_ + 144], AXX, negate=True),
                             reads=[("ps", s_ // 2)], writes=[("stats", "nx", hh)])
                    for hh in range(8):
                        s_ = (hh % 2) * 4 + hh // 2
                        P.op("s", lambda e, s_=s_, hh=hh, av8=av8: e.activation(
                                av8[:, hh, :], ps[:, 256 * s_:256 * s_ + 144], AF.Exp,
                                bias=stats[:, 32 + hh:33 + hh], scale=1.0,
                                accum_out=stats[:, 40 + hh:41 + hh]),
                             reads=[("ps", s_ // 2), ("stats", "nx", hh)],
                             writes=[("auxa", hh), ("stats", "l", hh)])
                    P.op("v", lambda e: e.reciprocal(stats[:, 40:48], stats[:, 40:48]),
                         reads=[("stats", "l", h) for h in range(8)],
                         writes=[("stats", "linv")])
                    for hh in range(8):
                        P.op("v", lambda e, hh=hh, av8=av8: e.tensor_scalar(
                                av8[:, hh, :], av8[:, hh, :],
                                stats[:, 40 + hh:41 + hh], None, ALU.mult),
                             reads=[("auxa", hh), ("stats", "linv")],
                             writes=[("auxa", hh)])

                    if BLV < 2:
                        continue

                    def atr(e, av8=av8):
                        for hh in range(8):
                            e.transpose(psb[:, 4096 + 128 * hh:4096 + 128 * (hh + 1)],
                                        av8[:, hh, 0:128], identb_sb[:, :])
                            ins = e.transpose(
                                psb[0:16, 5120 + 128 * hh:5120 + 128 * (hh + 1)],
                                av8[:, hh, 128:144], identb_sb[:, :])
                        return ins
                    P.op("t", atr,
                         reads=[("auxa", h) for h in range(8)] + [("idb",)],
                         writes=[("ps", 4), ("ps", 5)])
                    P.op("v", lambda e: e.tensor_copy(at1sb[:, :], psb[:, 4096:5120]),
                         reads=[("ps", 4)] + [("auxa", h) for h in range(8)],
                         writes=[("auxat1",)])
                    P.op("s", lambda e: e.copy(at2sb[:, :], psb[0:16, 5120:6144]),
                         reads=[("ps", 5)], writes=[("auxat2",)])
                    if BLV < 3:
                        continue

                    def avmm(e, tl=tl, hg=hg):
                        for hh in range(8):
                            h = 8 * hg + hh
                            po = 64 * (hh % 2)
                            c2 = hh // 2
                            e.matmul(ps[po:po + 64,
                                        3072 + 128 * c2:3072 + 128 * (c2 + 1)],
                                     vv[:, tl, 64 * h:64 * (h + 1)],
                                     at1sb[:, 128 * hh:128 * (hh + 1)],
                                     start=(hh <= 1), stop=False,
                                     skip_group_check=(hh > 1))
                            ins = e.matmul(
                                ps[po:po + 64, 3072 + 128 * c2:3072 + 128 * (c2 + 1)],
                                vv[0:16, tl + 1, 64 * h:64 * (h + 1)],
                                at2sb[:, 128 * hh:128 * (hh + 1)],
                                start=False, stop=(hh >= 6),
                                skip_group_check=(hh < 6))
                        return ins
                    P.op("t", avmm,
                         reads=[("auxat1",), ("auxat2",), ("auxv", tl),
                                ("auxv", tl + 1)],
                         writes=[("ps", 6)])
                    P.op("v", lambda e, hg=hg: e.tensor_copy(
                            osb[:, 512 * hg:512 * (hg + 1)], ps[:, 3072:3584]),
                         reads=[("ps", 6)],
                         writes=[("auxo", hg)])
                if BLV < 4:
                    continue
                ov = osb.rearrange("p (c n) -> p c n", c=8)
                for n2 in range(2):
                    def pmm(e, n2=n2):
                        for k in range(8):
                            ins = e.matmul(ps[:, 512 * n2:512 * (n2 + 1)],
                                           ov[:, k, :],
                                           wp_v[:, k, 512 * n2:512 * (n2 + 1)],
                                           start=(k == 0), stop=(k == 7))
                        return ins
                    P.op("t", pmm, reads=[("auxo", 0), ("auxo", 1), ("arw", "proj")],
                         writes=[("ps", n2)])
                P.op("v", lambda e: e.tensor_tensor(stile[:, :], ps[:, 0:1024],
                                                    bb[:, :], ALU.add),
                     reads=[("ps", 0), ("ps", 1), ("bb",)], writes=[("st",)])
                P.dma("sa" if t_ % 2 == 0 else "sb",
                      s_scr.ap()[128 * t_:128 * (t_ + 1), :], stile[:, :],
                      reads=[("st",)], writes=[("scr_s", t_)])

        if "s0" in taps:
            P.dma("sa", tap_aps["s0"], s_scr.ap(),
                  reads=[("scr_s", t_) for t_ in range(NT)])

        if upto in ("B", "Bq", "Bnm", "Bs", "Bs0", "Bsm", "Bat", "Bav"):
            return _finish()
        def emit_cw(which, ne, tile, normalize):
            lg_sb = logit0 if which == "l0" else logit1
            cwt = cwtm0 if which == "l0" else cwtm1
            lg = lg_sb[:, ne * tile:ne * (tile + 1)]
            cwtmp = smallv[:, SV_CW:SV_CW + ne]
            lkey = (which,)
            if which == "l1":
                P.op("v", lambda e, lg=lg: e.tensor_tensor(
                        smallv[:, SV_CW + 32:SV_CW + 32 + ne], lg,
                        bgrow_sb[:, 0:ne], ALU.add),
                     reads=[lkey, ("bgB",)], writes=[("svl1",)])
                lg = smallv[:, SV_CW + 32:SV_CW + 32 + ne]
                lkey = ("svl1",)
            P.op("v", lambda e, lg=lg: e.reduce_max(stats[:, ST_NM:ST_NM + 1], lg,
                                                    AXX, negate=True),
                 reads=[lkey], writes=[("stats", "nm")])
            P.op("s", lambda e, lg=lg: e.activation(cwtmp, lg, AF.Exp,
                                                    bias=stats[:, ST_NM:ST_NM + 1],
                                                    scale=1.0,
                                                    accum_out=stats[:, ST_SQ:ST_SQ + 1]),
                 reads=[lkey, ("stats", "nm")], writes=[("svcw",), ("stats", "sq")])
            P.op("v", lambda e: e.reciprocal(stats[:, ST_IV:ST_IV + 1],
                                             stats[:, ST_SQ:ST_SQ + 1]),
                 reads=[("stats", "sq")], writes=[("stats", "iv")])
            P.op("v", lambda e: e.tensor_scalar(cwtmp, cwtmp,
                                                stats[:, ST_IV:ST_IV + 1], None,
                                                ALU.mult),
                 reads=[("svcw",), ("stats", "iv")], writes=[("svcw",)])
            P.op("v", lambda e: e.reduce_max(stats[:, ST_M1:ST_M1 + 1], cwtmp, AXX),
                 reads=[("svcw",)], writes=[("stats", "m1")])
            P.op("v", lambda e: e.tensor_scalar(ntile[:, 0:ne], cwtmp,
                                                stats[:, ST_M1:ST_M1 + 1], None,
                                                ALU.is_ge),
                 reads=[("svcw",), ("stats", "m1")], writes=[("nt",)])
            P.op("v", lambda e: e.scalar_tensor_tensor(ntile[:, 0:ne], ntile[:, 0:ne],
                                                       -4.0, cwtmp, ALU.mult, ALU.add),
                 reads=[("nt",), ("svcw",)], writes=[("nt",)])
            P.op("v", lambda e: e.reduce_max(stats[:, ST_M2:ST_M2 + 1], ntile[:, 0:ne],
                                             AXX),
                 reads=[("nt",)], writes=[("stats", "m2")])
            P.op("v", lambda e: e.tensor_scalar(ntile[:, 0:ne], cwtmp,
                                                stats[:, ST_M2:ST_M2 + 1], None,
                                                ALU.is_ge),
                 reads=[("svcw",), ("stats", "m2")], writes=[("nt",)])
            P.op("v", lambda e: e.tensor_tensor(cwtmp, cwtmp, ntile[:, 0:ne], ALU.mult),
                 reads=[("svcw",), ("nt",)], writes=[("svcw",)])
            if normalize:
                P.op("v", lambda e: e.tensor_tensor(stats[:, ST_DEN:ST_DEN + 1],
                                                    stats[:, ST_M1:ST_M1 + 1],
                                                    stats[:, ST_M2:ST_M2 + 1], ALU.add),
                     reads=[("stats", "m1"), ("stats", "m2")],
                     writes=[("stats", "den")])
                P.op("v", lambda e: e.tensor_scalar(stats[:, ST_DEN:ST_DEN + 1],
                                                    stats[:, ST_DEN:ST_DEN + 1],
                                                    1e-8, None, ALU.add),
                     reads=[("stats", "den")], writes=[("stats", "den")])
                P.op("v", lambda e: e.reciprocal(stats[:, ST_DEN:ST_DEN + 1],
                                                 stats[:, ST_DEN:ST_DEN + 1]),
                     reads=[("stats", "den")], writes=[("stats", "den")])
                P.op("v", lambda e: e.tensor_scalar(cwtmp, cwtmp,
                                                    stats[:, ST_DEN:ST_DEN + 1], None,
                                                    ALU.mult),
                     reads=[("svcw",), ("stats", "den")], writes=[("svcw",)])
            P.op("v", lambda e, tile=tile: e.tensor_copy(
                    cwt[:, ne * tile:ne * (tile + 1)], cwtmp),
                 reads=[("svcw",)], writes=[("cw" + which, tile)])

        for t_ in range(NT):
            emit_cw("l0", NE0, t_, False)
        if "cw0" in taps:
            P.dma("sb", tap_aps["cw0"], cwtm0[:, :],
                  reads=[("cwl0", t_) for t_ in range(NT)])

        if upto == "cw0":
            return _finish()
        HA, HB = 0, 4096
        SLOT = [0, 12288]
        P.rekey(AUXB, [("aux", "hA"), ("aux", "hB")])
        P.rekey([("arw", "qkv", 0), ("arw", "qkv", 1), ("arw", "qkv", 2),
                 ("arw", "proj")],
                [("arw", "ug", 0), ("arw", "uu", 0), ("arw", "ud", 0),
                 ("arw", "ug", 1), ("arw", "uu", 1), ("arw", "ud", 1)])

        def c_unit_dma(u):
            slot = SLOT[u % 2]
            ch = "wa" if u % 2 == 0 else "wb"
            gv = arw[:, slot:slot + 4096].rearrange("p (k n) -> p k n", k=8)
            uv = arw[:, slot + 4096:slot + 8192].rearrange("p (k n) -> p k n", k=8)
            dv = arw[:, slot + 8192:slot + 12288].rearrange("p (k n) -> p k n", k=4)
            if u < NE0:
                P.dma(ch, gv, Wg0[u].rearrange("(k p) n -> p k n", p=128),
                      writes=[("arw", "ug", u % 2)])
                P.dma(ch, uv, Wu0[u].rearrange("(k p) n -> p k n", p=128),
                      writes=[("arw", "uu", u % 2)])
                P.dma(ch, dv, Wd0[u].rearrange("(k p) n -> p k n", p=128),
                      writes=[("arw", "ud", u % 2)])
            else:
                f = u - NE0
                P.dma(ch, gv,
                      Sg.rearrange("(k p) n -> p k n", p=128)[:, :, 512 * f:512 * (f + 1)],
                      writes=[("arw", "ug", u % 2)])
                P.dma(ch, uv,
                      Su.rearrange("(k p) n -> p k n", p=128)[:, :, 512 * f:512 * (f + 1)],
                      writes=[("arw", "uu", u % 2)])
                P.dma(ch, dv,
                      Sd.rearrange("(k p) n -> p k n", p=128)[:, 4 * f:4 * (f + 1), :],
                      writes=[("arw", "ud", u % 2)])

        def c_unit_compute(u, th):
            slot = SLOT[u % 2]
            hoff = HA if u % 2 == 0 else HB
            hkey = ("aux", "hA" if u % 2 == 0 else "hB")
            gv = arw[:, slot:slot + 4096].rearrange("p (k n) -> p k n", k=8)
            uv = arw[:, slot + 4096:slot + 8192].rearrange("p (k n) -> p k n", k=8)
            dv = arw[:, slot + 8192:slot + 12288].rearrange("p (k n) -> p k n", k=4)
            hv = aux[:, hoff:hoff + 4 * TH].rearrange("p (m n) -> p m n", m=4)
            for m in range(4):
                for n2 in range(NW5):
                    def gmm(e, m=m, n2=n2, th=th, gv=gv):
                        for k in range(8):
                            ins = e.matmul(ps[:, 0:W5],
                                           gv[:, k, 128 * m:128 * (m + 1)],
                                           xkv[:, k, 8 + TH * th + W5 * n2:
                                               8 + TH * th + W5 * (n2 + 1)],
                                           start=(k == 0), stop=(k == 7))
                        return ins
                    P.op("t", gmm, reads=[("arw", "ug", u % 2), ("xnT",)],
                         writes=[("ps", 0)])
                    P.op("s", lambda e: e.activation(tgt[:, 0:W5], ps[:, 0:W5],
                                                     AF.Silu),
                         reads=[("ps", 0)], writes=[("tg",)])

                    def umm(e, m=m, n2=n2, th=th, uv=uv):
                        for k in range(8):
                            ins = e.matmul(ps[:, 512:512 + W5],
                                           uv[:, k, 128 * m:128 * (m + 1)],
                                           xkv[:, k, 8 + TH * th + W5 * n2:
                                               8 + TH * th + W5 * (n2 + 1)],
                                           start=(k == 0), stop=(k == 7))
                        return ins
                    P.op("t", umm, reads=[("arw", "uu", u % 2), ("xnT",)],
                         writes=[("ps", 1)])
                    hslice = hv[:, m, W5 * n2:W5 * (n2 + 1)]
                    P.op("v", lambda e, hslice=hslice: e.tensor_tensor(
                            hslice, tgt[:, 0:W5], ps[:, 512:512 + W5], ALU.mult),
                         reads=[("tg",), ("ps", 1)], writes=[hkey])
            for sl in range(NSL):
                for n in range(2):
                    bk = 2 + (sl * 2 + n) % 2

                    def dmm(e, sl=sl, n=n, bk=bk, hv=hv, dv=dv):
                        for k in range(4):
                            ins = e.matmul(ps[:, 512 * bk:512 * (bk + 1)],
                                           hv[:, k, 128 * sl:128 * (sl + 1)],
                                           dv[:, k, 512 * n:512 * (n + 1)],
                                           start=(k == 0), stop=(k == 3))
                        return ins
                    P.op("t", dmm, reads=[hkey, ("arw", "ud", u % 2)],
                         writes=[("ps", bk)])
                    acs = big[:, 1024 * sl + 512 * n:1024 * sl + 512 * (n + 1)]
                    t_ = NSL * th + sl
                    if u < NE0:
                        cwc = cwtm0[:, NE0 * t_ + u:NE0 * t_ + u + 1]
                        P.op("v", lambda e, acs=acs, bk=bk, cwc=cwc:
                                e.scalar_tensor_tensor(
                                    acs, ps[:, 512 * bk:512 * (bk + 1)], cwc, acs,
                                    ALU.mult, ALU.add),
                             reads=[("ps", bk), ("acc", sl, n), ("cwl0", t_)],
                             writes=[("acc", sl, n)])
                    else:
                        P.op("v", lambda e, acs=acs, bk=bk: e.tensor_tensor(
                                acs, acs, ps[:, 512 * bk:512 * (bk + 1)], ALU.add),
                             reads=[("ps", bk), ("acc", sl, n)],
                             writes=[("acc", sl, n)])

        P.rekey([("xt", 1), ("stl", 1), ("nt", 1)],
                [("acc", sl, n) for sl in range(NSL) for n in range(2)])
        for th in range(2):
            for sl in range(NSL):
                for n in range(2):
                    P.op("v", lambda e, sl=sl, n=n: e.memset(
                            big[:, 1024 * sl + 512 * n:1024 * sl + 512 * (n + 1)], 0.0),
                         writes=[("acc", sl, n)])
            c_unit_dma(0)
            c_unit_dma(1)
            for u in range(18):
                c_unit_compute(u, th)
                if u + 2 < 18:
                    c_unit_dma(u + 2)
            for tt in range(NSL):
                t_ = NSL * th + tt
                P.op("s", lambda e, tt=tt: e.activation(
                        ntile[:, :], big[:, 1024 * tt:1024 * (tt + 1)], AF.Silu),
                     reads=[("acc", tt, 0), ("acc", tt, 1)], writes=[("nt",)])
                P.dma("la", xtile[:, :], x_in[8 + 128 * t_:8 + 128 * (t_ + 1), :],
                      writes=[("xt",)])
                P.dma("lb", stile[:, :], s_scr.ap()[128 * t_:128 * (t_ + 1), :],
                      reads=[("scr_s", t_)], writes=[("st",)])
                P.op("v", lambda e: e.tensor_tensor(ntile[:, :], stile[:, :],
                                                    ntile[:, :], ALU.mult),
                     reads=[("st",), ("nt",)], writes=[("nt",)])
                P.op("v", lambda e: e.tensor_tensor(xtile[:, :], xtile[:, :],
                                                    ntile[:, :], ALU.add),
                     reads=[("nt",), ("xt",)], writes=[("xt",)])
                P.dma("sa" if t_ % 2 == 0 else "sb",
                      x1_scr.ap()[128 * t_:128 * (t_ + 1), :], xtile[:, :],
                      reads=[("xt",)], writes=[("scr_x1", t_)])
        if "acc0" in taps:
            P.dma("sb", tap_aps["acc0"], big[:, 0:NSL * 1024],
                  reads=[("acc", sl, n) for sl in range(NSL) for n in range(2)])
        if "x1" in taps:
            P.dma("sa", tap_aps["x1"], x1_scr.ap(),
                  reads=[("scr_x1", t_) for t_ in range(NT)])

        if upto == "C":
            return _finish()
        P.rekey([("aux", "hA"), ("aux", "hB")],
                [("aux", "xf", 0), ("aux", "xf", 1)])
        P.rekey([("acc", sl, n) for sl in range(NSL) for n in range(2)],
                [("xt", 1), ("stl", 1), ("nt", 1)])
        for t_ in range(NT):
            emit_ln(128, x1_scr.ap()[128 * t_:128 * (t_ + 1), :], 8 + 128 * t_,
                    ("l1", NE1), t_ * NE1, "la" if t_ % 2 == 0 else "lb",
                    p=t_ % 2)
        if "logits1" in taps:
            P.dma("sb", tap_aps["logits1"], logit1[:, 0:NT * NE1], reads=[("l1",)])

        P.dma("la", rows2[0:1, 128:128 + NE1], bg1.unsqueeze(0), writes=[("r2", "bg")])
        P.op("t", lambda e: e.matmul(ps[:, 1024:1024 + NE1],
                                     rows2[0:1, 0:128],
                                     rows2[0:1, 128:128 + NE1],
                                     start=True, stop=True),
             reads=[("r2", "ones"), ("r2", "bg")], writes=[("ps", 2)])
        P.op("v", lambda e: e.tensor_copy(bgrow_sb, ps[:, 1024:1024 + NE1]),
             reads=[("ps", 2)], writes=[("bgB",)])

        for t_ in range(NT):
            emit_cw("l1", NE1, t_, True)
        if "cw1" in taps:
            P.dma("sb", tap_aps["cw1"], cwtm1[:, :],
                  reads=[("cwl1", t_) for t_ in range(NT)])

        if upto == "ln1":
            return _finish()
        P.dma("la", cwrows[R_CP:R_CP + 16, 0:DI], Cp, writes=[("cp",)])
        P.dma("lb", rows2[32:48, 0:128], bin_.rearrange("(c p) -> c p", p=128),
              reads=[("r2", "stage")], writes=[("r2", "stage")])
        P.op("t", lambda e: e.transpose(ps[:, 3584:3584 + 16],
                                        rows2[32:48, 0:128],
                                        identf_sb[32:48, 32:48]),
             reads=[("r2", "stage"), ("idf",)], writes=[("ps", 7)])
        P.op("v", lambda e: e.tensor_copy(smallv[:, SV_BIN:SV_BIN + 16],
                                          ps[:, 3584:3584 + 16]),
             reads=[("ps", 7)], writes=[("sv", "bin")])
        P.dma("lb", rows2[32:48, 0:128], Dp.rearrange("(c p) -> c p", p=128),
              reads=[("r2", "stage")], writes=[("r2", "stage")])
        P.op("t", lambda e: e.transpose(ps[:, 3584:3584 + 16],
                                        rows2[32:48, 0:128],
                                        identf_sb[32:48, 32:48]),
             reads=[("r2", "stage"), ("idf",)], writes=[("ps", 7)])
        P.op("v", lambda e: e.tensor_copy(smallv[:, SV_DP:SV_DP + 16],
                                          ps[:, 3584:3584 + 16]),
             reads=[("ps", 7)], writes=[("sv", "dp")])
        for c in range(16):
            P.dma("la" if c % 2 == 0 else "lb", xtile[:, 0:16],
                  Bp[128 * c:128 * (c + 1), :], reads=[("xt",)], writes=[("xt",)])
            P.op("t", lambda e, c=c: e.transpose(ps[:, 3584:3584 + 16],
                                                 cwrows[R_CP:R_CP + 16,
                                                        128 * c:128 * (c + 1)],
                                                 identf_sb[32:48, 32:48]),
                 reads=[("cp",), ("idf",)], writes=[("ps", 7)])
            P.op("v", lambda e: e.tensor_tensor(ntile[:, 0:16], xtile[:, 0:16],
                                                ps[:, 3584:3584 + 16], ALU.mult),
                 reads=[("xt",), ("ps", 7)], writes=[("nt",)])
            P.op("v", lambda e, c=c: e.reduce_sum(smallv[:, SV_R + c:SV_R + c + 1],
                                                  ntile[:, 0:16], AXX),
                 reads=[("nt",)], writes=[("sv", "r", c)])
        if "rvec" in taps:
            P.dma("sb", tap_aps["rvec"], smallv[:, SV_R:SV_R + 16],
                  reads=[("sv", "r", c) for c in range(16)])

        bias_bcast(bout, "la")

        P.rekey([("acc", sl, n) for sl in range(NSL) for n in range(2)]
                + [("xt", 1), ("stl", 1), ("nt", 1)],
                [("bd", "xi"), ("bd", "c"), ("bd", "y"), ("bd", "dec")])
        xi_sb = big[:, 0:T]
        csc_sb = big[:, 2048:2048 + T]
        y_sb = big[:, 4096:4096 + T]
        dec_sb = big[:, 6144:6144 + T]
        P.op("v", lambda e: e.memset(dec_sb, DECAY), writes=[("bd", "dec")])

        P.rekey([("arw", "ug", 0), ("arw", "ug", 1),
                 ("arw", "uu", 0), ("arw", "uu", 1), ("arw", "ud", 0),
                 ("arw", "ud", 1)],
                [("arw", "win"), ("arw", "wout")])
        win_v = arw[:, 0:16384].rearrange("p (k n) -> p k n", k=8)
        wout_v = arw[:, 16384:32768].rearrange("p (k n) -> p k n", k=16)
        P.dma("wa", win_v[:, :, 0:1024],
              Win.rearrange("(k p) n -> p k n", p=128)[:, :, 0:1024],
              writes=[("arw", "win")])
        P.dma("wb", win_v[:, :, 1024:2048],
              Win.rearrange("(k p) n -> p k n", p=128)[:, :, 1024:2048],
              writes=[("arw", "win")])
        for hw_ in range(2):
            P.dma("wa", wout_v[:, 8 * hw_:8 * (hw_ + 1), :],
                  Wout.rearrange("(k p) n -> p k n", p=128)[:, 8 * hw_:8 * (hw_ + 1), :],
                  writes=[("arw", "wout")])

        NXI = T // 512
        for c in range(16):
            po_ = 2048 * (c % 2)
            bnks = [4 * (c % 2) + j for j in range(NXI)]

            def xmm(e, c=c, po_=po_):
                for n2 in range(NXI):
                    for k in range(8):
                        ins = e.matmul(ps[:, po_ + 512 * n2:po_ + 512 * (n2 + 1)],
                                       win_v[:, k, 128 * c:128 * (c + 1)],
                                       xkv[:, k, 8 + 512 * n2:8 + 512 * (n2 + 1)],
                                       start=(k == 0), stop=(k == 7))
                return ins
            P.op("t", xmm, reads=[("arw", "win"), ("xnT",)],
                 writes=[("ps", bnk) for bnk in bnks])
            P.op("s", lambda e, c=c, po_=po_: e.activation(
                    xi_sb, ps[:, po_:po_ + T], AF.Silu,
                    bias=smallv[:, SV_BIN + c:SV_BIN + c + 1], scale=1.0),
                 reads=[("ps", bnk) for bnk in bnks] + [("sv", "bin")],
                 writes=[("bd", "xi")])
            if "xi" in taps:
                P.dma("sb", tap_aps["xi"][c], xi_sb, reads=[("bd", "xi")])
            P.op("v", lambda e: e.tensor_tensor_scan(csc_sb, dec_sb, xi_sb, 0.0,
                                                     ALU.mult, ALU.add),
                 reads=[("bd", "dec"), ("bd", "xi")], writes=[("bd", "c")])
            P.op("v", lambda e, c=c: e.tensor_copy(smallv[:, SV_FR + c:SV_FR + c + 1],
                                                   csc_sb[:, T - 1:T]),
                 reads=[("bd", "c")], writes=[("sv", "f", c)])
            P.op("v", lambda e, c=c: e.tensor_scalar(y_sb, xi_sb,
                                                     smallv[:, SV_DP + c:
                                                            SV_DP + c + 1],
                                                     None, ALU.mult),
                 reads=[("bd", "xi"), ("sv", "dp")], writes=[("bd", "y")])
            P.op("v", lambda e, c=c: e.scalar_tensor_tensor(
                    y_sb, csc_sb, smallv[:, SV_R + c:SV_R + c + 1], y_sb,
                    ALU.mult, ALU.add),
                 reads=[("bd", "c"), ("sv", "r", c), ("bd", "y")],
                 writes=[("bd", "y")])
            P.dma("sa" if c % 2 == 0 else "sb", yp_scr.ap()[c], y_sb,
                  reads=[("bd", "y")], writes=[("scr_yp", c)])
        if "ypart" in taps:
            P.dma("sa", tap_aps["ypart"], yp_scr.ap(),
                  reads=[("scr_yp", c) for c in range(16)])

        P.op("t", lambda e: e.transpose(ps[0:16, 1024:1024 + 128],
                                        smallv[:, SV_FR:SV_FR + 16], identf_sb[:, :]),
             reads=[("sv", "f", c) for c in range(16)] + [("idf",)],
             writes=[("ps", 2)])
        P.op("v", lambda e: e.tensor_copy(cwrows[R_FT:R_FT + 16, 0:128],
                                          ps[0:16, 1024:1024 + 128]),
             reads=[("ps", 2)], writes=[("fT",)])
        P.dma("cc", cc_in.ap().rearrange("a (c p) -> (a c) p", p=128),
              cwrows[R_FT:R_FT + 16, 0:128], reads=[("fT",)], writes=[("ccin",)])
        P.op("g", lambda e: e.collective_compute(
                "AllGather", ALU.bypass, replica_groups=[list(range(world))],
                ins=[cc_in.ap()], outs=[cc_out.ap()]),
             reads=[("ccin",)], writes=[("ccout",)], sem="ccc")
        P.dma("cc", cwrows[R_FALL:R_FALL + world, 0:DI], cc_out.ap(),
              reads=[("ccout",)], writes=[("Fall",)])

        def fsel_mm(e):
            for n2 in range(DI // 512):
                ins = e.matmul(ps[0:1, 512 * n2:512 * (n2 + 1)], fsel_sb[0:world, :],
                               cwrows[R_FALL:R_FALL + world, 512 * n2:512 * (n2 + 1)],
                               start=True, stop=True)
            return ins
        P.op("t", fsel_mm, reads=[("Fall",), ("fsel",)],
             writes=[("ps", 0), ("ps", 1), ("ps", 2), ("ps", 3)])
        frow = auxf[0:1, 3584:3584 + DI]
        P.rekey([("aux", "xf")], [("aux", "dpow"), ("aux", "ybf", 0),
                                  ("aux", "ybf", 1), ("aux", "ybL", 0),
                                  ("aux", "ybL", 1), ("aux", "frow")])
        P.op("v", lambda e: e.tensor_copy(frow, ps[0:1, 0:DI]),
             reads=[("ps", 0), ("ps", 1), ("ps", 2), ("ps", 3)],
             writes=[("aux", "frow")])

        P.dma("sb", fr_scr.ap(), frow, reads=[("aux", "frow")], writes=[("frs",)])
        P.dma("lb", rows2[32:48, 0:128],
              fr_scr.ap().rearrange("a (c p) -> (a c) p", p=128),
              reads=[("frs",), ("r2", "stage")], writes=[("r2", "stage")])
        P.op("t", lambda e: e.transpose(ps[:, 3584:3584 + 16],
                                        rows2[32:48, 0:128],
                                        identf_sb[32:48, 32:48]),
             reads=[("r2", "stage"), ("idf",)], writes=[("ps", 7)])
        P.op("v", lambda e: e.tensor_copy(smallv[:, SV_FP:SV_FP + 16],
                                          ps[:, 3584:3584 + 16]),
             reads=[("ps", 7)], writes=[("sv", "fp")])
        if "fprev" in taps:
            P.dma("sb", tap_aps["fprev"], smallv[:, SV_FP:SV_FP + 16],
                  reads=[("sv", "fp")])
        P.op("v", lambda e: e.tensor_tensor(smallv[:, SV_FP:SV_FP + 16],
                                            smallv[:, SV_FP:SV_FP + 16],
                                            smallv[:, SV_R:SV_R + 16], ALU.mult),
             reads=[("sv", "fp")] + [("sv", "r", c) for c in range(16)],
             writes=[("sv", "fp")])

        dpow_sb = auxf[:, 512:512 + T]
        ybfs = [aux[:, 0:1024], aux[:, 12288:13312]]
        ybLs = [auxf[:, 512 + T:512 + T + TH], auxf[:, 6656:6656 + TH]]
        P.dma("la", dpow_sb, dpow_in, writes=[("aux", "dpow")])

        def yb_load(th, c):
            p = c % 2
            P.dma("la" if c % 2 == 0 else "lb", ybLs[p],
                  yp_scr.ap()[c][:, TH * th:TH * (th + 1)],
                  reads=[("scr_yp", c), ("aux", "ybL", p)],
                  writes=[("aux", "ybL", p)])
            P.op("v", lambda e, c=c, th=th, p=p: e.scalar_tensor_tensor(
                    ybfs[p][:, 0:TH], dpow_sb[:, TH * th:TH * (th + 1)],
                    smallv[:, SV_FP + c:SV_FP + c + 1], ybLs[p],
                    ALU.mult, ALU.add),
                 reads=[("aux", "dpow"), ("sv", "fp"), ("aux", "ybL", p),
                        ("aux", "ybf", p)],
                 writes=[("aux", "ybf", p)])

        P.rekey([("bd", "xi"), ("bd", "c"), ("bd", "y"), ("bd", "dec")],
                [("s1acc", g) for g in range(NSL)])
        for th in range(2):
            for sl in range(NSL):
                gsl = th * NSL + sl
                P.op("v", lambda e, sl=sl: e.tensor_copy(
                        big[:, 1024 * sl:1024 * (sl + 1)], bb[:, :]),
                     reads=[("bb",)], writes=[("s1acc", sl)])
            yb_load(th, 0)
            for c in range(16):
                p = c % 2
                for sl in range(NSL):
                    for n in range(2):
                        bk = 4 + (sl * 2 + n) % 2
                        gsl = th * NSL + sl

                        def wmm(e, c=c, sl=sl, n=n, bk=bk, p=p):
                            return e.matmul(ps[:, 512 * bk:512 * (bk + 1)],
                                            ybfs[p][:, 128 * sl:128 * (sl + 1)],
                                            wout_v[:, c, 512 * n:512 * (n + 1)],
                                            start=True, stop=True)
                        P.op("t", wmm, reads=[("aux", "ybf", p), ("arw", "wout")],
                             writes=[("ps", bk)])
                        acs = big[:, 1024 * sl + 512 * n:1024 * sl + 512 * (n + 1)]
                        P.op("v", lambda e, acs=acs, bk=bk: e.tensor_tensor(
                                acs, acs, ps[:, 512 * bk:512 * (bk + 1)], ALU.add),
                             reads=[("ps", bk), ("s1acc", sl)],
                             writes=[("s1acc", sl)])
                        if sl == 0 and n == 0 and c + 1 < 16:
                            yb_load(th, c + 1)
            for sl in range(NSL):
                t_ = th * NSL + sl
                P.dma("sa" if t_ % 2 == 0 else "sb",
                      s_scr.ap()[128 * t_:128 * (t_ + 1), :],
                      big[:, 1024 * sl:1024 * (sl + 1)],
                      reads=[("s1acc", sl)], writes=[("scr_s", t_)])
        if "s1" in taps:
            P.dma("sa", tap_aps["s1"], s_scr.ap(),
                  reads=[("scr_s", t_) for t_ in range(NT)])

        if upto == "D":
            return _finish()
        P.rekey([("arw", "win"), ("arw", "wout")],
                [("arw", "w1", 0), ("arw", "w1", 1), ("arw", "w1", 2),
                 ("arw", "w1", 3), ("arw", "w2")])
        P.rekey([("aux", "dpow"), ("aux", "ybf", 0), ("aux", "ybf", 1),
                 ("aux", "ybL", 0), ("aux", "ybL", 1), ("aux", "frow")],
                [("hE", k) for k in range(16)])
        P.rekey([("s1acc", g) for g in range(NSL)],
                [("eacc", sl, n) for sl in range(NSL) for n in range(2)])
        w1s = [arw[:, 4096 * j:4096 * (j + 1)].rearrange("p (k n) -> p k n", k=8)
               for j in range(4)]
        w2v = arw[:, 16384:32768].rearrange("p (k n) -> p k n", k=16)
        hEv = aux[:, 0:16 * TH].rearrange("p (k n) -> p k n", k=16)

        for th in range(2):
            for sl in range(NSL):
                for n in range(2):
                    P.op("v", lambda e, sl=sl, n=n: e.memset(
                            big[:, 1024 * sl + 512 * n:1024 * sl + 512 * (n + 1)], 0.0),
                         writes=[("eacc", sl, n)])
            for eix in range(NE1):
                P.dma("la", rows2[32:64, 0:128],
                      b1[eix].rearrange("(c p) -> c p", p=128),
                      reads=[("r2", "stage")], writes=[("r2", "stage")])
                P.op("t", lambda e: e.transpose(ps[:, 3584:3584 + 32],
                                                rows2[32:64, 0:128],
                                                identf_sb[32:64, 32:64]),
                     reads=[("r2", "stage"), ("idf",)], writes=[("ps", 7)])
                P.op("v", lambda e: e.tensor_copy(smallv[:, SV_B1:SV_B1 + 32],
                                                  ps[:, 3584:3584 + 32]),
                     reads=[("ps", 7)], writes=[("sv", "b1")])
                bias_bcast(b2[eix], "lb")
                for hk in range(2):
                    for cc4 in range(4):
                        P.dma("wa", w1s[cc4],
                              W1[eix].rearrange("(k p) n -> p k n", p=128)
                              [:, :, 2048 * hk + 512 * cc4:2048 * hk + 512 * (cc4 + 1)],
                              writes=[("arw", "w1", cc4)])
                    for hw_ in range(2):
                        P.dma("wa", w2v[:, 8 * hw_:8 * (hw_ + 1), :],
                              W2[eix].rearrange("(k p) n -> p k n", p=128)
                              [:, 16 * hk + 8 * hw_:16 * hk + 8 * (hw_ + 1), :],
                              writes=[("arw", "w2")])
                    for cc4 in range(4):
                        cslot = cc4
                        for m in range(4):
                            mg = 4 * cc4 + m
                            for n2 in range(NW5):
                                bk = (0, 1, 6)[(2 * m + n2) % 3]

                                def hmm(e, m=m, n2=n2, bk=bk, cslot=cslot, th=th):
                                    for k in range(8):
                                        ins = e.matmul(
                                            ps[:, 512 * bk:512 * bk + W5],
                                            w1s[cslot][:, k, 128 * m:128 * (m + 1)],
                                            xkv[:, k, 8 + TH * th + W5 * n2:
                                                8 + TH * th + W5 * (n2 + 1)],
                                            start=(k == 0), stop=(k == 7))
                                    return ins
                                P.op("t", hmm, reads=[("arw", "w1", cslot), ("xnT",)],
                                     writes=[("ps", bk)])
                                hsl = hEv[:, mg, W5 * n2:W5 * (n2 + 1)]
                                bcol = SV_B1 + 16 * hk + mg
                                P.op("s", lambda e, hsl=hsl, bk=bk, bcol=bcol:
                                        e.activation(
                                        hsl, ps[:, 512 * bk:512 * bk + W5],
                                        AF.Gelu_apprx_tanh,
                                        bias=smallv[:, bcol:bcol + 1], scale=1.0),
                                     reads=[("ps", bk), ("sv", "b1")],
                                     writes=[("hE", mg)])
                    for sl in range(NSL):
                        for n in range(2):
                            bk = 2 + (sl * 2 + n) % 2

                            def emm(e, sl=sl, n=n, bk=bk):
                                for k in range(16):
                                    ins = e.matmul(ps[:, 512 * bk:512 * (bk + 1)],
                                                   hEv[:, k, 128 * sl:128 * (sl + 1)],
                                                   w2v[:, k, 512 * n:512 * (n + 1)],
                                                   start=(k == 0), stop=(k == 15))
                                return ins
                            P.op("t", emm,
                                 reads=[("hE", k) for k in range(16)] + [("arw", "w2")],
                                 writes=[("ps", bk)])
                            t_ = NSL * th + sl
                            acs = big[:, 1024 * sl + 512 * n:1024 * sl + 512 * (n + 1)]
                            cwc = cwtm1[:, NE1 * t_ + eix:NE1 * t_ + eix + 1]
                            P.op("v", lambda e, acs=acs, bk=bk, cwc=cwc:
                                    e.scalar_tensor_tensor(
                                        acs, ps[:, 512 * bk:512 * (bk + 1)], cwc, acs,
                                        ALU.mult, ALU.add),
                                 reads=[("ps", bk), ("eacc", sl, n), ("cwl1", t_)],
                                 writes=[("eacc", sl, n)])
                for sl in range(NSL):
                    t_ = NSL * th + sl
                    cwc = cwtm1[:, NE1 * t_ + eix:NE1 * t_ + eix + 1]
                    for n in range(2):
                        acs = big[:, 1024 * sl + 512 * n:1024 * sl + 512 * (n + 1)]
                        P.op("v", lambda e, acs=acs, n=n, cwc=cwc:
                                e.scalar_tensor_tensor(
                                    acs, bb[:, 512 * n:512 * (n + 1)], cwc, acs,
                                    ALU.mult, ALU.add),
                             reads=[("bb",), ("eacc", sl, n), ("cwl1", t_)],
                             writes=[("eacc", sl, n)])
            for tt in range(NSL):
                t_ = NSL * th + tt
                P.op("s", lambda e, tt=tt: e.activation(
                        ntile[:, :], big[:, 1024 * tt:1024 * (tt + 1)], AF.Silu),
                     reads=[("eacc", tt, 0), ("eacc", tt, 1)], writes=[("nt",)])
                P.dma("la", xtile[:, :], x1_scr.ap()[128 * t_:128 * (t_ + 1), :],
                      reads=[("scr_x1", t_)], writes=[("xt",)])
                P.dma("lb", stile[:, :], s_scr.ap()[128 * t_:128 * (t_ + 1), :],
                      reads=[("scr_s", t_)], writes=[("st",)])
                P.op("v", lambda e: e.tensor_tensor(ntile[:, :], stile[:, :],
                                                    ntile[:, :], ALU.mult),
                     reads=[("st",), ("nt",)], writes=[("nt",)])
                P.op("v", lambda e: e.tensor_tensor(xtile[:, :], xtile[:, :],
                                                    ntile[:, :], ALU.add),
                     reads=[("nt",), ("xt",)], writes=[("xt",)])
                P.dma("sa" if t_ % 2 == 0 else "sb",
                      out[128 * t_:128 * (t_ + 1), :], xtile[:, :],
                      reads=[("xt",)], writes=[("outw", t_)])

        return _finish()



def make_masks(q):
    i = np.arange(128)[:, None]
    j = np.arange(144)[None, :]
    interior = (j >= i) & (j <= i + 16)
    first = ((i <= 8) & (j >= 8) & (j <= i + 16)) | ((i > 8) & interior)
    last = ((128 - 1 - i <= 7) & (j >= i) & (j <= 135)) | ((i < 120) & interior)
    v1 = first if q == 0 else interior
    v2 = last if q == 3 else interior
    m = np.full((3, 128, 144), NEG, np.float32)
    for v, msk in enumerate((interior, v1, v2)):
        m[v][msk] = 0.0
    full = np.tile(m, (1, 1, 8))
    return full.astype(ml_dtypes.bfloat16)


def make_core_inputs(inputs, T=2048):
    x = np.asarray(inputs["L0_attn_x"])
    B, L, _ = x.shape
    percore = L // 4
    identf = np.eye(128, dtype=np.float32)
    identb = np.eye(128, dtype=ml_dtypes.bfloat16)
    dpow = np.tile((DECAY ** (np.arange(T, dtype=np.float64) + 1)
                    ).astype(np.float32)[None, :], (128, 1))
    wkeys = [k for k in inputs if k != "L0_attn_x"]
    in_maps = []
    for c in range(8):
        bsel, q = c // 4, c % 4
        lo, hi = q * percore, (q + 1) * percore
        xs = np.zeros((T + 16, D), np.float32)
        src_lo, src_hi = max(0, lo - 8), min(L, hi + 8)
        xs[8 - (lo - src_lo):8 + (src_hi - lo)] = x[bsel, src_lo:src_hi]
        fsel = np.zeros((8, 1), np.float32)
        if q > 0:
            fsel[c - 1, 0] = 1.0
        im = {k: np.ascontiguousarray(np.asarray(inputs[k])) for k in wkeys}
        im["x"] = xs
        im["identf"] = identf
        im["identb"] = identb
        im["maskp"] = np.ascontiguousarray(make_masks(q))
        im["dpow"] = dpow
        im["fsel"] = fsel
        in_maps.append(im)
    return in_maps


def assemble(results, T=2048):
    outs = [np.asarray(r["out"]) for r in results]
    return np.stack([np.concatenate(outs[0:4], 0),
                     np.concatenate(outs[4:8], 0)], 0)



def kernel(**inputs):
    from concourse.bass_utils import run_bass_kernel_spmd
    nc, _meta = build(T=2048)
    in_maps = make_core_inputs(inputs, T=2048)
    res = run_bass_kernel_spmd(nc, in_maps, core_ids=list(range(8)))
    return assemble(res.results, T=2048)


# version 06
# speedup vs baseline: 1.1944x; 1.0153x over previous
import contextlib
import numpy as np
import ml_dtypes
import concourse.bass as bass
import concourse.mybir as mybir

F32 = mybir.dt.float32
BF16 = mybir.dt.bfloat16
AF = mybir.ActivationFunctionType
ALU = mybir.AluOpType
AXX = mybir.AxisListType.X

D = 1024
NH = 16
HD = 64
ASCALE = HD ** (-0.5)
DI = 2048
NE0 = 16
F0 = 512
NE1 = 8
F1 = 4096
DECAY = 0.9
NEG = -30000.0

DMA_CHANS = {"wa": "g", "wb": "g", "wc": "s", "la": "y", "lb": "y", "sa": "y",
             "sb": "y", "cc": "g"}
ENG_SEM = {"t": "T", "v": "V", "s": "S", "g": "G", "y": "Y"}


class Prog:
    def __init__(self):
        self.ops = []
        self.cnt = {}
        self.writers = {}
        self.readers = {}
        self.waited = {}

    def _waits(self, eng, toks):
        best = {}
        for (sem, val) in toks:
            if eng == "t" and sem == "T":
                continue
            if val > best.get(sem, 0):
                best[sem] = val
        waits = []
        for sem, val in best.items():
            if self.waited.get((eng, sem), 0) < val:
                waits.append((sem, val))
                self.waited[(eng, sem)] = val
        return waits

    def op(self, eng, emit, reads=(), writes=(), sem=None, amt=1):
        sem = sem or ENG_SEM[eng]
        if eng in ("v", "s"):
            pbanks = {buf[1] for buf in tuple(reads) + tuple(writes)
                      if isinstance(buf, tuple) and buf and buf[0] == "ps"}
            if pbanks:
                writes = tuple(writes) + tuple(("plock", b) for b in sorted(pbanks))
        toks = []
        for buf in reads:
            toks += self.writers.get(buf, [])
        for buf in writes:
            toks += self.writers.get(buf, [])
            toks += self.readers.get(buf, [])
        waits = self._waits(eng, toks)
        self.cnt[sem] = self.cnt.get(sem, 0) + amt
        tok = (sem, self.cnt[sem])
        for buf in writes:
            self.writers[buf] = [tok]
            self.readers[buf] = []
        for buf in reads:
            self.readers.setdefault(buf, []).append(tok)
        self.ops.append((eng, waits, emit, sem, amt))
        return tok

    def dma(self, chan, out_ap, in_ap, reads=(), writes=()):
        return self.op(DMA_CHANS[chan],
                       lambda e: e.dma_start(out=out_ap, in_=in_ap),
                       reads=tuple(reads) + (("chan", chan),),
                       writes=tuple(writes) + (("chan", chan),),
                       sem=chan, amt=16)

    def rekey(self, old_keys, new_keys):
        toks = []
        for k in old_keys:
            toks += self.writers.pop(k, [])
            toks += self.readers.pop(k, [])
        for k in new_keys:
            self.writers[k] = self.writers.get(k, []) + toks

    def barrier(self, eng, bufs):
        toks = []
        for k in bufs:
            toks += self.writers.get(k, [])
            toks += self.readers.get(k, [])
        waits = self._waits(eng, toks)
        if waits:
            self.ops.append((eng, waits, lambda e: None, ENG_SEM[eng], 0))

    def emit_all(self, blk, sems):
        streams = {}
        for rec in self.ops:
            streams.setdefault(rec[0], []).append(rec)

        def run(e, engname):
            for (_eng, waits, emit, sem, amt) in streams[engname]:
                for (s_, v_) in waits:
                    e.wait_ge(sems[s_], v_)
                ins = emit(e)
                if ins is not None and amt > 0:
                    ins.then_inc(sems[sem], amt)

        for name, meth in (("t", blk.tensor), ("v", blk.vector), ("s", blk.scalar),
                           ("g", blk.gpsimd), ("y", blk.sync)):
            if name in streams:
                meth(lambda e, name=name: run(e, name))


class _Done(Exception):
    pass


def build(T=2048, taps=(), world=8, upto="E"):
    assert T % 512 == 0
    NT = T // 128
    NST = T // 512
    TH = T // 2
    NSL = TH // 128
    TW = T + 16
    W5 = min(512, TH)
    NW5 = max(1, TH // 512)

    nc = bass.Bass("TRN2", target_bir_lowering=False)
    dt = nc.dram_tensor

    x_in = dt("x", [TW, D], F32, kind="ExternalInput").ap()
    Wqkv = dt("L0_attn_Wqkv", [D, 3 * D], F32, kind="ExternalInput").ap()
    Wproj = dt("L0_attn_Wproj", [D, D], F32, kind="ExternalInput").ap()
    bproj = dt("L0_attn_bproj", [D], F32, kind="ExternalInput").ap()
    Wr = dt("L0_ffn_Wr", [D, NE0], F32, kind="ExternalInput").ap()
    Wg0 = dt("L0_ffn_Wg", [NE0, D, F0], F32, kind="ExternalInput").ap()
    Wu0 = dt("L0_ffn_Wu", [NE0, D, F0], F32, kind="ExternalInput").ap()
    Wd0 = dt("L0_ffn_Wd", [NE0, F0, D], F32, kind="ExternalInput").ap()
    Sg = dt("L0_ffn_Sg", [D, D], F32, kind="ExternalInput").ap()
    Su = dt("L0_ffn_Su", [D, D], F32, kind="ExternalInput").ap()
    Sd = dt("L0_ffn_Sd", [D, D], F32, kind="ExternalInput").ap()
    Win = dt("L1_attn_Win", [D, DI], F32, kind="ExternalInput").ap()
    bin_ = dt("L1_attn_bin_", [DI], F32, kind="ExternalInput").ap()
    Bp = dt("L1_attn_Bp", [DI, 16], F32, kind="ExternalInput").ap()
    Cp = dt("L1_attn_Cp", [16, DI], F32, kind="ExternalInput").ap()
    Dp = dt("L1_attn_Dp", [DI], F32, kind="ExternalInput").ap()
    Wout = dt("L1_attn_Wout", [DI, D], F32, kind="ExternalInput").ap()
    bout = dt("L1_attn_bout", [D], F32, kind="ExternalInput").ap()
    Wg1 = dt("L1_ffn_Wg", [D, NE1], F32, kind="ExternalInput").ap()
    bg1 = dt("L1_ffn_bg", [NE1], F32, kind="ExternalInput").ap()
    W1 = dt("L1_ffn_W1", [NE1, D, F1], F32, kind="ExternalInput").ap()
    b1 = dt("L1_ffn_b1", [NE1, F1], F32, kind="ExternalInput").ap()
    W2 = dt("L1_ffn_W2", [NE1, F1, D], F32, kind="ExternalInput").ap()
    b2 = dt("L1_ffn_b2", [NE1, D], F32, kind="ExternalInput").ap()
    identf = dt("identf", [128, 128], F32, kind="ExternalInput").ap()
    identb = dt("identb", [128, 128], BF16, kind="ExternalInput").ap()
    maskp = dt("maskp", [3, 128, 1152], BF16, kind="ExternalInput").ap()
    dpow_in = dt("dpow", [128, T], F32, kind="ExternalInput").ap()
    fsel_in = dt("fsel", [world, 1], F32, kind="ExternalInput").ap()

    out = dt("out", [T, D], F32, kind="ExternalOutput").ap()
    tap_shapes = {"xnT": ([128, 8, TW], BF16), "logits0": ([128, NT * NE0], F32),
                  "logits1": ([128, NT * NE1], F32),
                  "s0": ([T, D], F32), "x1": ([T, D], F32),
                  "acc0": ([128, NSL * D], F32), "xi": ([16, 128, T], F32),
                  "ypart": ([16, 128, T], F32), "s1": ([T, D], F32),
                  "cw0": ([128, NT * NE0], F32), "cw1": ([128, NT * NE1], F32),
                  "fprev": ([128, 16], F32), "rvec": ([128, 16], F32)}
    tap_aps = {nm: dt("tap_" + nm, shp, dty, kind="ExternalOutput").ap()
               for nm, (shp, dty) in tap_shapes.items() if nm in taps}

    s_scr = dt("s_scr", [T, D], F32)
    x1_scr = dt("x1_scr", [T, D], F32)
    yp_scr = dt("yp_scr", [16, 128, T], F32)
    cc_in = dt("cc_in", [1, DI], F32)
    cc_out = dt("cc_out", [world, DI], F32)
    fr_scr = dt("fr_scr", [1, DI], F32)

    with contextlib.ExitStack() as es:
        E = es.enter_context
        xnT = E(nc.sbuf_tensor("xnT", [128, 8 * TW], BF16))
        arw = E(nc.sbuf_tensor("arw", [128, 32768], BF16))
        big = E(nc.sbuf_tensor("big", [128, 8192], F32))
        aux = E(nc.sbuf_tensor("aux", [128, 16896], BF16))
        xtile = E(nc.sbuf_tensor("xtile", [128, D], F32))
        stile = E(nc.sbuf_tensor("stile", [128, D], F32))
        ntile = E(nc.sbuf_tensor("ntile", [128, D], F32))
        masks = E(nc.sbuf_tensor("masks", [128, 3 * 1152], BF16))
        stats = E(nc.sbuf_tensor("stats", [128, 64], F32))
        logit0 = E(nc.sbuf_tensor("logit0", [128, NT * NE0], F32))
        logit1 = E(nc.sbuf_tensor("logit1", [128, NT * NE1], F32))
        cwtm0 = E(nc.sbuf_tensor("cwtm0", [128, NT * NE0], F32))
        cwtm1 = E(nc.sbuf_tensor("cwtm1", [128, NT * NE1], F32))
        cwrows = E(nc.sbuf_tensor("cwrows", [128, 2048], F32))
        rows2 = E(nc.sbuf_tensor("rows2", [128, 1536], F32))
        bb = E(nc.sbuf_tensor("bb", [128, 1024], F32))
        smallv = E(nc.sbuf_tensor("smallv", [128, 176], F32))
        wrg = E(nc.sbuf_tensor("wrg", [128, 192], F32))
        tgt = E(nc.sbuf_tensor("tgt", [128, 512], BF16))
        identf_sb = E(nc.sbuf_tensor("identf_sb", [128, 128], F32))
        identb_sb = E(nc.sbuf_tensor("identb_sb", [128, 128], BF16))
        fsel_sb = E(nc.sbuf_tensor("fsel_sb", [world, 1], F32))
        ps = E(nc.psum_tensor("ps", [128, 4096], F32))

        psb = ps[:, :].bitcast(BF16)
        auxf = aux[:, :].bitcast(F32)
        blk = E(nc.Block())
        P = Prog()

        SV_BIN, SV_DP, SV_R, SV_FP, SV_FR, SV_B1, SV_CW = 0, 16, 32, 48, 64, 96, 128
        ST_NM, ST_SQ, ST_IV, ST_M1, ST_M2, ST_DEN = 0, 1, 2, 3, 4, 5
        R_CP, R_FT, R_FALL = 32, 96, 0
        bgrow_sb = stats[:, 48:56]

        def _finish():
            sem_names = set(list(P.cnt.keys()) + ["T", "V", "S", "G", "Y"])
            P.barrier("y", [k for k in P.writers if k[0] == "outw"]
                      + [("chan", c) for c in ("sa", "sb", "wa", "wb", "wc",
                                               "la", "lb", "cc")
                         if P.cnt.get(c)])
            sems = {k: E(nc.semaphore("sem_" + k)) for k in sorted(sem_names)}
            P.emit_all(blk, sems)
            return nc, dict(tap_aps=tap_aps, n_ops=len(P.ops))

        P.dma("la", identf_sb[:, :], identf, writes=[("idf",)])
        P.dma("lb", identb_sb[:, :], identb, writes=[("idb",)])
        P.dma("la", masks[:, :].rearrange("p (v n) -> p v n", v=3),
              maskp.rearrange("v p n -> p v n"), writes=[("masks",)])
        P.dma("lb", fsel_sb[:, :], fsel_in, writes=[("fsel",)])
        P.op("v", lambda e: e.memset(rows2[0:1, 0:128], 1.0), writes=[("r2", "ones")])
        for k in range(8):
            P.dma("la", wrg[:, 16 * k:16 * (k + 1)], Wr[128 * k:128 * (k + 1), :],
                  writes=[("wrg", 0)])
            P.dma("lb", wrg[:, 128 + 8 * k:128 + 8 * (k + 1)],
                  Wg1[128 * k:128 * (k + 1), :], writes=[("wrg", 1)])

        def bias_bcast(src_dram, chan="la"):
            P.dma(chan, rows2[0:1, 512:1536], src_dram.unsqueeze(0) if src_dram.ndim == 1 else src_dram,
                  reads=[("r2", "bstage")], writes=[("r2", "bstage")])

            def mm(e):
                for n2 in range(2):
                    ins = e.matmul(ps[:, 2048 + 512 * n2:2048 + 512 * (n2 + 1)],
                                   rows2[0:1, 0:128],
                                   rows2[0:1, 512 + 512 * n2:512 + 512 * (n2 + 1)],
                                   start=True, stop=True)
                return ins
            P.op("t", mm, reads=[("r2", "ones"), ("r2", "bstage")],
                 writes=[("ps", 4), ("ps", 5)])
            P.op("v", lambda e: e.tensor_copy(bb[:, :], ps[:, 2048:3072]),
                 reads=[("ps", 4), ("ps", 5)], writes=[("bb",)])

        def emit_ln(rows, xsrc, dst_off, lg_kind, lg_col, chan, p=0):
            xt = xtile if p == 0 else big[:, 0:1024]
            stl = stile if p == 0 else big[:, 1024:2048]
            nt = ntile if p == 0 else big[:, 2048:3072]
            so = 8 * p
            KX = ("xt",) if p == 0 else ("xt", 1)
            KN = ("nt",) if p == 0 else ("nt", 1)
            KS = ("st",) if p == 0 else ("stl", 1)
            pcol = 1024 * p
            rb = 4 + p
            axo = 1024 * p
            P.dma(chan, xt[0:rows, :], xsrc, writes=[KX])
            P.op("v", lambda e: e.reduce_sum(stats[0:rows, so + ST_NM:so + ST_NM + 1],
                                             xt[0:rows, :], AXX),
                 reads=[KX], writes=[("stats", "nm", p)])
            P.op("s", lambda e: e.activation(stl[0:rows, :], xt[0:rows, :],
                                             AF.Square,
                                             accum_out=stats[0:rows, so + ST_SQ:so + ST_SQ + 1]),
                 reads=[KX],
                 writes=[KS, ("stats", "sq", p)])
            P.op("v", lambda e: e.tensor_scalar(stats[0:rows, so + ST_NM:so + ST_NM + 1],
                                                stats[0:rows, so + ST_NM:so + ST_NM + 1],
                                                -1.0 / D, None, ALU.mult),
                 reads=[("stats", "nm", p)], writes=[("stats", "nm", p)])
            P.op("v", lambda e: e.tensor_tensor(stats[0:rows, so + ST_M1:so + ST_M1 + 1],
                                                stats[0:rows, so + ST_NM:so + ST_NM + 1],
                                                stats[0:rows, so + ST_NM:so + ST_NM + 1],
                                                ALU.mult),
                 reads=[("stats", "nm", p)], writes=[("stats", "m1", p)])
            P.op("v", lambda e: e.tensor_scalar(stats[0:rows, so + ST_IV:so + ST_IV + 1],
                                                stats[0:rows, so + ST_SQ:so + ST_SQ + 1],
                                                1.0 / D, 1e-5, ALU.mult, ALU.add),
                 reads=[("stats", "sq", p)], writes=[("stats", "iv", p)])
            P.op("v", lambda e: e.tensor_tensor(stats[0:rows, so + ST_IV:so + ST_IV + 1],
                                                stats[0:rows, so + ST_IV:so + ST_IV + 1],
                                                stats[0:rows, so + ST_M1:so + ST_M1 + 1],
                                                ALU.subtract),
                 reads=[("stats", "iv", p), ("stats", "m1", p)],
                 writes=[("stats", "iv", p)])
            P.op("s", lambda e: e.activation(stats[0:rows, so + ST_IV:so + ST_IV + 1],
                                             stats[0:rows, so + ST_IV:so + ST_IV + 1],
                                             AF.Sqrt),
                 reads=[("stats", "iv", p)], writes=[("stats", "iv", p)])
            P.op("v", lambda e: e.reciprocal(stats[0:rows, so + ST_IV:so + ST_IV + 1],
                                             stats[0:rows, so + ST_IV:so + ST_IV + 1]),
                 reads=[("stats", "iv", p)], writes=[("stats", "iv", p)])
            P.op("v", lambda e: e.tensor_scalar(nt[0:rows, :], xt[0:rows, :],
                                                stats[0:rows, so + ST_NM:so + ST_NM + 1],
                                                stats[0:rows, so + ST_IV:so + ST_IV + 1],
                                                ALU.add, ALU.mult),
                 reads=[KX, ("stats", "nm", p), ("stats", "iv", p)], writes=[KN])

            def tr(e):
                for k in range(8):
                    col = pcol + 128 * k
                    ins = e.transpose(ps[:, col:col + rows],
                                      nt[0:rows, 128 * k:128 * (k + 1)],
                                      identf_sb[0:rows, 0:rows])
                return ins
            P.op("t", tr, reads=[KN, ("idf",)],
                 writes=[("ps", 2 * p), ("ps", 2 * p + 1)])
            xv = xnT[:, :].rearrange("p (k n) -> p k n", k=8)
            psv8 = ps[:, pcol:pcol + 1024].rearrange("p (k n) -> p k n", k=8)
            P.op("v", lambda e: e.tensor_copy(xv[:, :, dst_off:dst_off + rows],
                                              psv8[:, :, 0:rows]),
                 reads=[("ps", 2 * p), ("ps", 2 * p + 1)], writes=[("xnT",)])

            if lg_kind is not None:
                which, ne = lg_kind
                skip_mm = which.startswith("skip")
                which = which.replace("skip", "")
                wcol = 0 if which == "l0" else 128
                lg_sb = logit0 if which == "l0" else logit1

                P.op("s", lambda e: e.copy(auxf[:, axo:axo + 1024],
                                           ps[:, pcol:pcol + 1024]),
                     reads=[("ps", 2 * p), ("ps", 2 * p + 1)],
                     writes=[("aux", "xf", p)])
                if skip_mm:
                    return

                def rmm(e):
                    for k in range(8):
                        ins = e.matmul(ps[:, 512 * rb:512 * rb + ne],
                                       auxf[:, axo + 128 * k:axo + 128 * (k + 1)],
                                       wrg[:, wcol:wcol + 8 * ne]
                                       .rearrange("p (k j) -> p k j", k=8)[:, k, :],
                                       start=(k == 0), stop=(k == 7))
                    return ins
                P.op("t", rmm, reads=[("aux", "xf", p),
                                      ("wrg", 0 if which == "l0" else 1)],
                     writes=[("ps", rb)])
                P.op("v", lambda e: e.tensor_copy(lg_sb[:, lg_col:lg_col + ne],
                                                  ps[:, 512 * rb:512 * rb + ne]),
                     reads=[("ps", rb)], writes=[(which,)])

        emit_ln(8, x_in[0:8, :], 0, None, 0, "la", p=0)
        for t_ in range(NT):
            lgk = (None if upto == "A0" else
                   (("skipl0", NE0) if upto == "A1" else ("l0", NE0)))
            emit_ln(128, x_in[8 + 128 * t_:8 + 128 * (t_ + 1), :], 8 + 128 * t_,
                    lgk, t_ * NE0, "la" if t_ % 2 == 0 else "lb", p=(t_ + 1) % 2)
        emit_ln(8, x_in[8 + T:16 + T, :], 8 + T, None, 0, "lb", p=1)

        if "xnT" in taps:
            P.dma("sa", tap_aps["xnT"], xnT[:, :].rearrange("p (k n) -> p k n", k=8),
                  reads=[("xnT",)])
        if "logits0" in taps:
            P.dma("sb", tap_aps["logits0"], logit0[:, 0:NT * NE0], reads=[("l0",)])

        if upto in ("A", "A0", "A1"):
            return _finish()
        WQKV, WPROJ = 0, 24576
        wq_v = arw[:, WQKV:WQKV + 24576].rearrange("p (k n) -> p k n", k=8)
        wp_v = arw[:, WPROJ:WPROJ + 8192].rearrange("p (k n) -> p k n", k=8)
        for j, ch in ((0, "wa"), (1, "wb"), (2, "wa")):
            P.dma(ch, wq_v[:, :, 1024 * j:1024 * (j + 1)],
                  Wqkv.rearrange("(k p) n -> p k n", p=128)[:, :, 1024 * j:1024 * (j + 1)],
                  writes=[("arw", "qkv", j)])
        P.dma("wb", wp_v, Wproj.rearrange("(k p) n -> p k n", p=128),
              writes=[("arw", "proj")])
        bias_bcast(bproj, "la")

        AQ, AK, AV_, AT2, AA, AO = 0, 4096, 8320, 13440, 14464, 15616
        qv = aux[:, AQ:AQ + 4096].rearrange("p (k n) -> p k n", k=8)
        kv = aux[:, AK:AK + 4224].rearrange("p (k n) -> p k n", k=8)
        vv = aux[:, AV_:AV_ + 5120].rearrange("p (j n) -> p j n", j=5)
        at2sb = aux[0:16, AT2:AT2 + 1024]
        asb = aux[:, AA:AA + 1152]
        at1sb = aux[:, AA:AA + 1024]
        osb = aux[:, AO:AO + 1024]
        xkv = xnT[:, :].rearrange("p (k n) -> p k n", k=8)
        mk3 = masks[:, :].rearrange("p (v s x) -> p v s x", v=3, s=8)
        AUXB = ([("auxq", m) for m in range(8)] + [("auxk", m) for m in range(8)]
                + [("auxv", j) for j in range(5)]
                + [("auxa", h) for h in range(8)] + [("auxat1",), ("auxat2",),
                   ("auxo", 0), ("auxo", 1)])
        P.rekey([("aux", "xf", 0), ("aux", "xf", 1)], AUXB)

        BLV = {"Bs0": -1, "Bs": 0, "Bsm": 1, "Bat": 2, "Bav": 3}.get(upto, 4)
        for st in range(NST):
            s0c = 512 * st
            for m in range(8):
                bk = m % 2

                def qmm(e, m=m, bk=bk, s0c=s0c):
                    for k in range(8):
                        ins = e.matmul(ps[:, 512 * bk:512 * bk + 512],
                                       wq_v[:, k, 128 * m:128 * (m + 1)],
                                       xkv[:, k, 8 + s0c:8 + s0c + 512],
                                       start=(k == 0), stop=(k == 7))
                    return ins
                P.op("t", qmm, reads=[("arw", "qkv", 0), ("xnT",)], writes=[("ps", bk)])
                P.op("s", lambda e, m=m, bk=bk: e.activation(
                        qv[:, m, :], ps[:, 512 * bk:512 * bk + 512], AF.Copy,
                        scale=ASCALE),
                     reads=[("ps", bk)], writes=[("auxq", m)])
            for m in range(8):
                bk = m % 2

                def kmm(e, m=m, bk=bk, s0c=s0c):
                    for k in range(8):
                        e.matmul(ps[:, 512 * bk:512 * bk + 512],
                                 wq_v[:, k, 1024 + 128 * m:1024 + 128 * (m + 1)],
                                 xkv[:, k, s0c:s0c + 512],
                                 start=(k == 0), stop=(k == 7))
                    for k in range(8):
                        ins = e.matmul(ps[:, 2048 + 16 * bk:2048 + 16 * (bk + 1)],
                                       wq_v[:, k, 1024 + 128 * m:1024 + 128 * (m + 1)],
                                       xkv[:, k, s0c + 512:s0c + 528],
                                       start=(k == 0), stop=(k == 7))
                    return ins
                P.op("t", kmm, reads=[("arw", "qkv", 1), ("xnT",)],
                     writes=[("ps", bk), ("ps", 4)])

                def kev(e, m=m, bk=bk):
                    e.copy(kv[:, m, 0:512], ps[:, 512 * bk:512 * bk + 512])
                    return e.copy(kv[:, m, 512:528],
                                  ps[:, 2048 + 16 * bk:2048 + 16 * (bk + 1)])
                P.op("s", kev, reads=[("ps", bk), ("ps", 4)], writes=[("auxk", m)])
            for j in range(5):
                rj = 128 if (j < 4 or st < NST - 1) else 16

                def vmm(e, j=j, s0c=s0c, rj=rj):
                    for n2 in range(2):
                        for k in range(8):
                            ins = e.matmul(ps[0:rj, 512 * n2:512 * (n2 + 1)],
                                           xkv[:, k, s0c + 128 * j:s0c + 128 * j + rj],
                                           wq_v[:, k, 2048 + 512 * n2:2048 + 512 * (n2 + 1)],
                                           start=(k == 0), stop=(k == 7))
                    return ins
                P.op("t", vmm, reads=[("arw", "qkv", 2), ("xnT",)],
                     writes=[("ps", 0), ("ps", 1)])
                P.op("v", lambda e, j=j, rj=rj: e.tensor_copy(vv[0:rj, j, :],
                                                              ps[0:rj, 0:1024]),
                     reads=[("ps", 0), ("ps", 1)], writes=[("auxv", j)])

            if upto == "Bq":
                continue
            for tl in range(4):
                t_ = 4 * st + tl
                mvar = 1 if t_ == 0 else (2 if t_ == NT - 1 else 0)
                for hg in range(2):
                    def smm(e, tl=tl, hg=hg):
                        for hh in range(8):
                            h = 8 * hg + hh
                            po, chn = 64 * (h % 2), h // 2
                            s_ = (hh % 2) * 4 + hh // 2
                            ins = e.matmul(
                                ps[:, 256 * s_:256 * s_ + 144],
                                qv[po:po + 64, chn, 128 * tl:128 * (tl + 1)],
                                kv[po:po + 64, chn, 128 * tl:128 * tl + 144],
                                start=True, stop=True)
                        return ins
                    P.op("t", smm,
                         reads=[("auxq", m) for m in range(8)]
                               + [("auxk", m) for m in range(8)],
                         writes=[("ps", 0), ("ps", 1), ("ps", 2), ("ps", 3)])
                    if BLV < 0:
                        continue
                    if upto == "Bnm":
                        def mka(e, mvar=mvar):
                            for hh in range(8):
                                ins = e.tensor_tensor(
                                    ps[:, 256 * hh:256 * hh + 144],
                                    ps[:, 256 * hh:256 * hh + 144],
                                    mk3[:, mvar, hh, :], ALU.add)
                            return ins
                        P.op("v", mka,
                             reads=[("ps", 0), ("ps", 1), ("ps", 2), ("ps", 3),
                                    ("masks",)],
                             writes=[("ps", 0), ("ps", 1), ("ps", 2), ("ps", 3)])
                    else:
                        psv = ps[:, 0:2048].rearrange("p (sl x) -> p sl x", sl=8)
                        P.op("v", lambda e, psv=psv, mvar=mvar: e.tensor_tensor(
                                psv[:, :, 0:144], psv[:, :, 0:144],
                                mk3[:, mvar, :, :], ALU.add),
                             reads=[("ps", 0), ("ps", 1), ("ps", 2), ("ps", 3),
                                    ("masks",)],
                             writes=[("ps", 0), ("ps", 1), ("ps", 2), ("ps", 3)])
                    av8 = asb[:, :].rearrange("p (sl x) -> p sl x", sl=8)
                    if BLV < 1:
                        continue
                    for hh in range(8):
                        s_ = (hh % 2) * 4 + hh // 2
                        P.op("v", lambda e, s_=s_, hh=hh: e.reduce_max(
                                stats[:, 32 + hh:33 + hh],
                                ps[:, 256 * s_:256 * s_ + 144], AXX, negate=True),
                             reads=[("ps", s_ // 2)], writes=[("stats", "nx", hh)])
                    for hh in range(8):
                        s_ = (hh % 2) * 4 + hh // 2
                        P.op("s", lambda e, s_=s_, hh=hh, av8=av8: e.activation(
                                av8[:, hh, :], ps[:, 256 * s_:256 * s_ + 144], AF.Exp,
                                bias=stats[:, 32 + hh:33 + hh], scale=1.0,
                                accum_out=stats[:, 40 + hh:41 + hh]),
                             reads=[("ps", s_ // 2), ("stats", "nx", hh)],
                             writes=[("auxa", hh), ("stats", "l", hh)])
                    P.op("v", lambda e: e.reciprocal(stats[:, 40:48], stats[:, 40:48]),
                         reads=[("stats", "l", h) for h in range(8)],
                         writes=[("stats", "linv")])
                    for hh in range(8):
                        P.op("v", lambda e, hh=hh, av8=av8: e.tensor_scalar(
                                av8[:, hh, :], av8[:, hh, :],
                                stats[:, 40 + hh:41 + hh], None, ALU.mult),
                             reads=[("auxa", hh), ("stats", "linv")],
                             writes=[("auxa", hh)])

                    if BLV < 2:
                        continue

                    def atr(e, av8=av8):
                        for hh in range(8):
                            e.transpose(psb[:, 4096 + 128 * hh:4096 + 128 * (hh + 1)],
                                        av8[:, hh, 0:128], identb_sb[:, :])
                            ins = e.transpose(
                                psb[0:16, 5120 + 128 * hh:5120 + 128 * (hh + 1)],
                                av8[:, hh, 128:144], identb_sb[:, :])
                        return ins
                    P.op("t", atr,
                         reads=[("auxa", h) for h in range(8)] + [("idb",)],
                         writes=[("ps", 4), ("ps", 5)])
                    P.op("v", lambda e: e.tensor_copy(at1sb[:, :], psb[:, 4096:5120]),
                         reads=[("ps", 4)] + [("auxa", h) for h in range(8)],
                         writes=[("auxat1",)])
                    P.op("s", lambda e: e.copy(at2sb[:, :], psb[0:16, 5120:6144]),
                         reads=[("ps", 5)], writes=[("auxat2",)])
                    if BLV < 3:
                        continue

                    def avmm(e, tl=tl, hg=hg):
                        for hh in range(8):
                            h = 8 * hg + hh
                            po = 64 * (hh % 2)
                            c2 = hh // 2
                            e.matmul(ps[po:po + 64,
                                        3072 + 128 * c2:3072 + 128 * (c2 + 1)],
                                     vv[:, tl, 64 * h:64 * (h + 1)],
                                     at1sb[:, 128 * hh:128 * (hh + 1)],
                                     start=(hh <= 1), stop=False,
                                     skip_group_check=(hh > 1))
                            ins = e.matmul(
                                ps[po:po + 64, 3072 + 128 * c2:3072 + 128 * (c2 + 1)],
                                vv[0:16, tl + 1, 64 * h:64 * (h + 1)],
                                at2sb[:, 128 * hh:128 * (hh + 1)],
                                start=False, stop=(hh >= 6),
                                skip_group_check=(hh < 6))
                        return ins
                    P.op("t", avmm,
                         reads=[("auxat1",), ("auxat2",), ("auxv", tl),
                                ("auxv", tl + 1)],
                         writes=[("ps", 6)])
                    P.op("v", lambda e, hg=hg: e.tensor_copy(
                            osb[:, 512 * hg:512 * (hg + 1)], ps[:, 3072:3584]),
                         reads=[("ps", 6)],
                         writes=[("auxo", hg)])
                if BLV < 4:
                    continue
                ov = osb.rearrange("p (c n) -> p c n", c=8)
                for n2 in range(2):
                    def pmm(e, n2=n2):
                        for k in range(8):
                            ins = e.matmul(ps[:, 512 * n2:512 * (n2 + 1)],
                                           ov[:, k, :],
                                           wp_v[:, k, 512 * n2:512 * (n2 + 1)],
                                           start=(k == 0), stop=(k == 7))
                        return ins
                    P.op("t", pmm, reads=[("auxo", 0), ("auxo", 1), ("arw", "proj")],
                         writes=[("ps", n2)])
                P.op("v", lambda e: e.tensor_tensor(stile[:, :], ps[:, 0:1024],
                                                    bb[:, :], ALU.add),
                     reads=[("ps", 0), ("ps", 1), ("bb",)], writes=[("st",)])
                P.dma("sa" if t_ % 2 == 0 else "sb",
                      s_scr.ap()[128 * t_:128 * (t_ + 1), :], stile[:, :],
                      reads=[("st",)], writes=[("scr_s", t_)])

        if "s0" in taps:
            P.dma("sa", tap_aps["s0"], s_scr.ap(),
                  reads=[("scr_s", t_) for t_ in range(NT)])

        if upto in ("B", "Bq", "Bnm", "Bs", "Bs0", "Bsm", "Bat", "Bav"):
            return _finish()
        def emit_cw(which, ne, tile, normalize):
            lg_sb = logit0 if which == "l0" else logit1
            cwt = cwtm0 if which == "l0" else cwtm1
            lg = lg_sb[:, ne * tile:ne * (tile + 1)]
            cwtmp = smallv[:, SV_CW:SV_CW + ne]
            lkey = (which,)
            if which == "l1":
                P.op("v", lambda e, lg=lg: e.tensor_tensor(
                        smallv[:, SV_CW + 32:SV_CW + 32 + ne], lg,
                        bgrow_sb[:, 0:ne], ALU.add),
                     reads=[lkey, ("bgB",)], writes=[("svl1",)])
                lg = smallv[:, SV_CW + 32:SV_CW + 32 + ne]
                lkey = ("svl1",)
            P.op("v", lambda e, lg=lg: e.reduce_max(stats[:, ST_NM:ST_NM + 1], lg,
                                                    AXX, negate=True),
                 reads=[lkey], writes=[("stats", "nm")])
            P.op("s", lambda e, lg=lg: e.activation(cwtmp, lg, AF.Exp,
                                                    bias=stats[:, ST_NM:ST_NM + 1],
                                                    scale=1.0,
                                                    accum_out=stats[:, ST_SQ:ST_SQ + 1]),
                 reads=[lkey, ("stats", "nm")], writes=[("svcw",), ("stats", "sq")])
            P.op("v", lambda e: e.reciprocal(stats[:, ST_IV:ST_IV + 1],
                                             stats[:, ST_SQ:ST_SQ + 1]),
                 reads=[("stats", "sq")], writes=[("stats", "iv")])
            P.op("v", lambda e: e.tensor_scalar(cwtmp, cwtmp,
                                                stats[:, ST_IV:ST_IV + 1], None,
                                                ALU.mult),
                 reads=[("svcw",), ("stats", "iv")], writes=[("svcw",)])
            P.op("v", lambda e: e.reduce_max(stats[:, ST_M1:ST_M1 + 1], cwtmp, AXX),
                 reads=[("svcw",)], writes=[("stats", "m1")])
            P.op("v", lambda e: e.tensor_scalar(ntile[:, 0:ne], cwtmp,
                                                stats[:, ST_M1:ST_M1 + 1], None,
                                                ALU.is_ge),
                 reads=[("svcw",), ("stats", "m1")], writes=[("nt",)])
            P.op("v", lambda e: e.scalar_tensor_tensor(ntile[:, 0:ne], ntile[:, 0:ne],
                                                       -4.0, cwtmp, ALU.mult, ALU.add),
                 reads=[("nt",), ("svcw",)], writes=[("nt",)])
            P.op("v", lambda e: e.reduce_max(stats[:, ST_M2:ST_M2 + 1], ntile[:, 0:ne],
                                             AXX),
                 reads=[("nt",)], writes=[("stats", "m2")])
            P.op("v", lambda e: e.tensor_scalar(ntile[:, 0:ne], cwtmp,
                                                stats[:, ST_M2:ST_M2 + 1], None,
                                                ALU.is_ge),
                 reads=[("svcw",), ("stats", "m2")], writes=[("nt",)])
            P.op("v", lambda e: e.tensor_tensor(cwtmp, cwtmp, ntile[:, 0:ne], ALU.mult),
                 reads=[("svcw",), ("nt",)], writes=[("svcw",)])
            if normalize:
                P.op("v", lambda e: e.tensor_tensor(stats[:, ST_DEN:ST_DEN + 1],
                                                    stats[:, ST_M1:ST_M1 + 1],
                                                    stats[:, ST_M2:ST_M2 + 1], ALU.add),
                     reads=[("stats", "m1"), ("stats", "m2")],
                     writes=[("stats", "den")])
                P.op("v", lambda e: e.tensor_scalar(stats[:, ST_DEN:ST_DEN + 1],
                                                    stats[:, ST_DEN:ST_DEN + 1],
                                                    1e-8, None, ALU.add),
                     reads=[("stats", "den")], writes=[("stats", "den")])
                P.op("v", lambda e: e.reciprocal(stats[:, ST_DEN:ST_DEN + 1],
                                                 stats[:, ST_DEN:ST_DEN + 1]),
                     reads=[("stats", "den")], writes=[("stats", "den")])
                P.op("v", lambda e: e.tensor_scalar(cwtmp, cwtmp,
                                                    stats[:, ST_DEN:ST_DEN + 1], None,
                                                    ALU.mult),
                     reads=[("svcw",), ("stats", "den")], writes=[("svcw",)])
            P.op("v", lambda e, tile=tile: e.tensor_copy(
                    cwt[:, ne * tile:ne * (tile + 1)], cwtmp),
                 reads=[("svcw",)], writes=[("cw" + which, tile)])

        for t_ in range(NT):
            emit_cw("l0", NE0, t_, False)
        if "cw0" in taps:
            P.dma("sb", tap_aps["cw0"], cwtm0[:, :],
                  reads=[("cwl0", t_) for t_ in range(NT)])

        if upto == "cw0":
            return _finish()
        HA, HB = 0, 4096
        SLOT = [0, 12288]
        P.rekey(AUXB, [("aux", "hA"), ("aux", "hB")])
        P.rekey([("arw", "qkv", 0), ("arw", "qkv", 1), ("arw", "qkv", 2),
                 ("arw", "proj")],
                [("arw", "ug", 0), ("arw", "uu", 0), ("arw", "ud", 0),
                 ("arw", "ug", 1), ("arw", "uu", 1), ("arw", "ud", 1)])

        def c_unit_dma(u):
            slot = SLOT[u % 2]
            ch = "wa" if u % 2 == 0 else "wb"
            gv = arw[:, slot:slot + 4096].rearrange("p (k n) -> p k n", k=8)
            uv = arw[:, slot + 4096:slot + 8192].rearrange("p (k n) -> p k n", k=8)
            dv = arw[:, slot + 8192:slot + 12288].rearrange("p (k n) -> p k n", k=4)
            if u < NE0:
                P.dma(ch, gv, Wg0[u].rearrange("(k p) n -> p k n", p=128),
                      writes=[("arw", "ug", u % 2)])
                P.dma(ch, uv, Wu0[u].rearrange("(k p) n -> p k n", p=128),
                      writes=[("arw", "uu", u % 2)])
                P.dma(ch, dv, Wd0[u].rearrange("(k p) n -> p k n", p=128),
                      writes=[("arw", "ud", u % 2)])
            else:
                f = u - NE0
                P.dma(ch, gv,
                      Sg.rearrange("(k p) n -> p k n", p=128)[:, :, 512 * f:512 * (f + 1)],
                      writes=[("arw", "ug", u % 2)])
                P.dma(ch, uv,
                      Su.rearrange("(k p) n -> p k n", p=128)[:, :, 512 * f:512 * (f + 1)],
                      writes=[("arw", "uu", u % 2)])
                P.dma(ch, dv,
                      Sd.rearrange("(k p) n -> p k n", p=128)[:, 4 * f:4 * (f + 1), :],
                      writes=[("arw", "ud", u % 2)])

        def c_unit_compute(u, th):
            slot = SLOT[u % 2]
            hoff = HA if u % 2 == 0 else HB
            hkey = ("aux", "hA" if u % 2 == 0 else "hB")
            gv = arw[:, slot:slot + 4096].rearrange("p (k n) -> p k n", k=8)
            uv = arw[:, slot + 4096:slot + 8192].rearrange("p (k n) -> p k n", k=8)
            dv = arw[:, slot + 8192:slot + 12288].rearrange("p (k n) -> p k n", k=4)
            hv = aux[:, hoff:hoff + 4 * TH].rearrange("p (m n) -> p m n", m=4)
            for m in range(4):
                for n2 in range(NW5):
                    def gmm(e, m=m, n2=n2, th=th, gv=gv):
                        for k in range(8):
                            ins = e.matmul(ps[:, 0:W5],
                                           gv[:, k, 128 * m:128 * (m + 1)],
                                           xkv[:, k, 8 + TH * th + W5 * n2:
                                               8 + TH * th + W5 * (n2 + 1)],
                                           start=(k == 0), stop=(k == 7))
                        return ins
                    P.op("t", gmm, reads=[("arw", "ug", u % 2), ("xnT",)],
                         writes=[("ps", 0)])
                    P.op("s", lambda e: e.activation(tgt[:, 0:W5], ps[:, 0:W5],
                                                     AF.Silu),
                         reads=[("ps", 0)], writes=[("tg",)])

                    def umm(e, m=m, n2=n2, th=th, uv=uv):
                        for k in range(8):
                            ins = e.matmul(ps[:, 512:512 + W5],
                                           uv[:, k, 128 * m:128 * (m + 1)],
                                           xkv[:, k, 8 + TH * th + W5 * n2:
                                               8 + TH * th + W5 * (n2 + 1)],
                                           start=(k == 0), stop=(k == 7))
                        return ins
                    P.op("t", umm, reads=[("arw", "uu", u % 2), ("xnT",)],
                         writes=[("ps", 1)])
                    hslice = hv[:, m, W5 * n2:W5 * (n2 + 1)]
                    P.op("v", lambda e, hslice=hslice: e.tensor_tensor(
                            hslice, tgt[:, 0:W5], ps[:, 512:512 + W5], ALU.mult),
                         reads=[("tg",), ("ps", 1)], writes=[hkey])
            for sl in range(NSL):
                for n in range(2):
                    bk = 2 + (sl * 2 + n) % 2

                    def dmm(e, sl=sl, n=n, bk=bk, hv=hv, dv=dv):
                        for k in range(4):
                            ins = e.matmul(ps[:, 512 * bk:512 * (bk + 1)],
                                           hv[:, k, 128 * sl:128 * (sl + 1)],
                                           dv[:, k, 512 * n:512 * (n + 1)],
                                           start=(k == 0), stop=(k == 3))
                        return ins
                    P.op("t", dmm, reads=[hkey, ("arw", "ud", u % 2)],
                         writes=[("ps", bk)])
                    acs = big[:, 1024 * sl + 512 * n:1024 * sl + 512 * (n + 1)]
                    t_ = NSL * th + sl
                    if u < NE0:
                        cwc = cwtm0[:, NE0 * t_ + u:NE0 * t_ + u + 1]
                        P.op("v", lambda e, acs=acs, bk=bk, cwc=cwc:
                                e.scalar_tensor_tensor(
                                    acs, ps[:, 512 * bk:512 * (bk + 1)], cwc, acs,
                                    ALU.mult, ALU.add),
                             reads=[("ps", bk), ("acc", sl, n), ("cwl0", t_)],
                             writes=[("acc", sl, n)])
                    else:
                        P.op("v", lambda e, acs=acs, bk=bk: e.tensor_tensor(
                                acs, acs, ps[:, 512 * bk:512 * (bk + 1)], ALU.add),
                             reads=[("ps", bk), ("acc", sl, n)],
                             writes=[("acc", sl, n)])

        P.rekey([("xt", 1), ("stl", 1), ("nt", 1)],
                [("acc", sl, n) for sl in range(NSL) for n in range(2)])
        for th in range(2):
            for sl in range(NSL):
                for n in range(2):
                    P.op("v", lambda e, sl=sl, n=n: e.memset(
                            big[:, 1024 * sl + 512 * n:1024 * sl + 512 * (n + 1)], 0.0),
                         writes=[("acc", sl, n)])
            c_unit_dma(0)
            c_unit_dma(1)
            for u in range(18):
                c_unit_compute(u, th)
                if u + 2 < 18:
                    c_unit_dma(u + 2)
            for tt in range(NSL):
                t_ = NSL * th + tt
                P.op("s", lambda e, tt=tt: e.activation(
                        ntile[:, :], big[:, 1024 * tt:1024 * (tt + 1)], AF.Silu),
                     reads=[("acc", tt, 0), ("acc", tt, 1)], writes=[("nt",)])
                P.dma("la", xtile[:, :], x_in[8 + 128 * t_:8 + 128 * (t_ + 1), :],
                      writes=[("xt",)])
                P.dma("lb", stile[:, :], s_scr.ap()[128 * t_:128 * (t_ + 1), :],
                      reads=[("scr_s", t_)], writes=[("st",)])
                P.op("v", lambda e: e.tensor_tensor(ntile[:, :], stile[:, :],
                                                    ntile[:, :], ALU.mult),
                     reads=[("st",), ("nt",)], writes=[("nt",)])
                P.op("v", lambda e: e.tensor_tensor(xtile[:, :], xtile[:, :],
                                                    ntile[:, :], ALU.add),
                     reads=[("nt",), ("xt",)], writes=[("xt",)])
                P.dma("sa" if t_ % 2 == 0 else "sb",
                      x1_scr.ap()[128 * t_:128 * (t_ + 1), :], xtile[:, :],
                      reads=[("xt",)], writes=[("scr_x1", t_)])
        if "acc0" in taps:
            P.dma("sb", tap_aps["acc0"], big[:, 0:NSL * 1024],
                  reads=[("acc", sl, n) for sl in range(NSL) for n in range(2)])
        if "x1" in taps:
            P.dma("sa", tap_aps["x1"], x1_scr.ap(),
                  reads=[("scr_x1", t_) for t_ in range(NT)])

        if upto == "C":
            return _finish()
        P.rekey([("aux", "hA"), ("aux", "hB")],
                [("aux", "xf", 0), ("aux", "xf", 1)])
        P.rekey([("acc", sl, n) for sl in range(NSL) for n in range(2)],
                [("xt", 1), ("stl", 1), ("nt", 1)])
        for t_ in range(NT):
            emit_ln(128, x1_scr.ap()[128 * t_:128 * (t_ + 1), :], 8 + 128 * t_,
                    ("l1", NE1), t_ * NE1, "la" if t_ % 2 == 0 else "lb",
                    p=t_ % 2)
        if "logits1" in taps:
            P.dma("sb", tap_aps["logits1"], logit1[:, 0:NT * NE1], reads=[("l1",)])

        P.dma("la", rows2[0:1, 128:128 + NE1], bg1.unsqueeze(0), writes=[("r2", "bg")])
        P.op("t", lambda e: e.matmul(ps[:, 1024:1024 + NE1],
                                     rows2[0:1, 0:128],
                                     rows2[0:1, 128:128 + NE1],
                                     start=True, stop=True),
             reads=[("r2", "ones"), ("r2", "bg")], writes=[("ps", 2)])
        P.op("v", lambda e: e.tensor_copy(bgrow_sb, ps[:, 1024:1024 + NE1]),
             reads=[("ps", 2)], writes=[("bgB",)])

        for t_ in range(NT):
            emit_cw("l1", NE1, t_, True)
        if "cw1" in taps:
            P.dma("sb", tap_aps["cw1"], cwtm1[:, :],
                  reads=[("cwl1", t_) for t_ in range(NT)])

        if upto == "ln1":
            return _finish()
        P.dma("la", cwrows[R_CP:R_CP + 16, 0:DI], Cp, writes=[("cp",)])
        P.dma("lb", rows2[32:48, 0:128], bin_.rearrange("(c p) -> c p", p=128),
              reads=[("r2", "stage")], writes=[("r2", "stage")])
        P.op("t", lambda e: e.transpose(ps[:, 3584:3584 + 16],
                                        rows2[32:48, 0:128],
                                        identf_sb[32:48, 32:48]),
             reads=[("r2", "stage"), ("idf",)], writes=[("ps", 7)])
        P.op("v", lambda e: e.tensor_copy(smallv[:, SV_BIN:SV_BIN + 16],
                                          ps[:, 3584:3584 + 16]),
             reads=[("ps", 7)], writes=[("sv", "bin")])
        P.dma("lb", rows2[32:48, 0:128], Dp.rearrange("(c p) -> c p", p=128),
              reads=[("r2", "stage")], writes=[("r2", "stage")])
        P.op("t", lambda e: e.transpose(ps[:, 3584:3584 + 16],
                                        rows2[32:48, 0:128],
                                        identf_sb[32:48, 32:48]),
             reads=[("r2", "stage"), ("idf",)], writes=[("ps", 7)])
        P.op("v", lambda e: e.tensor_copy(smallv[:, SV_DP:SV_DP + 16],
                                          ps[:, 3584:3584 + 16]),
             reads=[("ps", 7)], writes=[("sv", "dp")])
        for c in range(16):
            P.dma("la" if c % 2 == 0 else "lb", xtile[:, 0:16],
                  Bp[128 * c:128 * (c + 1), :], reads=[("xt",)], writes=[("xt",)])
            P.op("t", lambda e, c=c: e.transpose(ps[:, 3584:3584 + 16],
                                                 cwrows[R_CP:R_CP + 16,
                                                        128 * c:128 * (c + 1)],
                                                 identf_sb[32:48, 32:48]),
                 reads=[("cp",), ("idf",)], writes=[("ps", 7)])
            P.op("v", lambda e: e.tensor_tensor(ntile[:, 0:16], xtile[:, 0:16],
                                                ps[:, 3584:3584 + 16], ALU.mult),
                 reads=[("xt",), ("ps", 7)], writes=[("nt",)])
            P.op("v", lambda e, c=c: e.reduce_sum(smallv[:, SV_R + c:SV_R + c + 1],
                                                  ntile[:, 0:16], AXX),
                 reads=[("nt",)], writes=[("sv", "r", c)])
        if "rvec" in taps:
            P.dma("sb", tap_aps["rvec"], smallv[:, SV_R:SV_R + 16],
                  reads=[("sv", "r", c) for c in range(16)])

        bias_bcast(bout, "la")

        P.rekey([("acc", sl, n) for sl in range(NSL) for n in range(2)]
                + [("xt", 1), ("stl", 1), ("nt", 1)],
                [("bd", "xi"), ("bd", "c"), ("bd", "y"), ("bd", "dec")])
        xi_sb = big[:, 0:T]
        csc_sb = big[:, 2048:2048 + T]
        y_sb = big[:, 4096:4096 + T]
        dec_sb = big[:, 6144:6144 + T]
        P.op("v", lambda e: e.memset(dec_sb, DECAY), writes=[("bd", "dec")])

        P.rekey([("arw", "ug", 0), ("arw", "ug", 1),
                 ("arw", "uu", 0), ("arw", "uu", 1), ("arw", "ud", 0),
                 ("arw", "ud", 1)],
                [("arw", "win"), ("arw", "wout")])
        win_v = arw[:, 0:16384].rearrange("p (k n) -> p k n", k=8)
        wout_v = arw[:, 16384:32768].rearrange("p (k n) -> p k n", k=16)
        P.dma("wa", win_v[:, :, 0:1024],
              Win.rearrange("(k p) n -> p k n", p=128)[:, :, 0:1024],
              writes=[("arw", "win")])
        P.dma("wb", win_v[:, :, 1024:2048],
              Win.rearrange("(k p) n -> p k n", p=128)[:, :, 1024:2048],
              writes=[("arw", "win")])
        for hw_ in range(2):
            P.dma("wa", wout_v[:, 8 * hw_:8 * (hw_ + 1), :],
                  Wout.rearrange("(k p) n -> p k n", p=128)[:, 8 * hw_:8 * (hw_ + 1), :],
                  writes=[("arw", "wout")])

        NXI = T // 512
        for c in range(16):
            po_ = 2048 * (c % 2)
            bnks = [4 * (c % 2) + j for j in range(NXI)]

            def xmm(e, c=c, po_=po_):
                for n2 in range(NXI):
                    for k in range(8):
                        ins = e.matmul(ps[:, po_ + 512 * n2:po_ + 512 * (n2 + 1)],
                                       win_v[:, k, 128 * c:128 * (c + 1)],
                                       xkv[:, k, 8 + 512 * n2:8 + 512 * (n2 + 1)],
                                       start=(k == 0), stop=(k == 7))
                return ins
            P.op("t", xmm, reads=[("arw", "win"), ("xnT",)],
                 writes=[("ps", bnk) for bnk in bnks])
            P.op("s", lambda e, c=c, po_=po_: e.activation(
                    xi_sb, ps[:, po_:po_ + T], AF.Silu,
                    bias=smallv[:, SV_BIN + c:SV_BIN + c + 1], scale=1.0),
                 reads=[("ps", bnk) for bnk in bnks] + [("sv", "bin")],
                 writes=[("bd", "xi")])
            if "xi" in taps:
                P.dma("sb", tap_aps["xi"][c], xi_sb, reads=[("bd", "xi")])
            P.op("v", lambda e: e.tensor_tensor_scan(csc_sb, dec_sb, xi_sb, 0.0,
                                                     ALU.mult, ALU.add),
                 reads=[("bd", "dec"), ("bd", "xi")], writes=[("bd", "c")])
            P.op("v", lambda e, c=c: e.tensor_copy(smallv[:, SV_FR + c:SV_FR + c + 1],
                                                   csc_sb[:, T - 1:T]),
                 reads=[("bd", "c")], writes=[("sv", "f", c)])
            P.op("v", lambda e, c=c: e.tensor_scalar(y_sb, xi_sb,
                                                     smallv[:, SV_DP + c:
                                                            SV_DP + c + 1],
                                                     None, ALU.mult),
                 reads=[("bd", "xi"), ("sv", "dp")], writes=[("bd", "y")])
            P.op("v", lambda e, c=c: e.scalar_tensor_tensor(
                    y_sb, csc_sb, smallv[:, SV_R + c:SV_R + c + 1], y_sb,
                    ALU.mult, ALU.add),
                 reads=[("bd", "c"), ("sv", "r", c), ("bd", "y")],
                 writes=[("bd", "y")])
            P.dma("sa" if c % 2 == 0 else "sb", yp_scr.ap()[c], y_sb,
                  reads=[("bd", "y")], writes=[("scr_yp", c)])
        if "ypart" in taps:
            P.dma("sa", tap_aps["ypart"], yp_scr.ap(),
                  reads=[("scr_yp", c) for c in range(16)])

        P.op("t", lambda e: e.transpose(ps[0:16, 1024:1024 + 128],
                                        smallv[:, SV_FR:SV_FR + 16], identf_sb[:, :]),
             reads=[("sv", "f", c) for c in range(16)] + [("idf",)],
             writes=[("ps", 2)])
        P.op("v", lambda e: e.tensor_copy(cwrows[R_FT:R_FT + 16, 0:128],
                                          ps[0:16, 1024:1024 + 128]),
             reads=[("ps", 2)], writes=[("fT",)])
        P.dma("cc", cc_in.ap().rearrange("a (c p) -> (a c) p", p=128),
              cwrows[R_FT:R_FT + 16, 0:128], reads=[("fT",)], writes=[("ccin",)])
        P.op("g", lambda e: e.collective_compute(
                "AllGather", ALU.bypass, replica_groups=[list(range(world))],
                ins=[cc_in.ap()], outs=[cc_out.ap()]),
             reads=[("ccin",)], writes=[("ccout",)], sem="ccc")
        P.dma("cc", cwrows[R_FALL:R_FALL + world, 0:DI], cc_out.ap(),
              reads=[("ccout",)], writes=[("Fall",)])

        def fsel_mm(e):
            for n2 in range(DI // 512):
                ins = e.matmul(ps[0:1, 512 * n2:512 * (n2 + 1)], fsel_sb[0:world, :],
                               cwrows[R_FALL:R_FALL + world, 512 * n2:512 * (n2 + 1)],
                               start=True, stop=True)
            return ins
        P.op("t", fsel_mm, reads=[("Fall",), ("fsel",)],
             writes=[("ps", 0), ("ps", 1), ("ps", 2), ("ps", 3)])
        frow = auxf[0:1, 3584:3584 + DI]
        P.rekey([("aux", "xf")], [("aux", "dpow"), ("aux", "ybf", 0),
                                  ("aux", "ybf", 1), ("aux", "ybL", 0),
                                  ("aux", "ybL", 1), ("aux", "frow")])
        P.op("v", lambda e: e.tensor_copy(frow, ps[0:1, 0:DI]),
             reads=[("ps", 0), ("ps", 1), ("ps", 2), ("ps", 3)],
             writes=[("aux", "frow")])

        P.dma("sb", fr_scr.ap(), frow, reads=[("aux", "frow")], writes=[("frs",)])
        P.dma("lb", rows2[32:48, 0:128],
              fr_scr.ap().rearrange("a (c p) -> (a c) p", p=128),
              reads=[("frs",), ("r2", "stage")], writes=[("r2", "stage")])
        P.op("t", lambda e: e.transpose(ps[:, 3584:3584 + 16],
                                        rows2[32:48, 0:128],
                                        identf_sb[32:48, 32:48]),
             reads=[("r2", "stage"), ("idf",)], writes=[("ps", 7)])
        P.op("v", lambda e: e.tensor_copy(smallv[:, SV_FP:SV_FP + 16],
                                          ps[:, 3584:3584 + 16]),
             reads=[("ps", 7)], writes=[("sv", "fp")])
        if "fprev" in taps:
            P.dma("sb", tap_aps["fprev"], smallv[:, SV_FP:SV_FP + 16],
                  reads=[("sv", "fp")])
        P.op("v", lambda e: e.tensor_tensor(smallv[:, SV_FP:SV_FP + 16],
                                            smallv[:, SV_FP:SV_FP + 16],
                                            smallv[:, SV_R:SV_R + 16], ALU.mult),
             reads=[("sv", "fp")] + [("sv", "r", c) for c in range(16)],
             writes=[("sv", "fp")])

        dpow_sb = auxf[:, 512:512 + T]
        ybfs = [aux[:, 0:1024], aux[:, 12288:13312]]
        ybLs = [auxf[:, 512 + T:512 + T + TH], auxf[:, 6656:6656 + TH]]
        P.dma("la", dpow_sb, dpow_in, writes=[("aux", "dpow")])

        def yb_load(th, c):
            p = c % 2
            P.dma("la" if c % 2 == 0 else "lb", ybLs[p],
                  yp_scr.ap()[c][:, TH * th:TH * (th + 1)],
                  reads=[("scr_yp", c), ("aux", "ybL", p)],
                  writes=[("aux", "ybL", p)])
            P.op("v", lambda e, c=c, th=th, p=p: e.scalar_tensor_tensor(
                    ybfs[p][:, 0:TH], dpow_sb[:, TH * th:TH * (th + 1)],
                    smallv[:, SV_FP + c:SV_FP + c + 1], ybLs[p],
                    ALU.mult, ALU.add),
                 reads=[("aux", "dpow"), ("sv", "fp"), ("aux", "ybL", p),
                        ("aux", "ybf", p)],
                 writes=[("aux", "ybf", p)])

        P.rekey([("bd", "xi"), ("bd", "c"), ("bd", "y"), ("bd", "dec")],
                [("s1acc", g) for g in range(NSL)])
        for th in range(2):
            for sl in range(NSL):
                gsl = th * NSL + sl
                P.op("v", lambda e, sl=sl: e.tensor_copy(
                        big[:, 1024 * sl:1024 * (sl + 1)], bb[:, :]),
                     reads=[("bb",)], writes=[("s1acc", sl)])
            yb_load(th, 0)
            for c in range(16):
                p = c % 2
                for sl in range(NSL):
                    for n in range(2):
                        bk = 4 + (sl * 2 + n) % 4
                        gsl = th * NSL + sl

                        def wmm(e, c=c, sl=sl, n=n, bk=bk, p=p):
                            return e.matmul(ps[:, 512 * bk:512 * (bk + 1)],
                                            ybfs[p][:, 128 * sl:128 * (sl + 1)],
                                            wout_v[:, c, 512 * n:512 * (n + 1)],
                                            start=True, stop=True)
                        P.op("t", wmm, reads=[("aux", "ybf", p), ("arw", "wout")],
                             writes=[("ps", bk)])
                        acs = big[:, 1024 * sl + 512 * n:1024 * sl + 512 * (n + 1)]
                        P.op("v", lambda e, acs=acs, bk=bk: e.tensor_tensor(
                                acs, acs, ps[:, 512 * bk:512 * (bk + 1)], ALU.add),
                             reads=[("ps", bk), ("s1acc", sl)],
                             writes=[("s1acc", sl)])
                        if sl == 0 and n == 0 and c + 1 < 16:
                            yb_load(th, c + 1)
            for sl in range(NSL):
                t_ = th * NSL + sl
                P.dma("sa" if t_ % 2 == 0 else "sb",
                      s_scr.ap()[128 * t_:128 * (t_ + 1), :],
                      big[:, 1024 * sl:1024 * (sl + 1)],
                      reads=[("s1acc", sl)], writes=[("scr_s", t_)])
        if "s1" in taps:
            P.dma("sa", tap_aps["s1"], s_scr.ap(),
                  reads=[("scr_s", t_) for t_ in range(NT)])

        if upto == "D":
            return _finish()
        P.rekey([("arw", "win"), ("arw", "wout")],
                [("arw", "w1", 0), ("arw", "w1", 1), ("arw", "w1", 2),
                 ("arw", "w1", 3), ("arw", "w2")])
        P.rekey([("aux", "dpow"), ("aux", "ybf", 0), ("aux", "ybf", 1),
                 ("aux", "ybL", 0), ("aux", "ybL", 1), ("aux", "frow")],
                [("hE", k) for k in range(16)])
        P.rekey([("s1acc", g) for g in range(NSL)],
                [("eacc", sl, n) for sl in range(NSL) for n in range(2)])
        w1s = [arw[:, 4096 * j:4096 * (j + 1)].rearrange("p (k n) -> p k n", k=8)
               for j in range(4)]
        w2v = arw[:, 16384:32768].rearrange("p (k n) -> p k n", k=16)
        hEv = aux[:, 0:16 * TH].rearrange("p (k n) -> p k n", k=16)

        for th in range(2):
            for sl in range(NSL):
                for n in range(2):
                    P.op("v", lambda e, sl=sl, n=n: e.memset(
                            big[:, 1024 * sl + 512 * n:1024 * sl + 512 * (n + 1)], 0.0),
                         writes=[("eacc", sl, n)])
            for eix in range(NE1):
                P.dma("la", rows2[32:64, 0:128],
                      b1[eix].rearrange("(c p) -> c p", p=128),
                      reads=[("r2", "stage")], writes=[("r2", "stage")])
                P.op("t", lambda e: e.transpose(ps[:, 3584:3584 + 32],
                                                rows2[32:64, 0:128],
                                                identf_sb[32:64, 32:64]),
                     reads=[("r2", "stage"), ("idf",)], writes=[("ps", 7)])
                P.op("v", lambda e: e.tensor_copy(smallv[:, SV_B1:SV_B1 + 32],
                                                  ps[:, 3584:3584 + 32]),
                     reads=[("ps", 7)], writes=[("sv", "b1")])
                bias_bcast(b2[eix], "lb")
                for hk in range(2):
                    for cc4 in range(4):
                        P.dma("wa", w1s[cc4],
                              W1[eix].rearrange("(k p) n -> p k n", p=128)
                              [:, :, 2048 * hk + 512 * cc4:2048 * hk + 512 * (cc4 + 1)],
                              writes=[("arw", "w1", cc4)])
                    for hw_ in range(2):
                        P.dma("wa", w2v[:, 8 * hw_:8 * (hw_ + 1), :],
                              W2[eix].rearrange("(k p) n -> p k n", p=128)
                              [:, 16 * hk + 8 * hw_:16 * hk + 8 * (hw_ + 1), :],
                              writes=[("arw", "w2")])
                    for cc4 in range(4):
                        cslot = cc4
                        for m in range(4):
                            mg = 4 * cc4 + m
                            for n2 in range(NW5):
                                bk = (0, 1, 6)[(2 * m + n2) % 3]

                                def hmm(e, m=m, n2=n2, bk=bk, cslot=cslot, th=th):
                                    for k in range(8):
                                        ins = e.matmul(
                                            ps[:, 512 * bk:512 * bk + W5],
                                            w1s[cslot][:, k, 128 * m:128 * (m + 1)],
                                            xkv[:, k, 8 + TH * th + W5 * n2:
                                                8 + TH * th + W5 * (n2 + 1)],
                                            start=(k == 0), stop=(k == 7))
                                    return ins
                                P.op("t", hmm, reads=[("arw", "w1", cslot), ("xnT",)],
                                     writes=[("ps", bk)])
                                hsl = hEv[:, mg, W5 * n2:W5 * (n2 + 1)]
                                bcol = SV_B1 + 16 * hk + mg
                                P.op("s", lambda e, hsl=hsl, bk=bk, bcol=bcol:
                                        e.activation(
                                        hsl, ps[:, 512 * bk:512 * bk + W5],
                                        AF.Gelu_apprx_tanh,
                                        bias=smallv[:, bcol:bcol + 1], scale=1.0),
                                     reads=[("ps", bk), ("sv", "b1")],
                                     writes=[("hE", mg)])
                    for sl in range(NSL):
                        for n in range(2):
                            bk = 2 + (sl * 2 + n) % 2

                            def emm(e, sl=sl, n=n, bk=bk):
                                for k in range(16):
                                    ins = e.matmul(ps[:, 512 * bk:512 * (bk + 1)],
                                                   hEv[:, k, 128 * sl:128 * (sl + 1)],
                                                   w2v[:, k, 512 * n:512 * (n + 1)],
                                                   start=(k == 0), stop=(k == 15))
                                return ins
                            P.op("t", emm,
                                 reads=[("hE", k) for k in range(16)] + [("arw", "w2")],
                                 writes=[("ps", bk)])
                            t_ = NSL * th + sl
                            acs = big[:, 1024 * sl + 512 * n:1024 * sl + 512 * (n + 1)]
                            cwc = cwtm1[:, NE1 * t_ + eix:NE1 * t_ + eix + 1]
                            P.op("v", lambda e, acs=acs, bk=bk, cwc=cwc:
                                    e.scalar_tensor_tensor(
                                        acs, ps[:, 512 * bk:512 * (bk + 1)], cwc, acs,
                                        ALU.mult, ALU.add),
                                 reads=[("ps", bk), ("eacc", sl, n), ("cwl1", t_)],
                                 writes=[("eacc", sl, n)])
                for sl in range(NSL):
                    t_ = NSL * th + sl
                    cwc = cwtm1[:, NE1 * t_ + eix:NE1 * t_ + eix + 1]
                    for n in range(2):
                        acs = big[:, 1024 * sl + 512 * n:1024 * sl + 512 * (n + 1)]
                        P.op("v", lambda e, acs=acs, n=n, cwc=cwc:
                                e.scalar_tensor_tensor(
                                    acs, bb[:, 512 * n:512 * (n + 1)], cwc, acs,
                                    ALU.mult, ALU.add),
                             reads=[("bb",), ("eacc", sl, n), ("cwl1", t_)],
                             writes=[("eacc", sl, n)])
            for tt in range(NSL):
                t_ = NSL * th + tt
                P.op("s", lambda e, tt=tt: e.activation(
                        ntile[:, :], big[:, 1024 * tt:1024 * (tt + 1)], AF.Silu),
                     reads=[("eacc", tt, 0), ("eacc", tt, 1)], writes=[("nt",)])
                P.dma("la", xtile[:, :], x1_scr.ap()[128 * t_:128 * (t_ + 1), :],
                      reads=[("scr_x1", t_)], writes=[("xt",)])
                P.dma("lb", stile[:, :], s_scr.ap()[128 * t_:128 * (t_ + 1), :],
                      reads=[("scr_s", t_)], writes=[("st",)])
                P.op("v", lambda e: e.tensor_tensor(ntile[:, :], stile[:, :],
                                                    ntile[:, :], ALU.mult),
                     reads=[("st",), ("nt",)], writes=[("nt",)])
                P.op("v", lambda e: e.tensor_tensor(xtile[:, :], xtile[:, :],
                                                    ntile[:, :], ALU.add),
                     reads=[("nt",), ("xt",)], writes=[("xt",)])
                P.dma("sa" if t_ % 2 == 0 else "sb",
                      out[128 * t_:128 * (t_ + 1), :], xtile[:, :],
                      reads=[("xt",)], writes=[("outw", t_)])

        return _finish()



def make_masks(q):
    i = np.arange(128)[:, None]
    j = np.arange(144)[None, :]
    interior = (j >= i) & (j <= i + 16)
    first = ((i <= 8) & (j >= 8) & (j <= i + 16)) | ((i > 8) & interior)
    last = ((128 - 1 - i <= 7) & (j >= i) & (j <= 135)) | ((i < 120) & interior)
    v1 = first if q == 0 else interior
    v2 = last if q == 3 else interior
    m = np.full((3, 128, 144), NEG, np.float32)
    for v, msk in enumerate((interior, v1, v2)):
        m[v][msk] = 0.0
    full = np.tile(m, (1, 1, 8))
    return full.astype(ml_dtypes.bfloat16)


def make_core_inputs(inputs, T=2048):
    x = np.asarray(inputs["L0_attn_x"])
    B, L, _ = x.shape
    percore = L // 4
    identf = np.eye(128, dtype=np.float32)
    identb = np.eye(128, dtype=ml_dtypes.bfloat16)
    dpow = np.tile((DECAY ** (np.arange(T, dtype=np.float64) + 1)
                    ).astype(np.float32)[None, :], (128, 1))
    wkeys = [k for k in inputs if k != "L0_attn_x"]
    in_maps = []
    for c in range(8):
        bsel, q = c // 4, c % 4
        lo, hi = q * percore, (q + 1) * percore
        xs = np.zeros((T + 16, D), np.float32)
        src_lo, src_hi = max(0, lo - 8), min(L, hi + 8)
        xs[8 - (lo - src_lo):8 + (src_hi - lo)] = x[bsel, src_lo:src_hi]
        fsel = np.zeros((8, 1), np.float32)
        if q > 0:
            fsel[c - 1, 0] = 1.0
        im = {k: np.ascontiguousarray(np.asarray(inputs[k])) for k in wkeys}
        im["x"] = xs
        im["identf"] = identf
        im["identb"] = identb
        im["maskp"] = np.ascontiguousarray(make_masks(q))
        im["dpow"] = dpow
        im["fsel"] = fsel
        in_maps.append(im)
    return in_maps


def assemble(results, T=2048):
    outs = [np.asarray(r["out"]) for r in results]
    return np.stack([np.concatenate(outs[0:4], 0),
                     np.concatenate(outs[4:8], 0)], 0)



def kernel(**inputs):
    from concourse.bass_utils import run_bass_kernel_spmd
    nc, _meta = build(T=2048)
    in_maps = make_core_inputs(inputs, T=2048)
    res = run_bass_kernel_spmd(nc, in_maps, core_ids=list(range(8)))
    return assemble(res.results, T=2048)


# version 07
# speedup vs baseline: 1.2219x; 1.0230x over previous
import contextlib
import numpy as np
import ml_dtypes
import concourse.bass as bass
import concourse.mybir as mybir

F32 = mybir.dt.float32
BF16 = mybir.dt.bfloat16
AF = mybir.ActivationFunctionType
ALU = mybir.AluOpType
AXX = mybir.AxisListType.X

D = 1024
NH = 16
HD = 64
ASCALE = HD ** (-0.5)
DI = 2048
NE0 = 16
F0 = 512
NE1 = 8
F1 = 4096
DECAY = 0.9
NEG = -30000.0

DMA_CHANS = {"wa": "g", "wb": "g", "wc": "s", "la": "y", "lb": "y", "sa": "y",
             "sb": "y", "cc": "g"}
ENG_SEM = {"t": "T", "v": "V", "s": "S", "g": "G", "y": "Y"}


class Prog:
    def __init__(self):
        self.ops = []
        self.cnt = {}
        self.writers = {}
        self.readers = {}
        self.waited = {}

    def _waits(self, eng, toks):
        best = {}
        for (sem, val) in toks:
            if eng == "t" and sem == "T":
                continue
            if val > best.get(sem, 0):
                best[sem] = val
        waits = []
        for sem, val in best.items():
            if self.waited.get((eng, sem), 0) < val:
                waits.append((sem, val))
                self.waited[(eng, sem)] = val
        return waits

    def op(self, eng, emit, reads=(), writes=(), sem=None, amt=1):
        sem = sem or ENG_SEM[eng]
        if eng in ("v", "s"):
            pbanks = {buf[1] for buf in tuple(reads) + tuple(writes)
                      if isinstance(buf, tuple) and buf and buf[0] == "ps"}
            if pbanks:
                writes = tuple(writes) + tuple(("plock", b) for b in sorted(pbanks))
        toks = []
        for buf in reads:
            toks += self.writers.get(buf, [])
        for buf in writes:
            toks += self.writers.get(buf, [])
            toks += self.readers.get(buf, [])
        waits = self._waits(eng, toks)
        self.cnt[sem] = self.cnt.get(sem, 0) + amt
        tok = (sem, self.cnt[sem])
        for buf in writes:
            self.writers[buf] = [tok]
            self.readers[buf] = []
        for buf in reads:
            self.readers.setdefault(buf, []).append(tok)
        self.ops.append((eng, waits, emit, sem, amt))
        return tok

    def dma(self, chan, out_ap, in_ap, reads=(), writes=()):
        return self.op(DMA_CHANS[chan],
                       lambda e: e.dma_start(out=out_ap, in_=in_ap),
                       reads=tuple(reads) + (("chan", chan),),
                       writes=tuple(writes) + (("chan", chan),),
                       sem=chan, amt=16)

    def rekey(self, old_keys, new_keys):
        toks = []
        for k in old_keys:
            toks += self.writers.pop(k, [])
            toks += self.readers.pop(k, [])
        for k in new_keys:
            self.writers[k] = self.writers.get(k, []) + toks

    def barrier(self, eng, bufs):
        toks = []
        for k in bufs:
            toks += self.writers.get(k, [])
            toks += self.readers.get(k, [])
        waits = self._waits(eng, toks)
        if waits:
            self.ops.append((eng, waits, lambda e: None, ENG_SEM[eng], 0))

    def emit_all(self, blk, sems):
        streams = {}
        for rec in self.ops:
            streams.setdefault(rec[0], []).append(rec)

        def run(e, engname):
            for (_eng, waits, emit, sem, amt) in streams[engname]:
                for (s_, v_) in waits:
                    e.wait_ge(sems[s_], v_)
                ins = emit(e)
                if ins is not None and amt > 0:
                    ins.then_inc(sems[sem], amt)

        for name, meth in (("t", blk.tensor), ("v", blk.vector), ("s", blk.scalar),
                           ("g", blk.gpsimd), ("y", blk.sync)):
            if name in streams:
                meth(lambda e, name=name: run(e, name))


class _Done(Exception):
    pass


def build(T=2048, taps=(), world=8, upto="E"):
    assert T % 512 == 0
    NT = T // 128
    NST = T // 512
    TH = T // 2
    NSL = TH // 128
    TW = T + 16
    W5 = min(512, TH)
    NW5 = max(1, TH // 512)

    nc = bass.Bass("TRN2", target_bir_lowering=False)
    dt = nc.dram_tensor

    x_in = dt("x", [TW, D], F32, kind="ExternalInput").ap()
    Wqkv = dt("L0_attn_Wqkv", [D, 3 * D], F32, kind="ExternalInput").ap()
    Wproj = dt("L0_attn_Wproj", [D, D], F32, kind="ExternalInput").ap()
    bproj = dt("L0_attn_bproj", [D], F32, kind="ExternalInput").ap()
    Wr = dt("L0_ffn_Wr", [D, NE0], F32, kind="ExternalInput").ap()
    Wg0 = dt("L0_ffn_Wg", [NE0, D, F0], F32, kind="ExternalInput").ap()
    Wu0 = dt("L0_ffn_Wu", [NE0, D, F0], F32, kind="ExternalInput").ap()
    Wd0 = dt("L0_ffn_Wd", [NE0, F0, D], F32, kind="ExternalInput").ap()
    Sg = dt("L0_ffn_Sg", [D, D], F32, kind="ExternalInput").ap()
    Su = dt("L0_ffn_Su", [D, D], F32, kind="ExternalInput").ap()
    Sd = dt("L0_ffn_Sd", [D, D], F32, kind="ExternalInput").ap()
    Win = dt("L1_attn_Win", [D, DI], F32, kind="ExternalInput").ap()
    bin_ = dt("L1_attn_bin_", [DI], F32, kind="ExternalInput").ap()
    Bp = dt("L1_attn_Bp", [DI, 16], F32, kind="ExternalInput").ap()
    Cp = dt("L1_attn_Cp", [16, DI], F32, kind="ExternalInput").ap()
    Dp = dt("L1_attn_Dp", [DI], F32, kind="ExternalInput").ap()
    Wout = dt("L1_attn_Wout", [DI, D], F32, kind="ExternalInput").ap()
    bout = dt("L1_attn_bout", [D], F32, kind="ExternalInput").ap()
    Wg1 = dt("L1_ffn_Wg", [D, NE1], F32, kind="ExternalInput").ap()
    bg1 = dt("L1_ffn_bg", [NE1], F32, kind="ExternalInput").ap()
    W1 = dt("L1_ffn_W1", [NE1, D, F1], F32, kind="ExternalInput").ap()
    b1 = dt("L1_ffn_b1", [NE1, F1], F32, kind="ExternalInput").ap()
    W2 = dt("L1_ffn_W2", [NE1, F1, D], F32, kind="ExternalInput").ap()
    b2 = dt("L1_ffn_b2", [NE1, D], F32, kind="ExternalInput").ap()
    identf = dt("identf", [128, 128], F32, kind="ExternalInput").ap()
    identb = dt("identb", [128, 128], BF16, kind="ExternalInput").ap()
    maskp = dt("maskp", [3, 128, 1152], BF16, kind="ExternalInput").ap()
    dpow_in = dt("dpow", [128, T], F32, kind="ExternalInput").ap()
    fsel_in = dt("fsel", [world, 1], F32, kind="ExternalInput").ap()

    out = dt("out", [T, D], F32, kind="ExternalOutput").ap()
    tap_shapes = {"xnT": ([128, 8, TW], BF16), "logits0": ([128, NT * NE0], F32),
                  "logits1": ([128, NT * NE1], F32),
                  "s0": ([T, D], F32), "x1": ([T, D], F32),
                  "acc0": ([128, NSL * D], F32), "xi": ([16, 128, T], F32),
                  "ypart": ([16, 128, T], F32), "s1": ([T, D], F32),
                  "cw0": ([128, NT * NE0], F32), "cw1": ([128, NT * NE1], F32),
                  "fprev": ([128, 16], F32), "rvec": ([128, 16], F32)}
    tap_aps = {nm: dt("tap_" + nm, shp, dty, kind="ExternalOutput").ap()
               for nm, (shp, dty) in tap_shapes.items() if nm in taps}

    s_scr = dt("s_scr", [T, D], F32)
    x1_scr = dt("x1_scr", [T, D], F32)
    yp_scr = dt("yp_scr", [16, 128, T], F32)
    cc_in = dt("cc_in", [1, DI], F32)
    cc_out = dt("cc_out", [world, DI], F32)
    fr_scr = dt("fr_scr", [1, DI], F32)

    with contextlib.ExitStack() as es:
        E = es.enter_context
        xnT = E(nc.sbuf_tensor("xnT", [128, 8 * TW], BF16))
        arw = E(nc.sbuf_tensor("arw", [128, 32768], BF16))
        big = E(nc.sbuf_tensor("big", [128, 8192], F32))
        aux = E(nc.sbuf_tensor("aux", [128, 16896], BF16))
        xtile = E(nc.sbuf_tensor("xtile", [128, D], F32))
        stile = E(nc.sbuf_tensor("stile", [128, D], F32))
        ntile = E(nc.sbuf_tensor("ntile", [128, D], F32))
        masks = E(nc.sbuf_tensor("masks", [128, 3 * 1152], BF16))
        stats = E(nc.sbuf_tensor("stats", [128, 64], F32))
        logit0 = E(nc.sbuf_tensor("logit0", [128, NT * NE0], F32))
        logit1 = E(nc.sbuf_tensor("logit1", [128, NT * NE1], F32))
        cwtm0 = E(nc.sbuf_tensor("cwtm0", [128, NT * NE0], F32))
        cwtm1 = E(nc.sbuf_tensor("cwtm1", [128, NT * NE1], F32))
        cwrows = E(nc.sbuf_tensor("cwrows", [128, 2048], F32))
        rows2 = E(nc.sbuf_tensor("rows2", [128, 1536], F32))
        bb = E(nc.sbuf_tensor("bb", [128, 1024], F32))
        smallv = E(nc.sbuf_tensor("smallv", [128, 176], F32))
        wrg = E(nc.sbuf_tensor("wrg", [128, 192], F32))
        tgt = E(nc.sbuf_tensor("tgt", [128, 512], BF16))
        identf_sb = E(nc.sbuf_tensor("identf_sb", [128, 128], F32))
        identb_sb = E(nc.sbuf_tensor("identb_sb", [128, 128], BF16))
        fsel_sb = E(nc.sbuf_tensor("fsel_sb", [world, 1], F32))
        ps = E(nc.psum_tensor("ps", [128, 4096], F32))

        psb = ps[:, :].bitcast(BF16)
        auxf = aux[:, :].bitcast(F32)
        blk = E(nc.Block())
        P = Prog()

        SV_BIN, SV_DP, SV_R, SV_FP, SV_FR, SV_B1, SV_CW = 0, 16, 32, 48, 64, 96, 128
        ST_NM, ST_SQ, ST_IV, ST_M1, ST_M2, ST_DEN = 0, 1, 2, 3, 4, 5
        R_CP, R_FT, R_FALL = 32, 96, 0
        bgrow_sb = stats[:, 48:56]

        def _finish():
            sem_names = set(list(P.cnt.keys()) + ["T", "V", "S", "G", "Y"])
            P.barrier("y", [k for k in P.writers if k[0] == "outw"]
                      + [("chan", c) for c in ("sa", "sb", "wa", "wb", "wc",
                                               "la", "lb", "cc")
                         if P.cnt.get(c)])
            sems = {k: E(nc.semaphore("sem_" + k)) for k in sorted(sem_names)}
            P.emit_all(blk, sems)
            return nc, dict(tap_aps=tap_aps, n_ops=len(P.ops))

        P.dma("la", identf_sb[:, :], identf, writes=[("idf",)])
        P.dma("lb", identb_sb[:, :], identb, writes=[("idb",)])
        P.dma("la", masks[:, :].rearrange("p (v n) -> p v n", v=3),
              maskp.rearrange("v p n -> p v n"), writes=[("masks",)])
        P.dma("lb", fsel_sb[:, :], fsel_in, writes=[("fsel",)])
        P.op("v", lambda e: e.memset(rows2[0:1, 0:128], 1.0), writes=[("r2", "ones")])
        for k in range(8):
            P.dma("la", wrg[:, 16 * k:16 * (k + 1)], Wr[128 * k:128 * (k + 1), :],
                  writes=[("wrg", 0)])
            P.dma("lb", wrg[:, 128 + 8 * k:128 + 8 * (k + 1)],
                  Wg1[128 * k:128 * (k + 1), :], writes=[("wrg", 1)])

        def bias_bcast(src_dram, chan="la"):
            P.dma(chan, rows2[0:1, 512:1536], src_dram.unsqueeze(0) if src_dram.ndim == 1 else src_dram,
                  reads=[("r2", "bstage")], writes=[("r2", "bstage")])

            def mm(e):
                for n2 in range(2):
                    ins = e.matmul(ps[:, 2048 + 512 * n2:2048 + 512 * (n2 + 1)],
                                   rows2[0:1, 0:128],
                                   rows2[0:1, 512 + 512 * n2:512 + 512 * (n2 + 1)],
                                   start=True, stop=True)
                return ins
            P.op("t", mm, reads=[("r2", "ones"), ("r2", "bstage")],
                 writes=[("ps", 4), ("ps", 5)])
            P.op("v", lambda e: e.tensor_copy(bb[:, :], ps[:, 2048:3072]),
                 reads=[("ps", 4), ("ps", 5)], writes=[("bb",)])

        LN_CH = ("la", "lb", "sa", "sb")

        def ln_dma(chunk):
            for j, (rows, xsrc, dst_off, lg_kind, lg_col) in enumerate(chunk):
                P.dma(LN_CH[j], big[0:rows, 1024 * j:1024 * j + 1024], xsrc,
                      reads=[("xt", j)], writes=[("xt", j)])

        def ln_pass(specs):
            chunks = [specs[i:i + 4] for i in range(0, len(specs), 4)]
            xv = xnT[:, :].rearrange("p (k n) -> p k n", k=8)
            ln_dma(chunks[0])
            for ci, ch in enumerate(chunks):
                for j, (rows, xsrc, dst_off, lg_kind, lg_col) in enumerate(ch):
                    xt = big[:, 1024 * j:1024 * j + 1024]
                    so = 8 * j
                    P.op("v", lambda e, rows=rows, so=so, xt=xt: e.reduce_sum(
                            stats[0:rows, so + ST_NM:so + ST_NM + 1],
                            xt[0:rows, :], AXX),
                         reads=[("xt", j)], writes=[("stats", "nm", j)])
                    P.op("s", lambda e, rows=rows, so=so, xt=xt: e.activation(
                            stile[0:rows, :], xt[0:rows, :], AF.Square,
                            accum_out=stats[0:rows, so + ST_SQ:so + ST_SQ + 1]),
                         reads=[("xt", j)], writes=[("st",), ("stats", "sq", j)])
                for j, (rows, xsrc, dst_off, lg_kind, lg_col) in enumerate(ch):
                    so = 8 * j
                    P.op("v", lambda e, rows=rows, so=so: e.tensor_scalar(
                            stats[0:rows, so + ST_NM:so + ST_NM + 1],
                            stats[0:rows, so + ST_NM:so + ST_NM + 1],
                            -1.0 / D, None, ALU.mult),
                         reads=[("stats", "nm", j)], writes=[("stats", "nm", j)])
                    P.op("v", lambda e, rows=rows, so=so: e.tensor_tensor(
                            stats[0:rows, so + ST_M1:so + ST_M1 + 1],
                            stats[0:rows, so + ST_NM:so + ST_NM + 1],
                            stats[0:rows, so + ST_NM:so + ST_NM + 1], ALU.mult),
                         reads=[("stats", "nm", j)], writes=[("stats", "m1", j)])
                    P.op("v", lambda e, rows=rows, so=so: e.tensor_scalar(
                            stats[0:rows, so + ST_IV:so + ST_IV + 1],
                            stats[0:rows, so + ST_SQ:so + ST_SQ + 1],
                            1.0 / D, 1e-5, ALU.mult, ALU.add),
                         reads=[("stats", "sq", j)], writes=[("stats", "iv", j)])
                    P.op("v", lambda e, rows=rows, so=so: e.tensor_tensor(
                            stats[0:rows, so + ST_IV:so + ST_IV + 1],
                            stats[0:rows, so + ST_IV:so + ST_IV + 1],
                            stats[0:rows, so + ST_M1:so + ST_M1 + 1], ALU.subtract),
                         reads=[("stats", "iv", j), ("stats", "m1", j)],
                         writes=[("stats", "iv", j)])
                for j, (rows, xsrc, dst_off, lg_kind, lg_col) in enumerate(ch):
                    so = 8 * j
                    P.op("s", lambda e, rows=rows, so=so: e.activation(
                            stats[0:rows, so + ST_IV:so + ST_IV + 1],
                            stats[0:rows, so + ST_IV:so + ST_IV + 1], AF.Sqrt),
                         reads=[("stats", "iv", j)], writes=[("stats", "iv", j)])
                for j, (rows, xsrc, dst_off, lg_kind, lg_col) in enumerate(ch):
                    so = 8 * j
                    P.op("v", lambda e, rows=rows, so=so: e.reciprocal(
                            stats[0:rows, so + ST_IV:so + ST_IV + 1],
                            stats[0:rows, so + ST_IV:so + ST_IV + 1]),
                         reads=[("stats", "iv", j)], writes=[("stats", "iv", j)])
                for j, (rows, xsrc, dst_off, lg_kind, lg_col) in enumerate(ch):
                    so = 8 * j
                    xt = big[:, 1024 * j:1024 * j + 1024]
                    nt = big[:, 4096 + 1024 * j:4096 + 1024 * j + 1024]
                    P.op("v", lambda e, rows=rows, so=so, xt=xt, nt=nt:
                            e.tensor_scalar(
                            nt[0:rows, :], xt[0:rows, :],
                            stats[0:rows, so + ST_NM:so + ST_NM + 1],
                            stats[0:rows, so + ST_IV:so + ST_IV + 1],
                            ALU.add, ALU.mult),
                         reads=[("xt", j), ("stats", "nm", j), ("stats", "iv", j)],
                         writes=[("nt", j)])
                if ci + 1 < len(chunks):
                    ln_dma(chunks[ci + 1])
                for j, (rows, xsrc, dst_off, lg_kind, lg_col) in enumerate(ch):
                    nt = big[:, 4096 + 1024 * j:4096 + 1024 * j + 1024]
                    pcol = 1024 * j

                    def tr(e, rows=rows, nt=nt, pcol=pcol):
                        for k in range(8):
                            col = pcol + 128 * k
                            ins = e.transpose(ps[:, col:col + rows],
                                              nt[0:rows, 128 * k:128 * (k + 1)],
                                              identf_sb[0:rows, 0:rows])
                        return ins
                    P.op("t", tr, reads=[("nt", j), ("idf",)],
                         writes=[("ps", 2 * j), ("ps", 2 * j + 1)])
                    psv8 = ps[:, pcol:pcol + 1024].rearrange("p (k n) -> p k n", k=8)
                    P.op("v", lambda e, rows=rows, dst_off=dst_off, psv8=psv8:
                            e.tensor_copy(xv[:, :, dst_off:dst_off + rows],
                                          psv8[:, :, 0:rows]),
                         reads=[("ps", 2 * j), ("ps", 2 * j + 1)], writes=[("xnT",)])
                    if lg_kind is None:
                        continue
                    axo = 1024 * j
                    P.op("s", lambda e, axo=axo, pcol=pcol: e.copy(
                            auxf[:, axo:axo + 1024], ps[:, pcol:pcol + 1024]),
                         reads=[("ps", 2 * j), ("ps", 2 * j + 1)],
                         writes=[("aux", "xf", j)])
                for j, (rows, xsrc, dst_off, lg_kind, lg_col) in enumerate(ch):
                    if lg_kind is None:
                        continue
                    which, ne = lg_kind
                    skip_mm = which.startswith("skip")
                    which = which.replace("skip", "")
                    wcol = 0 if which == "l0" else 128
                    lg_sb = logit0 if which == "l0" else logit1
                    axo = 1024 * j
                    pcol = 1024 * j
                    if skip_mm:
                        continue

                    def rmm(e, ne=ne, axo=axo, pcol=pcol, wcol=wcol):
                        for k in range(8):
                            ins = e.matmul(ps[:, pcol:pcol + ne],
                                           auxf[:, axo + 128 * k:axo + 128 * (k + 1)],
                                           wrg[:, wcol:wcol + 8 * ne]
                                           .rearrange("p (k j) -> p k j", k=8)[:, k, :],
                                           start=(k == 0), stop=(k == 7))
                        return ins
                    P.op("t", rmm, reads=[("aux", "xf", j),
                                          ("wrg", 0 if which == "l0" else 1)],
                         writes=[("ps", 2 * j)])
                    P.op("v", lambda e, lg_sb=lg_sb, lg_col=lg_col, ne=ne, pcol=pcol:
                            e.tensor_copy(lg_sb[:, lg_col:lg_col + ne],
                                          ps[:, pcol:pcol + ne]),
                         reads=[("ps", 2 * j)], writes=[(which,)])

        lgk = (None if upto == "A0" else
               (("skipl0", NE0) if upto == "A1" else ("l0", NE0)))
        ln_pass([(8, x_in[0:8, :], 0, None, 0)]
                + [(128, x_in[8 + 128 * t_:8 + 128 * (t_ + 1), :], 8 + 128 * t_,
                    lgk, t_ * NE0) for t_ in range(NT)]
                + [(8, x_in[8 + T:16 + T, :], 8 + T, None, 0)])

        if "xnT" in taps:
            P.dma("sa", tap_aps["xnT"], xnT[:, :].rearrange("p (k n) -> p k n", k=8),
                  reads=[("xnT",)])
        if "logits0" in taps:
            P.dma("sb", tap_aps["logits0"], logit0[:, 0:NT * NE0], reads=[("l0",)])

        if upto in ("A", "A0", "A1"):
            return _finish()
        WQKV, WPROJ = 0, 24576
        wq_v = arw[:, WQKV:WQKV + 24576].rearrange("p (k n) -> p k n", k=8)
        wp_v = arw[:, WPROJ:WPROJ + 8192].rearrange("p (k n) -> p k n", k=8)
        for j, ch in ((0, "wa"), (1, "wb"), (2, "wa")):
            P.dma(ch, wq_v[:, :, 1024 * j:1024 * (j + 1)],
                  Wqkv.rearrange("(k p) n -> p k n", p=128)[:, :, 1024 * j:1024 * (j + 1)],
                  writes=[("arw", "qkv", j)])
        P.dma("wb", wp_v, Wproj.rearrange("(k p) n -> p k n", p=128),
              writes=[("arw", "proj")])
        bias_bcast(bproj, "la")

        AQ, AK, AV_, AT2, AA, AO = 0, 4096, 8320, 13440, 14464, 15616
        qv = aux[:, AQ:AQ + 4096].rearrange("p (k n) -> p k n", k=8)
        kv = aux[:, AK:AK + 4224].rearrange("p (k n) -> p k n", k=8)
        vv = aux[:, AV_:AV_ + 5120].rearrange("p (j n) -> p j n", j=5)
        at2sb = aux[0:16, AT2:AT2 + 1024]
        asb = aux[:, AA:AA + 1152]
        at1sb = aux[:, AA:AA + 1024]
        osb = aux[:, AO:AO + 1024]
        xkv = xnT[:, :].rearrange("p (k n) -> p k n", k=8)
        mk3 = masks[:, :].rearrange("p (v s x) -> p v s x", v=3, s=8)
        AUXB = ([("auxq", m) for m in range(8)] + [("auxk", m) for m in range(8)]
                + [("auxv", j) for j in range(5)]
                + [("auxa", h) for h in range(8)] + [("auxat1",), ("auxat2",),
                   ("auxo", 0), ("auxo", 1)])
        P.rekey([("aux", "xf", 0), ("aux", "xf", 1), ("aux", "xf", 2), ("aux", "xf", 3)], AUXB)

        BLV = {"Bs0": -1, "Bs": 0, "Bsm": 1, "Bat": 2, "Bav": 3}.get(upto, 4)
        for st in range(NST):
            s0c = 512 * st
            for m in range(8):
                bk = m % 2

                def qmm(e, m=m, bk=bk, s0c=s0c):
                    for k in range(8):
                        ins = e.matmul(ps[:, 512 * bk:512 * bk + 512],
                                       wq_v[:, k, 128 * m:128 * (m + 1)],
                                       xkv[:, k, 8 + s0c:8 + s0c + 512],
                                       start=(k == 0), stop=(k == 7))
                    return ins
                P.op("t", qmm, reads=[("arw", "qkv", 0), ("xnT",)], writes=[("ps", bk)])
                P.op("s", lambda e, m=m, bk=bk: e.activation(
                        qv[:, m, :], ps[:, 512 * bk:512 * bk + 512], AF.Copy,
                        scale=ASCALE),
                     reads=[("ps", bk)], writes=[("auxq", m)])
            for m in range(8):
                bk = m % 2

                def kmm(e, m=m, bk=bk, s0c=s0c):
                    for k in range(8):
                        e.matmul(ps[:, 512 * bk:512 * bk + 512],
                                 wq_v[:, k, 1024 + 128 * m:1024 + 128 * (m + 1)],
                                 xkv[:, k, s0c:s0c + 512],
                                 start=(k == 0), stop=(k == 7))
                    for k in range(8):
                        ins = e.matmul(ps[:, 2048 + 16 * bk:2048 + 16 * (bk + 1)],
                                       wq_v[:, k, 1024 + 128 * m:1024 + 128 * (m + 1)],
                                       xkv[:, k, s0c + 512:s0c + 528],
                                       start=(k == 0), stop=(k == 7))
                    return ins
                P.op("t", kmm, reads=[("arw", "qkv", 1), ("xnT",)],
                     writes=[("ps", bk), ("ps", 4)])

                def kev(e, m=m, bk=bk):
                    e.copy(kv[:, m, 0:512], ps[:, 512 * bk:512 * bk + 512])
                    return e.copy(kv[:, m, 512:528],
                                  ps[:, 2048 + 16 * bk:2048 + 16 * (bk + 1)])
                P.op("s", kev, reads=[("ps", bk), ("ps", 4)], writes=[("auxk", m)])
            for j in range(5):
                rj = 128 if (j < 4 or st < NST - 1) else 16

                def vmm(e, j=j, s0c=s0c, rj=rj):
                    for n2 in range(2):
                        for k in range(8):
                            ins = e.matmul(ps[0:rj, 512 * n2:512 * (n2 + 1)],
                                           xkv[:, k, s0c + 128 * j:s0c + 128 * j + rj],
                                           wq_v[:, k, 2048 + 512 * n2:2048 + 512 * (n2 + 1)],
                                           start=(k == 0), stop=(k == 7))
                    return ins
                P.op("t", vmm, reads=[("arw", "qkv", 2), ("xnT",)],
                     writes=[("ps", 0), ("ps", 1)])
                P.op("v", lambda e, j=j, rj=rj: e.tensor_copy(vv[0:rj, j, :],
                                                              ps[0:rj, 0:1024]),
                     reads=[("ps", 0), ("ps", 1)], writes=[("auxv", j)])

            if upto == "Bq":
                continue
            for tl in range(4):
                t_ = 4 * st + tl
                mvar = 1 if t_ == 0 else (2 if t_ == NT - 1 else 0)
                for hg in range(2):
                    def smm(e, tl=tl, hg=hg):
                        for hh in range(8):
                            h = 8 * hg + hh
                            po, chn = 64 * (h % 2), h // 2
                            s_ = (hh % 2) * 4 + hh // 2
                            ins = e.matmul(
                                ps[:, 256 * s_:256 * s_ + 144],
                                qv[po:po + 64, chn, 128 * tl:128 * (tl + 1)],
                                kv[po:po + 64, chn, 128 * tl:128 * tl + 144],
                                start=True, stop=True)
                        return ins
                    P.op("t", smm,
                         reads=[("auxq", m) for m in range(8)]
                               + [("auxk", m) for m in range(8)],
                         writes=[("ps", 0), ("ps", 1), ("ps", 2), ("ps", 3)])
                    if BLV < 0:
                        continue
                    if upto == "Bnm":
                        def mka(e, mvar=mvar):
                            for hh in range(8):
                                ins = e.tensor_tensor(
                                    ps[:, 256 * hh:256 * hh + 144],
                                    ps[:, 256 * hh:256 * hh + 144],
                                    mk3[:, mvar, hh, :], ALU.add)
                            return ins
                        P.op("v", mka,
                             reads=[("ps", 0), ("ps", 1), ("ps", 2), ("ps", 3),
                                    ("masks",)],
                             writes=[("ps", 0), ("ps", 1), ("ps", 2), ("ps", 3)])
                    else:
                        psv = ps[:, 0:2048].rearrange("p (sl x) -> p sl x", sl=8)
                        P.op("v", lambda e, psv=psv, mvar=mvar: e.tensor_tensor(
                                psv[:, :, 0:144], psv[:, :, 0:144],
                                mk3[:, mvar, :, :], ALU.add),
                             reads=[("ps", 0), ("ps", 1), ("ps", 2), ("ps", 3),
                                    ("masks",)],
                             writes=[("ps", 0), ("ps", 1), ("ps", 2), ("ps", 3)])
                    av8 = asb[:, :].rearrange("p (sl x) -> p sl x", sl=8)
                    if BLV < 1:
                        continue
                    for hh in range(8):
                        s_ = (hh % 2) * 4 + hh // 2
                        P.op("v", lambda e, s_=s_, hh=hh: e.reduce_max(
                                stats[:, 32 + hh:33 + hh],
                                ps[:, 256 * s_:256 * s_ + 144], AXX, negate=True),
                             reads=[("ps", s_ // 2)], writes=[("stats", "nx", hh)])
                    for hh in range(8):
                        s_ = (hh % 2) * 4 + hh // 2
                        P.op("s", lambda e, s_=s_, hh=hh, av8=av8: e.activation(
                                av8[:, hh, :], ps[:, 256 * s_:256 * s_ + 144], AF.Exp,
                                bias=stats[:, 32 + hh:33 + hh], scale=1.0,
                                accum_out=stats[:, 40 + hh:41 + hh]),
                             reads=[("ps", s_ // 2), ("stats", "nx", hh)],
                             writes=[("auxa", hh), ("stats", "l", hh)])
                    P.op("v", lambda e: e.reciprocal(stats[:, 40:48], stats[:, 40:48]),
                         reads=[("stats", "l", h) for h in range(8)],
                         writes=[("stats", "linv")])
                    for hh in range(8):
                        P.op("v", lambda e, hh=hh, av8=av8: e.tensor_scalar(
                                av8[:, hh, :], av8[:, hh, :],
                                stats[:, 40 + hh:41 + hh], None, ALU.mult),
                             reads=[("auxa", hh), ("stats", "linv")],
                             writes=[("auxa", hh)])

                    if BLV < 2:
                        continue

                    def atr(e, av8=av8):
                        for hh in range(8):
                            e.transpose(psb[:, 4096 + 128 * hh:4096 + 128 * (hh + 1)],
                                        av8[:, hh, 0:128], identb_sb[:, :])
                            ins = e.transpose(
                                psb[0:16, 5120 + 128 * hh:5120 + 128 * (hh + 1)],
                                av8[:, hh, 128:144], identb_sb[:, :])
                        return ins
                    P.op("t", atr,
                         reads=[("auxa", h) for h in range(8)] + [("idb",)],
                         writes=[("ps", 4), ("ps", 5)])
                    P.op("v", lambda e: e.tensor_copy(at1sb[:, :], psb[:, 4096:5120]),
                         reads=[("ps", 4)] + [("auxa", h) for h in range(8)],
                         writes=[("auxat1",)])
                    P.op("s", lambda e: e.copy(at2sb[:, :], psb[0:16, 5120:6144]),
                         reads=[("ps", 5)], writes=[("auxat2",)])
                    if BLV < 3:
                        continue

                    def avmm(e, tl=tl, hg=hg):
                        for hh in range(8):
                            h = 8 * hg + hh
                            po = 64 * (hh % 2)
                            c2 = hh // 2
                            e.matmul(ps[po:po + 64,
                                        3072 + 128 * c2:3072 + 128 * (c2 + 1)],
                                     vv[:, tl, 64 * h:64 * (h + 1)],
                                     at1sb[:, 128 * hh:128 * (hh + 1)],
                                     start=(hh <= 1), stop=False,
                                     skip_group_check=(hh > 1))
                            ins = e.matmul(
                                ps[po:po + 64, 3072 + 128 * c2:3072 + 128 * (c2 + 1)],
                                vv[0:16, tl + 1, 64 * h:64 * (h + 1)],
                                at2sb[:, 128 * hh:128 * (hh + 1)],
                                start=False, stop=(hh >= 6),
                                skip_group_check=(hh < 6))
                        return ins
                    P.op("t", avmm,
                         reads=[("auxat1",), ("auxat2",), ("auxv", tl),
                                ("auxv", tl + 1)],
                         writes=[("ps", 6)])
                    P.op("v", lambda e, hg=hg: e.tensor_copy(
                            osb[:, 512 * hg:512 * (hg + 1)], ps[:, 3072:3584]),
                         reads=[("ps", 6)],
                         writes=[("auxo", hg)])
                if BLV < 4:
                    continue
                ov = osb.rearrange("p (c n) -> p c n", c=8)
                for n2 in range(2):
                    def pmm(e, n2=n2):
                        for k in range(8):
                            ins = e.matmul(ps[:, 512 * n2:512 * (n2 + 1)],
                                           ov[:, k, :],
                                           wp_v[:, k, 512 * n2:512 * (n2 + 1)],
                                           start=(k == 0), stop=(k == 7))
                        return ins
                    P.op("t", pmm, reads=[("auxo", 0), ("auxo", 1), ("arw", "proj")],
                         writes=[("ps", n2)])
                P.op("v", lambda e: e.tensor_tensor(stile[:, :], ps[:, 0:1024],
                                                    bb[:, :], ALU.add),
                     reads=[("ps", 0), ("ps", 1), ("bb",)], writes=[("st",)])
                P.dma("sa" if t_ % 2 == 0 else "sb",
                      s_scr.ap()[128 * t_:128 * (t_ + 1), :], stile[:, :],
                      reads=[("st",)], writes=[("scr_s", t_)])

        if "s0" in taps:
            P.dma("sa", tap_aps["s0"], s_scr.ap(),
                  reads=[("scr_s", t_) for t_ in range(NT)])

        if upto in ("B", "Bq", "Bnm", "Bs", "Bs0", "Bsm", "Bat", "Bav"):
            return _finish()
        def emit_cw(which, ne, tile, normalize):
            lg_sb = logit0 if which == "l0" else logit1
            cwt = cwtm0 if which == "l0" else cwtm1
            lg = lg_sb[:, ne * tile:ne * (tile + 1)]
            cwtmp = smallv[:, SV_CW:SV_CW + ne]
            lkey = (which,)
            if which == "l1":
                P.op("v", lambda e, lg=lg: e.tensor_tensor(
                        smallv[:, SV_CW + 32:SV_CW + 32 + ne], lg,
                        bgrow_sb[:, 0:ne], ALU.add),
                     reads=[lkey, ("bgB",)], writes=[("svl1",)])
                lg = smallv[:, SV_CW + 32:SV_CW + 32 + ne]
                lkey = ("svl1",)
            P.op("v", lambda e, lg=lg: e.reduce_max(stats[:, ST_NM:ST_NM + 1], lg,
                                                    AXX, negate=True),
                 reads=[lkey], writes=[("stats", "nm")])
            P.op("s", lambda e, lg=lg: e.activation(cwtmp, lg, AF.Exp,
                                                    bias=stats[:, ST_NM:ST_NM + 1],
                                                    scale=1.0,
                                                    accum_out=stats[:, ST_SQ:ST_SQ + 1]),
                 reads=[lkey, ("stats", "nm")], writes=[("svcw",), ("stats", "sq")])
            P.op("v", lambda e: e.reciprocal(stats[:, ST_IV:ST_IV + 1],
                                             stats[:, ST_SQ:ST_SQ + 1]),
                 reads=[("stats", "sq")], writes=[("stats", "iv")])
            P.op("v", lambda e: e.tensor_scalar(cwtmp, cwtmp,
                                                stats[:, ST_IV:ST_IV + 1], None,
                                                ALU.mult),
                 reads=[("svcw",), ("stats", "iv")], writes=[("svcw",)])
            P.op("v", lambda e: e.reduce_max(stats[:, ST_M1:ST_M1 + 1], cwtmp, AXX),
                 reads=[("svcw",)], writes=[("stats", "m1")])
            P.op("v", lambda e: e.tensor_scalar(ntile[:, 0:ne], cwtmp,
                                                stats[:, ST_M1:ST_M1 + 1], None,
                                                ALU.is_ge),
                 reads=[("svcw",), ("stats", "m1")], writes=[("nt",)])
            P.op("v", lambda e: e.scalar_tensor_tensor(ntile[:, 0:ne], ntile[:, 0:ne],
                                                       -4.0, cwtmp, ALU.mult, ALU.add),
                 reads=[("nt",), ("svcw",)], writes=[("nt",)])
            P.op("v", lambda e: e.reduce_max(stats[:, ST_M2:ST_M2 + 1], ntile[:, 0:ne],
                                             AXX),
                 reads=[("nt",)], writes=[("stats", "m2")])
            P.op("v", lambda e: e.tensor_scalar(ntile[:, 0:ne], cwtmp,
                                                stats[:, ST_M2:ST_M2 + 1], None,
                                                ALU.is_ge),
                 reads=[("svcw",), ("stats", "m2")], writes=[("nt",)])
            P.op("v", lambda e: e.tensor_tensor(cwtmp, cwtmp, ntile[:, 0:ne], ALU.mult),
                 reads=[("svcw",), ("nt",)], writes=[("svcw",)])
            if normalize:
                P.op("v", lambda e: e.tensor_tensor(stats[:, ST_DEN:ST_DEN + 1],
                                                    stats[:, ST_M1:ST_M1 + 1],
                                                    stats[:, ST_M2:ST_M2 + 1], ALU.add),
                     reads=[("stats", "m1"), ("stats", "m2")],
                     writes=[("stats", "den")])
                P.op("v", lambda e: e.tensor_scalar(stats[:, ST_DEN:ST_DEN + 1],
                                                    stats[:, ST_DEN:ST_DEN + 1],
                                                    1e-8, None, ALU.add),
                     reads=[("stats", "den")], writes=[("stats", "den")])
                P.op("v", lambda e: e.reciprocal(stats[:, ST_DEN:ST_DEN + 1],
                                                 stats[:, ST_DEN:ST_DEN + 1]),
                     reads=[("stats", "den")], writes=[("stats", "den")])
                P.op("v", lambda e: e.tensor_scalar(cwtmp, cwtmp,
                                                    stats[:, ST_DEN:ST_DEN + 1], None,
                                                    ALU.mult),
                     reads=[("svcw",), ("stats", "den")], writes=[("svcw",)])
            P.op("v", lambda e, tile=tile: e.tensor_copy(
                    cwt[:, ne * tile:ne * (tile + 1)], cwtmp),
                 reads=[("svcw",)], writes=[("cw" + which, tile)])

        for t_ in range(NT):
            emit_cw("l0", NE0, t_, False)
        if "cw0" in taps:
            P.dma("sb", tap_aps["cw0"], cwtm0[:, :],
                  reads=[("cwl0", t_) for t_ in range(NT)])

        if upto == "cw0":
            return _finish()
        HA, HB = 0, 4096
        SLOT = [0, 12288]
        P.rekey(AUXB, [("aux", "hA"), ("aux", "hB")])
        P.rekey([("arw", "qkv", 0), ("arw", "qkv", 1), ("arw", "qkv", 2),
                 ("arw", "proj")],
                [("arw", "ug", 0), ("arw", "uu", 0), ("arw", "ud", 0),
                 ("arw", "ug", 1), ("arw", "uu", 1), ("arw", "ud", 1)])

        def c_unit_dma(u):
            slot = SLOT[u % 2]
            ch = "wa" if u % 2 == 0 else "wb"
            gv = arw[:, slot:slot + 4096].rearrange("p (k n) -> p k n", k=8)
            uv = arw[:, slot + 4096:slot + 8192].rearrange("p (k n) -> p k n", k=8)
            dv = arw[:, slot + 8192:slot + 12288].rearrange("p (k n) -> p k n", k=4)
            if u < NE0:
                P.dma(ch, gv, Wg0[u].rearrange("(k p) n -> p k n", p=128),
                      writes=[("arw", "ug", u % 2)])
                P.dma(ch, uv, Wu0[u].rearrange("(k p) n -> p k n", p=128),
                      writes=[("arw", "uu", u % 2)])
                P.dma(ch, dv, Wd0[u].rearrange("(k p) n -> p k n", p=128),
                      writes=[("arw", "ud", u % 2)])
            else:
                f = u - NE0
                P.dma(ch, gv,
                      Sg.rearrange("(k p) n -> p k n", p=128)[:, :, 512 * f:512 * (f + 1)],
                      writes=[("arw", "ug", u % 2)])
                P.dma(ch, uv,
                      Su.rearrange("(k p) n -> p k n", p=128)[:, :, 512 * f:512 * (f + 1)],
                      writes=[("arw", "uu", u % 2)])
                P.dma(ch, dv,
                      Sd.rearrange("(k p) n -> p k n", p=128)[:, 4 * f:4 * (f + 1), :],
                      writes=[("arw", "ud", u % 2)])

        def c_unit_compute(u, th):
            slot = SLOT[u % 2]
            hoff = HA if u % 2 == 0 else HB
            hkey = ("aux", "hA" if u % 2 == 0 else "hB")
            gv = arw[:, slot:slot + 4096].rearrange("p (k n) -> p k n", k=8)
            uv = arw[:, slot + 4096:slot + 8192].rearrange("p (k n) -> p k n", k=8)
            dv = arw[:, slot + 8192:slot + 12288].rearrange("p (k n) -> p k n", k=4)
            hv = aux[:, hoff:hoff + 4 * TH].rearrange("p (m n) -> p m n", m=4)
            for m in range(4):
                for n2 in range(NW5):
                    def gmm(e, m=m, n2=n2, th=th, gv=gv):
                        for k in range(8):
                            ins = e.matmul(ps[:, 0:W5],
                                           gv[:, k, 128 * m:128 * (m + 1)],
                                           xkv[:, k, 8 + TH * th + W5 * n2:
                                               8 + TH * th + W5 * (n2 + 1)],
                                           start=(k == 0), stop=(k == 7))
                        return ins
                    P.op("t", gmm, reads=[("arw", "ug", u % 2), ("xnT",)],
                         writes=[("ps", 0)])
                    P.op("s", lambda e: e.activation(tgt[:, 0:W5], ps[:, 0:W5],
                                                     AF.Silu),
                         reads=[("ps", 0)], writes=[("tg",)])

                    def umm(e, m=m, n2=n2, th=th, uv=uv):
                        for k in range(8):
                            ins = e.matmul(ps[:, 512:512 + W5],
                                           uv[:, k, 128 * m:128 * (m + 1)],
                                           xkv[:, k, 8 + TH * th + W5 * n2:
                                               8 + TH * th + W5 * (n2 + 1)],
                                           start=(k == 0), stop=(k == 7))
                        return ins
                    P.op("t", umm, reads=[("arw", "uu", u % 2), ("xnT",)],
                         writes=[("ps", 1)])
                    hslice = hv[:, m, W5 * n2:W5 * (n2 + 1)]
                    P.op("v", lambda e, hslice=hslice: e.tensor_tensor(
                            hslice, tgt[:, 0:W5], ps[:, 512:512 + W5], ALU.mult),
                         reads=[("tg",), ("ps", 1)], writes=[hkey])
            for sl in range(NSL):
                for n in range(2):
                    bk = 2 + (sl * 2 + n) % 2

                    def dmm(e, sl=sl, n=n, bk=bk, hv=hv, dv=dv):
                        for k in range(4):
                            ins = e.matmul(ps[:, 512 * bk:512 * (bk + 1)],
                                           hv[:, k, 128 * sl:128 * (sl + 1)],
                                           dv[:, k, 512 * n:512 * (n + 1)],
                                           start=(k == 0), stop=(k == 3))
                        return ins
                    P.op("t", dmm, reads=[hkey, ("arw", "ud", u % 2)],
                         writes=[("ps", bk)])
                    acs = big[:, 1024 * sl + 512 * n:1024 * sl + 512 * (n + 1)]
                    t_ = NSL * th + sl
                    if u < NE0:
                        cwc = cwtm0[:, NE0 * t_ + u:NE0 * t_ + u + 1]
                        P.op("v", lambda e, acs=acs, bk=bk, cwc=cwc:
                                e.scalar_tensor_tensor(
                                    acs, ps[:, 512 * bk:512 * (bk + 1)], cwc, acs,
                                    ALU.mult, ALU.add),
                             reads=[("ps", bk), ("acc", sl, n), ("cwl0", t_)],
                             writes=[("acc", sl, n)])
                    else:
                        P.op("v", lambda e, acs=acs, bk=bk: e.tensor_tensor(
                                acs, acs, ps[:, 512 * bk:512 * (bk + 1)], ALU.add),
                             reads=[("ps", bk), ("acc", sl, n)],
                             writes=[("acc", sl, n)])

        P.rekey([("xt", j_) for j_ in range(4)] + [("nt", j_) for j_ in range(4)],
                [("acc", sl, n) for sl in range(NSL) for n in range(2)])
        for th in range(2):
            for sl in range(NSL):
                for n in range(2):
                    P.op("v", lambda e, sl=sl, n=n: e.memset(
                            big[:, 1024 * sl + 512 * n:1024 * sl + 512 * (n + 1)], 0.0),
                         writes=[("acc", sl, n)])
            c_unit_dma(0)
            c_unit_dma(1)
            for u in range(18):
                c_unit_compute(u, th)
                if u + 2 < 18:
                    c_unit_dma(u + 2)
            for tt in range(NSL):
                t_ = NSL * th + tt
                P.op("s", lambda e, tt=tt: e.activation(
                        ntile[:, :], big[:, 1024 * tt:1024 * (tt + 1)], AF.Silu),
                     reads=[("acc", tt, 0), ("acc", tt, 1)], writes=[("nt",)])
                P.dma("la", xtile[:, :], x_in[8 + 128 * t_:8 + 128 * (t_ + 1), :],
                      writes=[("xt",)])
                P.dma("lb", stile[:, :], s_scr.ap()[128 * t_:128 * (t_ + 1), :],
                      reads=[("scr_s", t_)], writes=[("st",)])
                P.op("v", lambda e: e.tensor_tensor(ntile[:, :], stile[:, :],
                                                    ntile[:, :], ALU.mult),
                     reads=[("st",), ("nt",)], writes=[("nt",)])
                P.op("v", lambda e: e.tensor_tensor(xtile[:, :], xtile[:, :],
                                                    ntile[:, :], ALU.add),
                     reads=[("nt",), ("xt",)], writes=[("xt",)])
                P.dma("sa" if t_ % 2 == 0 else "sb",
                      x1_scr.ap()[128 * t_:128 * (t_ + 1), :], xtile[:, :],
                      reads=[("xt",)], writes=[("scr_x1", t_)])
        if "acc0" in taps:
            P.dma("sb", tap_aps["acc0"], big[:, 0:NSL * 1024],
                  reads=[("acc", sl, n) for sl in range(NSL) for n in range(2)])
        if "x1" in taps:
            P.dma("sa", tap_aps["x1"], x1_scr.ap(),
                  reads=[("scr_x1", t_) for t_ in range(NT)])

        if upto == "C":
            return _finish()
        P.rekey([("aux", "hA"), ("aux", "hB")], [("aux", "xf", 0), ("aux", "xf", 1), ("aux", "xf", 2), ("aux", "xf", 3)])
        P.rekey([("acc", sl, n) for sl in range(NSL) for n in range(2)],
                [("xt", j_) for j_ in range(4)] + [("nt", j_) for j_ in range(4)])
        ln_pass([(128, x1_scr.ap()[128 * t_:128 * (t_ + 1), :], 8 + 128 * t_,
                  ("l1", NE1), t_ * NE1) for t_ in range(NT)])
        if "logits1" in taps:
            P.dma("sb", tap_aps["logits1"], logit1[:, 0:NT * NE1], reads=[("l1",)])

        P.dma("la", rows2[0:1, 128:128 + NE1], bg1.unsqueeze(0), writes=[("r2", "bg")])
        P.op("t", lambda e: e.matmul(ps[:, 1024:1024 + NE1],
                                     rows2[0:1, 0:128],
                                     rows2[0:1, 128:128 + NE1],
                                     start=True, stop=True),
             reads=[("r2", "ones"), ("r2", "bg")], writes=[("ps", 2)])
        P.op("v", lambda e: e.tensor_copy(bgrow_sb, ps[:, 1024:1024 + NE1]),
             reads=[("ps", 2)], writes=[("bgB",)])

        for t_ in range(NT):
            emit_cw("l1", NE1, t_, True)
        if "cw1" in taps:
            P.dma("sb", tap_aps["cw1"], cwtm1[:, :],
                  reads=[("cwl1", t_) for t_ in range(NT)])

        if upto == "ln1":
            return _finish()
        P.dma("la", cwrows[R_CP:R_CP + 16, 0:DI], Cp, writes=[("cp",)])
        P.dma("lb", rows2[32:48, 0:128], bin_.rearrange("(c p) -> c p", p=128),
              reads=[("r2", "stage")], writes=[("r2", "stage")])
        P.op("t", lambda e: e.transpose(ps[:, 3584:3584 + 16],
                                        rows2[32:48, 0:128],
                                        identf_sb[32:48, 32:48]),
             reads=[("r2", "stage"), ("idf",)], writes=[("ps", 7)])
        P.op("v", lambda e: e.tensor_copy(smallv[:, SV_BIN:SV_BIN + 16],
                                          ps[:, 3584:3584 + 16]),
             reads=[("ps", 7)], writes=[("sv", "bin")])
        P.dma("lb", rows2[32:48, 0:128], Dp.rearrange("(c p) -> c p", p=128),
              reads=[("r2", "stage")], writes=[("r2", "stage")])
        P.op("t", lambda e: e.transpose(ps[:, 3584:3584 + 16],
                                        rows2[32:48, 0:128],
                                        identf_sb[32:48, 32:48]),
             reads=[("r2", "stage"), ("idf",)], writes=[("ps", 7)])
        P.op("v", lambda e: e.tensor_copy(smallv[:, SV_DP:SV_DP + 16],
                                          ps[:, 3584:3584 + 16]),
             reads=[("ps", 7)], writes=[("sv", "dp")])
        for c in range(16):
            P.dma("la" if c % 2 == 0 else "lb", xtile[:, 0:16],
                  Bp[128 * c:128 * (c + 1), :], reads=[("xt",)], writes=[("xt",)])
            P.op("t", lambda e, c=c: e.transpose(ps[:, 3584:3584 + 16],
                                                 cwrows[R_CP:R_CP + 16,
                                                        128 * c:128 * (c + 1)],
                                                 identf_sb[32:48, 32:48]),
                 reads=[("cp",), ("idf",)], writes=[("ps", 7)])
            P.op("v", lambda e: e.tensor_tensor(ntile[:, 0:16], xtile[:, 0:16],
                                                ps[:, 3584:3584 + 16], ALU.mult),
                 reads=[("xt",), ("ps", 7)], writes=[("nt",)])
            P.op("v", lambda e, c=c: e.reduce_sum(smallv[:, SV_R + c:SV_R + c + 1],
                                                  ntile[:, 0:16], AXX),
                 reads=[("nt",)], writes=[("sv", "r", c)])
        if "rvec" in taps:
            P.dma("sb", tap_aps["rvec"], smallv[:, SV_R:SV_R + 16],
                  reads=[("sv", "r", c) for c in range(16)])

        bias_bcast(bout, "la")

        P.rekey([("xt", j_) for j_ in range(4)] + [("nt", j_) for j_ in range(4)],
                [("bd", "xi"), ("bd", "c"), ("bd", "y"), ("bd", "dec")])
        xi_sb = big[:, 0:T]
        csc_sb = big[:, 2048:2048 + T]
        y_sb = big[:, 4096:4096 + T]
        dec_sb = big[:, 6144:6144 + T]
        P.op("v", lambda e: e.memset(dec_sb, DECAY), writes=[("bd", "dec")])

        P.rekey([("arw", "ug", 0), ("arw", "ug", 1),
                 ("arw", "uu", 0), ("arw", "uu", 1), ("arw", "ud", 0),
                 ("arw", "ud", 1)],
                [("arw", "win"), ("arw", "wout")])
        win_v = arw[:, 0:16384].rearrange("p (k n) -> p k n", k=8)
        wout_v = arw[:, 16384:32768].rearrange("p (k n) -> p k n", k=16)
        P.dma("wa", win_v[:, :, 0:1024],
              Win.rearrange("(k p) n -> p k n", p=128)[:, :, 0:1024],
              writes=[("arw", "win")])
        P.dma("wb", win_v[:, :, 1024:2048],
              Win.rearrange("(k p) n -> p k n", p=128)[:, :, 1024:2048],
              writes=[("arw", "win")])
        for hw_ in range(2):
            P.dma("wa", wout_v[:, 8 * hw_:8 * (hw_ + 1), :],
                  Wout.rearrange("(k p) n -> p k n", p=128)[:, 8 * hw_:8 * (hw_ + 1), :],
                  writes=[("arw", "wout")])

        NXI = T // 512
        for c in range(16):
            po_ = 2048 * (c % 2)
            bnks = [4 * (c % 2) + j for j in range(NXI)]

            def xmm(e, c=c, po_=po_):
                for n2 in range(NXI):
                    for k in range(8):
                        ins = e.matmul(ps[:, po_ + 512 * n2:po_ + 512 * (n2 + 1)],
                                       win_v[:, k, 128 * c:128 * (c + 1)],
                                       xkv[:, k, 8 + 512 * n2:8 + 512 * (n2 + 1)],
                                       start=(k == 0), stop=(k == 7))
                return ins
            P.op("t", xmm, reads=[("arw", "win"), ("xnT",)],
                 writes=[("ps", bnk) for bnk in bnks])
            P.op("s", lambda e, c=c, po_=po_: e.activation(
                    xi_sb, ps[:, po_:po_ + T], AF.Silu,
                    bias=smallv[:, SV_BIN + c:SV_BIN + c + 1], scale=1.0),
                 reads=[("ps", bnk) for bnk in bnks] + [("sv", "bin")],
                 writes=[("bd", "xi")])
            if "xi" in taps:
                P.dma("sb", tap_aps["xi"][c], xi_sb, reads=[("bd", "xi")])
            P.op("v", lambda e: e.tensor_tensor_scan(csc_sb, dec_sb, xi_sb, 0.0,
                                                     ALU.mult, ALU.add),
                 reads=[("bd", "dec"), ("bd", "xi")], writes=[("bd", "c")])
            P.op("v", lambda e, c=c: e.tensor_copy(smallv[:, SV_FR + c:SV_FR + c + 1],
                                                   csc_sb[:, T - 1:T]),
                 reads=[("bd", "c")], writes=[("sv", "f", c)])
            P.op("v", lambda e, c=c: e.tensor_scalar(y_sb, xi_sb,
                                                     smallv[:, SV_DP + c:
                                                            SV_DP + c + 1],
                                                     None, ALU.mult),
                 reads=[("bd", "xi"), ("sv", "dp")], writes=[("bd", "y")])
            P.op("v", lambda e, c=c: e.scalar_tensor_tensor(
                    y_sb, csc_sb, smallv[:, SV_R + c:SV_R + c + 1], y_sb,
                    ALU.mult, ALU.add),
                 reads=[("bd", "c"), ("sv", "r", c), ("bd", "y")],
                 writes=[("bd", "y")])
            P.dma("sa" if c % 2 == 0 else "sb", yp_scr.ap()[c], y_sb,
                  reads=[("bd", "y")], writes=[("scr_yp", c)])
        if "ypart" in taps:
            P.dma("sa", tap_aps["ypart"], yp_scr.ap(),
                  reads=[("scr_yp", c) for c in range(16)])

        P.op("t", lambda e: e.transpose(ps[0:16, 1024:1024 + 128],
                                        smallv[:, SV_FR:SV_FR + 16], identf_sb[:, :]),
             reads=[("sv", "f", c) for c in range(16)] + [("idf",)],
             writes=[("ps", 2)])
        P.op("v", lambda e: e.tensor_copy(cwrows[R_FT:R_FT + 16, 0:128],
                                          ps[0:16, 1024:1024 + 128]),
             reads=[("ps", 2)], writes=[("fT",)])
        P.dma("cc", cc_in.ap().rearrange("a (c p) -> (a c) p", p=128),
              cwrows[R_FT:R_FT + 16, 0:128], reads=[("fT",)], writes=[("ccin",)])
        P.op("g", lambda e: e.collective_compute(
                "AllGather", ALU.bypass, replica_groups=[list(range(world))],
                ins=[cc_in.ap()], outs=[cc_out.ap()]),
             reads=[("ccin",)], writes=[("ccout",)], sem="ccc")
        P.dma("cc", cwrows[R_FALL:R_FALL + world, 0:DI], cc_out.ap(),
              reads=[("ccout",)], writes=[("Fall",)])

        def fsel_mm(e):
            for n2 in range(DI // 512):
                ins = e.matmul(ps[0:1, 512 * n2:512 * (n2 + 1)], fsel_sb[0:world, :],
                               cwrows[R_FALL:R_FALL + world, 512 * n2:512 * (n2 + 1)],
                               start=True, stop=True)
            return ins
        P.op("t", fsel_mm, reads=[("Fall",), ("fsel",)],
             writes=[("ps", 0), ("ps", 1), ("ps", 2), ("ps", 3)])
        frow = auxf[0:1, 3584:3584 + DI]
        P.rekey([("aux", "xf", 0), ("aux", "xf", 1), ("aux", "xf", 2), ("aux", "xf", 3)], [("aux", "dpow"), ("aux", "ybf", 0),
                                  ("aux", "ybf", 1), ("aux", "ybL", 0),
                                  ("aux", "ybL", 1), ("aux", "frow")])
        P.op("v", lambda e: e.tensor_copy(frow, ps[0:1, 0:DI]),
             reads=[("ps", 0), ("ps", 1), ("ps", 2), ("ps", 3)],
             writes=[("aux", "frow")])

        P.dma("sb", fr_scr.ap(), frow, reads=[("aux", "frow")], writes=[("frs",)])
        P.dma("lb", rows2[32:48, 0:128],
              fr_scr.ap().rearrange("a (c p) -> (a c) p", p=128),
              reads=[("frs",), ("r2", "stage")], writes=[("r2", "stage")])
        P.op("t", lambda e: e.transpose(ps[:, 3584:3584 + 16],
                                        rows2[32:48, 0:128],
                                        identf_sb[32:48, 32:48]),
             reads=[("r2", "stage"), ("idf",)], writes=[("ps", 7)])
        P.op("v", lambda e: e.tensor_copy(smallv[:, SV_FP:SV_FP + 16],
                                          ps[:, 3584:3584 + 16]),
             reads=[("ps", 7)], writes=[("sv", "fp")])
        if "fprev" in taps:
            P.dma("sb", tap_aps["fprev"], smallv[:, SV_FP:SV_FP + 16],
                  reads=[("sv", "fp")])
        P.op("v", lambda e: e.tensor_tensor(smallv[:, SV_FP:SV_FP + 16],
                                            smallv[:, SV_FP:SV_FP + 16],
                                            smallv[:, SV_R:SV_R + 16], ALU.mult),
             reads=[("sv", "fp")] + [("sv", "r", c) for c in range(16)],
             writes=[("sv", "fp")])

        dpow_sb = auxf[:, 512:512 + T]
        ybfs = [aux[:, 0:1024], aux[:, 12288:13312]]
        ybLs = [auxf[:, 512 + T:512 + T + TH], auxf[:, 6656:6656 + TH]]
        P.dma("la", dpow_sb, dpow_in, writes=[("aux", "dpow")])

        def yb_load(th, c):
            p = c % 2
            P.dma("la" if c % 2 == 0 else "lb", ybLs[p],
                  yp_scr.ap()[c][:, TH * th:TH * (th + 1)],
                  reads=[("scr_yp", c), ("aux", "ybL", p)],
                  writes=[("aux", "ybL", p)])
            P.op("v", lambda e, c=c, th=th, p=p: e.scalar_tensor_tensor(
                    ybfs[p][:, 0:TH], dpow_sb[:, TH * th:TH * (th + 1)],
                    smallv[:, SV_FP + c:SV_FP + c + 1], ybLs[p],
                    ALU.mult, ALU.add),
                 reads=[("aux", "dpow"), ("sv", "fp"), ("aux", "ybL", p),
                        ("aux", "ybf", p)],
                 writes=[("aux", "ybf", p)])

        P.rekey([("bd", "xi"), ("bd", "c"), ("bd", "y"), ("bd", "dec")],
                [("s1acc", g) for g in range(NSL)])
        for th in range(2):
            for sl in range(NSL):
                gsl = th * NSL + sl
                P.op("v", lambda e, sl=sl: e.tensor_copy(
                        big[:, 1024 * sl:1024 * (sl + 1)], bb[:, :]),
                     reads=[("bb",)], writes=[("s1acc", sl)])
            yb_load(th, 0)
            for c in range(16):
                p = c % 2
                for sl in range(NSL):
                    for n in range(2):
                        bk = 4 + (sl * 2 + n) % 4
                        gsl = th * NSL + sl

                        def wmm(e, c=c, sl=sl, n=n, bk=bk, p=p):
                            return e.matmul(ps[:, 512 * bk:512 * (bk + 1)],
                                            ybfs[p][:, 128 * sl:128 * (sl + 1)],
                                            wout_v[:, c, 512 * n:512 * (n + 1)],
                                            start=True, stop=True)
                        P.op("t", wmm, reads=[("aux", "ybf", p), ("arw", "wout")],
                             writes=[("ps", bk)])
                        acs = big[:, 1024 * sl + 512 * n:1024 * sl + 512 * (n + 1)]
                        P.op("v", lambda e, acs=acs, bk=bk: e.tensor_tensor(
                                acs, acs, ps[:, 512 * bk:512 * (bk + 1)], ALU.add),
                             reads=[("ps", bk), ("s1acc", sl)],
                             writes=[("s1acc", sl)])
                        if sl == 0 and n == 0 and c + 1 < 16:
                            yb_load(th, c + 1)
            for sl in range(NSL):
                t_ = th * NSL + sl
                P.dma("sa" if t_ % 2 == 0 else "sb",
                      s_scr.ap()[128 * t_:128 * (t_ + 1), :],
                      big[:, 1024 * sl:1024 * (sl + 1)],
                      reads=[("s1acc", sl)], writes=[("scr_s", t_)])
        if "s1" in taps:
            P.dma("sa", tap_aps["s1"], s_scr.ap(),
                  reads=[("scr_s", t_) for t_ in range(NT)])

        if upto == "D":
            return _finish()
        P.rekey([("arw", "win"), ("arw", "wout")],
                [("arw", "w1", 0), ("arw", "w1", 1), ("arw", "w1", 2),
                 ("arw", "w1", 3), ("arw", "w2")])
        P.rekey([("aux", "dpow"), ("aux", "ybf", 0), ("aux", "ybf", 1),
                 ("aux", "ybL", 0), ("aux", "ybL", 1), ("aux", "frow")],
                [("hE", k) for k in range(16)])
        P.rekey([("s1acc", g) for g in range(NSL)],
                [("eacc", sl, n) for sl in range(NSL) for n in range(2)])
        w1s = [arw[:, 4096 * j:4096 * (j + 1)].rearrange("p (k n) -> p k n", k=8)
               for j in range(4)]
        w2v = arw[:, 16384:32768].rearrange("p (k n) -> p k n", k=16)
        hEv = aux[:, 0:16 * TH].rearrange("p (k n) -> p k n", k=16)

        for th in range(2):
            for sl in range(NSL):
                for n in range(2):
                    P.op("v", lambda e, sl=sl, n=n: e.memset(
                            big[:, 1024 * sl + 512 * n:1024 * sl + 512 * (n + 1)], 0.0),
                         writes=[("eacc", sl, n)])
            for eix in range(NE1):
                P.dma("la", rows2[32:64, 0:128],
                      b1[eix].rearrange("(c p) -> c p", p=128),
                      reads=[("r2", "stage")], writes=[("r2", "stage")])
                P.op("t", lambda e: e.transpose(ps[:, 3584:3584 + 32],
                                                rows2[32:64, 0:128],
                                                identf_sb[32:64, 32:64]),
                     reads=[("r2", "stage"), ("idf",)], writes=[("ps", 7)])
                P.op("v", lambda e: e.tensor_copy(smallv[:, SV_B1:SV_B1 + 32],
                                                  ps[:, 3584:3584 + 32]),
                     reads=[("ps", 7)], writes=[("sv", "b1")])
                bias_bcast(b2[eix], "lb")
                for hk in range(2):
                    for cc4 in range(4):
                        P.dma("wa", w1s[cc4],
                              W1[eix].rearrange("(k p) n -> p k n", p=128)
                              [:, :, 2048 * hk + 512 * cc4:2048 * hk + 512 * (cc4 + 1)],
                              writes=[("arw", "w1", cc4)])
                    for hw_ in range(2):
                        P.dma("wa", w2v[:, 8 * hw_:8 * (hw_ + 1), :],
                              W2[eix].rearrange("(k p) n -> p k n", p=128)
                              [:, 16 * hk + 8 * hw_:16 * hk + 8 * (hw_ + 1), :],
                              writes=[("arw", "w2")])
                    for cc4 in range(4):
                        cslot = cc4
                        for m in range(4):
                            mg = 4 * cc4 + m
                            for n2 in range(NW5):
                                bk = (0, 1, 6)[(2 * m + n2) % 3]

                                def hmm(e, m=m, n2=n2, bk=bk, cslot=cslot, th=th):
                                    for k in range(8):
                                        ins = e.matmul(
                                            ps[:, 512 * bk:512 * bk + W5],
                                            w1s[cslot][:, k, 128 * m:128 * (m + 1)],
                                            xkv[:, k, 8 + TH * th + W5 * n2:
                                                8 + TH * th + W5 * (n2 + 1)],
                                            start=(k == 0), stop=(k == 7))
                                    return ins
                                P.op("t", hmm, reads=[("arw", "w1", cslot), ("xnT",)],
                                     writes=[("ps", bk)])
                                hsl = hEv[:, mg, W5 * n2:W5 * (n2 + 1)]
                                bcol = SV_B1 + 16 * hk + mg
                                P.op("s", lambda e, hsl=hsl, bk=bk, bcol=bcol:
                                        e.activation(
                                        hsl, ps[:, 512 * bk:512 * bk + W5],
                                        AF.Gelu_apprx_tanh,
                                        bias=smallv[:, bcol:bcol + 1], scale=1.0),
                                     reads=[("ps", bk), ("sv", "b1")],
                                     writes=[("hE", mg)])
                    for sl in range(NSL):
                        for n in range(2):
                            bk = 2 + (sl * 2 + n) % 2

                            def emm(e, sl=sl, n=n, bk=bk):
                                for k in range(16):
                                    ins = e.matmul(ps[:, 512 * bk:512 * (bk + 1)],
                                                   hEv[:, k, 128 * sl:128 * (sl + 1)],
                                                   w2v[:, k, 512 * n:512 * (n + 1)],
                                                   start=(k == 0), stop=(k == 15))
                                return ins
                            P.op("t", emm,
                                 reads=[("hE", k) for k in range(16)] + [("arw", "w2")],
                                 writes=[("ps", bk)])
                            t_ = NSL * th + sl
                            acs = big[:, 1024 * sl + 512 * n:1024 * sl + 512 * (n + 1)]
                            cwc = cwtm1[:, NE1 * t_ + eix:NE1 * t_ + eix + 1]
                            P.op("v", lambda e, acs=acs, bk=bk, cwc=cwc:
                                    e.scalar_tensor_tensor(
                                        acs, ps[:, 512 * bk:512 * (bk + 1)], cwc, acs,
                                        ALU.mult, ALU.add),
                                 reads=[("ps", bk), ("eacc", sl, n), ("cwl1", t_)],
                                 writes=[("eacc", sl, n)])
                for sl in range(NSL):
                    t_ = NSL * th + sl
                    cwc = cwtm1[:, NE1 * t_ + eix:NE1 * t_ + eix + 1]
                    for n in range(2):
                        acs = big[:, 1024 * sl + 512 * n:1024 * sl + 512 * (n + 1)]
                        P.op("v", lambda e, acs=acs, n=n, cwc=cwc:
                                e.scalar_tensor_tensor(
                                    acs, bb[:, 512 * n:512 * (n + 1)], cwc, acs,
                                    ALU.mult, ALU.add),
                             reads=[("bb",), ("eacc", sl, n), ("cwl1", t_)],
                             writes=[("eacc", sl, n)])
            for tt in range(NSL):
                t_ = NSL * th + tt
                P.op("s", lambda e, tt=tt: e.activation(
                        ntile[:, :], big[:, 1024 * tt:1024 * (tt + 1)], AF.Silu),
                     reads=[("eacc", tt, 0), ("eacc", tt, 1)], writes=[("nt",)])
                P.dma("la", xtile[:, :], x1_scr.ap()[128 * t_:128 * (t_ + 1), :],
                      reads=[("scr_x1", t_)], writes=[("xt",)])
                P.dma("lb", stile[:, :], s_scr.ap()[128 * t_:128 * (t_ + 1), :],
                      reads=[("scr_s", t_)], writes=[("st",)])
                P.op("v", lambda e: e.tensor_tensor(ntile[:, :], stile[:, :],
                                                    ntile[:, :], ALU.mult),
                     reads=[("st",), ("nt",)], writes=[("nt",)])
                P.op("v", lambda e: e.tensor_tensor(xtile[:, :], xtile[:, :],
                                                    ntile[:, :], ALU.add),
                     reads=[("nt",), ("xt",)], writes=[("xt",)])
                P.dma("sa" if t_ % 2 == 0 else "sb",
                      out[128 * t_:128 * (t_ + 1), :], xtile[:, :],
                      reads=[("xt",)], writes=[("outw", t_)])

        return _finish()



def make_masks(q):
    i = np.arange(128)[:, None]
    j = np.arange(144)[None, :]
    interior = (j >= i) & (j <= i + 16)
    first = ((i <= 8) & (j >= 8) & (j <= i + 16)) | ((i > 8) & interior)
    last = ((128 - 1 - i <= 7) & (j >= i) & (j <= 135)) | ((i < 120) & interior)
    v1 = first if q == 0 else interior
    v2 = last if q == 3 else interior
    m = np.full((3, 128, 144), NEG, np.float32)
    for v, msk in enumerate((interior, v1, v2)):
        m[v][msk] = 0.0
    full = np.tile(m, (1, 1, 8))
    return full.astype(ml_dtypes.bfloat16)


def make_core_inputs(inputs, T=2048):
    x = np.asarray(inputs["L0_attn_x"])
    B, L, _ = x.shape
    percore = L // 4
    identf = np.eye(128, dtype=np.float32)
    identb = np.eye(128, dtype=ml_dtypes.bfloat16)
    dpow = np.tile((DECAY ** (np.arange(T, dtype=np.float64) + 1)
                    ).astype(np.float32)[None, :], (128, 1))
    wkeys = [k for k in inputs if k != "L0_attn_x"]
    in_maps = []
    for c in range(8):
        bsel, q = c // 4, c % 4
        lo, hi = q * percore, (q + 1) * percore
        xs = np.zeros((T + 16, D), np.float32)
        src_lo, src_hi = max(0, lo - 8), min(L, hi + 8)
        xs[8 - (lo - src_lo):8 + (src_hi - lo)] = x[bsel, src_lo:src_hi]
        fsel = np.zeros((8, 1), np.float32)
        if q > 0:
            fsel[c - 1, 0] = 1.0
        im = {k: np.ascontiguousarray(np.asarray(inputs[k])) for k in wkeys}
        im["x"] = xs
        im["identf"] = identf
        im["identb"] = identb
        im["maskp"] = np.ascontiguousarray(make_masks(q))
        im["dpow"] = dpow
        im["fsel"] = fsel
        in_maps.append(im)
    return in_maps


def assemble(results, T=2048):
    outs = [np.asarray(r["out"]) for r in results]
    return np.stack([np.concatenate(outs[0:4], 0),
                     np.concatenate(outs[4:8], 0)], 0)



def kernel(**inputs):
    from concourse.bass_utils import run_bass_kernel_spmd
    nc, _meta = build(T=2048)
    in_maps = make_core_inputs(inputs, T=2048)
    res = run_bass_kernel_spmd(nc, in_maps, core_ids=list(range(8)))
    return assemble(res.results, T=2048)


# version 08 (final)
# speedup vs baseline: 1.2245x; 1.0021x over previous
import contextlib
import numpy as np
import ml_dtypes
import concourse.bass as bass
import concourse.mybir as mybir

F32 = mybir.dt.float32
BF16 = mybir.dt.bfloat16
AF = mybir.ActivationFunctionType
ALU = mybir.AluOpType
AXX = mybir.AxisListType.X

D = 1024
NH = 16
HD = 64
ASCALE = HD ** (-0.5)
DI = 2048
NE0 = 16
F0 = 512
NE1 = 8
F1 = 4096
DECAY = 0.9
NEG = -30000.0

DMA_CHANS = {"wa": "g", "wb": "g", "wc": "s", "la": "y", "lb": "y", "sa": "y",
             "sb": "y", "cc": "g"}
ENG_SEM = {"t": "T", "v": "V", "s": "S", "g": "G", "y": "Y"}


class Prog:
    def __init__(self):
        self.ops = []
        self.cnt = {}
        self.writers = {}
        self.readers = {}
        self.waited = {}

    def _waits(self, eng, toks):
        best = {}
        for (sem, val) in toks:
            if eng == "t" and sem == "T":
                continue
            if val > best.get(sem, 0):
                best[sem] = val
        waits = []
        for sem, val in best.items():
            if self.waited.get((eng, sem), 0) < val:
                waits.append((sem, val))
                self.waited[(eng, sem)] = val
        return waits

    def op(self, eng, emit, reads=(), writes=(), sem=None, amt=1):
        sem = sem or ENG_SEM[eng]
        if eng in ("v", "s"):
            pbanks = {buf[1] for buf in tuple(reads) + tuple(writes)
                      if isinstance(buf, tuple) and buf and buf[0] == "ps"}
            if pbanks:
                writes = tuple(writes) + tuple(("plock", b) for b in sorted(pbanks))
        toks = []
        for buf in reads:
            toks += self.writers.get(buf, [])
        for buf in writes:
            toks += self.writers.get(buf, [])
            toks += self.readers.get(buf, [])
        waits = self._waits(eng, toks)
        self.cnt[sem] = self.cnt.get(sem, 0) + amt
        tok = (sem, self.cnt[sem])
        for buf in writes:
            self.writers[buf] = [tok]
            self.readers[buf] = []
        for buf in reads:
            self.readers.setdefault(buf, []).append(tok)
        self.ops.append((eng, waits, emit, sem, amt))
        return tok

    def dma(self, chan, out_ap, in_ap, reads=(), writes=()):
        return self.op(DMA_CHANS[chan],
                       lambda e: e.dma_start(out=out_ap, in_=in_ap),
                       reads=tuple(reads) + (("chan", chan),),
                       writes=tuple(writes) + (("chan", chan),),
                       sem=chan, amt=16)

    def rekey(self, old_keys, new_keys):
        toks = []
        for k in old_keys:
            toks += self.writers.pop(k, [])
            toks += self.readers.pop(k, [])
        for k in new_keys:
            self.writers[k] = self.writers.get(k, []) + toks

    def barrier(self, eng, bufs):
        toks = []
        for k in bufs:
            toks += self.writers.get(k, [])
            toks += self.readers.get(k, [])
        waits = self._waits(eng, toks)
        if waits:
            self.ops.append((eng, waits, lambda e: None, ENG_SEM[eng], 0))

    def emit_all(self, blk, sems):
        streams = {}
        for rec in self.ops:
            streams.setdefault(rec[0], []).append(rec)

        def run(e, engname):
            for (_eng, waits, emit, sem, amt) in streams[engname]:
                for (s_, v_) in waits:
                    e.wait_ge(sems[s_], v_)
                ins = emit(e)
                if ins is not None and amt > 0:
                    ins.then_inc(sems[sem], amt)

        for name, meth in (("t", blk.tensor), ("v", blk.vector), ("s", blk.scalar),
                           ("g", blk.gpsimd), ("y", blk.sync)):
            if name in streams:
                meth(lambda e, name=name: run(e, name))


class _Done(Exception):
    pass


def build(T=2048, taps=(), world=8, upto="E"):
    assert T % 512 == 0
    NT = T // 128
    NST = T // 512
    TH = T // 2
    NSL = TH // 128
    TW = T + 16
    W5 = min(512, TH)
    NW5 = max(1, TH // 512)

    nc = bass.Bass("TRN2", target_bir_lowering=False)
    dt = nc.dram_tensor

    x_in = dt("x", [TW, D], F32, kind="ExternalInput").ap()
    Wqkv = dt("L0_attn_Wqkv", [D, 3 * D], F32, kind="ExternalInput").ap()
    Wproj = dt("L0_attn_Wproj", [D, D], F32, kind="ExternalInput").ap()
    bproj = dt("L0_attn_bproj", [D], F32, kind="ExternalInput").ap()
    Wr = dt("L0_ffn_Wr", [D, NE0], F32, kind="ExternalInput").ap()
    Wg0 = dt("L0_ffn_Wg", [NE0, D, F0], F32, kind="ExternalInput").ap()
    Wu0 = dt("L0_ffn_Wu", [NE0, D, F0], F32, kind="ExternalInput").ap()
    Wd0 = dt("L0_ffn_Wd", [NE0, F0, D], F32, kind="ExternalInput").ap()
    Sg = dt("L0_ffn_Sg", [D, D], F32, kind="ExternalInput").ap()
    Su = dt("L0_ffn_Su", [D, D], F32, kind="ExternalInput").ap()
    Sd = dt("L0_ffn_Sd", [D, D], F32, kind="ExternalInput").ap()
    Win = dt("L1_attn_Win", [D, DI], F32, kind="ExternalInput").ap()
    bin_ = dt("L1_attn_bin_", [DI], F32, kind="ExternalInput").ap()
    Bp = dt("L1_attn_Bp", [DI, 16], F32, kind="ExternalInput").ap()
    Cp = dt("L1_attn_Cp", [16, DI], F32, kind="ExternalInput").ap()
    Dp = dt("L1_attn_Dp", [DI], F32, kind="ExternalInput").ap()
    Wout = dt("L1_attn_Wout", [DI, D], F32, kind="ExternalInput").ap()
    bout = dt("L1_attn_bout", [D], F32, kind="ExternalInput").ap()
    Wg1 = dt("L1_ffn_Wg", [D, NE1], F32, kind="ExternalInput").ap()
    bg1 = dt("L1_ffn_bg", [NE1], F32, kind="ExternalInput").ap()
    W1 = dt("L1_ffn_W1", [NE1, D, F1], F32, kind="ExternalInput").ap()
    b1 = dt("L1_ffn_b1", [NE1, F1], F32, kind="ExternalInput").ap()
    W2 = dt("L1_ffn_W2", [NE1, F1, D], F32, kind="ExternalInput").ap()
    b2 = dt("L1_ffn_b2", [NE1, D], F32, kind="ExternalInput").ap()
    identf = dt("identf", [128, 128], F32, kind="ExternalInput").ap()
    identb = dt("identb", [128, 128], BF16, kind="ExternalInput").ap()
    maskp = dt("maskp", [3, 128, 1152], BF16, kind="ExternalInput").ap()
    dpow_in = dt("dpow", [128, T], F32, kind="ExternalInput").ap()
    fsel_in = dt("fsel", [world, 1], F32, kind="ExternalInput").ap()

    out = dt("out", [T, D], F32, kind="ExternalOutput").ap()
    tap_shapes = {"xnT": ([128, 8, TW], BF16), "logits0": ([128, NT * NE0], F32),
                  "logits1": ([128, NT * NE1], F32),
                  "s0": ([T, D], F32), "x1": ([T, D], F32),
                  "acc0": ([128, NSL * D], F32), "xi": ([16, 128, T], F32),
                  "ypart": ([16, 128, T], F32), "s1": ([T, D], F32),
                  "cw0": ([128, NT * NE0], F32), "cw1": ([128, NT * NE1], F32),
                  "fprev": ([128, 16], F32), "rvec": ([128, 16], F32)}
    tap_aps = {nm: dt("tap_" + nm, shp, dty, kind="ExternalOutput").ap()
               for nm, (shp, dty) in tap_shapes.items() if nm in taps}

    s_scr = dt("s_scr", [T, D], F32)
    x1_scr = dt("x1_scr", [T, D], F32)
    yp_scr = dt("yp_scr", [16, 128, T], F32)
    cc_in = dt("cc_in", [1, DI], F32)
    cc_out = dt("cc_out", [world, DI], F32)
    fr_scr = dt("fr_scr", [1, DI], F32)

    with contextlib.ExitStack() as es:
        E = es.enter_context
        xnT = E(nc.sbuf_tensor("xnT", [128, 8 * TW], BF16))
        arw = E(nc.sbuf_tensor("arw", [128, 32768], BF16))
        big = E(nc.sbuf_tensor("big", [128, 8192], F32))
        aux = E(nc.sbuf_tensor("aux", [128, 16896], BF16))
        xtile = E(nc.sbuf_tensor("xtile", [128, D], F32))
        stile = E(nc.sbuf_tensor("stile", [128, D], F32))
        ntile = E(nc.sbuf_tensor("ntile", [128, D], F32))
        masks = E(nc.sbuf_tensor("masks", [128, 3 * 1152], BF16))
        stats = E(nc.sbuf_tensor("stats", [128, 64], F32))
        logit0 = E(nc.sbuf_tensor("logit0", [128, NT * NE0], F32))
        logit1 = E(nc.sbuf_tensor("logit1", [128, NT * NE1], F32))
        cwtm0 = E(nc.sbuf_tensor("cwtm0", [128, NT * NE0], F32))
        cwtm1 = E(nc.sbuf_tensor("cwtm1", [128, NT * NE1], F32))
        cwrows = E(nc.sbuf_tensor("cwrows", [128, 2048], F32))
        rows2 = E(nc.sbuf_tensor("rows2", [128, 1536], F32))
        bb = E(nc.sbuf_tensor("bb", [128, 1024], F32))
        smallv = E(nc.sbuf_tensor("smallv", [128, 176], F32))
        wrg = E(nc.sbuf_tensor("wrg", [128, 192], F32))
        tgt = E(nc.sbuf_tensor("tgt", [128, 512], BF16))
        identf_sb = E(nc.sbuf_tensor("identf_sb", [128, 128], F32))
        identb_sb = E(nc.sbuf_tensor("identb_sb", [128, 128], BF16))
        fsel_sb = E(nc.sbuf_tensor("fsel_sb", [world, 1], F32))
        ps = E(nc.psum_tensor("ps", [128, 4096], F32))

        psb = ps[:, :].bitcast(BF16)
        auxf = aux[:, :].bitcast(F32)
        blk = E(nc.Block())
        P = Prog()

        SV_BIN, SV_DP, SV_R, SV_FP, SV_FR, SV_B1, SV_CW = 0, 16, 32, 48, 64, 96, 128
        ST_NM, ST_SQ, ST_IV, ST_M1, ST_M2, ST_DEN = 0, 1, 2, 3, 4, 5
        R_CP, R_FT, R_FALL = 32, 96, 0
        bgrow_sb = stats[:, 48:56]

        def _finish():
            sem_names = set(list(P.cnt.keys()) + ["T", "V", "S", "G", "Y"])
            P.barrier("y", [k for k in P.writers if k[0] == "outw"]
                      + [("chan", c) for c in ("sa", "sb", "wa", "wb", "wc",
                                               "la", "lb", "cc")
                         if P.cnt.get(c)])
            sems = {k: E(nc.semaphore("sem_" + k)) for k in sorted(sem_names)}
            P.emit_all(blk, sems)
            return nc, dict(tap_aps=tap_aps, n_ops=len(P.ops))

        P.dma("la", identf_sb[:, :], identf, writes=[("idf",)])
        P.dma("lb", identb_sb[:, :], identb, writes=[("idb",)])
        P.dma("la", masks[:, :].rearrange("p (v n) -> p v n", v=3),
              maskp.rearrange("v p n -> p v n"), writes=[("masks",)])
        P.dma("lb", fsel_sb[:, :], fsel_in, writes=[("fsel",)])
        P.op("v", lambda e: e.memset(rows2[0:1, 0:128], 1.0), writes=[("r2", "ones")])
        for k in range(8):
            P.dma("la", wrg[:, 16 * k:16 * (k + 1)], Wr[128 * k:128 * (k + 1), :],
                  writes=[("wrg", 0)])
            P.dma("lb", wrg[:, 128 + 8 * k:128 + 8 * (k + 1)],
                  Wg1[128 * k:128 * (k + 1), :], writes=[("wrg", 1)])

        def bias_bcast(src_dram, chan="la"):
            P.dma(chan, rows2[0:1, 512:1536], src_dram.unsqueeze(0) if src_dram.ndim == 1 else src_dram,
                  reads=[("r2", "bstage")], writes=[("r2", "bstage")])

            def mm(e):
                for n2 in range(2):
                    ins = e.matmul(ps[:, 2048 + 512 * n2:2048 + 512 * (n2 + 1)],
                                   rows2[0:1, 0:128],
                                   rows2[0:1, 512 + 512 * n2:512 + 512 * (n2 + 1)],
                                   start=True, stop=True)
                return ins
            P.op("t", mm, reads=[("r2", "ones"), ("r2", "bstage")],
                 writes=[("ps", 4), ("ps", 5)])
            P.op("v", lambda e: e.tensor_copy(bb[:, :], ps[:, 2048:3072]),
                 reads=[("ps", 4), ("ps", 5)], writes=[("bb",)])

        LN_CH = ("la", "lb", "sa", "sb")

        def ln_dma(chunk):
            for j, (rows, xsrc, dst_off, lg_kind, lg_col) in enumerate(chunk):
                P.dma(LN_CH[j], big[0:rows, 1024 * j:1024 * j + 1024], xsrc,
                      reads=[("xt", j)], writes=[("xt", j)])

        def ln_pass(specs):
            chunks = [specs[i:i + 4] for i in range(0, len(specs), 4)]
            xv = xnT[:, :].rearrange("p (k n) -> p k n", k=8)
            ln_dma(chunks[0])
            for ci, ch in enumerate(chunks):
                for j, (rows, xsrc, dst_off, lg_kind, lg_col) in enumerate(ch):
                    xt = big[:, 1024 * j:1024 * j + 1024]
                    so = 8 * j
                    P.op("v", lambda e, rows=rows, so=so, xt=xt: e.reduce_sum(
                            stats[0:rows, so + ST_NM:so + ST_NM + 1],
                            xt[0:rows, :], AXX),
                         reads=[("xt", j)], writes=[("stats", "nm", j)])
                    P.op("s", lambda e, rows=rows, so=so, xt=xt: e.activation(
                            stile[0:rows, :], xt[0:rows, :], AF.Square,
                            accum_out=stats[0:rows, so + ST_SQ:so + ST_SQ + 1]),
                         reads=[("xt", j)], writes=[("st",), ("stats", "sq", j)])
                for j, (rows, xsrc, dst_off, lg_kind, lg_col) in enumerate(ch):
                    so = 8 * j
                    P.op("v", lambda e, rows=rows, so=so: e.tensor_scalar(
                            stats[0:rows, so + ST_NM:so + ST_NM + 1],
                            stats[0:rows, so + ST_NM:so + ST_NM + 1],
                            -1.0 / D, None, ALU.mult),
                         reads=[("stats", "nm", j)], writes=[("stats", "nm", j)])
                    P.op("v", lambda e, rows=rows, so=so: e.tensor_tensor(
                            stats[0:rows, so + ST_M1:so + ST_M1 + 1],
                            stats[0:rows, so + ST_NM:so + ST_NM + 1],
                            stats[0:rows, so + ST_NM:so + ST_NM + 1], ALU.mult),
                         reads=[("stats", "nm", j)], writes=[("stats", "m1", j)])
                    P.op("v", lambda e, rows=rows, so=so: e.tensor_scalar(
                            stats[0:rows, so + ST_IV:so + ST_IV + 1],
                            stats[0:rows, so + ST_SQ:so + ST_SQ + 1],
                            1.0 / D, 1e-5, ALU.mult, ALU.add),
                         reads=[("stats", "sq", j)], writes=[("stats", "iv", j)])
                    P.op("v", lambda e, rows=rows, so=so: e.tensor_tensor(
                            stats[0:rows, so + ST_IV:so + ST_IV + 1],
                            stats[0:rows, so + ST_IV:so + ST_IV + 1],
                            stats[0:rows, so + ST_M1:so + ST_M1 + 1], ALU.subtract),
                         reads=[("stats", "iv", j), ("stats", "m1", j)],
                         writes=[("stats", "iv", j)])
                for j, (rows, xsrc, dst_off, lg_kind, lg_col) in enumerate(ch):
                    so = 8 * j
                    P.op("s", lambda e, rows=rows, so=so: e.activation(
                            stats[0:rows, so + ST_IV:so + ST_IV + 1],
                            stats[0:rows, so + ST_IV:so + ST_IV + 1], AF.Sqrt),
                         reads=[("stats", "iv", j)], writes=[("stats", "iv", j)])
                for j, (rows, xsrc, dst_off, lg_kind, lg_col) in enumerate(ch):
                    so = 8 * j
                    P.op("v", lambda e, rows=rows, so=so: e.reciprocal(
                            stats[0:rows, so + ST_IV:so + ST_IV + 1],
                            stats[0:rows, so + ST_IV:so + ST_IV + 1]),
                         reads=[("stats", "iv", j)], writes=[("stats", "iv", j)])
                for j, (rows, xsrc, dst_off, lg_kind, lg_col) in enumerate(ch):
                    so = 8 * j
                    xt = big[:, 1024 * j:1024 * j + 1024]
                    nt = big[:, 4096 + 1024 * j:4096 + 1024 * j + 1024]
                    P.op("v", lambda e, rows=rows, so=so, xt=xt, nt=nt:
                            e.tensor_scalar(
                            nt[0:rows, :], xt[0:rows, :],
                            stats[0:rows, so + ST_NM:so + ST_NM + 1],
                            stats[0:rows, so + ST_IV:so + ST_IV + 1],
                            ALU.add, ALU.mult),
                         reads=[("xt", j), ("stats", "nm", j), ("stats", "iv", j)],
                         writes=[("nt", j)])
                if ci + 1 < len(chunks):
                    ln_dma(chunks[ci + 1])
                for j, (rows, xsrc, dst_off, lg_kind, lg_col) in enumerate(ch):
                    nt = big[:, 4096 + 1024 * j:4096 + 1024 * j + 1024]
                    pcol = 1024 * j

                    def tr(e, rows=rows, nt=nt, pcol=pcol):
                        for k in range(8):
                            col = pcol + 128 * k
                            ins = e.transpose(ps[:, col:col + rows],
                                              nt[0:rows, 128 * k:128 * (k + 1)],
                                              identf_sb[0:rows, 0:rows])
                        return ins
                    P.op("t", tr, reads=[("nt", j), ("idf",)],
                         writes=[("ps", 2 * j), ("ps", 2 * j + 1)])
                    psv8 = ps[:, pcol:pcol + 1024].rearrange("p (k n) -> p k n", k=8)
                    P.op("v", lambda e, rows=rows, dst_off=dst_off, psv8=psv8:
                            e.tensor_copy(xv[:, :, dst_off:dst_off + rows],
                                          psv8[:, :, 0:rows]),
                         reads=[("ps", 2 * j), ("ps", 2 * j + 1)], writes=[("xnT",)])
                    if lg_kind is None:
                        continue
                    axo = 1024 * j
                    P.op("s", lambda e, axo=axo, pcol=pcol: e.copy(
                            auxf[:, axo:axo + 1024], ps[:, pcol:pcol + 1024]),
                         reads=[("ps", 2 * j), ("ps", 2 * j + 1)],
                         writes=[("aux", "xf", j)])
                for j, (rows, xsrc, dst_off, lg_kind, lg_col) in enumerate(ch):
                    if lg_kind is None:
                        continue
                    which, ne = lg_kind
                    skip_mm = which.startswith("skip")
                    which = which.replace("skip", "")
                    wcol = 0 if which == "l0" else 128
                    lg_sb = logit0 if which == "l0" else logit1
                    axo = 1024 * j
                    pcol = 1024 * j
                    if skip_mm:
                        continue

                    def rmm(e, ne=ne, axo=axo, pcol=pcol, wcol=wcol):
                        for k in range(8):
                            ins = e.matmul(ps[:, pcol:pcol + ne],
                                           auxf[:, axo + 128 * k:axo + 128 * (k + 1)],
                                           wrg[:, wcol:wcol + 8 * ne]
                                           .rearrange("p (k j) -> p k j", k=8)[:, k, :],
                                           start=(k == 0), stop=(k == 7))
                        return ins
                    P.op("t", rmm, reads=[("aux", "xf", j),
                                          ("wrg", 0 if which == "l0" else 1)],
                         writes=[("ps", 2 * j)])
                    P.op("v", lambda e, lg_sb=lg_sb, lg_col=lg_col, ne=ne, pcol=pcol:
                            e.tensor_copy(lg_sb[:, lg_col:lg_col + ne],
                                          ps[:, pcol:pcol + ne]),
                         reads=[("ps", 2 * j)], writes=[(which,)])

        lgk = (None if upto == "A0" else
               (("skipl0", NE0) if upto == "A1" else ("l0", NE0)))
        ln_pass([(8, x_in[0:8, :], 0, None, 0)]
                + [(128, x_in[8 + 128 * t_:8 + 128 * (t_ + 1), :], 8 + 128 * t_,
                    lgk, t_ * NE0) for t_ in range(NT)]
                + [(8, x_in[8 + T:16 + T, :], 8 + T, None, 0)])

        if "xnT" in taps:
            P.dma("sa", tap_aps["xnT"], xnT[:, :].rearrange("p (k n) -> p k n", k=8),
                  reads=[("xnT",)])
        if "logits0" in taps:
            P.dma("sb", tap_aps["logits0"], logit0[:, 0:NT * NE0], reads=[("l0",)])

        if upto in ("A", "A0", "A1"):
            return _finish()
        WQKV, WPROJ = 0, 24576
        wq_v = arw[:, WQKV:WQKV + 24576].rearrange("p (k n) -> p k n", k=8)
        wp_v = arw[:, WPROJ:WPROJ + 8192].rearrange("p (k n) -> p k n", k=8)
        for j, ch in ((0, "wa"), (1, "wb"), (2, "wa")):
            P.dma(ch, wq_v[:, :, 1024 * j:1024 * (j + 1)],
                  Wqkv.rearrange("(k p) n -> p k n", p=128)[:, :, 1024 * j:1024 * (j + 1)],
                  writes=[("arw", "qkv", j)])
        P.dma("wb", wp_v, Wproj.rearrange("(k p) n -> p k n", p=128),
              writes=[("arw", "proj")])
        bias_bcast(bproj, "la")

        AQ, AK, AV_, AT2, AA, AO = 0, 4096, 8320, 13440, 14464, 15616
        qv = aux[:, AQ:AQ + 4096].rearrange("p (k n) -> p k n", k=8)
        kv = aux[:, AK:AK + 4224].rearrange("p (k n) -> p k n", k=8)
        vv = aux[:, AV_:AV_ + 5120].rearrange("p (j n) -> p j n", j=5)
        at2sb = aux[0:16, AT2:AT2 + 1024]
        asb = aux[:, AA:AA + 1152]
        at1sb = aux[:, AA:AA + 1024]
        osb = aux[:, AO:AO + 1024]
        xkv = xnT[:, :].rearrange("p (k n) -> p k n", k=8)
        mk3 = masks[:, :].rearrange("p (v s x) -> p v s x", v=3, s=8)
        AUXB = ([("auxq", m) for m in range(8)] + [("auxk", m) for m in range(8)]
                + [("auxv", j) for j in range(5)]
                + [("auxa", h) for h in range(8)] + [("auxat1",), ("auxat2",),
                   ("auxo", 0), ("auxo", 1)])
        P.rekey([("aux", "xf", 0), ("aux", "xf", 1), ("aux", "xf", 2), ("aux", "xf", 3)], AUXB)

        BLV = {"Bs0": -1, "Bs": 0, "Bsm": 1, "Bat": 2, "Bav": 3}.get(upto, 4)
        for st in range(NST):
            s0c = 512 * st
            for m in range(8):
                bk = m % 2

                def qmm(e, m=m, bk=bk, s0c=s0c):
                    for k in range(8):
                        ins = e.matmul(ps[:, 512 * bk:512 * bk + 512],
                                       wq_v[:, k, 128 * m:128 * (m + 1)],
                                       xkv[:, k, 8 + s0c:8 + s0c + 512],
                                       start=(k == 0), stop=(k == 7))
                    return ins
                P.op("t", qmm, reads=[("arw", "qkv", 0), ("xnT",)], writes=[("ps", bk)])
                P.op("s", lambda e, m=m, bk=bk: e.activation(
                        qv[:, m, :], ps[:, 512 * bk:512 * bk + 512], AF.Copy,
                        scale=ASCALE),
                     reads=[("ps", bk)], writes=[("auxq", m)])
            for m in range(8):
                bk = m % 2

                def kmm(e, m=m, bk=bk, s0c=s0c):
                    for k in range(8):
                        e.matmul(ps[:, 512 * bk:512 * bk + 512],
                                 wq_v[:, k, 1024 + 128 * m:1024 + 128 * (m + 1)],
                                 xkv[:, k, s0c:s0c + 512],
                                 start=(k == 0), stop=(k == 7))
                    for k in range(8):
                        ins = e.matmul(ps[:, 2048 + 16 * bk:2048 + 16 * (bk + 1)],
                                       wq_v[:, k, 1024 + 128 * m:1024 + 128 * (m + 1)],
                                       xkv[:, k, s0c + 512:s0c + 528],
                                       start=(k == 0), stop=(k == 7))
                    return ins
                P.op("t", kmm, reads=[("arw", "qkv", 1), ("xnT",)],
                     writes=[("ps", bk), ("ps", 4)])

                def kev(e, m=m, bk=bk):
                    e.copy(kv[:, m, 0:512], ps[:, 512 * bk:512 * bk + 512])
                    return e.copy(kv[:, m, 512:528],
                                  ps[:, 2048 + 16 * bk:2048 + 16 * (bk + 1)])
                P.op("s", kev, reads=[("ps", bk), ("ps", 4)], writes=[("auxk", m)])
            for j in range(5):
                rj = 128 if (j < 4 or st < NST - 1) else 16

                def vmm(e, j=j, s0c=s0c, rj=rj):
                    for n2 in range(2):
                        for k in range(8):
                            ins = e.matmul(ps[0:rj, 512 * n2:512 * (n2 + 1)],
                                           xkv[:, k, s0c + 128 * j:s0c + 128 * j + rj],
                                           wq_v[:, k, 2048 + 512 * n2:2048 + 512 * (n2 + 1)],
                                           start=(k == 0), stop=(k == 7))
                    return ins
                P.op("t", vmm, reads=[("arw", "qkv", 2), ("xnT",)],
                     writes=[("ps", 0), ("ps", 1)])
                P.op("v", lambda e, j=j, rj=rj: e.tensor_copy(vv[0:rj, j, :],
                                                              ps[0:rj, 0:1024]),
                     reads=[("ps", 0), ("ps", 1)], writes=[("auxv", j)])

            if upto == "Bq":
                continue
            for tl in range(4):
                t_ = 4 * st + tl
                mvar = 1 if t_ == 0 else (2 if t_ == NT - 1 else 0)
                for hg in range(2):
                    def smm(e, tl=tl, hg=hg):
                        for hh in range(8):
                            h = 8 * hg + hh
                            po, chn = 64 * (h % 2), h // 2
                            s_ = (hh % 2) * 4 + hh // 2
                            ins = e.matmul(
                                ps[:, 256 * s_:256 * s_ + 144],
                                qv[po:po + 64, chn, 128 * tl:128 * (tl + 1)],
                                kv[po:po + 64, chn, 128 * tl:128 * tl + 144],
                                start=True, stop=True)
                        return ins
                    P.op("t", smm,
                         reads=[("auxq", m) for m in range(8)]
                               + [("auxk", m) for m in range(8)],
                         writes=[("ps", 0), ("ps", 1), ("ps", 2), ("ps", 3)])
                    if BLV < 0:
                        continue
                    if upto == "Bnm":
                        def mka(e, mvar=mvar):
                            for hh in range(8):
                                ins = e.tensor_tensor(
                                    ps[:, 256 * hh:256 * hh + 144],
                                    ps[:, 256 * hh:256 * hh + 144],
                                    mk3[:, mvar, hh, :], ALU.add)
                            return ins
                        P.op("v", mka,
                             reads=[("ps", 0), ("ps", 1), ("ps", 2), ("ps", 3),
                                    ("masks",)],
                             writes=[("ps", 0), ("ps", 1), ("ps", 2), ("ps", 3)])
                    else:
                        sbs = big[:, 4096 + 1152 * hg:4096 + 1152 * (hg + 1)]
                        sbs4 = sbs.rearrange("p (b a x) -> p a b x", b=4, a=2)
                        sbs3 = sbs.rearrange("p (h x) -> p h x", h=8)
                        psv4 = ps[:, 0:2048].rearrange("p (a b x) -> p a b x",
                                                       a=2, b=4)
                        mk4 = masks[:, :].rearrange("p (v a b x) -> p v a b x",
                                                    v=3, a=2, b=4)
                        P.op("v", lambda e, sbs4=sbs4, psv4=psv4, mvar=mvar,
                                mk4=mk4: e.tensor_tensor(
                                sbs4, psv4[:, :, :, 0:144],
                                mk4[:, mvar], ALU.add),
                             reads=[("ps", 0), ("ps", 1), ("ps", 2), ("ps", 3),
                                    ("masks",)],
                             writes=[("sbs", hg)])
                    av8 = asb[:, :].rearrange("p (sl x) -> p sl x", sl=8)
                    if BLV < 1:
                        continue
                    P.op("v", lambda e, sbs3=sbs3: e.reduce_max(
                            stats[:, 32:40], sbs3, AXX),
                         reads=[("sbs", hg)], writes=[("stats", "nx")])
                    P.op("v", lambda e: e.reduce_max(stats[:, 40:41],
                                                     stats[:, 32:40], AXX,
                                                     negate=True),
                         reads=[("stats", "nx")], writes=[("stats", "gm")])
                    P.op("s", lambda e, av8=av8, sbs3=sbs3: e.activation(
                            av8, sbs3, AF.Exp,
                            bias=stats[:, 40:41], scale=1.0),
                         reads=[("sbs", hg), ("stats", "gm")],
                         writes=[("auxa", h) for h in range(8)])
                    P.op("v", lambda e, av8=av8: e.reduce_sum(
                            stats[:, 56:64], av8, AXX),
                         reads=[("auxa", h) for h in range(8)],
                         writes=[("stats", "l")])
                    P.op("v", lambda e: e.reciprocal(stats[:, 56:64], stats[:, 56:64]),
                         reads=[("stats", "l")], writes=[("stats", "l")])
                    for hh in range(8):
                        P.op("v", lambda e, hh=hh, av8=av8: e.tensor_scalar(
                                av8[:, hh, :], av8[:, hh, :],
                                stats[:, 56 + hh:57 + hh], None, ALU.mult),
                             reads=[("auxa", hh), ("stats", "l")],
                             writes=[("auxa", hh)])

                    if BLV < 2:
                        continue

                    def atr(e, av8=av8):
                        for hh in range(8):
                            e.transpose(psb[:, 4096 + 128 * hh:4096 + 128 * (hh + 1)],
                                        av8[:, hh, 0:128], identb_sb[:, :])
                            ins = e.transpose(
                                psb[0:16, 5120 + 128 * hh:5120 + 128 * (hh + 1)],
                                av8[:, hh, 128:144], identb_sb[:, :])
                        return ins
                    P.op("t", atr,
                         reads=[("auxa", h) for h in range(8)] + [("idb",)],
                         writes=[("ps", 4), ("ps", 5)])
                    P.op("v", lambda e: e.tensor_copy(at1sb[:, :], psb[:, 4096:5120]),
                         reads=[("ps", 4)] + [("auxa", h) for h in range(8)],
                         writes=[("auxat1",)])
                    P.op("s", lambda e: e.copy(at2sb[:, :], psb[0:16, 5120:6144]),
                         reads=[("ps", 5)], writes=[("auxat2",)])
                    if BLV < 3:
                        continue

                    def avmm(e, tl=tl, hg=hg):
                        for hh in range(8):
                            h = 8 * hg + hh
                            po = 64 * (hh % 2)
                            c2 = hh // 2
                            e.matmul(ps[po:po + 64,
                                        3072 + 128 * c2:3072 + 128 * (c2 + 1)],
                                     vv[:, tl, 64 * h:64 * (h + 1)],
                                     at1sb[:, 128 * hh:128 * (hh + 1)],
                                     start=(hh <= 1), stop=False,
                                     skip_group_check=(hh > 1))
                            ins = e.matmul(
                                ps[po:po + 64, 3072 + 128 * c2:3072 + 128 * (c2 + 1)],
                                vv[0:16, tl + 1, 64 * h:64 * (h + 1)],
                                at2sb[:, 128 * hh:128 * (hh + 1)],
                                start=False, stop=(hh >= 6),
                                skip_group_check=(hh < 6))
                        return ins
                    P.op("t", avmm,
                         reads=[("auxat1",), ("auxat2",), ("auxv", tl),
                                ("auxv", tl + 1)],
                         writes=[("ps", 6)])
                    P.op("v", lambda e, hg=hg: e.tensor_copy(
                            osb[:, 512 * hg:512 * (hg + 1)], ps[:, 3072:3584]),
                         reads=[("ps", 6)],
                         writes=[("auxo", hg)])
                if BLV < 4:
                    continue
                ov = osb.rearrange("p (c n) -> p c n", c=8)
                for n2 in range(2):
                    def pmm(e, n2=n2):
                        for k in range(8):
                            ins = e.matmul(ps[:, 512 * n2:512 * (n2 + 1)],
                                           ov[:, k, :],
                                           wp_v[:, k, 512 * n2:512 * (n2 + 1)],
                                           start=(k == 0), stop=(k == 7))
                        return ins
                    P.op("t", pmm, reads=[("auxo", 0), ("auxo", 1), ("arw", "proj")],
                         writes=[("ps", n2)])
                P.op("v", lambda e: e.tensor_tensor(stile[:, :], ps[:, 0:1024],
                                                    bb[:, :], ALU.add),
                     reads=[("ps", 0), ("ps", 1), ("bb",)], writes=[("st",)])
                P.dma("sa" if t_ % 2 == 0 else "sb",
                      s_scr.ap()[128 * t_:128 * (t_ + 1), :], stile[:, :],
                      reads=[("st",)], writes=[("scr_s", t_)])

        if "s0" in taps:
            P.dma("sa", tap_aps["s0"], s_scr.ap(),
                  reads=[("scr_s", t_) for t_ in range(NT)])

        if upto in ("B", "Bq", "Bnm", "Bs", "Bs0", "Bsm", "Bat", "Bav"):
            return _finish()
        def emit_cw(which, ne, tile, normalize):
            lg_sb = logit0 if which == "l0" else logit1
            cwt = cwtm0 if which == "l0" else cwtm1
            lg = lg_sb[:, ne * tile:ne * (tile + 1)]
            cwtmp = smallv[:, SV_CW:SV_CW + ne]
            lkey = (which,)
            if which == "l1":
                P.op("v", lambda e, lg=lg: e.tensor_tensor(
                        smallv[:, SV_CW + 32:SV_CW + 32 + ne], lg,
                        bgrow_sb[:, 0:ne], ALU.add),
                     reads=[lkey, ("bgB",)], writes=[("svl1",)])
                lg = smallv[:, SV_CW + 32:SV_CW + 32 + ne]
                lkey = ("svl1",)
            P.op("v", lambda e, lg=lg: e.reduce_max(stats[:, ST_NM:ST_NM + 1], lg,
                                                    AXX, negate=True),
                 reads=[lkey], writes=[("stats", "nm")])
            P.op("s", lambda e, lg=lg: e.activation(cwtmp, lg, AF.Exp,
                                                    bias=stats[:, ST_NM:ST_NM + 1],
                                                    scale=1.0,
                                                    accum_out=stats[:, ST_SQ:ST_SQ + 1]),
                 reads=[lkey, ("stats", "nm")], writes=[("svcw",), ("stats", "sq")])
            P.op("v", lambda e: e.reciprocal(stats[:, ST_IV:ST_IV + 1],
                                             stats[:, ST_SQ:ST_SQ + 1]),
                 reads=[("stats", "sq")], writes=[("stats", "iv")])
            P.op("v", lambda e: e.tensor_scalar(cwtmp, cwtmp,
                                                stats[:, ST_IV:ST_IV + 1], None,
                                                ALU.mult),
                 reads=[("svcw",), ("stats", "iv")], writes=[("svcw",)])
            P.op("v", lambda e: e.reduce_max(stats[:, ST_M1:ST_M1 + 1], cwtmp, AXX),
                 reads=[("svcw",)], writes=[("stats", "m1")])
            P.op("v", lambda e: e.tensor_scalar(ntile[:, 0:ne], cwtmp,
                                                stats[:, ST_M1:ST_M1 + 1], None,
                                                ALU.is_ge),
                 reads=[("svcw",), ("stats", "m1")], writes=[("nt",)])
            P.op("v", lambda e: e.scalar_tensor_tensor(ntile[:, 0:ne], ntile[:, 0:ne],
                                                       -4.0, cwtmp, ALU.mult, ALU.add),
                 reads=[("nt",), ("svcw",)], writes=[("nt",)])
            P.op("v", lambda e: e.reduce_max(stats[:, ST_M2:ST_M2 + 1], ntile[:, 0:ne],
                                             AXX),
                 reads=[("nt",)], writes=[("stats", "m2")])
            P.op("v", lambda e: e.tensor_scalar(ntile[:, 0:ne], cwtmp,
                                                stats[:, ST_M2:ST_M2 + 1], None,
                                                ALU.is_ge),
                 reads=[("svcw",), ("stats", "m2")], writes=[("nt",)])
            P.op("v", lambda e: e.tensor_tensor(cwtmp, cwtmp, ntile[:, 0:ne], ALU.mult),
                 reads=[("svcw",), ("nt",)], writes=[("svcw",)])
            if normalize:
                P.op("v", lambda e: e.tensor_tensor(stats[:, ST_DEN:ST_DEN + 1],
                                                    stats[:, ST_M1:ST_M1 + 1],
                                                    stats[:, ST_M2:ST_M2 + 1], ALU.add),
                     reads=[("stats", "m1"), ("stats", "m2")],
                     writes=[("stats", "den")])
                P.op("v", lambda e: e.tensor_scalar(stats[:, ST_DEN:ST_DEN + 1],
                                                    stats[:, ST_DEN:ST_DEN + 1],
                                                    1e-8, None, ALU.add),
                     reads=[("stats", "den")], writes=[("stats", "den")])
                P.op("v", lambda e: e.reciprocal(stats[:, ST_DEN:ST_DEN + 1],
                                                 stats[:, ST_DEN:ST_DEN + 1]),
                     reads=[("stats", "den")], writes=[("stats", "den")])
                P.op("v", lambda e: e.tensor_scalar(cwtmp, cwtmp,
                                                    stats[:, ST_DEN:ST_DEN + 1], None,
                                                    ALU.mult),
                     reads=[("svcw",), ("stats", "den")], writes=[("svcw",)])
            P.op("v", lambda e, tile=tile: e.tensor_copy(
                    cwt[:, ne * tile:ne * (tile + 1)], cwtmp),
                 reads=[("svcw",)], writes=[("cw" + which, tile)])

        for t_ in range(NT):
            emit_cw("l0", NE0, t_, False)
        if "cw0" in taps:
            P.dma("sb", tap_aps["cw0"], cwtm0[:, :],
                  reads=[("cwl0", t_) for t_ in range(NT)])

        if upto == "cw0":
            return _finish()
        HA, HB = 0, 4096
        SLOT = [0, 12288]
        P.rekey(AUXB, [("aux", "hA"), ("aux", "hB")])
        P.rekey([("arw", "qkv", 0), ("arw", "qkv", 1), ("arw", "qkv", 2),
                 ("arw", "proj")],
                [("arw", "ug", 0), ("arw", "uu", 0), ("arw", "ud", 0),
                 ("arw", "ug", 1), ("arw", "uu", 1), ("arw", "ud", 1)])

        def c_unit_dma(u):
            slot = SLOT[u % 2]
            ch = "wa" if u % 2 == 0 else "wb"
            gv = arw[:, slot:slot + 4096].rearrange("p (k n) -> p k n", k=8)
            uv = arw[:, slot + 4096:slot + 8192].rearrange("p (k n) -> p k n", k=8)
            dv = arw[:, slot + 8192:slot + 12288].rearrange("p (k n) -> p k n", k=4)
            if u < NE0:
                P.dma(ch, gv, Wg0[u].rearrange("(k p) n -> p k n", p=128),
                      writes=[("arw", "ug", u % 2)])
                P.dma(ch, uv, Wu0[u].rearrange("(k p) n -> p k n", p=128),
                      writes=[("arw", "uu", u % 2)])
                P.dma(ch, dv, Wd0[u].rearrange("(k p) n -> p k n", p=128),
                      writes=[("arw", "ud", u % 2)])
            else:
                f = u - NE0
                P.dma(ch, gv,
                      Sg.rearrange("(k p) n -> p k n", p=128)[:, :, 512 * f:512 * (f + 1)],
                      writes=[("arw", "ug", u % 2)])
                P.dma(ch, uv,
                      Su.rearrange("(k p) n -> p k n", p=128)[:, :, 512 * f:512 * (f + 1)],
                      writes=[("arw", "uu", u % 2)])
                P.dma(ch, dv,
                      Sd.rearrange("(k p) n -> p k n", p=128)[:, 4 * f:4 * (f + 1), :],
                      writes=[("arw", "ud", u % 2)])

        def c_unit_compute(u, th):
            slot = SLOT[u % 2]
            hoff = HA if u % 2 == 0 else HB
            hkey = ("aux", "hA" if u % 2 == 0 else "hB")
            gv = arw[:, slot:slot + 4096].rearrange("p (k n) -> p k n", k=8)
            uv = arw[:, slot + 4096:slot + 8192].rearrange("p (k n) -> p k n", k=8)
            dv = arw[:, slot + 8192:slot + 12288].rearrange("p (k n) -> p k n", k=4)
            hv = aux[:, hoff:hoff + 4 * TH].rearrange("p (m n) -> p m n", m=4)
            for m in range(4):
                for n2 in range(NW5):
                    def gmm(e, m=m, n2=n2, th=th, gv=gv):
                        for k in range(8):
                            ins = e.matmul(ps[:, 0:W5],
                                           gv[:, k, 128 * m:128 * (m + 1)],
                                           xkv[:, k, 8 + TH * th + W5 * n2:
                                               8 + TH * th + W5 * (n2 + 1)],
                                           start=(k == 0), stop=(k == 7))
                        return ins
                    P.op("t", gmm, reads=[("arw", "ug", u % 2), ("xnT",)],
                         writes=[("ps", 0)])
                    P.op("s", lambda e: e.activation(tgt[:, 0:W5], ps[:, 0:W5],
                                                     AF.Silu),
                         reads=[("ps", 0)], writes=[("tg",)])

                    def umm(e, m=m, n2=n2, th=th, uv=uv):
                        for k in range(8):
                            ins = e.matmul(ps[:, 512:512 + W5],
                                           uv[:, k, 128 * m:128 * (m + 1)],
                                           xkv[:, k, 8 + TH * th + W5 * n2:
                                               8 + TH * th + W5 * (n2 + 1)],
                                           start=(k == 0), stop=(k == 7))
                        return ins
                    P.op("t", umm, reads=[("arw", "uu", u % 2), ("xnT",)],
                         writes=[("ps", 1)])
                    hslice = hv[:, m, W5 * n2:W5 * (n2 + 1)]
                    P.op("v", lambda e, hslice=hslice: e.tensor_tensor(
                            hslice, tgt[:, 0:W5], ps[:, 512:512 + W5], ALU.mult),
                         reads=[("tg",), ("ps", 1)], writes=[hkey])
            for sl in range(NSL):
                for n in range(2):
                    bk = 2 + (sl * 2 + n) % 2

                    def dmm(e, sl=sl, n=n, bk=bk, hv=hv, dv=dv):
                        for k in range(4):
                            ins = e.matmul(ps[:, 512 * bk:512 * (bk + 1)],
                                           hv[:, k, 128 * sl:128 * (sl + 1)],
                                           dv[:, k, 512 * n:512 * (n + 1)],
                                           start=(k == 0), stop=(k == 3))
                        return ins
                    P.op("t", dmm, reads=[hkey, ("arw", "ud", u % 2)],
                         writes=[("ps", bk)])
                    acs = big[:, 1024 * sl + 512 * n:1024 * sl + 512 * (n + 1)]
                    t_ = NSL * th + sl
                    if u < NE0:
                        cwc = cwtm0[:, NE0 * t_ + u:NE0 * t_ + u + 1]
                        P.op("v", lambda e, acs=acs, bk=bk, cwc=cwc:
                                e.scalar_tensor_tensor(
                                    acs, ps[:, 512 * bk:512 * (bk + 1)], cwc, acs,
                                    ALU.mult, ALU.add),
                             reads=[("ps", bk), ("acc", sl, n), ("cwl0", t_)],
                             writes=[("acc", sl, n)])
                    else:
                        P.op("v", lambda e, acs=acs, bk=bk: e.tensor_tensor(
                                acs, acs, ps[:, 512 * bk:512 * (bk + 1)], ALU.add),
                             reads=[("ps", bk), ("acc", sl, n)],
                             writes=[("acc", sl, n)])

        P.rekey([("xt", j_) for j_ in range(4)] + [("nt", j_) for j_ in range(4)]
                + [("sbs", 0), ("sbs", 1)],
                [("acc", sl, n) for sl in range(NSL) for n in range(2)])
        for th in range(2):
            for sl in range(NSL):
                for n in range(2):
                    P.op("v", lambda e, sl=sl, n=n: e.memset(
                            big[:, 1024 * sl + 512 * n:1024 * sl + 512 * (n + 1)], 0.0),
                         writes=[("acc", sl, n)])
            c_unit_dma(0)
            c_unit_dma(1)
            for u in range(18):
                c_unit_compute(u, th)
                if u + 2 < 18:
                    c_unit_dma(u + 2)
            for tt in range(NSL):
                t_ = NSL * th + tt
                P.op("s", lambda e, tt=tt: e.activation(
                        ntile[:, :], big[:, 1024 * tt:1024 * (tt + 1)], AF.Silu),
                     reads=[("acc", tt, 0), ("acc", tt, 1)], writes=[("nt",)])
                P.dma("la", xtile[:, :], x_in[8 + 128 * t_:8 + 128 * (t_ + 1), :],
                      writes=[("xt",)])
                P.dma("lb", stile[:, :], s_scr.ap()[128 * t_:128 * (t_ + 1), :],
                      reads=[("scr_s", t_)], writes=[("st",)])
                P.op("v", lambda e: e.tensor_tensor(ntile[:, :], stile[:, :],
                                                    ntile[:, :], ALU.mult),
                     reads=[("st",), ("nt",)], writes=[("nt",)])
                P.op("v", lambda e: e.tensor_tensor(xtile[:, :], xtile[:, :],
                                                    ntile[:, :], ALU.add),
                     reads=[("nt",), ("xt",)], writes=[("xt",)])
                P.dma("sa" if t_ % 2 == 0 else "sb",
                      x1_scr.ap()[128 * t_:128 * (t_ + 1), :], xtile[:, :],
                      reads=[("xt",)], writes=[("scr_x1", t_)])
        if "acc0" in taps:
            P.dma("sb", tap_aps["acc0"], big[:, 0:NSL * 1024],
                  reads=[("acc", sl, n) for sl in range(NSL) for n in range(2)])
        if "x1" in taps:
            P.dma("sa", tap_aps["x1"], x1_scr.ap(),
                  reads=[("scr_x1", t_) for t_ in range(NT)])

        if upto == "C":
            return _finish()
        P.rekey([("aux", "hA"), ("aux", "hB")], [("aux", "xf", 0), ("aux", "xf", 1), ("aux", "xf", 2), ("aux", "xf", 3)])
        P.rekey([("acc", sl, n) for sl in range(NSL) for n in range(2)],
                [("xt", j_) for j_ in range(4)] + [("nt", j_) for j_ in range(4)])
        ln_pass([(128, x1_scr.ap()[128 * t_:128 * (t_ + 1), :], 8 + 128 * t_,
                  ("l1", NE1), t_ * NE1) for t_ in range(NT)])
        if "logits1" in taps:
            P.dma("sb", tap_aps["logits1"], logit1[:, 0:NT * NE1], reads=[("l1",)])

        P.dma("la", rows2[0:1, 128:128 + NE1], bg1.unsqueeze(0), writes=[("r2", "bg")])
        P.op("t", lambda e: e.matmul(ps[:, 1024:1024 + NE1],
                                     rows2[0:1, 0:128],
                                     rows2[0:1, 128:128 + NE1],
                                     start=True, stop=True),
             reads=[("r2", "ones"), ("r2", "bg")], writes=[("ps", 2)])
        P.op("v", lambda e: e.tensor_copy(bgrow_sb, ps[:, 1024:1024 + NE1]),
             reads=[("ps", 2)], writes=[("bgB",)])

        for t_ in range(NT):
            emit_cw("l1", NE1, t_, True)
        if "cw1" in taps:
            P.dma("sb", tap_aps["cw1"], cwtm1[:, :],
                  reads=[("cwl1", t_) for t_ in range(NT)])

        if upto == "ln1":
            return _finish()
        P.dma("la", cwrows[R_CP:R_CP + 16, 0:DI], Cp, writes=[("cp",)])
        P.dma("lb", rows2[32:48, 0:128], bin_.rearrange("(c p) -> c p", p=128),
              reads=[("r2", "stage")], writes=[("r2", "stage")])
        P.op("t", lambda e: e.transpose(ps[:, 3584:3584 + 16],
                                        rows2[32:48, 0:128],
                                        identf_sb[32:48, 32:48]),
             reads=[("r2", "stage"), ("idf",)], writes=[("ps", 7)])
        P.op("v", lambda e: e.tensor_copy(smallv[:, SV_BIN:SV_BIN + 16],
                                          ps[:, 3584:3584 + 16]),
             reads=[("ps", 7)], writes=[("sv", "bin")])
        P.dma("lb", rows2[32:48, 0:128], Dp.rearrange("(c p) -> c p", p=128),
              reads=[("r2", "stage")], writes=[("r2", "stage")])
        P.op("t", lambda e: e.transpose(ps[:, 3584:3584 + 16],
                                        rows2[32:48, 0:128],
                                        identf_sb[32:48, 32:48]),
             reads=[("r2", "stage"), ("idf",)], writes=[("ps", 7)])
        P.op("v", lambda e: e.tensor_copy(smallv[:, SV_DP:SV_DP + 16],
                                          ps[:, 3584:3584 + 16]),
             reads=[("ps", 7)], writes=[("sv", "dp")])
        for c in range(16):
            P.dma("la" if c % 2 == 0 else "lb", xtile[:, 0:16],
                  Bp[128 * c:128 * (c + 1), :], reads=[("xt",)], writes=[("xt",)])
            P.op("t", lambda e, c=c: e.transpose(ps[:, 3584:3584 + 16],
                                                 cwrows[R_CP:R_CP + 16,
                                                        128 * c:128 * (c + 1)],
                                                 identf_sb[32:48, 32:48]),
                 reads=[("cp",), ("idf",)], writes=[("ps", 7)])
            P.op("v", lambda e: e.tensor_tensor(ntile[:, 0:16], xtile[:, 0:16],
                                                ps[:, 3584:3584 + 16], ALU.mult),
                 reads=[("xt",), ("ps", 7)], writes=[("nt",)])
            P.op("v", lambda e, c=c: e.reduce_sum(smallv[:, SV_R + c:SV_R + c + 1],
                                                  ntile[:, 0:16], AXX),
                 reads=[("nt",)], writes=[("sv", "r", c)])
        if "rvec" in taps:
            P.dma("sb", tap_aps["rvec"], smallv[:, SV_R:SV_R + 16],
                  reads=[("sv", "r", c) for c in range(16)])

        bias_bcast(bout, "la")

        P.rekey([("xt", j_) for j_ in range(4)] + [("nt", j_) for j_ in range(4)],
                [("bd", "xi"), ("bd", "c"), ("bd", "y"), ("bd", "dec")])
        xi_sb = big[:, 0:T]
        csc_sb = big[:, 2048:2048 + T]
        y_sb = big[:, 4096:4096 + T]
        dec_sb = big[:, 6144:6144 + T]
        P.op("v", lambda e: e.memset(dec_sb, DECAY), writes=[("bd", "dec")])

        P.rekey([("arw", "ug", 0), ("arw", "ug", 1),
                 ("arw", "uu", 0), ("arw", "uu", 1), ("arw", "ud", 0),
                 ("arw", "ud", 1)],
                [("arw", "win"), ("arw", "wout")])
        win_v = arw[:, 0:16384].rearrange("p (k n) -> p k n", k=8)
        wout_v = arw[:, 16384:32768].rearrange("p (k n) -> p k n", k=16)
        P.dma("wa", win_v[:, :, 0:1024],
              Win.rearrange("(k p) n -> p k n", p=128)[:, :, 0:1024],
              writes=[("arw", "win")])
        P.dma("wb", win_v[:, :, 1024:2048],
              Win.rearrange("(k p) n -> p k n", p=128)[:, :, 1024:2048],
              writes=[("arw", "win")])
        for hw_ in range(2):
            P.dma("wa", wout_v[:, 8 * hw_:8 * (hw_ + 1), :],
                  Wout.rearrange("(k p) n -> p k n", p=128)[:, 8 * hw_:8 * (hw_ + 1), :],
                  writes=[("arw", "wout")])

        NXI = T // 512
        for c in range(16):
            po_ = 2048 * (c % 2)
            bnks = [4 * (c % 2) + j for j in range(NXI)]

            def xmm(e, c=c, po_=po_):
                for n2 in range(NXI):
                    for k in range(8):
                        ins = e.matmul(ps[:, po_ + 512 * n2:po_ + 512 * (n2 + 1)],
                                       win_v[:, k, 128 * c:128 * (c + 1)],
                                       xkv[:, k, 8 + 512 * n2:8 + 512 * (n2 + 1)],
                                       start=(k == 0), stop=(k == 7))
                return ins
            P.op("t", xmm, reads=[("arw", "win"), ("xnT",)],
                 writes=[("ps", bnk) for bnk in bnks])
            P.op("s", lambda e, c=c, po_=po_: e.activation(
                    xi_sb, ps[:, po_:po_ + T], AF.Silu,
                    bias=smallv[:, SV_BIN + c:SV_BIN + c + 1], scale=1.0),
                 reads=[("ps", bnk) for bnk in bnks] + [("sv", "bin")],
                 writes=[("bd", "xi")])
            if "xi" in taps:
                P.dma("sb", tap_aps["xi"][c], xi_sb, reads=[("bd", "xi")])
            P.op("v", lambda e: e.tensor_tensor_scan(csc_sb, dec_sb, xi_sb, 0.0,
                                                     ALU.mult, ALU.add),
                 reads=[("bd", "dec"), ("bd", "xi")], writes=[("bd", "c")])
            P.op("v", lambda e, c=c: e.tensor_copy(smallv[:, SV_FR + c:SV_FR + c + 1],
                                                   csc_sb[:, T - 1:T]),
                 reads=[("bd", "c")], writes=[("sv", "f", c)])
            P.op("v", lambda e, c=c: e.tensor_scalar(y_sb, xi_sb,
                                                     smallv[:, SV_DP + c:
                                                            SV_DP + c + 1],
                                                     None, ALU.mult),
                 reads=[("bd", "xi"), ("sv", "dp")], writes=[("bd", "y")])
            P.op("v", lambda e, c=c: e.scalar_tensor_tensor(
                    y_sb, csc_sb, smallv[:, SV_R + c:SV_R + c + 1], y_sb,
                    ALU.mult, ALU.add),
                 reads=[("bd", "c"), ("sv", "r", c), ("bd", "y")],
                 writes=[("bd", "y")])
            P.dma("sa" if c % 2 == 0 else "sb", yp_scr.ap()[c], y_sb,
                  reads=[("bd", "y")], writes=[("scr_yp", c)])
        if "ypart" in taps:
            P.dma("sa", tap_aps["ypart"], yp_scr.ap(),
                  reads=[("scr_yp", c) for c in range(16)])

        P.op("t", lambda e: e.transpose(ps[0:16, 1024:1024 + 128],
                                        smallv[:, SV_FR:SV_FR + 16], identf_sb[:, :]),
             reads=[("sv", "f", c) for c in range(16)] + [("idf",)],
             writes=[("ps", 2)])
        P.op("v", lambda e: e.tensor_copy(cwrows[R_FT:R_FT + 16, 0:128],
                                          ps[0:16, 1024:1024 + 128]),
             reads=[("ps", 2)], writes=[("fT",)])
        P.dma("cc", cc_in.ap().rearrange("a (c p) -> (a c) p", p=128),
              cwrows[R_FT:R_FT + 16, 0:128], reads=[("fT",)], writes=[("ccin",)])
        P.op("g", lambda e: e.collective_compute(
                "AllGather", ALU.bypass, replica_groups=[list(range(world))],
                ins=[cc_in.ap()], outs=[cc_out.ap()]),
             reads=[("ccin",)], writes=[("ccout",)], sem="ccc")
        P.dma("cc", cwrows[R_FALL:R_FALL + world, 0:DI], cc_out.ap(),
              reads=[("ccout",)], writes=[("Fall",)])

        def fsel_mm(e):
            for n2 in range(DI // 512):
                ins = e.matmul(ps[0:1, 512 * n2:512 * (n2 + 1)], fsel_sb[0:world, :],
                               cwrows[R_FALL:R_FALL + world, 512 * n2:512 * (n2 + 1)],
                               start=True, stop=True)
            return ins
        P.op("t", fsel_mm, reads=[("Fall",), ("fsel",)],
             writes=[("ps", 0), ("ps", 1), ("ps", 2), ("ps", 3)])
        frow = auxf[0:1, 3584:3584 + DI]
        P.rekey([("aux", "xf", 0), ("aux", "xf", 1), ("aux", "xf", 2), ("aux", "xf", 3)], [("aux", "dpow"), ("aux", "ybf", 0),
                                  ("aux", "ybf", 1), ("aux", "ybL", 0),
                                  ("aux", "ybL", 1), ("aux", "frow")])
        P.op("v", lambda e: e.tensor_copy(frow, ps[0:1, 0:DI]),
             reads=[("ps", 0), ("ps", 1), ("ps", 2), ("ps", 3)],
             writes=[("aux", "frow")])

        P.dma("sb", fr_scr.ap(), frow, reads=[("aux", "frow")], writes=[("frs",)])
        P.dma("lb", rows2[32:48, 0:128],
              fr_scr.ap().rearrange("a (c p) -> (a c) p", p=128),
              reads=[("frs",), ("r2", "stage")], writes=[("r2", "stage")])
        P.op("t", lambda e: e.transpose(ps[:, 3584:3584 + 16],
                                        rows2[32:48, 0:128],
                                        identf_sb[32:48, 32:48]),
             reads=[("r2", "stage"), ("idf",)], writes=[("ps", 7)])
        P.op("v", lambda e: e.tensor_copy(smallv[:, SV_FP:SV_FP + 16],
                                          ps[:, 3584:3584 + 16]),
             reads=[("ps", 7)], writes=[("sv", "fp")])
        if "fprev" in taps:
            P.dma("sb", tap_aps["fprev"], smallv[:, SV_FP:SV_FP + 16],
                  reads=[("sv", "fp")])
        P.op("v", lambda e: e.tensor_tensor(smallv[:, SV_FP:SV_FP + 16],
                                            smallv[:, SV_FP:SV_FP + 16],
                                            smallv[:, SV_R:SV_R + 16], ALU.mult),
             reads=[("sv", "fp")] + [("sv", "r", c) for c in range(16)],
             writes=[("sv", "fp")])

        dpow_sb = auxf[:, 512:512 + T]
        ybfs = [aux[:, 0:1024], aux[:, 12288:13312]]
        ybLs = [auxf[:, 512 + T:512 + T + TH], auxf[:, 6656:6656 + TH]]
        P.dma("la", dpow_sb, dpow_in, writes=[("aux", "dpow")])

        def yb_load(th, c):
            p = c % 2
            P.dma("la" if c % 2 == 0 else "lb", ybLs[p],
                  yp_scr.ap()[c][:, TH * th:TH * (th + 1)],
                  reads=[("scr_yp", c), ("aux", "ybL", p)],
                  writes=[("aux", "ybL", p)])
            P.op("v", lambda e, c=c, th=th, p=p: e.scalar_tensor_tensor(
                    ybfs[p][:, 0:TH], dpow_sb[:, TH * th:TH * (th + 1)],
                    smallv[:, SV_FP + c:SV_FP + c + 1], ybLs[p],
                    ALU.mult, ALU.add),
                 reads=[("aux", "dpow"), ("sv", "fp"), ("aux", "ybL", p),
                        ("aux", "ybf", p)],
                 writes=[("aux", "ybf", p)])

        P.rekey([("bd", "xi"), ("bd", "c"), ("bd", "y"), ("bd", "dec")],
                [("s1acc", g) for g in range(NSL)])
        for th in range(2):
            for sl in range(NSL):
                gsl = th * NSL + sl
                P.op("v", lambda e, sl=sl: e.tensor_copy(
                        big[:, 1024 * sl:1024 * (sl + 1)], bb[:, :]),
                     reads=[("bb",)], writes=[("s1acc", sl)])
            yb_load(th, 0)
            for c in range(16):
                p = c % 2
                for sl in range(NSL):
                    for n in range(2):
                        bk = 4 + (sl * 2 + n) % 4
                        gsl = th * NSL + sl

                        def wmm(e, c=c, sl=sl, n=n, bk=bk, p=p):
                            return e.matmul(ps[:, 512 * bk:512 * (bk + 1)],
                                            ybfs[p][:, 128 * sl:128 * (sl + 1)],
                                            wout_v[:, c, 512 * n:512 * (n + 1)],
                                            start=True, stop=True)
                        P.op("t", wmm, reads=[("aux", "ybf", p), ("arw", "wout")],
                             writes=[("ps", bk)])
                        acs = big[:, 1024 * sl + 512 * n:1024 * sl + 512 * (n + 1)]
                        P.op("v", lambda e, acs=acs, bk=bk: e.tensor_tensor(
                                acs, acs, ps[:, 512 * bk:512 * (bk + 1)], ALU.add),
                             reads=[("ps", bk), ("s1acc", sl)],
                             writes=[("s1acc", sl)])
                        if sl == 0 and n == 0 and c + 1 < 16:
                            yb_load(th, c + 1)
            for sl in range(NSL):
                t_ = th * NSL + sl
                P.dma("sa" if t_ % 2 == 0 else "sb",
                      s_scr.ap()[128 * t_:128 * (t_ + 1), :],
                      big[:, 1024 * sl:1024 * (sl + 1)],
                      reads=[("s1acc", sl)], writes=[("scr_s", t_)])
        if "s1" in taps:
            P.dma("sa", tap_aps["s1"], s_scr.ap(),
                  reads=[("scr_s", t_) for t_ in range(NT)])

        if upto == "D":
            return _finish()
        P.rekey([("arw", "win"), ("arw", "wout")],
                [("arw", "w1", 0), ("arw", "w1", 1), ("arw", "w1", 2),
                 ("arw", "w1", 3), ("arw", "w2")])
        P.rekey([("aux", "dpow"), ("aux", "ybf", 0), ("aux", "ybf", 1),
                 ("aux", "ybL", 0), ("aux", "ybL", 1), ("aux", "frow")],
                [("hE", k) for k in range(16)])
        P.rekey([("s1acc", g) for g in range(NSL)],
                [("eacc", sl, n) for sl in range(NSL) for n in range(2)])
        w1s = [arw[:, 4096 * j:4096 * (j + 1)].rearrange("p (k n) -> p k n", k=8)
               for j in range(4)]
        w2v = arw[:, 16384:32768].rearrange("p (k n) -> p k n", k=16)
        hEv = aux[:, 0:16 * TH].rearrange("p (k n) -> p k n", k=16)

        for th in range(2):
            for sl in range(NSL):
                for n in range(2):
                    P.op("v", lambda e, sl=sl, n=n: e.memset(
                            big[:, 1024 * sl + 512 * n:1024 * sl + 512 * (n + 1)], 0.0),
                         writes=[("eacc", sl, n)])
            for eix in range(NE1):
                P.dma("la", rows2[32:64, 0:128],
                      b1[eix].rearrange("(c p) -> c p", p=128),
                      reads=[("r2", "stage")], writes=[("r2", "stage")])
                P.op("t", lambda e: e.transpose(ps[:, 3584:3584 + 32],
                                                rows2[32:64, 0:128],
                                                identf_sb[32:64, 32:64]),
                     reads=[("r2", "stage"), ("idf",)], writes=[("ps", 7)])
                P.op("v", lambda e: e.tensor_copy(smallv[:, SV_B1:SV_B1 + 32],
                                                  ps[:, 3584:3584 + 32]),
                     reads=[("ps", 7)], writes=[("sv", "b1")])
                bias_bcast(b2[eix], "lb")
                for hk in range(2):
                    for cc4 in range(4):
                        P.dma("wa", w1s[cc4],
                              W1[eix].rearrange("(k p) n -> p k n", p=128)
                              [:, :, 2048 * hk + 512 * cc4:2048 * hk + 512 * (cc4 + 1)],
                              writes=[("arw", "w1", cc4)])
                    for hw_ in range(2):
                        P.dma("wa", w2v[:, 8 * hw_:8 * (hw_ + 1), :],
                              W2[eix].rearrange("(k p) n -> p k n", p=128)
                              [:, 16 * hk + 8 * hw_:16 * hk + 8 * (hw_ + 1), :],
                              writes=[("arw", "w2")])
                    for cc4 in range(4):
                        cslot = cc4
                        for m in range(4):
                            mg = 4 * cc4 + m
                            for n2 in range(NW5):
                                bk = (0, 1, 6)[(2 * m + n2) % 3]

                                def hmm(e, m=m, n2=n2, bk=bk, cslot=cslot, th=th):
                                    for k in range(8):
                                        ins = e.matmul(
                                            ps[:, 512 * bk:512 * bk + W5],
                                            w1s[cslot][:, k, 128 * m:128 * (m + 1)],
                                            xkv[:, k, 8 + TH * th + W5 * n2:
                                                8 + TH * th + W5 * (n2 + 1)],
                                            start=(k == 0), stop=(k == 7))
                                    return ins
                                P.op("t", hmm, reads=[("arw", "w1", cslot), ("xnT",)],
                                     writes=[("ps", bk)])
                                hsl = hEv[:, mg, W5 * n2:W5 * (n2 + 1)]
                                bcol = SV_B1 + 16 * hk + mg
                                P.op("s", lambda e, hsl=hsl, bk=bk, bcol=bcol:
                                        e.activation(
                                        hsl, ps[:, 512 * bk:512 * bk + W5],
                                        AF.Gelu_apprx_tanh,
                                        bias=smallv[:, bcol:bcol + 1], scale=1.0),
                                     reads=[("ps", bk), ("sv", "b1")],
                                     writes=[("hE", mg)])
                    for sl in range(NSL):
                        for n in range(2):
                            bk = 2 + (sl * 2 + n) % 2

                            def emm(e, sl=sl, n=n, bk=bk):
                                for k in range(16):
                                    ins = e.matmul(ps[:, 512 * bk:512 * (bk + 1)],
                                                   hEv[:, k, 128 * sl:128 * (sl + 1)],
                                                   w2v[:, k, 512 * n:512 * (n + 1)],
                                                   start=(k == 0), stop=(k == 15))
                                return ins
                            P.op("t", emm,
                                 reads=[("hE", k) for k in range(16)] + [("arw", "w2")],
                                 writes=[("ps", bk)])
                            t_ = NSL * th + sl
                            acs = big[:, 1024 * sl + 512 * n:1024 * sl + 512 * (n + 1)]
                            cwc = cwtm1[:, NE1 * t_ + eix:NE1 * t_ + eix + 1]
                            P.op("v", lambda e, acs=acs, bk=bk, cwc=cwc:
                                    e.scalar_tensor_tensor(
                                        acs, ps[:, 512 * bk:512 * (bk + 1)], cwc, acs,
                                        ALU.mult, ALU.add),
                                 reads=[("ps", bk), ("eacc", sl, n), ("cwl1", t_)],
                                 writes=[("eacc", sl, n)])
                for sl in range(NSL):
                    t_ = NSL * th + sl
                    cwc = cwtm1[:, NE1 * t_ + eix:NE1 * t_ + eix + 1]
                    for n in range(2):
                        acs = big[:, 1024 * sl + 512 * n:1024 * sl + 512 * (n + 1)]
                        P.op("v", lambda e, acs=acs, n=n, cwc=cwc:
                                e.scalar_tensor_tensor(
                                    acs, bb[:, 512 * n:512 * (n + 1)], cwc, acs,
                                    ALU.mult, ALU.add),
                             reads=[("bb",), ("eacc", sl, n), ("cwl1", t_)],
                             writes=[("eacc", sl, n)])
            for tt in range(NSL):
                t_ = NSL * th + tt
                P.op("s", lambda e, tt=tt: e.activation(
                        ntile[:, :], big[:, 1024 * tt:1024 * (tt + 1)], AF.Silu),
                     reads=[("eacc", tt, 0), ("eacc", tt, 1)], writes=[("nt",)])
                P.dma("la", xtile[:, :], x1_scr.ap()[128 * t_:128 * (t_ + 1), :],
                      reads=[("scr_x1", t_)], writes=[("xt",)])
                P.dma("lb", stile[:, :], s_scr.ap()[128 * t_:128 * (t_ + 1), :],
                      reads=[("scr_s", t_)], writes=[("st",)])
                P.op("v", lambda e: e.tensor_tensor(ntile[:, :], stile[:, :],
                                                    ntile[:, :], ALU.mult),
                     reads=[("st",), ("nt",)], writes=[("nt",)])
                P.op("v", lambda e: e.tensor_tensor(xtile[:, :], xtile[:, :],
                                                    ntile[:, :], ALU.add),
                     reads=[("nt",), ("xt",)], writes=[("xt",)])
                P.dma("sa" if t_ % 2 == 0 else "sb",
                      out[128 * t_:128 * (t_ + 1), :], xtile[:, :],
                      reads=[("xt",)], writes=[("outw", t_)])

        return _finish()



def make_masks(q):
    i = np.arange(128)[:, None]
    j = np.arange(144)[None, :]
    interior = (j >= i) & (j <= i + 16)
    first = ((i <= 8) & (j >= 8) & (j <= i + 16)) | ((i > 8) & interior)
    last = ((128 - 1 - i <= 7) & (j >= i) & (j <= 135)) | ((i < 120) & interior)
    v1 = first if q == 0 else interior
    v2 = last if q == 3 else interior
    m = np.full((3, 128, 144), NEG, np.float32)
    for v, msk in enumerate((interior, v1, v2)):
        m[v][msk] = 0.0
    full = np.tile(m, (1, 1, 8))
    return full.astype(ml_dtypes.bfloat16)


def make_core_inputs(inputs, T=2048):
    x = np.asarray(inputs["L0_attn_x"])
    B, L, _ = x.shape
    percore = L // 4
    identf = np.eye(128, dtype=np.float32)
    identb = np.eye(128, dtype=ml_dtypes.bfloat16)
    dpow = np.tile((DECAY ** (np.arange(T, dtype=np.float64) + 1)
                    ).astype(np.float32)[None, :], (128, 1))
    wkeys = [k for k in inputs if k != "L0_attn_x"]
    in_maps = []
    for c in range(8):
        bsel, q = c // 4, c % 4
        lo, hi = q * percore, (q + 1) * percore
        xs = np.zeros((T + 16, D), np.float32)
        src_lo, src_hi = max(0, lo - 8), min(L, hi + 8)
        xs[8 - (lo - src_lo):8 + (src_hi - lo)] = x[bsel, src_lo:src_hi]
        fsel = np.zeros((8, 1), np.float32)
        if q > 0:
            fsel[c - 1, 0] = 1.0
        im = {k: np.ascontiguousarray(np.asarray(inputs[k])) for k in wkeys}
        im["x"] = xs
        im["identf"] = identf
        im["identb"] = identb
        im["maskp"] = np.ascontiguousarray(make_masks(q))
        im["dpow"] = dpow
        im["fsel"] = fsel
        in_maps.append(im)
    return in_maps


def assemble(results, T=2048):
    outs = [np.asarray(r["out"]) for r in results]
    return np.stack([np.concatenate(outs[0:4], 0),
                     np.concatenate(outs[4:8], 0)], 0)



def kernel(**inputs):
    from concourse.bass_utils import run_bass_kernel_spmd
    nc, _meta = build(T=2048)
    in_maps = make_core_inputs(inputs, T=2048)
    res = run_bass_kernel_spmd(nc, in_maps, core_ids=list(range(8)))
    return assemble(res.results, T=2048)
